# version 09
# baseline (speedup 1.0000x reference)
import math
import numpy as np
import ml_dtypes
from contextlib import ExitStack

import concourse.bass as bass
import concourse.mybir as mybir
from concourse.alu_op_type import AluOpType

F32 = mybir.dt.float32
BF16 = mybir.dt.bfloat16
AF = mybir.ActivationFunctionType
BF = ml_dtypes.bfloat16
ENGINES = ("tensor", "vector", "scalar", "gpsimd", "sync")
DMA_SEMS = {"vector": ["vd0"], "scalar": ["sd0", "sd1"],
            "gpsimd": ["gd0", "gd1", "gd2", "gd3"],
            "sync": ["yd0", "yd1", "yd2", "yd3"]}
ALL_SEMS = ["pe", "ve", "ac", "gp", "sy"] + [s for l in DMA_SEMS.values() for s in l]
ENG_SEM = {"tensor": "pe", "vector": "ve", "scalar": "ac", "gpsimd": "gp", "sync": "sy"}


class Ev:
    __slots__ = ("eng", "sem", "idx", "by", "used", "val")

    def __init__(self, eng, sem, idx, by):
        self.eng, self.sem, self.idx, self.by = eng, sem, idx, by
        self.used = False
        self.val = None


class Buf:
    __slots__ = ("w", "r", "name")

    def __init__(self, name=""):
        self.w = None
        self.r = []
        self.name = name


class Sched:
    def __init__(self):
        self.recs = {e: [] for e in ENGINES}
        self.dma_rot = {e: 0 for e in ENGINES}
        self.last_dma = {}
        self.tail = {e: None for e in ENGINES}
        self.pending = {e: [] for e in ENGINES}

    def barrier(self):
        evs = [e for e in self.tail.values() if e is not None]
        evs += [e for e in self.last_dma.values() if e is not None]
        for eng in ENGINES:
            self.pending[eng] = list(evs)

    def op(self, eng, fn, reads=(), writes=(), waits=(), dma=False, inc_by=None):
        if dma:
            sl = DMA_SEMS[eng]
            sem = sl[self.dma_rot[eng] % len(sl)]
            self.dma_rot[eng] += 1
        else:
            sem = ENG_SEM[eng]
        by = inc_by if inc_by is not None else (16 if dma else 1)
        ev = Ev(eng, sem, len(self.recs[eng]), by)
        wl = []
        if dma:
            ev.used = True
            prev = self.last_dma.get(sem)
            if prev is not None:
                wl.append(prev)
            self.last_dma[sem] = ev
        for b in reads:
            if b.w is not None:
                wl.append(b.w)
        for b in writes:
            if b.w is not None:
                wl.append(b.w)
            wl.extend(b.r)
        for e in waits:
            wl.append(e)
        seen, wl2 = set(), []
        for e in wl:
            if e is None or id(e) in seen:
                continue
            seen.add(id(e))
            wl2.append(e)
        if self.pending[eng]:
            for e in self.pending[eng]:
                if id(e) not in seen and e is not ev:
                    seen.add(id(e))
                    wl2.append(e)
            self.pending[eng] = []
        self.recs[eng].append((fn, wl2, ev))
        self.tail[eng] = ev
        for b in reads:
            b.r.append(ev)
        for b in writes:
            b.w = ev
            b.r = []
        return ev

    def mark(self, *evs):
        for e in evs:
            if e is not None:
                e.used = True

    def finalize(self, nc, block, sems):
        for eng in ENGINES:
            cnt = {}
            for fn, wl, ev in self.recs[eng]:
                for e in wl:
                    e.used = True
        for eng in ENGINES:
            cnt = {}
            for fn, wl, ev in self.recs[eng]:
                if ev.used:
                    cnt[ev.sem] = cnt.get(ev.sem, 0) + ev.by
                    ev.val = cnt[ev.sem]
        def make(eng):
            def emit(h):
                hw = {}
                for fn, wl, ev in self.recs[eng]:
                    for e in wl:
                        if e.val is None:
                            continue
                        key = e.sem
                        if hw.get(key, 0) >= e.val:
                            continue
                        h.wait_ge(sems[e.sem], e.val)
                        hw[key] = e.val
                    inst = fn(h)
                    if ev.used and ev.val is not None:
                        assert inst is not None, f"{eng} op needs inc but returned None"
                        inst.then_inc(sems[ev.sem], ev.by)
            return emit
        for eng in ENGINES:
            if not self.recs[eng]:
                continue
            getattr(block, eng)(make(eng))


class Cfg:
    def __init__(self, NH=16, S=4096, T=2048, QT=512, n_cores=8):
        self.NH = NH
        self.HC = NH
        self.H = NH * 128
        self.S = S
        self.T = T
        self.QT = QT
        self.KC = S // 128
        self.NQT = T // QT
        self.E, self.I = 8, 16
        self.EI = 128
        self.SCALE = 1.0 / math.sqrt(128.0)
        self.VD = min(512, self.H)
        self.NVD = self.H // self.VD
        self.TB = S // 128
        self.TBo = T // 128
        self.n_cores = n_cores
        self.EPS = 1e-5
        self.TW = min(2 * QT, T)
        self.NTW = T // self.TW
        self.JW = min(512, self.TW)
        self.JT = self.TW // self.JW


def build(cfg: Cfg, debug=False):
    nc = bass.Bass("TRN2", target_bir_lowering=False)
    K = Sched()
    H, S, T, QT, NH, HC, KC = cfg.H, cfg.S, cfg.T, cfg.QT, cfg.NH, cfg.HC, cfg.KC
    NQT, EI, VD, NVD, TB = cfg.NQT, cfg.EI, cfg.VD, cfg.NVD, cfg.TB
    TW, NTW, JW, JT = cfg.TW, cfg.NTW, cfg.JW, cfg.JT
    SC = cfg.SCALE
    IH = 1.0 / H

    xT = nc.dram_tensor("xT", [H, T], F32, kind="ExternalInput")
    wq_h = nc.dram_tensor("wq", [NH, HC * 128, 128], F32, kind="ExternalInput")
    wk_h = nc.dram_tensor("wk", [NH, HC * 128, 128], F32, kind="ExternalInput")
    wv_h = nc.dram_tensor("wv", [NVD, HC * 128, VD], F32, kind="ExternalInput")
    wo_h = nc.dram_tensor("wo", [HC, HC * 128, 128], F32, kind="ExternalInput")
    bqk_h = nc.dram_tensor("bqk", [128, 2 * NH], F32, kind="ExternalInput")
    bo_h = nc.dram_tensor("bo", [128, HC], F32, kind="ExternalInput")
    bv_h = nc.dram_tensor("bv", [1, H], F32, kind="ExternalInput")
    wr_h = nc.dram_tensor("wr", [128, HC * 8], F32, kind="ExternalInput")
    wd_h = nc.dram_tensor("wd", [128, HC * EI], F32, kind="ExternalInput")
    wu_h = nc.dram_tensor("wu", [EI, H], F32, kind="ExternalInput")
    cst_h = nc.dram_tensor("cst", [128, 256], BF16, kind="ExternalInput")
    cstf_h = nc.dram_tensor("cstf", [128, 128], F32, kind="ExternalInput")
    outT = nc.dram_tensor("outT", [H, T], F32, kind="ExternalOutput")

    _scr_kind = "ExternalOutput" if debug else "Internal"
    TBo = cfg.TBo
    qscr = nc.dram_tensor("qscr", [NH, 128, T], BF16, kind=_scr_kind)
    kag_i = nc.dram_tensor("kag_i", [NH, 128, T], BF16)
    kag_o = nc.dram_tensor("kag_o", [2, NH, 128, T], BF16, kind=_scr_kind)
    vag_i = nc.dram_tensor("vag_i", [TBo, NH, 128, 128], BF16)
    vag_o = nc.dram_tensor("vag_o", [2, TBo, NH, 128, 128], BF16, kind=_scr_kind)
    yscr = nc.dram_tensor("yscr", [H, T], F32, kind=_scr_kind)
    aodump = nc.dram_tensor("aodump", [128, HC * T], BF16, kind="ExternalOutput") if debug else None
    r1dump = nc.dram_tensor("r1dump", [NTW, 128, HC * TW], F32, kind="ExternalOutput") if debug else None
    accdump = nc.dram_tensor("accdump", [NH, T // 128, 128, 8], F32, kind="ExternalOutput") if debug else None
    stdump = nc.dram_tensor("stdump", [NTW, 128, 6 * JT * JW], F32, kind="ExternalOutput") if debug else None

    es = ExitStack()
    sbt = lambda name, shape, dt: es.enter_context(nc.sbuf_tensor(name, shape, dt))

    cst = sbt("cst_sb", [128, 256], BF16)
    cstf = sbt("cstf_sb", [128, 128], F32)
    sel_sb = sbt("sel_sb", [8, 128], BF16)
    ones_sb = sbt("ones_sb", [128, 128], BF16)
    bqk_sb = sbt("bqk_sb", [128, 2 * NH], F32)
    bo_sb = sbt("bo_sb", [128, HC], F32)
    bv_sb = sbt("bv_sb", [1, H], BF16)
    wr_sb = sbt("wr_sb", [128, HC * 8], BF16)
    wd_sb = sbt("wd_sb", [128, HC * EI], BF16)
    wu_sb = sbt("wu_sb", [128, H], BF16)

    NTB = T // 128
    RSTR = JT * JW
    def _al(x):
        return (x + 63) // 64 * 64
    A_XT = 0
    A_wv = _al(A_XT + HC * T)
    A_wqk = _al(A_wv + HC * VD)
    A_kst = _al(A_wqk + 2 * HC * 128)
    A_vst = _al(A_kst + 2 * 512)
    p1_end = _al(A_vst + 2 * 512)
    GQ = min(4, T // 128)
    QW = GQ * 128
    NGQ = T // QW
    SH = S // 2
    A_q2 = 0
    A_k2 = _al(A_q2 + 2 * T)
    A_v2 = _al(A_k2 + 2 * S)
    A_PT = _al(A_v2 + 2 * S)
    A_P2s = _al(A_PT + KC * QW)
    A_P2q = _al(A_P2s + 2 * S)
    A_junk = _al(A_P2q + 2 * S)
    A_acc = _al(A_junk + SH)
    A_lcol = _al(A_acc + 64)
    A_lirow = _al(A_lcol + 16)
    A_lrep = _al(A_lirow + 1536)
    p2_end = _al(A_lrep + 1024)
    A_r1 = 0
    A_xres = _al(A_r1 + 2 * HC * TW)
    A_yst = _al(A_xres + 2 * 4 * 512)
    A_rb = _al(A_yst + 2 * 4 * 512)
    A_sq = _al(A_rb + 2 * 512)
    A_stat = _al(A_sq + 2 * 512)
    A_wo = _al(A_stat + 2 * 6 * RSTR)
    wo_end = _al(A_wo + 2 * HC * 128)
    A_lgf = A_wo
    A_gT = _al(A_lgf + 2 * T)
    A_gtok = _al(A_gT + T)
    A_gtk2 = _al(A_gtok + 2 * NTB * 8)
    A_gsc = _al(A_gtk2 + 2 * NTB * 8)
    A_gsc2 = _al(A_gsc + 2 * NTB)
    A_gtk3 = _al(A_gsc2 + 2 * NTB)
    A_ag = _al(A_gtk3 + 2 * NTB * 8)
    p34_end = max(_al(A_ag + 4 * 512), wo_end)
    AO_cols = HC * T
    AO_off = max(p2_end, p34_end)
    AR_COLS = _al(max(AO_off + AO_cols, p1_end) + 64)
    arena = sbt("arena", [128, AR_COLS], BF16)
    assert AR_COLS * 2 <= 190 * 1024, f"arena {AR_COLS * 2} B/partition too big"

    XT = arena[:, A_XT:A_XT + HC * T]
    wv_sb = arena[:, A_wv:A_wv + HC * VD]
    wqk0 = A_wqk
    kst = arena[:, A_kst:A_kst + 2 * 512]
    vst = arena[:, A_vst:A_vst + 2 * 512]
    AO = arena[:, AO_off:AO_off + AO_cols]
    q2o, k2o, v2o = A_q2, A_k2, A_v2
    PTsl = arena[:, A_PT:A_PT + KC * QW]
    P2s = arena[:, A_P2s:A_P2s + 2 * S]
    P2q = arena[:, A_P2q:A_P2q + 2 * S]
    junk = arena[:, A_junk:A_junk + SH]
    accs = arena[:, A_acc:A_acc + 64].bitcast(F32)
    accsI = arena[:, A_acc:A_acc + 64].bitcast(mybir.dt.int32)
    lcol = arena[:, A_lcol:A_lcol + 16].bitcast(F32)
    lirow = arena[0:1, A_lirow:A_lirow + 1024].bitcast(F32)
    lirow_bf = arena[0:1, A_lirow + 1024:A_lirow + 1536]
    lrep_sb = arena[:, A_lrep:A_lrep + 1024].bitcast(F32)
    r1v = arena[:, A_r1:A_r1 + 2 * HC * TW].bitcast(F32)
    x_res = arena[:, A_xres:A_xres + 2 * 4 * 512].bitcast(F32)
    ystage = arena[:, A_yst:A_yst + 2 * 4 * 512].bitcast(F32)
    rbst = arena[:, A_rb:A_rb + 2 * 512]
    sqst = arena[:, A_sq:A_sq + 2 * 512]
    stat = arena[:, A_stat:A_stat + 2 * 6 * RSTR].bitcast(F32)
    wo0 = A_wo
    lgf = arena[0:8, A_lgf:A_lgf + 2 * T].bitcast(F32)
    gT_sb = arena[0:8, A_gT:A_gT + T]
    gtok = arena[:, A_gtok:A_gtok + 2 * NTB * 8].bitcast(F32)
    gtk2 = arena[:, A_gtk2:A_gtk2 + 2 * NTB * 8].bitcast(F32)
    gsc = arena[:, A_gsc:A_gsc + 2 * NTB].bitcast(F32)
    gsc2 = arena[:, A_gsc2:A_gsc2 + 2 * NTB].bitcast(F32)
    gtk3 = arena[:, A_gtk3:A_gtk3 + 2 * NTB * 8].bitcast(F32)
    agst = arena[:, A_ag:A_ag + 4 * 512]

    psA = es.enter_context(nc.psum_tensor("psA", [128, 2048], F32))
    psB = es.enter_context(nc.psum_tensor("psB", [128, 2048], F32))

    sems = {}
    for s in ALL_SEMS:
        sems[s] = es.enter_context(nc.semaphore("sem_" + s))
    block = es.enter_context(nc.Block())

    b_cst, b_cstf, b_sel, b_ones = Buf("cst"), Buf("cstf"), Buf("sel"), Buf("ones")
    b_bqk, b_bo, b_bv = Buf("bqk"), Buf("bo"), Buf("bv")
    b_wr, b_wd, b_wu = Buf("wr"), Buf("wd"), Buf("wu")
    b_XT = Buf("XT")
    b_wv = Buf("wv")
    b_wqk = [Buf("wqk%d" % i) for i in range(2)]
    b_kst = [Buf("kst%d" % i) for i in range(2)]
    b_vst = [Buf("vst%d" % i) for i in range(2)]
    b_psA = [Buf("psA%d" % i) for i in range(4)]
    b_psB = [Buf("psB%d" % i) for i in range(4)]
    e_kscr = [[] for _ in range(NH)]
    e_qscr = [[] for _ in range(NH)]
    e_vscr = [[] for _ in range(NH)]
    b_k2 = [Buf("k2_%d" % i) for i in range(2)]
    b_q2 = [Buf("q2_%d" % i) for i in range(2)]
    b_v2 = [Buf("v2_%d" % i) for i in range(2)]
    b_P2s = [Buf("P2s%d" % i) for i in range(2)]
    b_P2q = [Buf("P2q%d" % i) for i in range(2)]
    b_PT = Buf("PT")
    b_junk = Buf("junk")
    b_acc = [Buf("acc%d" % i) for i in range(2)]
    b_lcol = Buf("lcol")
    b_lirow = Buf("lirow")
    b_lrep = Buf("lrep")
    b_AO = [[Buf("AO_%d_%d" % (c, j)) for j in range(T // 512 if T >= 512 else 1)] for c in range(HC)]
    b_wo = [Buf("wo%d" % i) for i in range(2)]
    b_x = [Buf("x%d" % i) for i in range(4)]
    b_r1 = [[Buf("r1_%d_%d" % (o, j)) for j in range(JT)] for o in range(HC)]
    b_rb = [Buf("rb%d" % i) for i in range(2)]
    b_sq = [Buf("sq%d" % i) for i in range(2)]
    b_stat = [Buf("stat%d" % j) for j in range(4)]
    b_yst = [Buf("yst%d" % i) for i in range(2)]
    b_yst2 = [Buf("yst2_%d" % i) for i in range(2)]
    b_stat2 = [Buf("stat2_%d" % i) for i in range(2)]
    b_yscr = [[Buf("yscr_%d_%d" % (o, j)) for j in range(T // 512 if T >= 512 else 1)] for o in range(HC)]
    b_lgf, b_gtok, b_gtk2, b_gsc, b_gT = Buf("lgf"), Buf("gtok"), Buf("gtk2"), Buf("gsc"), Buf("gT")
    b_gsc2, b_gtk3 = Buf("gsc2"), Buf("gtk3")
    b_ag = [Buf("ag%d" % i) for i in range(4)]

    AOW = 512 if T >= 512 else T

    def aoslice(c, t0, w):
        return AO[:, c * T + t0:c * T + t0 + w]

    def aobufs(t0, w, cs=None):
        out = []
        for c in (range(HC) if cs is None else cs):
            for j in range(t0 // AOW, (t0 + w + AOW - 1) // AOW):
                out.append(b_AO[c][j])
        return out

    g = "gpsimd"
    K.op(g, lambda h: h.dma_start(cst[:, :], cst_h[:, :]), writes=[b_cst], dma=True)
    K.op(g, lambda h: h.dma_start(cstf[:, :], cstf_h[:, :]), writes=[b_cstf], dma=True)
    K.op(g, lambda h: h.dma_start(sel_sb[:, :], cst_h[0:8, 128:256]), writes=[b_sel], dma=True)
    K.op(g, lambda h: h.dma_start(bqk_sb[:, :], bqk_h[:, :]), writes=[b_bqk], dma=True)
    K.op(g, lambda h: h.dma_start(bo_sb[:, :], bo_h[:, :]), writes=[b_bo], dma=True)
    K.op(g, lambda h: h.dma_start(bv_sb[:, :], bv_h[:, :]), writes=[b_bv], dma=True)
    K.op(g, lambda h: h.dma_start(wr_sb[:, :], wr_h[:, :]), writes=[b_wr], dma=True)
    K.op(g, lambda h: h.dma_start(wd_sb[:, :], wd_h[:, :]), writes=[b_wd], dma=True)
    K.op(g, lambda h: h.dma_start(wu_sb[:, :], wu_h[:, :]), writes=[b_wu], dma=True)
    for c in range(HC):
        K.op(g, (lambda c: lambda h: h.dma_start(
            XT[:, c * T:(c + 1) * T], xT[c * 128:(c + 1) * 128, :]))(c),
            writes=[b_XT], dma=True)
    K.op("vector", lambda h: h.memset(ones_sb[:, :], 1.0), writes=[b_ones])

    ident = cst[:, 0:128]
    ident8f = cstf[0:8, 0:8]
    identf = cstf[:, :]

    for dt in range(NVD):
        K.op(g, (lambda dt: lambda h: h.dma_start(
            wv_sb[:, :].rearrange("p (c d) -> p c d", d=VD),
            wv_h[dt, :, :].rearrange("(c p) d -> p c d", p=128)))(dt),
            writes=[b_wv], dma=True)
        for tb in range(TBo):
            slot = (dt * TBo + tb) % 4
            def mmg(h, dt=dt, tb=tb, slot=slot):
                for c in range(HC):
                    h.matmul(psA[:, slot * 512:slot * 512 + VD],
                             XT[:, c * T + tb * 128:c * T + (tb + 1) * 128],
                             wv_sb[:, c * VD:(c + 1) * VD],
                             start=(c == 0), stop=False)
                return h.matmul(psA[:, slot * 512:slot * 512 + VD],
                                ones_sb[0:1, 0:128], bv_sb[:, dt * VD:(dt + 1) * VD],
                                start=False, stop=True)
            K.op("tensor", mmg, reads=[b_XT, b_wv, b_ones, b_bv], writes=[b_psA[slot]])
            vs = (dt * TBo + tb) % 2
            K.op("vector", (lambda slot, vs: lambda h: h.tensor_copy(
                vst[:, vs * 512:vs * 512 + VD], psA[:, slot * 512:slot * 512 + VD]))(slot, vs),
                reads=[b_psA[slot]], writes=[b_vst[vs]])
            nh_t = VD // 128
            def spill(h, dt=dt, tb=tb, vs=vs, nh_t=nh_t):
                return h.dma_start(vag_i[tb, dt * nh_t:(dt + 1) * nh_t, :, :].transpose([1, 0, 2]),
                                   vst[:, vs * 512:vs * 512 + VD].rearrange("p (j d) -> p j d", d=128))
            ev = K.op("sync", spill, reads=[b_vst[vs]], dma=True)
            e_vscr[0].append(ev)

    def qk_proj(which):
        span = T
        tile_w = min(512, span)
        ntile = span // tile_w
        wsrc = wk_h if which == "k" else wq_h
        scr = kag_i if which == "k" else qscr
        escr = e_kscr if which == "k" else e_qscr
        bias0 = NH if which == "k" else 0
        for n in range(NH):
            slot = n % 2
            K.op(g, (lambda n, slot, wsrc=wsrc: lambda h: h.dma_start(
                arena[:, wqk0 + slot * HC * 128: wqk0 + (slot + 1) * HC * 128]
                .rearrange("p (c o) -> p c o", o=128),
                wsrc[n, :, :].rearrange("(c p) o -> p c o", p=128)))(n, slot),
                writes=[b_wqk[slot]], dma=True)
            for j in range(ntile):
                idx = n * ntile + j
                ps = idx % 4
                def mmg(h, n=n, j=j, slot=slot, ps=ps, tile_w=tile_w):
                    for c in range(HC):
                        mm = h.matmul(psB[:, ps * 512:ps * 512 + tile_w],
                                      arena[:, wqk0 + slot * HC * 128 + c * 128:
                                            wqk0 + slot * HC * 128 + (c + 1) * 128],
                                      XT[:, c * T + j * tile_w:c * T + (j + 1) * tile_w],
                                      start=(c == 0), stop=(c == HC - 1))
                    return mm
                K.op("tensor", mmg, reads=[b_XT, b_wqk[slot]], writes=[b_psB[ps]])
                ks = idx % 2
                K.op("scalar", (lambda ps, ks, n, tile_w: lambda h: h.activation(
                    kst[:, ks * 512:ks * 512 + tile_w], psB[:, ps * 512:ps * 512 + tile_w],
                    AF.Identity, bias=bqk_sb[:, bias0 + n:bias0 + n + 1], scale=1.0))(ps, ks, n, tile_w),
                    reads=[b_psB[ps], b_bqk], writes=[b_kst[ks]])
                ev = K.op("sync", (lambda n, j, ks, tile_w, scr=scr: lambda h: h.dma_start(
                    scr[n, :, j * tile_w:(j + 1) * tile_w],
                    kst[:, ks * 512:ks * 512 + tile_w]))(n, j, ks, tile_w),
                    reads=[b_kst[ks]], dma=True)
                escr[n].append(ev)
    qk_proj("k")
    qk_proj("q")

    pair_groups = [[i, i + 1] for i in range(0, cfg.n_cores, 2)]
    e_kag = K.op(g, (lambda: lambda h: h.collective_compute(
        "AllGather", AluOpType.bypass, replica_groups=pair_groups,
        ins=[kag_i.ap().opt()], outs=[kag_o.ap().opt()]))(),
        waits=[e for n in range(NH) for e in e_kscr[n]])
    e_vag = K.op(g, (lambda: lambda h: h.collective_compute(
        "AllGather", AluOpType.bypass, replica_groups=pair_groups,
        ins=[vag_i.ap().opt()], outs=[vag_o.ap().opt()]))(),
        waits=e_vscr[0])

    K.barrier()
    nSA = (SH + 511) // 512
    NBA = max(1, KC // 4)
    psA_banks = [b_psA[i] for i in range(min(4, nSA))]
    psB_banks = [b_psB[i] for i in range(min(4, nSA))]
    for n in range(NH):
        sl = n % 2
        for half in range(2):
            K.op("sync", (lambda n, sl, half: lambda h: h.dma_start(
                arena[:, k2o + sl * S + half * T:k2o + sl * S + (half + 1) * T],
                kag_o[half, n, :, :]))(n, sl, half),
                waits=[e_kag], writes=[b_k2[sl]], dma=True)
            K.op("sync", (lambda n, sl, half: lambda h: h.dma_start(
                arena[:, v2o + sl * S + half * T:v2o + sl * S + (half + 1) * T]
                .rearrange("p (tb d) -> p tb d", d=128),
                vag_o[half, :, n, :, :].transpose([1, 0, 2])))(n, sl, half),
                waits=[e_vag], writes=[b_v2[sl]], dma=True)
        K.op("sync", (lambda n, sl: lambda h: h.dma_start(
            arena[:, q2o + sl * T:q2o + (sl + 1) * T], qscr[n, :, :]))(n, sl),
            waits=e_qscr[n], writes=[b_q2[sl]], dma=True)
        for qg in range(NGQ):
            for qs in range(GQ):
                qi = qg * GQ + qs
                p = qi % 2
                a0 = p * 16
                def smm(h, n=n, qi=qi, sl=sl, half=0):
                    for m in range(nSA):
                        w = min(512, SH - m * 512)
                        ps = psA if half == 0 else psB
                        mm = h.matmul(ps[:, m * 512:m * 512 + w],
                                      arena[:, q2o + sl * T + qi * 128:q2o + sl * T + (qi + 1) * 128],
                                      arena[:, k2o + sl * S + half * SH + m * 512:
                                            k2o + sl * S + half * SH + m * 512 + w],
                                      start=True, stop=True)
                    return mm
                import functools
                sbst = 512 if SH >= 512 else 128
                nsb = SH // sbst
                K.op("tensor", functools.partial(smm, half=0),
                     reads=[b_k2[sl], b_q2[sl]], writes=psA_banks)
                K.op("scalar", (lambda a0, nsb, sbst=sbst: lambda h: h.activation(
                    junk[:, 0:nsb * 128].rearrange("p (a b) -> p a b", b=128),
                    psA[:, 0:SH].rearrange("p (a b) -> p a b", b=sbst)[:, :, 0:128],
                    AF.Exp, scale=SC / 4.0,
                    accum_out=accs[:, a0 + 0:a0 + 1]))(a0, nsb),
                    reads=psA_banks, writes=[b_junk, b_acc[p]])
                K.op("tensor", functools.partial(smm, half=1),
                     reads=[b_k2[sl], b_q2[sl]], writes=psB_banks)
                K.op("scalar", (lambda a0, nsb, sbst=sbst: lambda h: h.activation(
                    junk[:, 0:nsb * 128].rearrange("p (a b) -> p a b", b=128),
                    psB[:, 0:SH].rearrange("p (a b) -> p a b", b=sbst)[:, :, 0:128],
                    AF.Exp, scale=SC / 4.0,
                    accum_out=accs[:, a0 + 1:a0 + 2]))(a0, nsb),
                    reads=psB_banks, writes=[b_junk, b_acc[p]])
                dqb = (qi % (SH // 128)) * 128
                K.op("scalar", (lambda a0, dqb: lambda h: h.activation(
                    junk[:, 0:128], psA[:, dqb:dqb + 128], AF.Exp, scale=SC / 4.0,
                    accum_out=accs[:, a0 + 2:a0 + 3]))(a0, dqb),
                    reads=psA_banks, writes=[b_junk, b_acc[p]])
                K.op("scalar", (lambda a0, dqb: lambda h: h.activation(
                    junk[:, 0:128], psB[:, dqb:dqb + 128], AF.Exp, scale=SC / 4.0,
                    accum_out=accs[:, a0 + 3:a0 + 4]))(a0, dqb),
                    reads=psB_banks, writes=[b_junk, b_acc[p]])
                K.op("vector", (lambda a0: lambda h: h.tensor_tensor(
                    accs[:, a0 + 4:a0 + 5], accs[:, a0 + 0:a0 + 1],
                    accs[:, a0 + 1:a0 + 2], op=AluOpType.add))(a0),
                    reads=[b_acc[p]], writes=[b_acc[p]])
                K.op("vector", (lambda a0: lambda h: h.tensor_tensor(
                    accs[:, a0 + 8:a0 + 9], accs[:, a0 + 2:a0 + 3],
                    accs[:, a0 + 3:a0 + 4], op=AluOpType.add))(a0),
                    reads=[b_acc[p]], writes=[b_acc[p]])
                K.op("vector", (lambda a0: lambda h: h.tensor_tensor(
                    accs[:, a0 + 9:a0 + 10], accs[:, a0 + 4:a0 + 5],
                    accs[:, a0 + 8:a0 + 9], op=AluOpType.add))(a0),
                    reads=[b_acc[p]], writes=[b_acc[p]])
                K.op("vector", (lambda a0: lambda h: h.tensor_scalar(
                    accsI[:, a0 + 5:a0 + 6], accsI[:, a0 + 9:a0 + 10], 23, None,
                    op0=AluOpType.arith_shift_right))(a0),
                    reads=[b_acc[p]], writes=[b_acc[p]])
                K.op("vector", (lambda a0: lambda h: h.tensor_copy(
                    accs[:, a0 + 7:a0 + 8], accsI[:, a0 + 5:a0 + 6]))(a0),
                    reads=[b_acc[p]], writes=[b_acc[p]])
                K.op("vector", (lambda a0: lambda h: h.tensor_scalar(
                    accs[:, a0 + 6:a0 + 7], accs[:, a0 + 7:a0 + 8], -127.0,
                    -2.0 * 0.6931471805599453,
                    op0=AluOpType.add, op1=AluOpType.mult))(a0),
                    reads=[b_acc[p]], writes=[b_acc[p]])
                K.op("scalar", (lambda a0, p: lambda h: h.activation(
                    P2s[:, p * S:p * S + SH], psA[:, 0:SH], AF.Exp, scale=SC / 2.0,
                    bias=accs[:, a0 + 6:a0 + 7]))(a0, p),
                    reads=psA_banks + [b_acc[p]], writes=[b_P2s[p]])
                K.op("scalar", (lambda a0, p: lambda h: h.activation(
                    P2s[:, p * S + SH:p * S + 2 * SH], psB[:, 0:SH], AF.Exp, scale=SC / 2.0,
                    bias=accs[:, a0 + 6:a0 + 7]))(a0, p),
                    reads=psB_banks + [b_acc[p]], writes=[b_P2s[p]])
                K.op("vector", (lambda p: lambda h: h.tensor_tensor(
                    P2q[:, p * S:(p + 1) * S], P2s[:, p * S:(p + 1) * S],
                    P2s[:, p * S:(p + 1) * S], op=AluOpType.mult))(p),
                    reads=[b_P2s[p]], writes=[b_P2q[p]])
                if debug:
                    K.op("sync", (lambda n, qi, a0: lambda h: h.dma_start(
                        accdump[n, qi, :, :], accs[:, a0:a0 + 8]))(n, qi, a0),
                        reads=[b_acc[p]], dma=True)
                for bb_ in range(NBA):
                    bb = bb_
                    base = (bb % 2) * 512
                    def trb(h, p=p, bb=bb, base=base):
                        for k_ in range(4):
                            j = bb * 4 + k_
                            if j >= KC:
                                break
                            tr = h.transpose(
                                psB[:, base + k_ * 64:base + k_ * 64 + 64].bitcast(BF16),
                                P2q[:, p * S + j * 128:p * S + (j + 1) * 128],
                                ident)
                        return tr
                    K.op("tensor", trb, reads=[b_P2q[p], b_cst], writes=[b_psB[bb % 2]])
                    nj = min(4, KC - bb * 4)
                    def evb(h, bb=bb, base=base, qs=qs, nj=nj):
                        return h.tensor_copy(
                            PTsl[:, :].rearrange("p (j q) -> p j q", q=QW)
                            [:, bb * 4:bb * 4 + nj, qs * 128:(qs + 1) * 128],
                            psB[:, base:base + nj * 64].bitcast(BF16)
                            .rearrange("p (j q) -> p j q", q=128))
                    K.op("vector", evb, reads=[b_psB[bb % 2]], writes=[b_PT])
            def pv(h, sl=sl):
                for kt in range(KC):
                    mm = h.matmul(psB[:, 1024:1024 + QW],
                                  arena[:, v2o + sl * S + kt * 128:v2o + sl * S + (kt + 1) * 128],
                                  PTsl[:, kt * QW:(kt + 1) * QW],
                                  start=(kt == 0), stop=(kt == KC - 1))
                return mm
            K.op("tensor", pv, reads=[b_v2[sl], b_PT], writes=[b_psB[2]])
            def lmm(h):
                for kt in range(KC):
                    mm = h.matmul(psB[:, 1536:1536 + QW], ones_sb[:, 0:128],
                                  PTsl[:, kt * QW:(kt + 1) * QW],
                                  start=(kt == 0), stop=(kt == KC - 1))
                return mm
            K.op("tensor", lmm, reads=[b_ones, b_PT], writes=[b_psB[3]])
            K.op("vector", lambda h: h.reciprocal(lrep_sb[:, 0:QW], psB[:, 1536:1536 + QW]),
                 reads=[b_psB[3]], writes=[b_lrep])
            K.op("vector", (lambda n, qg: lambda h: h.tensor_tensor(
                aoslice(n, qg * QW, QW), psB[:, 1024:1024 + QW], lrep_sb[:, 0:QW],
                op=AluOpType.mult))(n, qg),
                reads=[b_psB[2], b_lrep], writes=aobufs(qg * QW, QW, [n]))

    if debug:
        K.op("sync", lambda h: h.dma_start(aodump[:, :], AO[:, :]),
             reads=[b for row in b_AO for b in row], dma=True)
    K.barrier()
    for tw in range(NTW):
        t0 = tw * TW
        for o in range(HC):
            i2 = tw * HC + o
            ws = i2 % 2
            K.op(g, (lambda o, ws: lambda h: h.dma_start(
                arena[:, wo0 + ws * HC * 128:wo0 + (ws + 1) * HC * 128]
                .rearrange("p (c q) -> p c q", q=128),
                wo_h[o, :, :].rearrange("(c p) q -> p c q", p=128)))(o, ws),
                writes=[b_wo[ws]], dma=True)
            for j in range(JT):
                gi = i2 * JT + j
                ps = gi % 2
                xs = gi % 4
                K.op("sync", (lambda o, j, t0, xs: lambda h: h.dma_start(
                    x_res[:, xs * 512:xs * 512 + JW],
                    xT[o * 128:(o + 1) * 128, t0 + j * JW:t0 + (j + 1) * JW]))(o, j, t0, xs),
                    writes=[b_x[xs]], dma=True)
                def pmm(h, o=o, j=j, ws=ws, ps=ps, t0=t0):
                    for c in range(HC):
                        mm = h.matmul(psA[:, ps * 512:ps * 512 + JW],
                                      arena[:, wo0 + ws * HC * 128 + c * 128:
                                            wo0 + ws * HC * 128 + (c + 1) * 128],
                                      aoslice(c, t0 + j * JW, JW),
                                      start=(c == 0), stop=(c == HC - 1))
                    return mm
                K.op("tensor", pmm, reads=[b_wo[ws]] + aobufs(t0 + j * JW, JW),
                     writes=[b_psA[ps]])
                K.op("vector", (lambda o, j, ps, xs: lambda h: h.scalar_tensor_tensor(
                    r1v[:, o * TW + j * JW:o * TW + (j + 1) * JW],
                    psA[:, ps * 512:ps * 512 + JW], bo_sb[:, o:o + 1],
                    x_res[:, xs * 512:xs * 512 + JW],
                    op0=AluOpType.add, op1=AluOpType.add))(o, j, ps, xs),
                    reads=[b_psA[ps], b_bo, b_x[xs]], writes=[b_r1[o][j]])
                rs_ = gi % 2
                K.op("vector", (lambda o, j, rs_: lambda h: h.tensor_copy(
                    rbst[:, rs_ * 512:rs_ * 512 + JW],
                    r1v[:, o * TW + j * JW:o * TW + (j + 1) * JW]))(o, j, rs_),
                    reads=[b_r1[o][j]], writes=[b_rb[rs_]])
                K.op("vector", (lambda rs_: lambda h: h.tensor_tensor(
                    sqst[:, rs_ * 512:rs_ * 512 + JW], rbst[:, rs_ * 512:rs_ * 512 + JW],
                    rbst[:, rs_ * 512:rs_ * 512 + JW], op=AluOpType.mult))(rs_),
                    reads=[b_rb[rs_]], writes=[b_sq[rs_]])
                def stmm(h, o=o, j=j, rs_=rs_):
                    h.matmul(psB[:, j * 512:j * 512 + JW], ones_sb[:, 0:128],
                             rbst[:, rs_ * 512:rs_ * 512 + JW],
                             start=(o == 0), stop=(o == HC - 1))
                    return h.matmul(psB[:, 1024 + j * 512:1024 + j * 512 + JW],
                                    ones_sb[:, 0:128], sqst[:, rs_ * 512:rs_ * 512 + JW],
                                    start=(o == 0), stop=(o == HC - 1))
                K.op("tensor", stmm, reads=[b_rb[rs_], b_sq[rs_], b_ones],
                     writes=[b_psB[j], b_psB[2 + j]])
        if debug:
            K.op("sync", (lambda tw: lambda h: h.dma_start(
                r1dump[tw, :, :], r1v[:, 0:HC * TW]))(tw),
                reads=[b for row in b_r1 for b in row], dma=True)
        for j in range(JT):
            mu = stat[:, 0 * RSTR + j * JW:0 * RSTR + j * JW + JW]
            rs = stat[:, 1 * RSTR + j * JW:1 * RSTR + j * JW + JW]
            mus = stat[:, 2 * RSTR + j * JW:2 * RSTR + j * JW + JW]
            scr2 = stat[:, 3 * RSTR + j * JW:3 * RSTR + j * JW + JW]
            bmu, brs, bmus, bscr = b_stat[0], b_stat[1], b_stat[2], b_stat[3]
            K.op("vector", (lambda mu, j: lambda h: h.tensor_scalar(
                mu, psB[:, j * 512:j * 512 + JW], IH, None, op0=AluOpType.mult))(mu, j),
                reads=[b_psB[j]], writes=[bmu])
            K.op("vector", (lambda scr2, j: lambda h: h.tensor_scalar(
                scr2, psB[:, 1024 + j * 512:1024 + j * 512 + JW], IH, cfg.EPS,
                op0=AluOpType.mult, op1=AluOpType.add))(scr2, j),
                reads=[b_psB[2 + j]], writes=[bscr])
            var2 = stat[:, 4 * RSTR + j * JW:4 * RSTR + j * JW + JW]
            rin2 = stat[:, 5 * RSTR + j * JW:5 * RSTR + j * JW + JW]
            bvar, brin = b_stat2[0], b_stat2[1]
            K.op("vector", (lambda rs, mu: lambda h: h.tensor_tensor(
                rs, mu, mu, op=AluOpType.mult))(rs, mu), reads=[bmu], writes=[brs])
            K.op("vector", (lambda var2, scr2, rs: lambda h: h.tensor_tensor(
                var2, scr2, rs, op=AluOpType.subtract))(var2, scr2, rs),
                reads=[brs, bscr], writes=[bvar])
            K.op("vector", (lambda rin2, var2: lambda h: h.reciprocal(rin2, var2))(rin2, var2),
                reads=[bvar], writes=[brin])
            K.op("scalar", (lambda rs, rin2: lambda h: h.activation(rs, rin2, AF.Sqrt))(rs, rin2),
                reads=[brin], writes=[brs])
            K.op("vector", (lambda mus, mu, rs: lambda h: h.tensor_tensor(
                mus, mu, rs, op=AluOpType.mult))(mus, mu, rs), reads=[bmu, brs], writes=[bmus])
            if debug and j == JT - 1:
                K.op("sync", (lambda tw: lambda h: h.dma_start(
                    stdump[tw, :, :], stat[:, 0:6 * RSTR]))(tw),
                    reads=[b_stat[0], b_stat[1], b_stat[2], b_stat[3],
                           b_stat2[0], b_stat2[1]], dma=True)
            for o in range(HC):
                ys = (o * JT + j) % 2
                K.op("vector", (lambda o, j, ys, rs: lambda h: h.tensor_tensor(
                    ystage[:, ys * 512:ys * 512 + JW],
                    r1v[:, o * TW + j * JW:o * TW + (j + 1) * JW], rs,
                    op=AluOpType.mult))(o, j, ys, rs),
                    reads=[b_r1[o][j], brs], writes=[b_yst[ys]])
                K.op("vector", (lambda ys, mus: lambda h: h.tensor_tensor(
                    ystage[:, (2 + ys) * 512:(2 + ys) * 512 + JW],
                    ystage[:, ys * 512:ys * 512 + JW],
                    mus, op=AluOpType.subtract))(ys, mus),
                    reads=[bmus, b_yst[ys]], writes=[b_yst2[ys]])
                K.op("scalar", (lambda o, j, ys, t0: lambda h: h.activation(
                    aoslice(o, t0 + j * JW, JW), ystage[:, (2 + ys) * 512:(2 + ys) * 512 + JW],
                    AF.Copy))(o, j, ys, t0),
                    reads=[b_yst2[ys]], writes=aobufs(t0 + j * JW, JW, [o]))
                ev = K.op("sync", (lambda o, j, ys, t0: lambda h: h.dma_start(
                    yscr[o * 128:(o + 1) * 128, t0 + j * JW:t0 + (j + 1) * JW],
                    ystage[:, (2 + ys) * 512:(2 + ys) * 512 + JW]))(o, j, ys, t0),
                    reads=[b_yst2[ys]], dma=True)
                b_yscr[o][(t0 + j * JW) // (512 if T >= 512 else T)].w = ev

    K.barrier()
    for jt in range(T // QT):
        def rmm(h, jt=jt):
            for c in range(HC):
                mm = h.matmul(psA[0:8, jt * 512:jt * 512 + QT],
                              wr_sb[:, c * 8:(c + 1) * 8],
                              aoslice(c, jt * QT, QT),
                              start=(c == 0), stop=(c == HC - 1))
            return mm
        K.op("tensor", rmm, reads=[b_wr] + aobufs(jt * QT, QT), writes=[b_psA[jt % 4]])
        K.op("vector", (lambda jt: lambda h: h.tensor_copy(
            lgf[:, jt * QT:(jt + 1) * QT], psA[0:8, jt * 512:jt * 512 + QT]))(jt),
            reads=[b_psA[jt % 4]], writes=[b_lgf])
    for tb in range(NTB):
        K.op("tensor", (lambda tb: lambda h: h.transpose(
            psA[:, (tb % 4) * 8:(tb % 4) * 8 + 8],
            lgf[0:8, tb * 128:(tb + 1) * 128], ident8f))(tb),
            reads=[b_lgf, b_cstf], writes=[b_psA[0]])
        if tb % 4 == 3 or tb == NTB - 1:
            lo = (tb // 4) * 4
            cnt = tb - lo + 1
            K.op("vector", (lambda lo, cnt: lambda h: h.tensor_copy(
                gtok[:, lo * 8:(lo + cnt) * 8], psA[:, 0:cnt * 8]))(lo, cnt),
                reads=[b_psA[0]], writes=[b_gtok])
    seg = gtok[:, :].rearrange("p (s e) -> p s e", e=8)
    seg2 = gtk2[:, :].rearrange("p (s e) -> p s e", e=8)
    seg3 = gtk3[:, :].rearrange("p (s e) -> p s e", e=8)
    gb = lambda: gsc[:, :, None].broadcast_to((128, NTB, 8))
    gb2 = lambda: gsc2[:, :, None].broadcast_to((128, NTB, 8))
    K.op("vector", lambda h: h.tensor_reduce(gsc[:, :, None], seg, axis=mybir.AxisListType.X,
                                             op=AluOpType.max), reads=[b_gtok], writes=[b_gsc])
    K.op("vector", lambda h: h.tensor_tensor(seg2, seg, gb(), op=AluOpType.subtract),
         reads=[b_gtok, b_gsc], writes=[b_gtk2])
    K.op("scalar", lambda h: h.activation(gtok[:, :], gtk2[:, :], AF.Exp),
         reads=[b_gtk2], writes=[b_gtok])
    K.op("vector", lambda h: h.tensor_reduce(gsc[:, :, None], seg, axis=mybir.AxisListType.X,
                                             op=AluOpType.add), reads=[b_gtok], writes=[b_gsc])
    K.op("vector", lambda h: h.reciprocal(gsc2[:, :], gsc[:, :]), reads=[b_gsc], writes=[b_gsc2])
    K.op("vector", lambda h: h.tensor_tensor(seg2, seg, gb2(), op=AluOpType.mult),
         reads=[b_gsc2, b_gtok], writes=[b_gtk2])
    K.op("vector", lambda h: h.tensor_reduce(gsc[:, :, None], seg2, axis=mybir.AxisListType.X,
                                             op=AluOpType.max), reads=[b_gtk2], writes=[b_gsc])
    K.op("vector", lambda h: h.tensor_tensor(seg, seg2, gb(), op=AluOpType.is_ge),
         reads=[b_gtk2, b_gsc], writes=[b_gtok])
    K.op("vector", lambda h: h.scalar_tensor_tensor(gtk3[:, :], gtok[:, :], -2.0, gtk2[:, :],
                                                    op0=AluOpType.mult, op1=AluOpType.add),
         reads=[b_gtk2, b_gtok], writes=[b_gtk3])
    K.op("vector", lambda h: h.tensor_reduce(gsc[:, :, None], seg3, axis=mybir.AxisListType.X,
                                             op=AluOpType.max), reads=[b_gtk3], writes=[b_gsc])
    K.op("vector", lambda h: h.tensor_tensor(seg, seg2, gb(), op=AluOpType.is_ge),
         reads=[b_gtk2, b_gsc], writes=[b_gtok])
    K.op("vector", lambda h: h.tensor_tensor(seg3, seg, seg2, op=AluOpType.mult),
         reads=[b_gtk2, b_gtok], writes=[b_gtk3])
    for tb in range(NTB):
        K.op("tensor", (lambda tb: lambda h: h.transpose(
            psA[0:8, 512 + (tb % 4) * 128:512 + (tb % 4) * 128 + 128],
            gtk3[:, tb * 8:(tb + 1) * 8], identf))(tb),
            reads=[b_gtk3, b_cstf], writes=[b_psA[1]])
        K.op("vector", (lambda tb: lambda h: h.tensor_copy(
            gT_sb[:, tb * 128:(tb + 1) * 128],
            psA[0:8, 512 + (tb % 4) * 128:512 + (tb % 4) * 128 + 128]))(tb),
            reads=[b_psA[1]], writes=[b_gT])

    for tw in range(NTW):
        t0 = tw * TW
        for j in range(JT):
            jj = tw * JT + j
            K.op("tensor", (lambda t0, j: lambda h: h.matmul(
                psB[:, 1536:1536 + JW], sel_sb[:, :], gT_sb[:, t0 + j * JW:t0 + (j + 1) * JW],
                start=True, stop=True))(t0, j),
                reads=[b_sel, b_gT], writes=[b_psB[3]])
            def dmm(h, t0=t0, j=j):
                for c in range(HC):
                    mm = h.matmul(psB[:, 1024:1024 + JW], wd_sb[:, c * EI:(c + 1) * EI][:, 0:128],
                                  aoslice(c, t0 + j * JW, JW), start=(c == 0), stop=(c == HC - 1))
                return mm
            K.op("tensor", dmm, reads=[b_wd] + aobufs(t0 + j * JW, JW), writes=[b_psB[2]])
            sb_ = jj % 2
            K.op("scalar", (lambda sb_: lambda h: h.activation(
                agst[:, sb_ * 512:sb_ * 512 + JW], psB[:, 1024:1024 + JW], AF.Silu))(sb_),
                reads=[b_psB[2]], writes=[b_ag[sb_]])
            K.op("vector", (lambda sb_: lambda h: h.tensor_tensor(
                agst[:, 1024 + sb_ * 512:1024 + sb_ * 512 + JW],
                agst[:, sb_ * 512:sb_ * 512 + JW], psB[:, 1536:1536 + JW],
                op=AluOpType.mult))(sb_),
                reads=[b_ag[sb_], b_psB[3]], writes=[b_ag[2 + sb_]])
            for o in range(HC):
                oi = o * JT + j
                ps = oi % 2
                xs = (tw * HC * JT + o * JT + j) % 2
                K.op("sync", (lambda o, j, t0, xs: lambda h: h.dma_start(
                    x_res[:, (2 + xs) * 512:(2 + xs) * 512 + JW],
                    yscr[o * 128:(o + 1) * 128, t0 + j * JW:t0 + (j + 1) * JW]))(o, j, t0, xs),
                    reads=[b_yscr[o][(t0 + j * JW) // (512 if T >= 512 else T)]],
                    writes=[b_x[2 + xs]], dma=True)
                K.op("tensor", (lambda o, sb_, ps: lambda h: h.matmul(
                    psB[:, ps * 512:ps * 512 + JW], wu_sb[:, o * 128:(o + 1) * 128],
                    agst[:, 1024 + sb_ * 512:1024 + sb_ * 512 + JW],
                    start=True, stop=True))(o, sb_, ps),
                    reads=[b_wu, b_ag[2 + sb_]], writes=[b_psB[ps]])
                K.op("vector", (lambda o, j, ps, xs: lambda h: h.tensor_tensor(
                    r1v[:, o * TW + j * JW:o * TW + (j + 1) * JW],
                    psB[:, ps * 512:ps * 512 + JW], x_res[:, (2 + xs) * 512:(2 + xs) * 512 + JW],
                    op=AluOpType.add))(o, j, ps, xs),
                    reads=[b_psB[ps], b_x[2 + xs]], writes=[b_r1[o][j]])
                rs_ = oi % 2
                K.op("vector", (lambda o, j, rs_: lambda h: h.tensor_copy(
                    rbst[:, rs_ * 512:rs_ * 512 + JW],
                    r1v[:, o * TW + j * JW:o * TW + (j + 1) * JW]))(o, j, rs_),
                    reads=[b_r1[o][j]], writes=[b_rb[rs_]])
                K.op("vector", (lambda rs_: lambda h: h.tensor_tensor(
                    sqst[:, rs_ * 512:rs_ * 512 + JW], rbst[:, rs_ * 512:rs_ * 512 + JW],
                    rbst[:, rs_ * 512:rs_ * 512 + JW], op=AluOpType.mult))(rs_),
                    reads=[b_rb[rs_]], writes=[b_sq[rs_]])
                def stmm(h, o=o, j=j, rs_=rs_):
                    h.matmul(psA[:, j * 512:j * 512 + JW], ones_sb[:, 0:128],
                             rbst[:, rs_ * 512:rs_ * 512 + JW],
                             start=(o == 0), stop=(o == HC - 1))
                    return h.matmul(psA[:, 1024 + j * 512:1024 + j * 512 + JW],
                                    ones_sb[:, 0:128], sqst[:, rs_ * 512:rs_ * 512 + JW],
                                    start=(o == 0), stop=(o == HC - 1))
                K.op("tensor", stmm, reads=[b_rb[rs_], b_sq[rs_], b_ones],
                     writes=[b_psA[j], b_psA[2 + j]])
        for j in range(JT):
            mu = stat[:, 0 * RSTR + j * JW:0 * RSTR + j * JW + JW]
            rs = stat[:, 1 * RSTR + j * JW:1 * RSTR + j * JW + JW]
            mus = stat[:, 2 * RSTR + j * JW:2 * RSTR + j * JW + JW]
            scr2 = stat[:, 3 * RSTR + j * JW:3 * RSTR + j * JW + JW]
            bmu, brs, bmus, bscr = b_stat[0], b_stat[1], b_stat[2], b_stat[3]
            K.op("vector", (lambda mu, j: lambda h: h.tensor_scalar(
                mu, psA[:, j * 512:j * 512 + JW], IH, None, op0=AluOpType.mult))(mu, j),
                reads=[b_psA[j]], writes=[bmu])
            K.op("vector", (lambda scr2, j: lambda h: h.tensor_scalar(
                scr2, psA[:, 1024 + j * 512:1024 + j * 512 + JW], IH, cfg.EPS,
                op0=AluOpType.mult, op1=AluOpType.add))(scr2, j),
                reads=[b_psA[2 + j]], writes=[bscr])
            var2 = stat[:, 4 * RSTR + j * JW:4 * RSTR + j * JW + JW]
            rin2 = stat[:, 5 * RSTR + j * JW:5 * RSTR + j * JW + JW]
            bvar, brin = b_stat2[0], b_stat2[1]
            K.op("vector", (lambda rs, mu: lambda h: h.tensor_tensor(
                rs, mu, mu, op=AluOpType.mult))(rs, mu), reads=[bmu], writes=[brs])
            K.op("vector", (lambda var2, scr2, rs: lambda h: h.tensor_tensor(
                var2, scr2, rs, op=AluOpType.subtract))(var2, scr2, rs),
                reads=[brs, bscr], writes=[bvar])
            K.op("vector", (lambda rin2, var2: lambda h: h.reciprocal(rin2, var2))(rin2, var2),
                reads=[bvar], writes=[brin])
            K.op("scalar", (lambda rs, rin2: lambda h: h.activation(rs, rin2, AF.Sqrt))(rs, rin2),
                reads=[brin], writes=[brs])
            K.op("vector", (lambda mus, mu, rs: lambda h: h.tensor_tensor(
                mus, mu, rs, op=AluOpType.mult))(mus, mu, rs), reads=[bmu, brs], writes=[bmus])
            for o in range(HC):
                ys = (o * JT + j) % 2
                K.op("vector", (lambda o, j, ys, rs: lambda h: h.tensor_tensor(
                    ystage[:, ys * 512:ys * 512 + JW],
                    r1v[:, o * TW + j * JW:o * TW + (j + 1) * JW], rs,
                    op=AluOpType.mult))(o, j, ys, rs),
                    reads=[b_r1[o][j], brs], writes=[b_yst[ys]])
                K.op("vector", (lambda ys, mus: lambda h: h.tensor_tensor(
                    ystage[:, (2 + ys) * 512:(2 + ys) * 512 + JW],
                    ystage[:, ys * 512:ys * 512 + JW],
                    mus, op=AluOpType.subtract))(ys, mus),
                    reads=[bmus, b_yst[ys]], writes=[b_yst2[ys]])
                evo = K.op("sync", (lambda o, j, ys, t0: lambda h: h.dma_start(
                    outT[o * 128:(o + 1) * 128, t0 + j * JW:t0 + (j + 1) * JW],
                    ystage[:, (2 + ys) * 512:(2 + ys) * 512 + JW]))(o, j, ys, t0),
                    reads=[b_yst2[ys]], dma=True)
    K.op("sync", lambda h: None, waits=[evo])
    K.mark(evo)

    K.finalize(nc, block, sems)
    es.close()
    return nc


def prep_inputs(cfg, x_bh, w_qkv, b_qkv, w_out, b_out, router_w, down_w, up_w, own_first=True):
    H, NH, HC, S, VD, NVD, EI = cfg.H, cfg.NH, cfg.HC, cfg.S, cfg.VD, cfg.NVD, cfg.EI
    f32 = np.float32
    d = {}
    d["xT"] = np.ascontiguousarray(x_bh.T.astype(f32))
    Wq, Wk, Wv = w_qkv[0:H], w_qkv[H:2 * H], w_qkv[2 * H:3 * H]
    d["wq"] = np.ascontiguousarray(
        Wq.reshape(NH, 128, H).transpose(0, 2, 1))
    d["wk"] = np.ascontiguousarray(Wk.reshape(NH, 128, H).transpose(0, 2, 1))
    d["wv"] = np.ascontiguousarray(
        Wv.T.reshape(H, NVD, VD).transpose(1, 0, 2))
    d["wo"] = np.ascontiguousarray(w_out.reshape(HC, 128, H).transpose(0, 2, 1))
    d["bqk"] = np.ascontiguousarray(
        np.concatenate([b_qkv[0:H].reshape(NH, 128).T, b_qkv[H:2 * H].reshape(NH, 128).T], axis=1)).astype(f32)
    d["bo"] = np.ascontiguousarray(b_out.reshape(HC, 128).T).astype(f32)
    d["bv"] = b_qkv[2 * H:3 * H].reshape(1, H).astype(f32)
    d["wr"] = np.ascontiguousarray(
        router_w.T.reshape(HC, 128, 8).transpose(1, 0, 2).reshape(128, HC * 8)).astype(f32)
    Dcat = down_w.reshape(EI, H)
    d["wd"] = np.ascontiguousarray(
        Dcat.T.reshape(HC, 128, EI).transpose(1, 0, 2).reshape(128, HC * EI)).astype(f32)
    Ucat = up_w.transpose(1, 0, 2).reshape(H, EI)
    d["wu"] = np.ascontiguousarray(Ucat.T).astype(f32)
    cst = np.zeros((128, 256), dtype=BF)
    cst[:, 0:128] = np.eye(128, dtype=BF)
    sel = np.zeros((8, 128), dtype=BF)
    for e in range(8):
        sel[e, e * 16:(e + 1) * 16] = 1
    cst[0:8, 128:256] = sel
    d["cst"] = cst
    d["cstf"] = np.eye(128, dtype=f32)
    return d


from concourse.bass_utils import run_bass_kernel_spmd


def kernel(L0_attn_hidden_states, L0_attn_w_qkv, L0_attn_b_qkv, L0_attn_w_out,
           L0_attn_b_out, L0_ffn_router_w, L0_ffn_down_w, L0_ffn_up_w):
    x = np.asarray(L0_attn_hidden_states)
    w_qkv = np.asarray(L0_attn_w_qkv)
    b_qkv = np.asarray(L0_attn_b_qkv)
    w_out = np.asarray(L0_attn_w_out)
    b_out = np.asarray(L0_attn_b_out)
    router_w = np.asarray(L0_ffn_router_w)
    down_w = np.asarray(L0_ffn_down_w)
    up_w = np.asarray(L0_ffn_up_w)

    B, Sfull, H = x.shape
    cfg = Cfg(NH=16, S=4096, T=2048, QT=512, n_cores=8)
    nc = build(cfg)

    wdict = prep_inputs(cfg, np.zeros((cfg.T, H), np.float32), w_qkv, b_qkv,
                        w_out, b_out, router_w, down_w, up_w)
    del wdict["xT"]

    in_maps = []
    for c in range(8):
        b, hf = c // 2, c % 2
        m = dict(wdict)
        m["xT"] = np.ascontiguousarray(x[b, hf * cfg.T:(hf + 1) * cfg.T, :].T.astype(np.float32))
        in_maps.append(m)

    res = run_bass_kernel_spmd(nc, in_maps, core_ids=list(range(8)))

    out = np.empty((B, Sfull, H), dtype=np.float32)
    for c in range(8):
        b, hf = c // 2, c % 2
        out[b, hf * cfg.T:(hf + 1) * cfg.T, :] = res.results[c]["outT"].T
    return out


# version 11
# speedup vs baseline: 1.0844x; 1.0844x over previous
import math
import numpy as np
import ml_dtypes
from contextlib import ExitStack

import concourse.bass as bass
import concourse.mybir as mybir
from concourse.alu_op_type import AluOpType

F32 = mybir.dt.float32
BF16 = mybir.dt.bfloat16
AF = mybir.ActivationFunctionType
BF = ml_dtypes.bfloat16
ENGINES = ("tensor", "vector", "scalar", "gpsimd", "sync")
DMA_SEMS = {"vector": ["vd0"], "scalar": ["sd0", "sd1"],
            "gpsimd": ["gd0", "gd1", "gd2", "gd3"],
            "sync": ["yd0", "yd1", "yd2", "yd3"]}
ALL_SEMS = ["pe", "ve", "ac", "gp", "sy"] + [s for l in DMA_SEMS.values() for s in l]
ENG_SEM = {"tensor": "pe", "vector": "ve", "scalar": "ac", "gpsimd": "gp", "sync": "sy"}


class Ev:
    __slots__ = ("eng", "sem", "idx", "by", "used", "val")

    def __init__(self, eng, sem, idx, by):
        self.eng, self.sem, self.idx, self.by = eng, sem, idx, by
        self.used = False
        self.val = None


class Buf:
    __slots__ = ("w", "r", "name")

    def __init__(self, name=""):
        self.w = None
        self.r = []
        self.name = name


class Sched:
    def __init__(self):
        self.recs = {e: [] for e in ENGINES}
        self.dma_rot = {e: 0 for e in ENGINES}
        self.last_dma = {}
        self.tail = {e: None for e in ENGINES}
        self.pending = {e: [] for e in ENGINES}

    def barrier(self):
        evs = [e for e in self.tail.values() if e is not None]
        evs += [e for e in self.last_dma.values() if e is not None]
        for eng in ENGINES:
            self.pending[eng] = list(evs)

    def op(self, eng, fn, reads=(), writes=(), waits=(), dma=False, inc_by=None):
        if dma:
            sl = DMA_SEMS[eng]
            sem = sl[self.dma_rot[eng] % len(sl)]
            self.dma_rot[eng] += 1
        else:
            sem = ENG_SEM[eng]
        by = inc_by if inc_by is not None else (16 if dma else 1)
        ev = Ev(eng, sem, len(self.recs[eng]), by)
        wl = []
        if dma:
            ev.used = True
            prev = self.last_dma.get(sem)
            if prev is not None:
                wl.append(prev)
            self.last_dma[sem] = ev
        for b in reads:
            if b.w is not None:
                wl.append(b.w)
        for b in writes:
            if b.w is not None:
                wl.append(b.w)
            wl.extend(b.r)
        for e in waits:
            wl.append(e)
        seen, wl2 = set(), []
        for e in wl:
            if e is None or id(e) in seen:
                continue
            seen.add(id(e))
            wl2.append(e)
        if self.pending[eng]:
            for e in self.pending[eng]:
                if id(e) not in seen and e is not ev:
                    seen.add(id(e))
                    wl2.append(e)
            self.pending[eng] = []
        self.recs[eng].append((fn, wl2, ev))
        self.tail[eng] = ev
        for b in reads:
            b.r.append(ev)
        for b in writes:
            b.w = ev
            b.r = []
        return ev

    def mark(self, *evs):
        for e in evs:
            if e is not None:
                e.used = True

    def finalize(self, nc, block, sems):
        for eng in ENGINES:
            cnt = {}
            for fn, wl, ev in self.recs[eng]:
                for e in wl:
                    e.used = True
        for eng in ENGINES:
            cnt = {}
            for fn, wl, ev in self.recs[eng]:
                if ev.used:
                    cnt[ev.sem] = cnt.get(ev.sem, 0) + ev.by
                    ev.val = cnt[ev.sem]
        def make(eng):
            def emit(h):
                hw = {}
                for fn, wl, ev in self.recs[eng]:
                    for e in wl:
                        if e.val is None:
                            continue
                        key = e.sem
                        if hw.get(key, 0) >= e.val:
                            continue
                        h.wait_ge(sems[e.sem], e.val)
                        hw[key] = e.val
                    inst = fn(h)
                    if ev.used and ev.val is not None:
                        assert inst is not None, f"{eng} op needs inc but returned None"
                        inst.then_inc(sems[ev.sem], ev.by)
            return emit
        for eng in ENGINES:
            if not self.recs[eng]:
                continue
            getattr(block, eng)(make(eng))


class Cfg:
    def __init__(self, NH=16, S=4096, T=2048, QT=512, n_cores=8):
        self.NH = NH
        self.HC = NH
        self.H = NH * 128
        self.S = S
        self.T = T
        self.QT = QT
        self.KC = S // 128
        self.NQT = T // QT
        self.E, self.I = 8, 16
        self.EI = 128
        self.SCALE = 1.0 / math.sqrt(128.0)
        self.VD = min(512, self.H)
        self.NVD = self.H // self.VD
        self.TB = S // 128
        self.TBo = T // 128
        self.n_cores = n_cores
        self.EPS = 1e-5
        self.TW = min(2 * QT, T)
        self.NTW = T // self.TW
        self.JW = min(512, self.TW)
        self.JT = self.TW // self.JW


def build(cfg: Cfg, debug=False):
    nc = bass.Bass("TRN2", target_bir_lowering=False)
    K = Sched()
    H, S, T, QT, NH, HC, KC = cfg.H, cfg.S, cfg.T, cfg.QT, cfg.NH, cfg.HC, cfg.KC
    NQT, EI, VD, NVD, TB = cfg.NQT, cfg.EI, cfg.VD, cfg.NVD, cfg.TB
    TW, NTW, JW, JT = cfg.TW, cfg.NTW, cfg.JW, cfg.JT
    SC = cfg.SCALE
    IH = 1.0 / H

    xT = nc.dram_tensor("xT", [H, T], F32, kind="ExternalInput")
    wq_h = nc.dram_tensor("wq", [NH, HC * 128, 128], F32, kind="ExternalInput")
    wk_h = nc.dram_tensor("wk", [NH, HC * 128, 128], F32, kind="ExternalInput")
    wv_h = nc.dram_tensor("wv", [NVD, HC * 128, VD], F32, kind="ExternalInput")
    wo_h = nc.dram_tensor("wo", [HC, HC * 128, 128], F32, kind="ExternalInput")
    bqk_h = nc.dram_tensor("bqk", [128, 2 * NH], F32, kind="ExternalInput")
    bo_h = nc.dram_tensor("bo", [128, HC], F32, kind="ExternalInput")
    bv_h = nc.dram_tensor("bv", [1, H], F32, kind="ExternalInput")
    wr_h = nc.dram_tensor("wr", [128, HC * 8], F32, kind="ExternalInput")
    wd_h = nc.dram_tensor("wd", [128, HC * EI], F32, kind="ExternalInput")
    wu_h = nc.dram_tensor("wu", [EI, H], F32, kind="ExternalInput")
    cst_h = nc.dram_tensor("cst", [128, 256], BF16, kind="ExternalInput")
    cstf_h = nc.dram_tensor("cstf", [128, 128], F32, kind="ExternalInput")
    outT = nc.dram_tensor("outT", [H, T], F32, kind="ExternalOutput")

    _scr_kind = "ExternalOutput" if debug else "Internal"
    TBo = cfg.TBo
    qscr = nc.dram_tensor("qscr", [NH, 128, T], BF16, kind=_scr_kind)
    kag_i = nc.dram_tensor("kag_i", [NH, 128, T], BF16)
    kag_o = nc.dram_tensor("kag_o", [2, NH, 128, T], BF16, kind=_scr_kind)
    vag_i = nc.dram_tensor("vag_i", [TBo, NH, 128, 128], BF16)
    vag_o = nc.dram_tensor("vag_o", [2, TBo, NH, 128, 128], BF16, kind=_scr_kind)
    yscr = nc.dram_tensor("yscr", [H, T], F32, kind=_scr_kind)
    aodump = nc.dram_tensor("aodump", [128, HC * T], BF16, kind="ExternalOutput") if debug else None
    r1dump = nc.dram_tensor("r1dump", [NTW, 128, HC * TW], F32, kind="ExternalOutput") if debug else None
    accdump = nc.dram_tensor("accdump", [NH, T // 128, 128, 8], F32, kind="ExternalOutput") if debug else None
    stdump = nc.dram_tensor("stdump", [NTW, 128, 6 * JT * JW], F32, kind="ExternalOutput") if debug else None

    es = ExitStack()
    sbt = lambda name, shape, dt: es.enter_context(nc.sbuf_tensor(name, shape, dt))

    cst = sbt("cst_sb", [128, 256], BF16)
    cstf = sbt("cstf_sb", [128, 128], F32)
    sel_sb = sbt("sel_sb", [8, 128], BF16)
    ones_sb = sbt("ones_sb", [128, 128], BF16)
    bqk_sb = sbt("bqk_sb", [128, 2 * NH], F32)
    bo_sb = sbt("bo_sb", [128, HC], F32)
    bv_sb = sbt("bv_sb", [1, H], BF16)
    wr_sb = sbt("wr_sb", [128, HC * 8], BF16)
    wd_sb = sbt("wd_sb", [128, HC * EI], BF16)
    wu_sb = sbt("wu_sb", [128, H], BF16)

    NTB = T // 128
    RSTR = JT * JW
    def _al(x):
        return (x + 63) // 64 * 64
    A_XT = 0
    A_wv = _al(A_XT + HC * T)
    A_wqk = _al(A_wv + HC * VD)
    A_kst = _al(A_wqk + 2 * HC * 128)
    A_vst = _al(A_kst + 2 * 512)
    p1_end = _al(A_vst + 2 * 512)
    GQ = min(4, T // 128)
    QW = GQ * 128
    NGQ = T // QW
    SH = S // 2
    A_q2 = 0
    A_k2 = _al(A_q2 + 2 * T)
    A_v2 = _al(A_k2 + 2 * S)
    A_PT = _al(A_v2 + 2 * S)
    A_P2s = _al(A_PT + KC * QW)
    A_P2q = _al(A_P2s + 2 * S)
    A_junk = _al(A_P2q + 2 * S)
    NQS_ = T // 128
    A_accQ = _al(A_junk + 2048)
    A_mnegs = _al(A_accQ + 2 * 4 * NH * NQS_)
    A_acc = _al(A_mnegs + 2 * NH * NQS_)
    A_lcol = _al(A_acc + 64)
    A_lirow = _al(A_lcol + 16)
    A_lrep = _al(A_lirow + 1536)
    p2_end = _al(A_lrep + 1024)
    A_r1 = 0
    A_xres = _al(A_r1 + 2 * HC * TW)
    A_yst = _al(A_xres + 2 * 4 * 512)
    A_rb = _al(A_yst + 2 * 4 * 512)
    A_sq = _al(A_rb + 2 * 512)
    A_stat = _al(A_sq + 2 * 512)
    A_wo = _al(A_stat + 2 * 6 * RSTR)
    wo_end = _al(A_wo + 2 * HC * 128)
    A_lgf = A_wo
    A_gT = _al(A_lgf + 2 * T)
    A_gtok = _al(A_gT + T)
    A_gtk2 = _al(A_gtok + 2 * NTB * 8)
    A_gsc = _al(A_gtk2 + 2 * NTB * 8)
    A_gsc2 = _al(A_gsc + 2 * NTB)
    A_gtk3 = _al(A_gsc2 + 2 * NTB)
    A_ag = _al(A_gtk3 + 2 * NTB * 8)
    p34_end = max(_al(A_ag + 4 * 512), wo_end)
    AO_cols = HC * T
    AO_off = max(p2_end, p34_end)
    AR_COLS = _al(max(AO_off + AO_cols, p1_end) + 64)
    arena = sbt("arena", [128, AR_COLS], BF16)
    assert AR_COLS * 2 <= 190 * 1024, f"arena {AR_COLS * 2} B/partition too big"

    XT = arena[:, A_XT:A_XT + HC * T]
    wv_sb = arena[:, A_wv:A_wv + HC * VD]
    wqk0 = A_wqk
    kst = arena[:, A_kst:A_kst + 2 * 512]
    vst = arena[:, A_vst:A_vst + 2 * 512]
    AO = arena[:, AO_off:AO_off + AO_cols]
    q2o, k2o, v2o = A_q2, A_k2, A_v2
    PTsl = arena[:, A_PT:A_PT + KC * QW]
    P2s = arena[:, A_P2s:A_P2s + 2 * S]
    P2q = arena[:, A_P2q:A_P2q + 2 * S]
    junk = arena[:, A_junk:A_junk + 2048]
    accQ = arena[:, A_accQ:A_accQ + 2 * 4 * NH * NQS_].bitcast(F32)
    accQI = arena[:, A_accQ:A_accQ + 2 * 4 * NH * NQS_].bitcast(mybir.dt.int32)
    mnegs = arena[:, A_mnegs:A_mnegs + 2 * NH * NQS_].bitcast(F32)
    accs = arena[:, A_acc:A_acc + 64].bitcast(F32)
    accsI = arena[:, A_acc:A_acc + 64].bitcast(mybir.dt.int32)
    lcol = arena[:, A_lcol:A_lcol + 16].bitcast(F32)
    lirow = arena[0:1, A_lirow:A_lirow + 1024].bitcast(F32)
    lirow_bf = arena[0:1, A_lirow + 1024:A_lirow + 1536]
    lrep_sb = arena[:, A_lrep:A_lrep + 1024].bitcast(F32)
    r1v = arena[:, A_r1:A_r1 + 2 * HC * TW].bitcast(F32)
    x_res = arena[:, A_xres:A_xres + 2 * 4 * 512].bitcast(F32)
    ystage = arena[:, A_yst:A_yst + 2 * 4 * 512].bitcast(F32)
    rbst = arena[:, A_rb:A_rb + 2 * 512]
    sqst = arena[:, A_sq:A_sq + 2 * 512]
    stat = arena[:, A_stat:A_stat + 2 * 6 * RSTR].bitcast(F32)
    wo0 = A_wo
    lgf = arena[0:8, A_lgf:A_lgf + 2 * T].bitcast(F32)
    gT_sb = arena[0:8, A_gT:A_gT + T]
    gtok = arena[:, A_gtok:A_gtok + 2 * NTB * 8].bitcast(F32)
    gtk2 = arena[:, A_gtk2:A_gtk2 + 2 * NTB * 8].bitcast(F32)
    gsc = arena[:, A_gsc:A_gsc + 2 * NTB].bitcast(F32)
    gsc2 = arena[:, A_gsc2:A_gsc2 + 2 * NTB].bitcast(F32)
    gtk3 = arena[:, A_gtk3:A_gtk3 + 2 * NTB * 8].bitcast(F32)
    agst = arena[:, A_ag:A_ag + 4 * 512]

    psA = es.enter_context(nc.psum_tensor("psA", [128, 2048], F32))
    psB = es.enter_context(nc.psum_tensor("psB", [128, 2048], F32))

    sems = {}
    for s in ALL_SEMS:
        sems[s] = es.enter_context(nc.semaphore("sem_" + s))
    block = es.enter_context(nc.Block())

    b_cst, b_cstf, b_sel, b_ones = Buf("cst"), Buf("cstf"), Buf("sel"), Buf("ones")
    b_bqk, b_bo, b_bv = Buf("bqk"), Buf("bo"), Buf("bv")
    b_wr, b_wd, b_wu = Buf("wr"), Buf("wd"), Buf("wu")
    b_XT = Buf("XT")
    b_wv = Buf("wv")
    b_wqk = [Buf("wqk%d" % i) for i in range(2)]
    b_kst = [Buf("kst%d" % i) for i in range(2)]
    b_vst = [Buf("vst%d" % i) for i in range(2)]
    b_psA = [Buf("psA%d" % i) for i in range(4)]
    b_psB = [Buf("psB%d" % i) for i in range(4)]
    e_kscr = [[] for _ in range(NH)]
    e_qscr = [[] for _ in range(NH)]
    e_vscr = [[] for _ in range(NH)]
    b_k2 = [Buf("k2_%d" % i) for i in range(2)]
    b_q2 = [Buf("q2_%d" % i) for i in range(2)]
    b_v2 = [Buf("v2_%d" % i) for i in range(2)]
    b_P2s = [Buf("P2s%d" % i) for i in range(2)]
    b_P2q = [Buf("P2q%d" % i) for i in range(2)]
    b_accQ = Buf("accQ")
    b_mnegs = Buf("mnegs")
    b_PT = Buf("PT")
    b_junk = Buf("junk")
    b_acc = [Buf("acc%d" % i) for i in range(2)]
    b_lcol = Buf("lcol")
    b_lirow = Buf("lirow")
    b_lrep = Buf("lrep")
    b_AO = [[Buf("AO_%d_%d" % (c, j)) for j in range(T // 512 if T >= 512 else 1)] for c in range(HC)]
    b_wo = [Buf("wo%d" % i) for i in range(2)]
    b_x = [Buf("x%d" % i) for i in range(4)]
    b_r1 = [[Buf("r1_%d_%d" % (o, j)) for j in range(JT)] for o in range(HC)]
    b_rb = [Buf("rb%d" % i) for i in range(2)]
    b_sq = [Buf("sq%d" % i) for i in range(2)]
    b_stat = [Buf("stat%d" % j) for j in range(4)]
    b_yst = [Buf("yst%d" % i) for i in range(2)]
    b_yst2 = [Buf("yst2_%d" % i) for i in range(2)]
    b_stat2 = [Buf("stat2_%d" % i) for i in range(2)]
    b_yscr = [[Buf("yscr_%d_%d" % (o, j)) for j in range(T // 512 if T >= 512 else 1)] for o in range(HC)]
    b_lgf, b_gtok, b_gtk2, b_gsc, b_gT = Buf("lgf"), Buf("gtok"), Buf("gtk2"), Buf("gsc"), Buf("gT")
    b_gsc2, b_gtk3 = Buf("gsc2"), Buf("gtk3")
    b_ag = [Buf("ag%d" % i) for i in range(4)]

    AOW = 512 if T >= 512 else T

    def aoslice(c, t0, w):
        return AO[:, c * T + t0:c * T + t0 + w]

    def aobufs(t0, w, cs=None):
        out = []
        for c in (range(HC) if cs is None else cs):
            for j in range(t0 // AOW, (t0 + w + AOW - 1) // AOW):
                out.append(b_AO[c][j])
        return out

    g = "gpsimd"
    K.op(g, lambda h: h.dma_start(cst[:, :], cst_h[:, :]), writes=[b_cst], dma=True)
    K.op(g, lambda h: h.dma_start(cstf[:, :], cstf_h[:, :]), writes=[b_cstf], dma=True)
    K.op(g, lambda h: h.dma_start(sel_sb[:, :], cst_h[0:8, 128:256]), writes=[b_sel], dma=True)
    K.op(g, lambda h: h.dma_start(bqk_sb[:, :], bqk_h[:, :]), writes=[b_bqk], dma=True)
    K.op(g, lambda h: h.dma_start(bo_sb[:, :], bo_h[:, :]), writes=[b_bo], dma=True)
    K.op(g, lambda h: h.dma_start(bv_sb[:, :], bv_h[:, :]), writes=[b_bv], dma=True)
    K.op(g, lambda h: h.dma_start(wr_sb[:, :], wr_h[:, :]), writes=[b_wr], dma=True)
    K.op(g, lambda h: h.dma_start(wd_sb[:, :], wd_h[:, :]), writes=[b_wd], dma=True)
    K.op(g, lambda h: h.dma_start(wu_sb[:, :], wu_h[:, :]), writes=[b_wu], dma=True)
    for c in range(HC):
        K.op(g, (lambda c: lambda h: h.dma_start(
            XT[:, c * T:(c + 1) * T], xT[c * 128:(c + 1) * 128, :]))(c),
            writes=[b_XT], dma=True)
    K.op("vector", lambda h: h.memset(ones_sb[:, :], 1.0), writes=[b_ones])

    ident = cst[:, 0:128]
    ident8f = cstf[0:8, 0:8]
    identf = cstf[:, :]

    for dt in range(NVD):
        K.op(g, (lambda dt: lambda h: h.dma_start(
            wv_sb[:, :].rearrange("p (c d) -> p c d", d=VD),
            wv_h[dt, :, :].rearrange("(c p) d -> p c d", p=128)))(dt),
            writes=[b_wv], dma=True)
        for tb in range(TBo):
            slot = (dt * TBo + tb) % 4
            def mmg(h, dt=dt, tb=tb, slot=slot):
                for c in range(HC):
                    h.matmul(psA[:, slot * 512:slot * 512 + VD],
                             XT[:, c * T + tb * 128:c * T + (tb + 1) * 128],
                             wv_sb[:, c * VD:(c + 1) * VD],
                             start=(c == 0), stop=False)
                return h.matmul(psA[:, slot * 512:slot * 512 + VD],
                                ones_sb[0:1, 0:128], bv_sb[:, dt * VD:(dt + 1) * VD],
                                start=False, stop=True)
            K.op("tensor", mmg, reads=[b_XT, b_wv, b_ones, b_bv], writes=[b_psA[slot]])
            vs = (dt * TBo + tb) % 2
            K.op("vector", (lambda slot, vs: lambda h: h.tensor_copy(
                vst[:, vs * 512:vs * 512 + VD], psA[:, slot * 512:slot * 512 + VD]))(slot, vs),
                reads=[b_psA[slot]], writes=[b_vst[vs]])
            nh_t = VD // 128
            def spill(h, dt=dt, tb=tb, vs=vs, nh_t=nh_t):
                return h.dma_start(vag_i[tb, dt * nh_t:(dt + 1) * nh_t, :, :].transpose([1, 0, 2]),
                                   vst[:, vs * 512:vs * 512 + VD].rearrange("p (j d) -> p j d", d=128))
            ev = K.op("sync", spill, reads=[b_vst[vs]], dma=True)
            e_vscr[0].append(ev)

    def qk_proj(which):
        span = T
        tile_w = min(512, span)
        ntile = span // tile_w
        wsrc = wk_h if which == "k" else wq_h
        scr = kag_i if which == "k" else qscr
        escr = e_kscr if which == "k" else e_qscr
        bias0 = NH if which == "k" else 0
        for n in range(NH):
            slot = n % 2
            K.op(g, (lambda n, slot, wsrc=wsrc: lambda h: h.dma_start(
                arena[:, wqk0 + slot * HC * 128: wqk0 + (slot + 1) * HC * 128]
                .rearrange("p (c o) -> p c o", o=128),
                wsrc[n, :, :].rearrange("(c p) o -> p c o", p=128)))(n, slot),
                writes=[b_wqk[slot]], dma=True)
            for j in range(ntile):
                idx = n * ntile + j
                ps = idx % 4
                def mmg(h, n=n, j=j, slot=slot, ps=ps, tile_w=tile_w):
                    for c in range(HC):
                        mm = h.matmul(psB[:, ps * 512:ps * 512 + tile_w],
                                      arena[:, wqk0 + slot * HC * 128 + c * 128:
                                            wqk0 + slot * HC * 128 + (c + 1) * 128],
                                      XT[:, c * T + j * tile_w:c * T + (j + 1) * tile_w],
                                      start=(c == 0), stop=(c == HC - 1))
                    return mm
                K.op("tensor", mmg, reads=[b_XT, b_wqk[slot]], writes=[b_psB[ps]])
                ks = idx % 2
                K.op("scalar", (lambda ps, ks, n, tile_w: lambda h: h.activation(
                    kst[:, ks * 512:ks * 512 + tile_w], psB[:, ps * 512:ps * 512 + tile_w],
                    AF.Identity, bias=bqk_sb[:, bias0 + n:bias0 + n + 1], scale=1.0))(ps, ks, n, tile_w),
                    reads=[b_psB[ps], b_bqk], writes=[b_kst[ks]])
                ev = K.op("sync", (lambda n, j, ks, tile_w, scr=scr: lambda h: h.dma_start(
                    scr[n, :, j * tile_w:(j + 1) * tile_w],
                    kst[:, ks * 512:ks * 512 + tile_w]))(n, j, ks, tile_w),
                    reads=[b_kst[ks]], dma=True)
                escr[n].append(ev)
    qk_proj("k")
    qk_proj("q")

    pair_groups = [[i, i + 1] for i in range(0, cfg.n_cores, 2)]
    e_kag = K.op(g, (lambda: lambda h: h.collective_compute(
        "AllGather", AluOpType.bypass, replica_groups=pair_groups,
        ins=[kag_i.ap().opt()], outs=[kag_o.ap().opt()]))(),
        waits=[e for n in range(NH) for e in e_kscr[n]])
    e_vag = K.op(g, (lambda: lambda h: h.collective_compute(
        "AllGather", AluOpType.bypass, replica_groups=pair_groups,
        ins=[vag_i.ap().opt()], outs=[vag_o.ap().opt()]))(),
        waits=e_vscr[0])

    K.barrier()
    nSA = (SH + 511) // 512
    NBA = max(1, KC // 4)
    psA_banks = [b_psA[i] for i in range(min(4, nSA))]
    psB_banks = [b_psB[i] for i in range(min(4, nSA))]
    NQS = T // 128
    import functools

    nstr = max(1, KC // 4)
    NV = NH * NQS
    SSW = (nstr + 2) * 128
    for n in range(NH):
        sl = n % 2
        K.op("sync", (lambda n, sl: lambda h: h.dma_start(
            arena[:, q2o + sl * T:q2o + (sl + 1) * T], qscr[n, :, :]))(n, sl),
            waits=e_qscr[n], writes=[b_q2[sl]], dma=True)
        for half in range(2):
            K.op("sync", (lambda n, sl, half: lambda h: h.dma_start(
                arena[:, k2o + sl * S + half * T:k2o + sl * S + (half + 1) * T],
                kag_o[half, n, :, :]))(n, sl, half),
                waits=[e_kag], writes=[b_k2[sl]], dma=True)
        for qi in range(NQS):
            pp = qi % 2
            psS = psA if pp == 0 else psB
            bps = b_psA if pp == 0 else b_psB
            hn = max(1, nstr // 2)
            def smp(h, n=n, qi=qi, sl=sl, psS=psS, hn=hn):
                k2v = arena[:, k2o + sl * S:k2o + (sl + 1) * S]
                qv = arena[:, q2o + sl * T + qi * 128:q2o + sl * T + (qi + 1) * 128]
                kstr = k2v.rearrange("p (a b) -> p a b", b=512)[:, :, 0:128]
                h.matmul(psS[:, 0:hn * 128], qv, kstr[:, 0:hn, :], start=True, stop=True)
                if nstr > hn:
                    h.matmul(psS[:, 512:512 + (nstr - hn) * 128], qv, kstr[:, hn:nstr, :],
                             start=True, stop=True)
                return h.matmul(psS[:, 1024:1024 + 256],
                                qv, k2v.rearrange("p (h b) -> p h b", h=2)
                                [:, :, qi * 128:(qi + 1) * 128],
                                start=True, stop=True)
            K.op("tensor", smp, reads=[b_k2[sl], b_q2[sl]], writes=[bps[0], bps[1], bps[2]])
            c0 = n * NQS + qi
            K.op("scalar", (lambda c0, psS, hn: lambda h: h.activation(
                junk[:, 0:hn * 128], psS[:, 0:hn * 128], AF.Exp, scale=SC / 4.0,
                accum_out=accQ[:, c0:c0 + 1]))(c0, psS, hn),
                reads=[bps[0], bps[1], bps[2]], writes=[b_junk, b_accQ])
            if nstr > hn:
                K.op("scalar", (lambda c0, psS, hn: lambda h: h.activation(
                    junk[:, 0:(nstr - hn) * 128], psS[:, 512:512 + (nstr - hn) * 128],
                    AF.Exp, scale=SC / 4.0,
                    accum_out=accQ[:, NV + c0:NV + c0 + 1]))(c0, psS, hn),
                    reads=[bps[0], bps[1], bps[2]], writes=[b_junk, b_accQ])
            else:
                K.op("vector", (lambda c0: lambda h: h.memset(
                    accQ[:, NV + c0:NV + c0 + 1], 0.0))(c0),
                    writes=[b_accQ])
            K.op("scalar", (lambda c0, psS: lambda h: h.activation(
                junk[:, 0:256], psS[:, 1024:1024 + 256], AF.Exp, scale=SC / 4.0,
                accum_out=accQ[:, 2 * NV + c0:2 * NV + c0 + 1]))(c0, psS),
                reads=[bps[0], bps[1], bps[2]], writes=[b_junk, b_accQ])
        r0 = n * NQS
        K.op("vector", (lambda r0: lambda h: h.tensor_tensor(
            accQ[:, 3 * NV + r0:3 * NV + r0 + NQS], accQ[:, r0:r0 + NQS],
            accQ[:, NV + r0:NV + r0 + NQS], op=AluOpType.add))(r0),
            reads=[b_accQ], writes=[b_accQ])
        K.op("vector", (lambda r0: lambda h: h.tensor_tensor(
            accQ[:, r0:r0 + NQS], accQ[:, 3 * NV + r0:3 * NV + r0 + NQS],
            accQ[:, 2 * NV + r0:2 * NV + r0 + NQS], op=AluOpType.add))(r0),
            reads=[b_accQ], writes=[b_accQ])
        K.op("vector", (lambda n: lambda h: h.tensor_scalar(
            accQI[:, NH * NQS + n * NQS:NH * NQS + (n + 1) * NQS],
            accQI[:, n * NQS:(n + 1) * NQS], 23, None,
            op0=AluOpType.arith_shift_right))(n),
            reads=[b_accQ], writes=[b_accQ])
        K.op("vector", (lambda n: lambda h: h.tensor_copy(
            accQ[:, 2 * NH * NQS + n * NQS:2 * NH * NQS + (n + 1) * NQS],
            accQI[:, NH * NQS + n * NQS:NH * NQS + (n + 1) * NQS]))(n),
            reads=[b_accQ], writes=[b_accQ])
        K.op("vector", (lambda n: lambda h: h.tensor_scalar(
            mnegs[:, n * NQS:(n + 1) * NQS],
            accQ[:, 2 * NH * NQS + n * NQS:2 * NH * NQS + (n + 1) * NQS],
            -127.0, -2.0 * 0.6931471805599453,
            op0=AluOpType.add, op1=AluOpType.mult))(n),
            reads=[b_accQ], writes=[b_mnegs])

    for n in range(NH):
        sl = n % 2
        K.op("sync", (lambda n, sl: lambda h: h.dma_start(
            arena[:, q2o + sl * T:q2o + (sl + 1) * T], qscr[n, :, :]))(n, sl),
            waits=e_qscr[n], writes=[b_q2[sl]], dma=True)
        for half in range(2):
            K.op("sync", (lambda n, sl, half: lambda h: h.dma_start(
                arena[:, k2o + sl * S + half * T:k2o + sl * S + (half + 1) * T],
                kag_o[half, n, :, :]))(n, sl, half),
                waits=[e_kag], writes=[b_k2[sl]], dma=True)
            K.op("sync", (lambda n, sl, half: lambda h: h.dma_start(
                arena[:, v2o + sl * S + half * T:v2o + sl * S + (half + 1) * T]
                .rearrange("p (tb d) -> p tb d", d=128),
                vag_o[half, :, n, :, :].transpose([1, 0, 2])))(n, sl, half),
                waits=[e_vag], writes=[b_v2[sl]], dma=True)
        for qg in range(NGQ):
            for qs in range(GQ):
                qi = qg * GQ + qs
                p = qi % 2
                def smm(h, n=n, qi=qi, sl=sl, half=0):
                    for m in range(nSA):
                        w = min(512, SH - m * 512)
                        ps = psA if half == 0 else psB
                        mm = h.matmul(ps[:, m * 512:m * 512 + w],
                                      arena[:, q2o + sl * T + qi * 128:q2o + sl * T + (qi + 1) * 128],
                                      arena[:, k2o + sl * S + half * SH + m * 512:
                                            k2o + sl * S + half * SH + m * 512 + w],
                                      start=True, stop=True)
                    return mm
                K.op("tensor", functools.partial(smm, half=0),
                     reads=[b_k2[sl], b_q2[sl]], writes=psA_banks)
                K.op("scalar", (lambda n, qi, p: lambda h: h.activation(
                    P2s[:, p * S:p * S + SH], psA[:, 0:SH], AF.Exp, scale=SC / 2.0,
                    bias=mnegs[:, n * NQS + qi:n * NQS + qi + 1]))(n, qi, p),
                    reads=psA_banks + [b_mnegs], writes=[b_P2s[p]])
                K.op("tensor", functools.partial(smm, half=1),
                     reads=[b_k2[sl], b_q2[sl]], writes=psB_banks)
                K.op("scalar", (lambda n, qi, p: lambda h: h.activation(
                    P2s[:, p * S + SH:p * S + 2 * SH], psB[:, 0:SH], AF.Exp, scale=SC / 2.0,
                    bias=mnegs[:, n * NQS + qi:n * NQS + qi + 1]))(n, qi, p),
                    reads=psB_banks + [b_mnegs], writes=[b_P2s[p]])
                K.op("vector", (lambda p: lambda h: h.tensor_tensor(
                    P2q[:, p * S:(p + 1) * S], P2s[:, p * S:(p + 1) * S],
                    P2s[:, p * S:(p + 1) * S], op=AluOpType.mult))(p),
                    reads=[b_P2s[p]], writes=[b_P2q[p]])
                for bb_ in range(NBA):
                    bb = bb_
                    base = (bb % 2) * 512
                    def trb(h, p=p, bb=bb, base=base):
                        for k_ in range(4):
                            j = bb * 4 + k_
                            if j >= KC:
                                break
                            tr = h.transpose(
                                psB[:, base + k_ * 64:base + k_ * 64 + 64].bitcast(BF16),
                                P2q[:, p * S + j * 128:p * S + (j + 1) * 128],
                                ident)
                        return tr
                    K.op("tensor", trb, reads=[b_P2q[p], b_cst], writes=[b_psB[bb % 2]])
                    nj = min(4, KC - bb * 4)
                    def evb(h, bb=bb, base=base, qs=qs, nj=nj):
                        return h.tensor_copy(
                            PTsl[:, :].rearrange("p (j q) -> p j q", q=QW)
                            [:, bb * 4:bb * 4 + nj, qs * 128:(qs + 1) * 128],
                            psB[:, base:base + nj * 64].bitcast(BF16)
                            .rearrange("p (j q) -> p j q", q=128))
                    K.op("vector", evb, reads=[b_psB[bb % 2]], writes=[b_PT])
            def pv(h, sl=sl):
                for kt in range(KC):
                    mm = h.matmul(psB[:, 1024:1024 + QW],
                                  arena[:, v2o + sl * S + kt * 128:v2o + sl * S + (kt + 1) * 128],
                                  PTsl[:, kt * QW:(kt + 1) * QW],
                                  start=(kt == 0), stop=(kt == KC - 1))
                return mm
            K.op("tensor", pv, reads=[b_v2[sl], b_PT], writes=[b_psB[2]])
            def lmm(h):
                for kt in range(KC):
                    mm = h.matmul(psB[:, 1536:1536 + QW], ones_sb[:, 0:128],
                                  PTsl[:, kt * QW:(kt + 1) * QW],
                                  start=(kt == 0), stop=(kt == KC - 1))
                return mm
            K.op("tensor", lmm, reads=[b_ones, b_PT], writes=[b_psB[3]])
            K.op("vector", lambda h: h.reciprocal(lrep_sb[:, 0:QW], psB[:, 1536:1536 + QW]),
                 reads=[b_psB[3]], writes=[b_lrep])
            K.op("vector", (lambda n, qg: lambda h: h.tensor_tensor(
                aoslice(n, qg * QW, QW), psB[:, 1024:1024 + QW], lrep_sb[:, 0:QW],
                op=AluOpType.mult))(n, qg),
                reads=[b_psB[2], b_lrep], writes=aobufs(qg * QW, QW, [n]))

    K.barrier()
    for tw in range(NTW):
        t0 = tw * TW
        for o in range(HC):
            i2 = tw * HC + o
            ws = i2 % 2
            K.op(g, (lambda o, ws: lambda h: h.dma_start(
                arena[:, wo0 + ws * HC * 128:wo0 + (ws + 1) * HC * 128]
                .rearrange("p (c q) -> p c q", q=128),
                wo_h[o, :, :].rearrange("(c p) q -> p c q", p=128)))(o, ws),
                writes=[b_wo[ws]], dma=True)
            for j in range(JT):
                gi = i2 * JT + j
                ps = gi % 2
                xs = gi % 4
                K.op("sync", (lambda o, j, t0, xs: lambda h: h.dma_start(
                    x_res[:, xs * 512:xs * 512 + JW],
                    xT[o * 128:(o + 1) * 128, t0 + j * JW:t0 + (j + 1) * JW]))(o, j, t0, xs),
                    writes=[b_x[xs]], dma=True)
                def pmm(h, o=o, j=j, ws=ws, ps=ps, t0=t0):
                    for c in range(HC):
                        mm = h.matmul(psA[:, ps * 512:ps * 512 + JW],
                                      arena[:, wo0 + ws * HC * 128 + c * 128:
                                            wo0 + ws * HC * 128 + (c + 1) * 128],
                                      aoslice(c, t0 + j * JW, JW),
                                      start=(c == 0), stop=(c == HC - 1))
                    return mm
                K.op("tensor", pmm, reads=[b_wo[ws]] + aobufs(t0 + j * JW, JW),
                     writes=[b_psA[ps]])
                K.op("vector", (lambda o, j, ps, xs: lambda h: h.scalar_tensor_tensor(
                    r1v[:, o * TW + j * JW:o * TW + (j + 1) * JW],
                    psA[:, ps * 512:ps * 512 + JW], bo_sb[:, o:o + 1],
                    x_res[:, xs * 512:xs * 512 + JW],
                    op0=AluOpType.add, op1=AluOpType.add))(o, j, ps, xs),
                    reads=[b_psA[ps], b_bo, b_x[xs]], writes=[b_r1[o][j]])
                rs_ = gi % 2
                K.op("vector", (lambda o, j, rs_: lambda h: h.tensor_copy(
                    rbst[:, rs_ * 512:rs_ * 512 + JW],
                    r1v[:, o * TW + j * JW:o * TW + (j + 1) * JW]))(o, j, rs_),
                    reads=[b_r1[o][j]], writes=[b_rb[rs_]])
                K.op("vector", (lambda rs_: lambda h: h.tensor_tensor(
                    sqst[:, rs_ * 512:rs_ * 512 + JW], rbst[:, rs_ * 512:rs_ * 512 + JW],
                    rbst[:, rs_ * 512:rs_ * 512 + JW], op=AluOpType.mult))(rs_),
                    reads=[b_rb[rs_]], writes=[b_sq[rs_]])
                def stmm(h, o=o, j=j, rs_=rs_):
                    h.matmul(psB[:, j * 512:j * 512 + JW], ones_sb[:, 0:128],
                             rbst[:, rs_ * 512:rs_ * 512 + JW],
                             start=(o == 0), stop=(o == HC - 1))
                    return h.matmul(psB[:, 1024 + j * 512:1024 + j * 512 + JW],
                                    ones_sb[:, 0:128], sqst[:, rs_ * 512:rs_ * 512 + JW],
                                    start=(o == 0), stop=(o == HC - 1))
                K.op("tensor", stmm, reads=[b_rb[rs_], b_sq[rs_], b_ones],
                     writes=[b_psB[j], b_psB[2 + j]])
        if debug:
            K.op("sync", (lambda tw: lambda h: h.dma_start(
                r1dump[tw, :, :], r1v[:, 0:HC * TW]))(tw),
                reads=[b for row in b_r1 for b in row], dma=True)
        for j in range(JT):
            mu = stat[:, 0 * RSTR + j * JW:0 * RSTR + j * JW + JW]
            rs = stat[:, 1 * RSTR + j * JW:1 * RSTR + j * JW + JW]
            mus = stat[:, 2 * RSTR + j * JW:2 * RSTR + j * JW + JW]
            scr2 = stat[:, 3 * RSTR + j * JW:3 * RSTR + j * JW + JW]
            bmu, brs, bmus, bscr = b_stat[0], b_stat[1], b_stat[2], b_stat[3]
            K.op("vector", (lambda mu, j: lambda h: h.tensor_scalar(
                mu, psB[:, j * 512:j * 512 + JW], IH, None, op0=AluOpType.mult))(mu, j),
                reads=[b_psB[j]], writes=[bmu])
            K.op("vector", (lambda scr2, j: lambda h: h.tensor_scalar(
                scr2, psB[:, 1024 + j * 512:1024 + j * 512 + JW], IH, cfg.EPS,
                op0=AluOpType.mult, op1=AluOpType.add))(scr2, j),
                reads=[b_psB[2 + j]], writes=[bscr])
            var2 = stat[:, 4 * RSTR + j * JW:4 * RSTR + j * JW + JW]
            rin2 = stat[:, 5 * RSTR + j * JW:5 * RSTR + j * JW + JW]
            bvar, brin = b_stat2[0], b_stat2[1]
            K.op("vector", (lambda rs, mu: lambda h: h.tensor_tensor(
                rs, mu, mu, op=AluOpType.mult))(rs, mu), reads=[bmu], writes=[brs])
            K.op("vector", (lambda var2, scr2, rs: lambda h: h.tensor_tensor(
                var2, scr2, rs, op=AluOpType.subtract))(var2, scr2, rs),
                reads=[brs, bscr], writes=[bvar])
            K.op("vector", (lambda rin2, var2: lambda h: h.reciprocal(rin2, var2))(rin2, var2),
                reads=[bvar], writes=[brin])
            K.op("scalar", (lambda rs, rin2: lambda h: h.activation(rs, rin2, AF.Sqrt))(rs, rin2),
                reads=[brin], writes=[brs])
            K.op("vector", (lambda mus, mu, rs: lambda h: h.tensor_tensor(
                mus, mu, rs, op=AluOpType.mult))(mus, mu, rs), reads=[bmu, brs], writes=[bmus])
            if debug and j == JT - 1:
                K.op("sync", (lambda tw: lambda h: h.dma_start(
                    stdump[tw, :, :], stat[:, 0:6 * RSTR]))(tw),
                    reads=[b_stat[0], b_stat[1], b_stat[2], b_stat[3],
                           b_stat2[0], b_stat2[1]], dma=True)
            for o in range(HC):
                ys = (o * JT + j) % 2
                K.op("vector", (lambda o, j, ys, rs: lambda h: h.tensor_tensor(
                    ystage[:, ys * 512:ys * 512 + JW],
                    r1v[:, o * TW + j * JW:o * TW + (j + 1) * JW], rs,
                    op=AluOpType.mult))(o, j, ys, rs),
                    reads=[b_r1[o][j], brs], writes=[b_yst[ys]])
                K.op("vector", (lambda ys, mus: lambda h: h.tensor_tensor(
                    ystage[:, (2 + ys) * 512:(2 + ys) * 512 + JW],
                    ystage[:, ys * 512:ys * 512 + JW],
                    mus, op=AluOpType.subtract))(ys, mus),
                    reads=[bmus, b_yst[ys]], writes=[b_yst2[ys]])
                K.op("scalar", (lambda o, j, ys, t0: lambda h: h.activation(
                    aoslice(o, t0 + j * JW, JW), ystage[:, (2 + ys) * 512:(2 + ys) * 512 + JW],
                    AF.Copy))(o, j, ys, t0),
                    reads=[b_yst2[ys]], writes=aobufs(t0 + j * JW, JW, [o]))
                ev = K.op("sync", (lambda o, j, ys, t0: lambda h: h.dma_start(
                    yscr[o * 128:(o + 1) * 128, t0 + j * JW:t0 + (j + 1) * JW],
                    ystage[:, (2 + ys) * 512:(2 + ys) * 512 + JW]))(o, j, ys, t0),
                    reads=[b_yst2[ys]], dma=True)
                b_yscr[o][(t0 + j * JW) // (512 if T >= 512 else T)].w = ev

    K.barrier()
    for jt in range(T // QT):
        def rmm(h, jt=jt):
            for c in range(HC):
                mm = h.matmul(psA[0:8, jt * 512:jt * 512 + QT],
                              wr_sb[:, c * 8:(c + 1) * 8],
                              aoslice(c, jt * QT, QT),
                              start=(c == 0), stop=(c == HC - 1))
            return mm
        K.op("tensor", rmm, reads=[b_wr] + aobufs(jt * QT, QT), writes=[b_psA[jt % 4]])
        K.op("vector", (lambda jt: lambda h: h.tensor_copy(
            lgf[:, jt * QT:(jt + 1) * QT], psA[0:8, jt * 512:jt * 512 + QT]))(jt),
            reads=[b_psA[jt % 4]], writes=[b_lgf])
    for tb in range(NTB):
        K.op("tensor", (lambda tb: lambda h: h.transpose(
            psA[:, (tb % 4) * 8:(tb % 4) * 8 + 8],
            lgf[0:8, tb * 128:(tb + 1) * 128], ident8f))(tb),
            reads=[b_lgf, b_cstf], writes=[b_psA[0]])
        if tb % 4 == 3 or tb == NTB - 1:
            lo = (tb // 4) * 4
            cnt = tb - lo + 1
            K.op("vector", (lambda lo, cnt: lambda h: h.tensor_copy(
                gtok[:, lo * 8:(lo + cnt) * 8], psA[:, 0:cnt * 8]))(lo, cnt),
                reads=[b_psA[0]], writes=[b_gtok])
    seg = gtok[:, :].rearrange("p (s e) -> p s e", e=8)
    seg2 = gtk2[:, :].rearrange("p (s e) -> p s e", e=8)
    seg3 = gtk3[:, :].rearrange("p (s e) -> p s e", e=8)
    gb = lambda: gsc[:, :, None].broadcast_to((128, NTB, 8))
    gb2 = lambda: gsc2[:, :, None].broadcast_to((128, NTB, 8))
    K.op("vector", lambda h: h.tensor_reduce(gsc[:, :, None], seg, axis=mybir.AxisListType.X,
                                             op=AluOpType.max), reads=[b_gtok], writes=[b_gsc])
    K.op("vector", lambda h: h.tensor_tensor(seg2, seg, gb(), op=AluOpType.subtract),
         reads=[b_gtok, b_gsc], writes=[b_gtk2])
    K.op("scalar", lambda h: h.activation(gtok[:, :], gtk2[:, :], AF.Exp),
         reads=[b_gtk2], writes=[b_gtok])
    K.op("vector", lambda h: h.tensor_reduce(gsc[:, :, None], seg, axis=mybir.AxisListType.X,
                                             op=AluOpType.add), reads=[b_gtok], writes=[b_gsc])
    K.op("vector", lambda h: h.reciprocal(gsc2[:, :], gsc[:, :]), reads=[b_gsc], writes=[b_gsc2])
    K.op("vector", lambda h: h.tensor_tensor(seg2, seg, gb2(), op=AluOpType.mult),
         reads=[b_gsc2, b_gtok], writes=[b_gtk2])
    K.op("vector", lambda h: h.tensor_reduce(gsc[:, :, None], seg2, axis=mybir.AxisListType.X,
                                             op=AluOpType.max), reads=[b_gtk2], writes=[b_gsc])
    K.op("vector", lambda h: h.tensor_tensor(seg, seg2, gb(), op=AluOpType.is_ge),
         reads=[b_gtk2, b_gsc], writes=[b_gtok])
    K.op("vector", lambda h: h.scalar_tensor_tensor(gtk3[:, :], gtok[:, :], -2.0, gtk2[:, :],
                                                    op0=AluOpType.mult, op1=AluOpType.add),
         reads=[b_gtk2, b_gtok], writes=[b_gtk3])
    K.op("vector", lambda h: h.tensor_reduce(gsc[:, :, None], seg3, axis=mybir.AxisListType.X,
                                             op=AluOpType.max), reads=[b_gtk3], writes=[b_gsc])
    K.op("vector", lambda h: h.tensor_tensor(seg, seg2, gb(), op=AluOpType.is_ge),
         reads=[b_gtk2, b_gsc], writes=[b_gtok])
    K.op("vector", lambda h: h.tensor_tensor(seg3, seg, seg2, op=AluOpType.mult),
         reads=[b_gtk2, b_gtok], writes=[b_gtk3])
    for tb in range(NTB):
        K.op("tensor", (lambda tb: lambda h: h.transpose(
            psA[0:8, 512 + (tb % 4) * 128:512 + (tb % 4) * 128 + 128],
            gtk3[:, tb * 8:(tb + 1) * 8], identf))(tb),
            reads=[b_gtk3, b_cstf], writes=[b_psA[1]])
        K.op("vector", (lambda tb: lambda h: h.tensor_copy(
            gT_sb[:, tb * 128:(tb + 1) * 128],
            psA[0:8, 512 + (tb % 4) * 128:512 + (tb % 4) * 128 + 128]))(tb),
            reads=[b_psA[1]], writes=[b_gT])

    for tw in range(NTW):
        t0 = tw * TW
        for j in range(JT):
            jj = tw * JT + j
            K.op("tensor", (lambda t0, j: lambda h: h.matmul(
                psB[:, 1536:1536 + JW], sel_sb[:, :], gT_sb[:, t0 + j * JW:t0 + (j + 1) * JW],
                start=True, stop=True))(t0, j),
                reads=[b_sel, b_gT], writes=[b_psB[3]])
            def dmm(h, t0=t0, j=j):
                for c in range(HC):
                    mm = h.matmul(psB[:, 1024:1024 + JW], wd_sb[:, c * EI:(c + 1) * EI][:, 0:128],
                                  aoslice(c, t0 + j * JW, JW), start=(c == 0), stop=(c == HC - 1))
                return mm
            K.op("tensor", dmm, reads=[b_wd] + aobufs(t0 + j * JW, JW), writes=[b_psB[2]])
            sb_ = jj % 2
            K.op("scalar", (lambda sb_: lambda h: h.activation(
                agst[:, sb_ * 512:sb_ * 512 + JW], psB[:, 1024:1024 + JW], AF.Silu))(sb_),
                reads=[b_psB[2]], writes=[b_ag[sb_]])
            K.op("vector", (lambda sb_: lambda h: h.tensor_tensor(
                agst[:, 1024 + sb_ * 512:1024 + sb_ * 512 + JW],
                agst[:, sb_ * 512:sb_ * 512 + JW], psB[:, 1536:1536 + JW],
                op=AluOpType.mult))(sb_),
                reads=[b_ag[sb_], b_psB[3]], writes=[b_ag[2 + sb_]])
            for o in range(HC):
                oi = o * JT + j
                ps = oi % 2
                xs = (tw * HC * JT + o * JT + j) % 2
                K.op("sync", (lambda o, j, t0, xs: lambda h: h.dma_start(
                    x_res[:, (2 + xs) * 512:(2 + xs) * 512 + JW],
                    yscr[o * 128:(o + 1) * 128, t0 + j * JW:t0 + (j + 1) * JW]))(o, j, t0, xs),
                    reads=[b_yscr[o][(t0 + j * JW) // (512 if T >= 512 else T)]],
                    writes=[b_x[2 + xs]], dma=True)
                K.op("tensor", (lambda o, sb_, ps: lambda h: h.matmul(
                    psB[:, ps * 512:ps * 512 + JW], wu_sb[:, o * 128:(o + 1) * 128],
                    agst[:, 1024 + sb_ * 512:1024 + sb_ * 512 + JW],
                    start=True, stop=True))(o, sb_, ps),
                    reads=[b_wu, b_ag[2 + sb_]], writes=[b_psB[ps]])
                K.op("vector", (lambda o, j, ps, xs: lambda h: h.tensor_tensor(
                    r1v[:, o * TW + j * JW:o * TW + (j + 1) * JW],
                    psB[:, ps * 512:ps * 512 + JW], x_res[:, (2 + xs) * 512:(2 + xs) * 512 + JW],
                    op=AluOpType.add))(o, j, ps, xs),
                    reads=[b_psB[ps], b_x[2 + xs]], writes=[b_r1[o][j]])
                rs_ = oi % 2
                K.op("vector", (lambda o, j, rs_: lambda h: h.tensor_copy(
                    rbst[:, rs_ * 512:rs_ * 512 + JW],
                    r1v[:, o * TW + j * JW:o * TW + (j + 1) * JW]))(o, j, rs_),
                    reads=[b_r1[o][j]], writes=[b_rb[rs_]])
                K.op("vector", (lambda rs_: lambda h: h.tensor_tensor(
                    sqst[:, rs_ * 512:rs_ * 512 + JW], rbst[:, rs_ * 512:rs_ * 512 + JW],
                    rbst[:, rs_ * 512:rs_ * 512 + JW], op=AluOpType.mult))(rs_),
                    reads=[b_rb[rs_]], writes=[b_sq[rs_]])
                def stmm(h, o=o, j=j, rs_=rs_):
                    h.matmul(psA[:, j * 512:j * 512 + JW], ones_sb[:, 0:128],
                             rbst[:, rs_ * 512:rs_ * 512 + JW],
                             start=(o == 0), stop=(o == HC - 1))
                    return h.matmul(psA[:, 1024 + j * 512:1024 + j * 512 + JW],
                                    ones_sb[:, 0:128], sqst[:, rs_ * 512:rs_ * 512 + JW],
                                    start=(o == 0), stop=(o == HC - 1))
                K.op("tensor", stmm, reads=[b_rb[rs_], b_sq[rs_], b_ones],
                     writes=[b_psA[j], b_psA[2 + j]])
        for j in range(JT):
            mu = stat[:, 0 * RSTR + j * JW:0 * RSTR + j * JW + JW]
            rs = stat[:, 1 * RSTR + j * JW:1 * RSTR + j * JW + JW]
            mus = stat[:, 2 * RSTR + j * JW:2 * RSTR + j * JW + JW]
            scr2 = stat[:, 3 * RSTR + j * JW:3 * RSTR + j * JW + JW]
            bmu, brs, bmus, bscr = b_stat[0], b_stat[1], b_stat[2], b_stat[3]
            K.op("vector", (lambda mu, j: lambda h: h.tensor_scalar(
                mu, psA[:, j * 512:j * 512 + JW], IH, None, op0=AluOpType.mult))(mu, j),
                reads=[b_psA[j]], writes=[bmu])
            K.op("vector", (lambda scr2, j: lambda h: h.tensor_scalar(
                scr2, psA[:, 1024 + j * 512:1024 + j * 512 + JW], IH, cfg.EPS,
                op0=AluOpType.mult, op1=AluOpType.add))(scr2, j),
                reads=[b_psA[2 + j]], writes=[bscr])
            var2 = stat[:, 4 * RSTR + j * JW:4 * RSTR + j * JW + JW]
            rin2 = stat[:, 5 * RSTR + j * JW:5 * RSTR + j * JW + JW]
            bvar, brin = b_stat2[0], b_stat2[1]
            K.op("vector", (lambda rs, mu: lambda h: h.tensor_tensor(
                rs, mu, mu, op=AluOpType.mult))(rs, mu), reads=[bmu], writes=[brs])
            K.op("vector", (lambda var2, scr2, rs: lambda h: h.tensor_tensor(
                var2, scr2, rs, op=AluOpType.subtract))(var2, scr2, rs),
                reads=[brs, bscr], writes=[bvar])
            K.op("vector", (lambda rin2, var2: lambda h: h.reciprocal(rin2, var2))(rin2, var2),
                reads=[bvar], writes=[brin])
            K.op("scalar", (lambda rs, rin2: lambda h: h.activation(rs, rin2, AF.Sqrt))(rs, rin2),
                reads=[brin], writes=[brs])
            K.op("vector", (lambda mus, mu, rs: lambda h: h.tensor_tensor(
                mus, mu, rs, op=AluOpType.mult))(mus, mu, rs), reads=[bmu, brs], writes=[bmus])
            for o in range(HC):
                ys = (o * JT + j) % 2
                K.op("vector", (lambda o, j, ys, rs: lambda h: h.tensor_tensor(
                    ystage[:, ys * 512:ys * 512 + JW],
                    r1v[:, o * TW + j * JW:o * TW + (j + 1) * JW], rs,
                    op=AluOpType.mult))(o, j, ys, rs),
                    reads=[b_r1[o][j], brs], writes=[b_yst[ys]])
                K.op("vector", (lambda ys, mus: lambda h: h.tensor_tensor(
                    ystage[:, (2 + ys) * 512:(2 + ys) * 512 + JW],
                    ystage[:, ys * 512:ys * 512 + JW],
                    mus, op=AluOpType.subtract))(ys, mus),
                    reads=[bmus, b_yst[ys]], writes=[b_yst2[ys]])
                evo = K.op("sync", (lambda o, j, ys, t0: lambda h: h.dma_start(
                    outT[o * 128:(o + 1) * 128, t0 + j * JW:t0 + (j + 1) * JW],
                    ystage[:, (2 + ys) * 512:(2 + ys) * 512 + JW]))(o, j, ys, t0),
                    reads=[b_yst2[ys]], dma=True)
    K.op("sync", lambda h: None, waits=[evo])
    K.mark(evo)

    K.finalize(nc, block, sems)
    es.close()
    return nc


def prep_inputs(cfg, x_bh, w_qkv, b_qkv, w_out, b_out, router_w, down_w, up_w, own_first=True):
    H, NH, HC, S, VD, NVD, EI = cfg.H, cfg.NH, cfg.HC, cfg.S, cfg.VD, cfg.NVD, cfg.EI
    f32 = np.float32
    d = {}
    d["xT"] = np.ascontiguousarray(x_bh.T.astype(f32))
    Wq, Wk, Wv = w_qkv[0:H], w_qkv[H:2 * H], w_qkv[2 * H:3 * H]
    d["wq"] = np.ascontiguousarray(
        Wq.reshape(NH, 128, H).transpose(0, 2, 1))
    d["wk"] = np.ascontiguousarray(Wk.reshape(NH, 128, H).transpose(0, 2, 1))
    d["wv"] = np.ascontiguousarray(
        Wv.T.reshape(H, NVD, VD).transpose(1, 0, 2))
    d["wo"] = np.ascontiguousarray(w_out.reshape(HC, 128, H).transpose(0, 2, 1))
    d["bqk"] = np.ascontiguousarray(
        np.concatenate([b_qkv[0:H].reshape(NH, 128).T, b_qkv[H:2 * H].reshape(NH, 128).T], axis=1)).astype(f32)
    d["bo"] = np.ascontiguousarray(b_out.reshape(HC, 128).T).astype(f32)
    d["bv"] = b_qkv[2 * H:3 * H].reshape(1, H).astype(f32)
    d["wr"] = np.ascontiguousarray(
        router_w.T.reshape(HC, 128, 8).transpose(1, 0, 2).reshape(128, HC * 8)).astype(f32)
    Dcat = down_w.reshape(EI, H)
    d["wd"] = np.ascontiguousarray(
        Dcat.T.reshape(HC, 128, EI).transpose(1, 0, 2).reshape(128, HC * EI)).astype(f32)
    Ucat = up_w.transpose(1, 0, 2).reshape(H, EI)
    d["wu"] = np.ascontiguousarray(Ucat.T).astype(f32)
    cst = np.zeros((128, 256), dtype=BF)
    cst[:, 0:128] = np.eye(128, dtype=BF)
    sel = np.zeros((8, 128), dtype=BF)
    for e in range(8):
        sel[e, e * 16:(e + 1) * 16] = 1
    cst[0:8, 128:256] = sel
    d["cst"] = cst
    d["cstf"] = np.eye(128, dtype=f32)
    return d


from concourse.bass_utils import run_bass_kernel_spmd


def kernel(L0_attn_hidden_states, L0_attn_w_qkv, L0_attn_b_qkv, L0_attn_w_out,
           L0_attn_b_out, L0_ffn_router_w, L0_ffn_down_w, L0_ffn_up_w):
    x = np.asarray(L0_attn_hidden_states)
    w_qkv = np.asarray(L0_attn_w_qkv)
    b_qkv = np.asarray(L0_attn_b_qkv)
    w_out = np.asarray(L0_attn_w_out)
    b_out = np.asarray(L0_attn_b_out)
    router_w = np.asarray(L0_ffn_router_w)
    down_w = np.asarray(L0_ffn_down_w)
    up_w = np.asarray(L0_ffn_up_w)

    B, Sfull, H = x.shape
    cfg = Cfg(NH=16, S=4096, T=2048, QT=512, n_cores=8)
    nc = build(cfg)

    wdict = prep_inputs(cfg, np.zeros((cfg.T, H), np.float32), w_qkv, b_qkv,
                        w_out, b_out, router_w, down_w, up_w)
    del wdict["xT"]

    in_maps = []
    for c in range(8):
        b, hf = c // 2, c % 2
        m = dict(wdict)
        m["xT"] = np.ascontiguousarray(x[b, hf * cfg.T:(hf + 1) * cfg.T, :].T.astype(np.float32))
        in_maps.append(m)

    res = run_bass_kernel_spmd(nc, in_maps, core_ids=list(range(8)))

    out = np.empty((B, Sfull, H), dtype=np.float32)
    for c in range(8):
        b, hf = c // 2, c % 2
        out[b, hf * cfg.T:(hf + 1) * cfg.T, :] = res.results[c]["outT"].T
    return out


# version 12
# speedup vs baseline: 1.1164x; 1.0296x over previous
import math
import numpy as np
import ml_dtypes
from contextlib import ExitStack

import concourse.bass as bass
import concourse.mybir as mybir
from concourse.alu_op_type import AluOpType

F32 = mybir.dt.float32
BF16 = mybir.dt.bfloat16
AF = mybir.ActivationFunctionType
BF = ml_dtypes.bfloat16
ENGINES = ("tensor", "vector", "scalar", "gpsimd", "sync")
DMA_SEMS = {"vector": ["vd0"], "scalar": ["sd0", "sd1"],
            "gpsimd": ["gd0", "gd1", "gd2", "gd3"],
            "sync": ["yd0", "yd1", "yd2", "yd3"]}
ALL_SEMS = ["pe", "ve", "ac", "gp", "sy"] + [s for l in DMA_SEMS.values() for s in l]
ENG_SEM = {"tensor": "pe", "vector": "ve", "scalar": "ac", "gpsimd": "gp", "sync": "sy"}


class Ev:
    __slots__ = ("eng", "sem", "idx", "by", "used", "val")

    def __init__(self, eng, sem, idx, by):
        self.eng, self.sem, self.idx, self.by = eng, sem, idx, by
        self.used = False
        self.val = None


class Buf:
    __slots__ = ("w", "r", "name")

    def __init__(self, name=""):
        self.w = None
        self.r = []
        self.name = name


class Sched:
    def __init__(self):
        self.recs = {e: [] for e in ENGINES}
        self.dma_rot = {e: 0 for e in ENGINES}
        self.last_dma = {}
        self.tail = {e: None for e in ENGINES}
        self.pending = {e: [] for e in ENGINES}

    def barrier(self):
        evs = [e for e in self.tail.values() if e is not None]
        evs += [e for e in self.last_dma.values() if e is not None]
        for eng in ENGINES:
            self.pending[eng] = list(evs)

    def op(self, eng, fn, reads=(), writes=(), waits=(), dma=False, inc_by=None):
        if dma:
            sl = DMA_SEMS[eng]
            sem = sl[self.dma_rot[eng] % len(sl)]
            self.dma_rot[eng] += 1
        else:
            sem = ENG_SEM[eng]
        by = inc_by if inc_by is not None else (16 if dma else 1)
        ev = Ev(eng, sem, len(self.recs[eng]), by)
        wl = []
        if dma:
            ev.used = True
            prev = self.last_dma.get(sem)
            if prev is not None:
                wl.append(prev)
            self.last_dma[sem] = ev
        for b in reads:
            if b.w is not None:
                wl.append(b.w)
        for b in writes:
            if b.w is not None:
                wl.append(b.w)
            wl.extend(b.r)
        for e in waits:
            wl.append(e)
        seen, wl2 = set(), []
        for e in wl:
            if e is None or id(e) in seen:
                continue
            seen.add(id(e))
            wl2.append(e)
        if self.pending[eng]:
            for e in self.pending[eng]:
                if id(e) not in seen and e is not ev:
                    seen.add(id(e))
                    wl2.append(e)
            self.pending[eng] = []
        self.recs[eng].append((fn, wl2, ev))
        self.tail[eng] = ev
        for b in reads:
            b.r.append(ev)
        for b in writes:
            b.w = ev
            b.r = []
        return ev

    def mark(self, *evs):
        for e in evs:
            if e is not None:
                e.used = True

    def finalize(self, nc, block, sems):
        for eng in ENGINES:
            cnt = {}
            for fn, wl, ev in self.recs[eng]:
                for e in wl:
                    e.used = True
        for eng in ENGINES:
            cnt = {}
            for fn, wl, ev in self.recs[eng]:
                if ev.used:
                    cnt[ev.sem] = cnt.get(ev.sem, 0) + ev.by
                    ev.val = cnt[ev.sem]
        def make(eng):
            def emit(h):
                hw = {}
                for fn, wl, ev in self.recs[eng]:
                    for e in wl:
                        if e.val is None:
                            continue
                        key = e.sem
                        if hw.get(key, 0) >= e.val:
                            continue
                        h.wait_ge(sems[e.sem], e.val)
                        hw[key] = e.val
                    inst = fn(h)
                    if ev.used and ev.val is not None:
                        assert inst is not None, f"{eng} op needs inc but returned None"
                        inst.then_inc(sems[ev.sem], ev.by)
            return emit
        for eng in ENGINES:
            if not self.recs[eng]:
                continue
            getattr(block, eng)(make(eng))


class Cfg:
    def __init__(self, NH=16, S=4096, T=2048, QT=512, n_cores=8):
        self.NH = NH
        self.HC = NH
        self.H = NH * 128
        self.S = S
        self.T = T
        self.QT = QT
        self.KC = S // 128
        self.NQT = T // QT
        self.E, self.I = 8, 16
        self.EI = 128
        self.SCALE = 1.0 / math.sqrt(128.0)
        self.VD = min(512, self.H)
        self.NVD = self.H // self.VD
        self.TB = S // 128
        self.TBo = T // 128
        self.n_cores = n_cores
        self.EPS = 1e-5
        self.TW = min(2 * QT, T)
        self.NTW = T // self.TW
        self.JW = min(512, self.TW)
        self.JT = self.TW // self.JW


def build(cfg: Cfg, debug=False):
    nc = bass.Bass("TRN2", target_bir_lowering=False)
    K = Sched()
    H, S, T, QT, NH, HC, KC = cfg.H, cfg.S, cfg.T, cfg.QT, cfg.NH, cfg.HC, cfg.KC
    NQT, EI, VD, NVD, TB = cfg.NQT, cfg.EI, cfg.VD, cfg.NVD, cfg.TB
    TW, NTW, JW, JT = cfg.TW, cfg.NTW, cfg.JW, cfg.JT
    SC = cfg.SCALE
    IH = 1.0 / H

    xT = nc.dram_tensor("xT", [H, T], F32, kind="ExternalInput")
    wq_h = nc.dram_tensor("wq", [NH, HC * 128, 128], F32, kind="ExternalInput")
    wk_h = nc.dram_tensor("wk", [NH, HC * 128, 128], F32, kind="ExternalInput")
    wv_h = nc.dram_tensor("wv", [NVD, HC * 128, VD], F32, kind="ExternalInput")
    wo_h = nc.dram_tensor("wo", [HC, HC * 128, 128], F32, kind="ExternalInput")
    bqk_h = nc.dram_tensor("bqk", [128, 2 * NH], F32, kind="ExternalInput")
    bo_h = nc.dram_tensor("bo", [128, HC], F32, kind="ExternalInput")
    bv_h = nc.dram_tensor("bv", [1, H], F32, kind="ExternalInput")
    wr_h = nc.dram_tensor("wr", [128, HC * 8], F32, kind="ExternalInput")
    wd_h = nc.dram_tensor("wd", [128, HC * EI], F32, kind="ExternalInput")
    wu_h = nc.dram_tensor("wu", [EI, H], F32, kind="ExternalInput")
    cst_h = nc.dram_tensor("cst", [128, 256], BF16, kind="ExternalInput")
    cstf_h = nc.dram_tensor("cstf", [128, 128], F32, kind="ExternalInput")
    outT = nc.dram_tensor("outT", [H, T], F32, kind="ExternalOutput")

    _scr_kind = "ExternalOutput" if debug else "Internal"
    TBo = cfg.TBo
    qscr = nc.dram_tensor("qscr", [NH, 128, T], BF16, kind=_scr_kind)
    kag_i = nc.dram_tensor("kag_i", [NH, 128, T], BF16)
    kag_o = nc.dram_tensor("kag_o", [2, NH, 128, T], BF16, kind=_scr_kind)
    vag_i = nc.dram_tensor("vag_i", [TBo, NH, 128, 128], BF16)
    vag_o = nc.dram_tensor("vag_o", [2, TBo, NH, 128, 128], BF16, kind=_scr_kind)
    yscr = nc.dram_tensor("yscr", [H, T], F32, kind=_scr_kind)
    aodump = nc.dram_tensor("aodump", [128, HC * T], BF16, kind="ExternalOutput") if debug else None
    r1dump = nc.dram_tensor("r1dump", [NTW, 128, HC * TW], F32, kind="ExternalOutput") if debug else None
    accdump = nc.dram_tensor("accdump", [NH, T // 128, 128, 8], F32, kind="ExternalOutput") if debug else None
    stdump = nc.dram_tensor("stdump", [NTW, 128, 6 * JT * JW], F32, kind="ExternalOutput") if debug else None

    es = ExitStack()
    sbt = lambda name, shape, dt: es.enter_context(nc.sbuf_tensor(name, shape, dt))

    cst = sbt("cst_sb", [128, 256], BF16)
    cstf = sbt("cstf_sb", [128, 128], F32)
    sel_sb = sbt("sel_sb", [8, 128], BF16)
    ones_sb = sbt("ones_sb", [128, 128], BF16)
    bqk_sb = sbt("bqk_sb", [128, 2 * NH], F32)
    bo_sb = sbt("bo_sb", [128, HC], F32)
    bv_sb = sbt("bv_sb", [1, H], BF16)
    wr_sb = sbt("wr_sb", [128, HC * 8], BF16)
    wd_sb = sbt("wd_sb", [128, HC * EI], BF16)
    wu_sb = sbt("wu_sb", [128, H], BF16)

    NTB = T // 128
    RSTR = JT * JW
    def _al(x):
        return (x + 63) // 64 * 64
    A_XT = 0
    A_wv = _al(A_XT + HC * T)
    A_wqk = _al(A_wv + HC * VD)
    A_kst = _al(A_wqk + 2 * HC * 128)
    A_vst = _al(A_kst + 2 * 512)
    p1_end = _al(A_vst + 2 * 512)
    GQ = min(4, T // 128)
    QW = GQ * 128
    NGQ = T // QW
    SH = S // 2
    A_q2 = 0
    A_k2 = _al(A_q2 + 2 * T)
    A_v2 = _al(A_k2 + 2 * S)
    A_PT = _al(A_v2 + 2 * S)
    A_P2s = _al(A_PT + KC * QW)
    A_P2q = _al(A_P2s + 2 * S)
    A_junk = _al(A_P2q + 2 * S)
    NQS_ = T // 128
    A_accQ = _al(A_junk + 2048)
    A_mnegs = _al(A_accQ + 2 * 4 * NH * NQS_)
    A_acc = _al(A_mnegs + 2 * NH * NQS_)
    A_lcol = _al(A_acc + 64)
    A_lirow = _al(A_lcol + 16)
    A_lrep = _al(A_lirow + 1536)
    p2_end = _al(A_lrep + 1024)
    A_r1 = 0
    A_xres = _al(A_r1 + 2 * HC * TW)
    A_yst = _al(A_xres + 2 * 4 * 512)
    A_rb = _al(A_yst + 2 * 4 * 512)
    A_sq = _al(A_rb + 2 * 512)
    A_stat = _al(A_sq + 2 * 512)
    A_wo = _al(A_stat + 2 * 6 * RSTR)
    wo_end = _al(A_wo + 2 * HC * 128)
    A_lgf = A_wo
    A_gT = _al(A_lgf + 2 * T)
    A_gtok = _al(A_gT + T)
    A_gtk2 = _al(A_gtok + 2 * NTB * 8)
    A_gsc = _al(A_gtk2 + 2 * NTB * 8)
    A_gsc2 = _al(A_gsc + 2 * NTB)
    A_gtk3 = _al(A_gsc2 + 2 * NTB)
    A_ag = _al(A_gtk3 + 2 * NTB * 8)
    p34_end = max(_al(A_ag + 4 * 512), wo_end)
    AO_cols = HC * T
    AO_off = max(p2_end, p34_end)
    AR_COLS = _al(max(AO_off + AO_cols, p1_end) + 64)
    arena = sbt("arena", [128, AR_COLS], BF16)
    assert AR_COLS * 2 <= 190 * 1024, f"arena {AR_COLS * 2} B/partition too big"

    XT = arena[:, A_XT:A_XT + HC * T]
    wv_sb = arena[:, A_wv:A_wv + HC * VD]
    wqk0 = A_wqk
    kst = arena[:, A_kst:A_kst + 2 * 512]
    vst = arena[:, A_vst:A_vst + 2 * 512]
    AO = arena[:, AO_off:AO_off + AO_cols]
    q2o, k2o, v2o = A_q2, A_k2, A_v2
    PTsl = arena[:, A_PT:A_PT + KC * QW]
    P2s = arena[:, A_P2s:A_P2s + 2 * S]
    P2q = arena[:, A_P2q:A_P2q + 2 * S]
    junk = arena[:, A_junk:A_junk + 2048]
    accQ = arena[:, A_accQ:A_accQ + 2 * 4 * NH * NQS_].bitcast(F32)
    accQI = arena[:, A_accQ:A_accQ + 2 * 4 * NH * NQS_].bitcast(mybir.dt.int32)
    mnegs = arena[:, A_mnegs:A_mnegs + 2 * NH * NQS_].bitcast(F32)
    accs = arena[:, A_acc:A_acc + 64].bitcast(F32)
    accsI = arena[:, A_acc:A_acc + 64].bitcast(mybir.dt.int32)
    lcol = arena[:, A_lcol:A_lcol + 16].bitcast(F32)
    lirow = arena[0:1, A_lirow:A_lirow + 1024].bitcast(F32)
    lirow_bf = arena[0:1, A_lirow + 1024:A_lirow + 1536]
    lrep_sb = arena[:, A_lrep:A_lrep + 1024].bitcast(F32)
    r1v = arena[:, A_r1:A_r1 + 2 * HC * TW].bitcast(F32)
    x_res = arena[:, A_xres:A_xres + 2 * 4 * 512].bitcast(F32)
    ystage = arena[:, A_yst:A_yst + 2 * 4 * 512].bitcast(F32)
    rbst = arena[:, A_rb:A_rb + 2 * 512]
    sqst = arena[:, A_sq:A_sq + 2 * 512]
    stat = arena[:, A_stat:A_stat + 2 * 6 * RSTR].bitcast(F32)
    wo0 = A_wo
    lgf = arena[0:8, A_lgf:A_lgf + 2 * T].bitcast(F32)
    gT_sb = arena[0:8, A_gT:A_gT + T]
    gtok = arena[:, A_gtok:A_gtok + 2 * NTB * 8].bitcast(F32)
    gtk2 = arena[:, A_gtk2:A_gtk2 + 2 * NTB * 8].bitcast(F32)
    gsc = arena[:, A_gsc:A_gsc + 2 * NTB].bitcast(F32)
    gsc2 = arena[:, A_gsc2:A_gsc2 + 2 * NTB].bitcast(F32)
    gtk3 = arena[:, A_gtk3:A_gtk3 + 2 * NTB * 8].bitcast(F32)
    agst = arena[:, A_ag:A_ag + 4 * 512]

    psA = es.enter_context(nc.psum_tensor("psA", [128, 2048], F32))
    psB = es.enter_context(nc.psum_tensor("psB", [128, 2048], F32))

    sems = {}
    for s in ALL_SEMS:
        sems[s] = es.enter_context(nc.semaphore("sem_" + s))
    block = es.enter_context(nc.Block())

    b_cst, b_cstf, b_sel, b_ones = Buf("cst"), Buf("cstf"), Buf("sel"), Buf("ones")
    b_bqk, b_bo, b_bv = Buf("bqk"), Buf("bo"), Buf("bv")
    b_wr, b_wd, b_wu = Buf("wr"), Buf("wd"), Buf("wu")
    b_XT = Buf("XT")
    b_wv = Buf("wv")
    b_wqk = [Buf("wqk%d" % i) for i in range(2)]
    b_kst = [Buf("kst%d" % i) for i in range(2)]
    b_vst = [Buf("vst%d" % i) for i in range(2)]
    b_psA = [Buf("psA%d" % i) for i in range(4)]
    b_psB = [Buf("psB%d" % i) for i in range(4)]
    e_kscr = [[] for _ in range(NH)]
    e_qscr = [[] for _ in range(NH)]
    e_vscr = [[] for _ in range(NH)]
    b_k2 = [Buf("k2_%d" % i) for i in range(2)]
    b_q2 = [Buf("q2_%d" % i) for i in range(2)]
    b_v2 = [Buf("v2_%d" % i) for i in range(2)]
    b_P2s = [Buf("P2s%d" % i) for i in range(2)]
    b_P2q = [Buf("P2q%d" % i) for i in range(2)]
    b_accQ = Buf("accQ")
    b_mnegs = Buf("mnegs")
    b_PT = Buf("PT")
    b_junk = Buf("junk")
    b_acc = [Buf("acc%d" % i) for i in range(2)]
    b_lcol = Buf("lcol")
    b_lirow = Buf("lirow")
    b_lrep = Buf("lrep")
    b_AO = [[Buf("AO_%d_%d" % (c, j)) for j in range(T // 512 if T >= 512 else 1)] for c in range(HC)]
    b_wo = [Buf("wo%d" % i) for i in range(2)]
    b_x = [Buf("x%d" % i) for i in range(4)]
    b_r1 = [[Buf("r1_%d_%d" % (o, j)) for j in range(JT)] for o in range(HC)]
    b_rb = [Buf("rb%d" % i) for i in range(2)]
    b_sq = [Buf("sq%d" % i) for i in range(2)]
    b_stat = [Buf("stat%d" % j) for j in range(4)]
    b_yst = [Buf("yst%d" % i) for i in range(2)]
    b_yst2 = [Buf("yst2_%d" % i) for i in range(2)]
    b_stat2 = [Buf("stat2_%d" % i) for i in range(2)]
    b_yscr = [[Buf("yscr_%d_%d" % (o, j)) for j in range(T // 512 if T >= 512 else 1)] for o in range(HC)]
    b_lgf, b_gtok, b_gtk2, b_gsc, b_gT = Buf("lgf"), Buf("gtok"), Buf("gtk2"), Buf("gsc"), Buf("gT")
    b_gsc2, b_gtk3 = Buf("gsc2"), Buf("gtk3")
    b_ag = [Buf("ag%d" % i) for i in range(4)]

    AOW = 512 if T >= 512 else T

    def aoslice(c, t0, w):
        return AO[:, c * T + t0:c * T + t0 + w]

    def aobufs(t0, w, cs=None):
        out = []
        for c in (range(HC) if cs is None else cs):
            for j in range(t0 // AOW, (t0 + w + AOW - 1) // AOW):
                out.append(b_AO[c][j])
        return out

    g = "gpsimd"
    K.op(g, lambda h: h.dma_start(cst[:, :], cst_h[:, :]), writes=[b_cst], dma=True)
    K.op(g, lambda h: h.dma_start(cstf[:, :], cstf_h[:, :]), writes=[b_cstf], dma=True)
    K.op(g, lambda h: h.dma_start(sel_sb[:, :], cst_h[0:8, 128:256]), writes=[b_sel], dma=True)
    K.op(g, lambda h: h.dma_start(bqk_sb[:, :], bqk_h[:, :]), writes=[b_bqk], dma=True)
    K.op(g, lambda h: h.dma_start(bo_sb[:, :], bo_h[:, :]), writes=[b_bo], dma=True)
    K.op(g, lambda h: h.dma_start(bv_sb[:, :], bv_h[:, :]), writes=[b_bv], dma=True)
    K.op(g, lambda h: h.dma_start(wr_sb[:, :], wr_h[:, :]), writes=[b_wr], dma=True)
    K.op(g, lambda h: h.dma_start(wd_sb[:, :], wd_h[:, :]), writes=[b_wd], dma=True)
    K.op(g, lambda h: h.dma_start(wu_sb[:, :], wu_h[:, :]), writes=[b_wu], dma=True)
    for c in range(HC):
        K.op(g, (lambda c: lambda h: h.dma_start(
            XT[:, c * T:(c + 1) * T], xT[c * 128:(c + 1) * 128, :]))(c),
            writes=[b_XT], dma=True)
    K.op("vector", lambda h: h.memset(ones_sb[:, :], 1.0), writes=[b_ones])

    ident = cst[:, 0:128]
    ident8f = cstf[0:8, 0:8]
    identf = cstf[:, :]

    for dt in range(NVD):
        K.op(g, (lambda dt: lambda h: h.dma_start(
            wv_sb[:, :].rearrange("p (c d) -> p c d", d=VD),
            wv_h[dt, :, :].rearrange("(c p) d -> p c d", p=128)))(dt),
            writes=[b_wv], dma=True)
        for tb in range(TBo):
            slot = (dt * TBo + tb) % 4
            def mmg(h, dt=dt, tb=tb, slot=slot):
                for c in range(HC):
                    h.matmul(psA[:, slot * 512:slot * 512 + VD],
                             XT[:, c * T + tb * 128:c * T + (tb + 1) * 128],
                             wv_sb[:, c * VD:(c + 1) * VD],
                             start=(c == 0), stop=False)
                return h.matmul(psA[:, slot * 512:slot * 512 + VD],
                                ones_sb[0:1, 0:128], bv_sb[:, dt * VD:(dt + 1) * VD],
                                start=False, stop=True)
            K.op("tensor", mmg, reads=[b_XT, b_wv, b_ones, b_bv], writes=[b_psA[slot]])
            vs = (dt * TBo + tb) % 2
            K.op("vector", (lambda slot, vs: lambda h: h.tensor_copy(
                vst[:, vs * 512:vs * 512 + VD], psA[:, slot * 512:slot * 512 + VD]))(slot, vs),
                reads=[b_psA[slot]], writes=[b_vst[vs]])
            nh_t = VD // 128
            def spill(h, dt=dt, tb=tb, vs=vs, nh_t=nh_t):
                return h.dma_start(vag_i[tb, dt * nh_t:(dt + 1) * nh_t, :, :].transpose([1, 0, 2]),
                                   vst[:, vs * 512:vs * 512 + VD].rearrange("p (j d) -> p j d", d=128))
            ev = K.op("sync", spill, reads=[b_vst[vs]], dma=True)
            e_vscr[0].append(ev)

    def qk_proj(which):
        span = T
        tile_w = min(512, span)
        ntile = span // tile_w
        wsrc = wk_h if which == "k" else wq_h
        scr = kag_i if which == "k" else qscr
        escr = e_kscr if which == "k" else e_qscr
        bias0 = NH if which == "k" else 0
        for n in range(NH):
            slot = n % 2
            K.op(g, (lambda n, slot, wsrc=wsrc: lambda h: h.dma_start(
                arena[:, wqk0 + slot * HC * 128: wqk0 + (slot + 1) * HC * 128]
                .rearrange("p (c o) -> p c o", o=128),
                wsrc[n, :, :].rearrange("(c p) o -> p c o", p=128)))(n, slot),
                writes=[b_wqk[slot]], dma=True)
            for j in range(ntile):
                idx = n * ntile + j
                ps = idx % 4
                def mmg(h, n=n, j=j, slot=slot, ps=ps, tile_w=tile_w):
                    for c in range(HC):
                        mm = h.matmul(psB[:, ps * 512:ps * 512 + tile_w],
                                      arena[:, wqk0 + slot * HC * 128 + c * 128:
                                            wqk0 + slot * HC * 128 + (c + 1) * 128],
                                      XT[:, c * T + j * tile_w:c * T + (j + 1) * tile_w],
                                      start=(c == 0), stop=(c == HC - 1))
                    return mm
                K.op("tensor", mmg, reads=[b_XT, b_wqk[slot]], writes=[b_psB[ps]])
                ks = idx % 2
                K.op("scalar", (lambda ps, ks, n, tile_w: lambda h: h.activation(
                    kst[:, ks * 512:ks * 512 + tile_w], psB[:, ps * 512:ps * 512 + tile_w],
                    AF.Identity, bias=bqk_sb[:, bias0 + n:bias0 + n + 1], scale=1.0))(ps, ks, n, tile_w),
                    reads=[b_psB[ps], b_bqk], writes=[b_kst[ks]])
                ev = K.op("sync", (lambda n, j, ks, tile_w, scr=scr: lambda h: h.dma_start(
                    scr[n, :, j * tile_w:(j + 1) * tile_w],
                    kst[:, ks * 512:ks * 512 + tile_w]))(n, j, ks, tile_w),
                    reads=[b_kst[ks]], dma=True)
                escr[n].append(ev)
    qk_proj("k")
    qk_proj("q")

    pair_groups = [[i, i + 1] for i in range(0, cfg.n_cores, 2)]
    e_kag = K.op(g, (lambda: lambda h: h.collective_compute(
        "AllGather", AluOpType.bypass, replica_groups=pair_groups,
        ins=[kag_i.ap().opt()], outs=[kag_o.ap().opt()]))(),
        waits=[e for n in range(NH) for e in e_kscr[n]])
    e_vag = K.op(g, (lambda: lambda h: h.collective_compute(
        "AllGather", AluOpType.bypass, replica_groups=pair_groups,
        ins=[vag_i.ap().opt()], outs=[vag_o.ap().opt()]))(),
        waits=e_vscr[0])

    K.barrier()
    nSA = (SH + 511) // 512
    NBA = max(1, KC // 4)
    psA_banks = [b_psA[i] for i in range(min(4, nSA))]
    psB_banks = [b_psB[i] for i in range(min(4, nSA))]
    NQS = T // 128
    import functools

    nstr = max(1, KC // 4)
    NV = NH * NQS
    SSW = (nstr + 2) * 128
    for n in range(NH):
        sl = n % 2
        K.op("sync", (lambda n, sl: lambda h: h.dma_start(
            arena[:, q2o + sl * T:q2o + (sl + 1) * T], qscr[n, :, :]))(n, sl),
            waits=e_qscr[n], writes=[b_q2[sl]], dma=True)
        for half in range(2):
            K.op("sync", (lambda n, sl, half: lambda h: h.dma_start(
                arena[:, k2o + sl * S + half * T:k2o + sl * S + (half + 1) * T],
                kag_o[half, n, :, :]))(n, sl, half),
                waits=[e_kag], writes=[b_k2[sl]], dma=True)
        for qi in range(NQS):
            pp = qi % 2
            psS = psA if pp == 0 else psB
            bps = b_psA if pp == 0 else b_psB
            hn = max(1, nstr // 2)
            def smp(h, n=n, qi=qi, sl=sl, psS=psS, hn=hn):
                k2v = arena[:, k2o + sl * S:k2o + (sl + 1) * S]
                qv = arena[:, q2o + sl * T + qi * 128:q2o + sl * T + (qi + 1) * 128]
                kstr = k2v.rearrange("p (a b) -> p a b", b=512)[:, :, 0:128]
                h.matmul(psS[:, 0:hn * 128], qv, kstr[:, 0:hn, :], start=True, stop=True)
                if nstr > hn:
                    h.matmul(psS[:, 512:512 + (nstr - hn) * 128], qv, kstr[:, hn:nstr, :],
                             start=True, stop=True)
                return h.matmul(psS[:, 1024:1024 + 256],
                                qv, k2v.rearrange("p (h b) -> p h b", h=2)
                                [:, :, qi * 128:(qi + 1) * 128],
                                start=True, stop=True)
            K.op("tensor", smp, reads=[b_k2[sl], b_q2[sl]], writes=[bps[0], bps[1], bps[2]])
            c0 = n * NQS + qi
            if nstr == 8:
                K.op("scalar", (lambda c0, psS: lambda h: h.activation(
                    junk[:, 0:1280], psS[:, 0:1280], AF.Exp, scale=SC / 4.0,
                    accum_out=accQ[:, c0:c0 + 1]))(c0, psS),
                    reads=[bps[0], bps[1], bps[2]], writes=[b_junk, b_accQ])
            else:
                K.op("scalar", (lambda c0, psS, hn: lambda h: h.activation(
                    junk[:, 0:hn * 128], psS[:, 0:hn * 128], AF.Exp, scale=SC / 4.0,
                    accum_out=accQ[:, NV + c0:NV + c0 + 1]))(c0, psS, hn),
                    reads=[bps[0], bps[1], bps[2]], writes=[b_junk, b_accQ])
                K.op("scalar", (lambda c0, psS: lambda h: h.activation(
                    junk[:, 0:256], psS[:, 1024:1280], AF.Exp, scale=SC / 4.0,
                    accum_out=accQ[:, 2 * NV + c0:2 * NV + c0 + 1]))(c0, psS),
                    reads=[bps[0], bps[1], bps[2]], writes=[b_junk, b_accQ])
                K.op("vector", (lambda c0: lambda h: h.tensor_tensor(
                    accQ[:, c0:c0 + 1], accQ[:, NV + c0:NV + c0 + 1],
                    accQ[:, 2 * NV + c0:2 * NV + c0 + 1], op=AluOpType.add))(c0),
                    reads=[b_accQ], writes=[b_accQ])
        K.op("vector", (lambda n: lambda h: h.tensor_scalar(
            accQI[:, NH * NQS + n * NQS:NH * NQS + (n + 1) * NQS],
            accQI[:, n * NQS:(n + 1) * NQS], 23, None,
            op0=AluOpType.arith_shift_right))(n),
            reads=[b_accQ], writes=[b_accQ])
        K.op("vector", (lambda n: lambda h: h.tensor_copy(
            accQ[:, 2 * NH * NQS + n * NQS:2 * NH * NQS + (n + 1) * NQS],
            accQI[:, NH * NQS + n * NQS:NH * NQS + (n + 1) * NQS]))(n),
            reads=[b_accQ], writes=[b_accQ])
        K.op("vector", (lambda n: lambda h: h.tensor_scalar(
            mnegs[:, n * NQS:(n + 1) * NQS],
            accQ[:, 2 * NH * NQS + n * NQS:2 * NH * NQS + (n + 1) * NQS],
            -127.0, -2.0 * 0.6931471805599453,
            op0=AluOpType.add, op1=AluOpType.mult))(n),
            reads=[b_accQ], writes=[b_mnegs])

    pend = []

    def flush_pending():
        for f in pend:
            f()
        pend.clear()

    for n in range(NH):
        sl = n % 2
        K.op("sync", (lambda n, sl: lambda h: h.dma_start(
            arena[:, q2o + sl * T:q2o + (sl + 1) * T], qscr[n, :, :]))(n, sl),
            waits=e_qscr[n], writes=[b_q2[sl]], dma=True)
        for half in range(2):
            K.op("sync", (lambda n, sl, half: lambda h: h.dma_start(
                arena[:, k2o + sl * S + half * T:k2o + sl * S + (half + 1) * T],
                kag_o[half, n, :, :]))(n, sl, half),
                waits=[e_kag], writes=[b_k2[sl]], dma=True)
            K.op("sync", (lambda n, sl, half: lambda h: h.dma_start(
                arena[:, v2o + sl * S + half * T:v2o + sl * S + (half + 1) * T]
                .rearrange("p (tb d) -> p tb d", d=128),
                vag_o[half, :, n, :, :].transpose([1, 0, 2])))(n, sl, half),
                waits=[e_vag], writes=[b_v2[sl]], dma=True)
        for qg in range(NGQ):
            for qs in range(GQ):
                qi = qg * GQ + qs
                p = qi % 2
                def smm(h, n=n, qi=qi, sl=sl, half=0):
                    for m in range(nSA):
                        w = min(512, SH - m * 512)
                        ps = psA if half == 0 else psB
                        mm = h.matmul(ps[:, m * 512:m * 512 + w],
                                      arena[:, q2o + sl * T + qi * 128:q2o + sl * T + (qi + 1) * 128],
                                      arena[:, k2o + sl * S + half * SH + m * 512:
                                            k2o + sl * S + half * SH + m * 512 + w],
                                      start=True, stop=True)
                    return mm
                K.op("tensor", functools.partial(smm, half=0),
                     reads=[b_k2[sl], b_q2[sl]], writes=psA_banks)
                K.op("tensor", functools.partial(smm, half=1),
                     reads=[b_k2[sl], b_q2[sl]], writes=psB_banks)
                K.op("scalar", (lambda n, qi, p: lambda h: h.activation(
                    P2s[:, p * S:p * S + SH], psA[:, 0:SH], AF.Exp, scale=SC / 2.0,
                    bias=mnegs[:, n * NQS + qi:n * NQS + qi + 1]))(n, qi, p),
                    reads=psA_banks + [b_mnegs], writes=[b_P2s[p]])
                K.op("scalar", (lambda n, qi, p: lambda h: h.activation(
                    P2s[:, p * S + SH:p * S + 2 * SH], psB[:, 0:SH], AF.Exp, scale=SC / 2.0,
                    bias=mnegs[:, n * NQS + qi:n * NQS + qi + 1]))(n, qi, p),
                    reads=psB_banks + [b_mnegs], writes=[b_P2s[p]])
                K.op("vector", (lambda p: lambda h: h.tensor_tensor(
                    P2q[:, p * S:(p + 1) * S], P2s[:, p * S:(p + 1) * S],
                    P2s[:, p * S:(p + 1) * S], op=AluOpType.mult))(p),
                    reads=[b_P2s[p]], writes=[b_P2q[p]])
                flush_pending()
                def def_T(p=p, qs=qs):
                    for bb in range(NBA):
                        base = (bb % 2) * 512
                        def trb(h, p=p, bb=bb, base=base):
                            for k_ in range(4):
                                j = bb * 4 + k_
                                if j >= KC:
                                    break
                                tr = h.transpose(
                                    psA[:, base + k_ * 64:base + k_ * 64 + 64].bitcast(BF16),
                                    P2q[:, p * S + j * 128:p * S + (j + 1) * 128],
                                    ident)
                            return tr
                        K.op("tensor", trb, reads=[b_P2q[p], b_cst], writes=[b_psA[bb % 2]])
                        nj = min(4, KC - bb * 4)
                        def evb(h, bb=bb, base=base, qs=qs, nj=nj):
                            return h.tensor_copy(
                                PTsl[:, :].rearrange("p (j q) -> p j q", q=QW)
                                [:, bb * 4:bb * 4 + nj, qs * 128:(qs + 1) * 128],
                                psA[:, base:base + nj * 64].bitcast(BF16)
                                .rearrange("p (j q) -> p j q", q=128))
                        K.op("vector", evb, reads=[b_psA[bb % 2]], writes=[b_PT])
                pend.append(def_T)
                if qs == GQ - 1:
                    def def_G(n=n, sl=sl, qg=qg):
                        def pv(h, sl=sl):
                            for kt in range(KC):
                                mm = h.matmul(psA[:, 1024:1024 + QW],
                                              arena[:, v2o + sl * S + kt * 128:v2o + sl * S + (kt + 1) * 128],
                                              PTsl[:, kt * QW:(kt + 1) * QW],
                                              start=(kt == 0), stop=(kt == KC - 1))
                            return mm
                        K.op("tensor", pv, reads=[b_v2[sl], b_PT], writes=[b_psA[2]])
                        def lmm(h):
                            for kt in range(KC):
                                mm = h.matmul(psA[:, 1536:1536 + QW], ones_sb[:, 0:128],
                                              PTsl[:, kt * QW:(kt + 1) * QW],
                                              start=(kt == 0), stop=(kt == KC - 1))
                            return mm
                        K.op("tensor", lmm, reads=[b_ones, b_PT], writes=[b_psA[3]])
                        K.op("vector", lambda h: h.reciprocal(lrep_sb[:, 0:QW], psA[:, 1536:1536 + QW]),
                             reads=[b_psA[3]], writes=[b_lrep])
                        K.op("vector", (lambda n, qg: lambda h: h.tensor_tensor(
                            aoslice(n, qg * QW, QW), psA[:, 1024:1024 + QW], lrep_sb[:, 0:QW],
                            op=AluOpType.mult))(n, qg),
                            reads=[b_psA[2], b_lrep], writes=aobufs(qg * QW, QW, [n]))
                    pend.append(def_G)
    flush_pending()

    K.barrier()
    for tw in range(NTW):
        t0 = tw * TW
        for o in range(HC):
            i2 = tw * HC + o
            ws = i2 % 2
            K.op(g, (lambda o, ws: lambda h: h.dma_start(
                arena[:, wo0 + ws * HC * 128:wo0 + (ws + 1) * HC * 128]
                .rearrange("p (c q) -> p c q", q=128),
                wo_h[o, :, :].rearrange("(c p) q -> p c q", p=128)))(o, ws),
                writes=[b_wo[ws]], dma=True)
            for j in range(JT):
                gi = i2 * JT + j
                ps = gi % 2
                xs = gi % 4
                K.op("sync", (lambda o, j, t0, xs: lambda h: h.dma_start(
                    x_res[:, xs * 512:xs * 512 + JW],
                    xT[o * 128:(o + 1) * 128, t0 + j * JW:t0 + (j + 1) * JW]))(o, j, t0, xs),
                    writes=[b_x[xs]], dma=True)
                def pmm(h, o=o, j=j, ws=ws, ps=ps, t0=t0):
                    for c in range(HC):
                        mm = h.matmul(psA[:, ps * 512:ps * 512 + JW],
                                      arena[:, wo0 + ws * HC * 128 + c * 128:
                                            wo0 + ws * HC * 128 + (c + 1) * 128],
                                      aoslice(c, t0 + j * JW, JW),
                                      start=(c == 0), stop=(c == HC - 1))
                    return mm
                K.op("tensor", pmm, reads=[b_wo[ws]] + aobufs(t0 + j * JW, JW),
                     writes=[b_psA[ps]])
                K.op("vector", (lambda o, j, ps, xs: lambda h: h.scalar_tensor_tensor(
                    r1v[:, o * TW + j * JW:o * TW + (j + 1) * JW],
                    psA[:, ps * 512:ps * 512 + JW], bo_sb[:, o:o + 1],
                    x_res[:, xs * 512:xs * 512 + JW],
                    op0=AluOpType.add, op1=AluOpType.add))(o, j, ps, xs),
                    reads=[b_psA[ps], b_bo, b_x[xs]], writes=[b_r1[o][j]])
                rs_ = gi % 2
                K.op("vector", (lambda o, j, rs_: lambda h: h.tensor_copy(
                    rbst[:, rs_ * 512:rs_ * 512 + JW],
                    r1v[:, o * TW + j * JW:o * TW + (j + 1) * JW]))(o, j, rs_),
                    reads=[b_r1[o][j]], writes=[b_rb[rs_]])
                K.op("vector", (lambda rs_: lambda h: h.tensor_tensor(
                    sqst[:, rs_ * 512:rs_ * 512 + JW], rbst[:, rs_ * 512:rs_ * 512 + JW],
                    rbst[:, rs_ * 512:rs_ * 512 + JW], op=AluOpType.mult))(rs_),
                    reads=[b_rb[rs_]], writes=[b_sq[rs_]])
                def stmm(h, o=o, j=j, rs_=rs_):
                    h.matmul(psB[:, j * 512:j * 512 + JW], ones_sb[:, 0:128],
                             rbst[:, rs_ * 512:rs_ * 512 + JW],
                             start=(o == 0), stop=(o == HC - 1))
                    return h.matmul(psB[:, 1024 + j * 512:1024 + j * 512 + JW],
                                    ones_sb[:, 0:128], sqst[:, rs_ * 512:rs_ * 512 + JW],
                                    start=(o == 0), stop=(o == HC - 1))
                K.op("tensor", stmm, reads=[b_rb[rs_], b_sq[rs_], b_ones],
                     writes=[b_psB[j], b_psB[2 + j]])
        if debug:
            K.op("sync", (lambda tw: lambda h: h.dma_start(
                r1dump[tw, :, :], r1v[:, 0:HC * TW]))(tw),
                reads=[b for row in b_r1 for b in row], dma=True)
        for j in range(JT):
            mu = stat[:, 0 * RSTR + j * JW:0 * RSTR + j * JW + JW]
            rs = stat[:, 1 * RSTR + j * JW:1 * RSTR + j * JW + JW]
            mus = stat[:, 2 * RSTR + j * JW:2 * RSTR + j * JW + JW]
            scr2 = stat[:, 3 * RSTR + j * JW:3 * RSTR + j * JW + JW]
            bmu, brs, bmus, bscr = b_stat[0], b_stat[1], b_stat[2], b_stat[3]
            K.op("vector", (lambda mu, j: lambda h: h.tensor_scalar(
                mu, psB[:, j * 512:j * 512 + JW], IH, None, op0=AluOpType.mult))(mu, j),
                reads=[b_psB[j]], writes=[bmu])
            K.op("vector", (lambda scr2, j: lambda h: h.tensor_scalar(
                scr2, psB[:, 1024 + j * 512:1024 + j * 512 + JW], IH, cfg.EPS,
                op0=AluOpType.mult, op1=AluOpType.add))(scr2, j),
                reads=[b_psB[2 + j]], writes=[bscr])
            var2 = stat[:, 4 * RSTR + j * JW:4 * RSTR + j * JW + JW]
            rin2 = stat[:, 5 * RSTR + j * JW:5 * RSTR + j * JW + JW]
            bvar, brin = b_stat2[0], b_stat2[1]
            K.op("vector", (lambda rs, mu: lambda h: h.tensor_tensor(
                rs, mu, mu, op=AluOpType.mult))(rs, mu), reads=[bmu], writes=[brs])
            K.op("vector", (lambda var2, scr2, rs: lambda h: h.tensor_tensor(
                var2, scr2, rs, op=AluOpType.subtract))(var2, scr2, rs),
                reads=[brs, bscr], writes=[bvar])
            K.op("vector", (lambda rin2, var2: lambda h: h.reciprocal(rin2, var2))(rin2, var2),
                reads=[bvar], writes=[brin])
            K.op("scalar", (lambda rs, rin2: lambda h: h.activation(rs, rin2, AF.Sqrt))(rs, rin2),
                reads=[brin], writes=[brs])
            K.op("vector", (lambda mus, mu, rs: lambda h: h.tensor_tensor(
                mus, mu, rs, op=AluOpType.mult))(mus, mu, rs), reads=[bmu, brs], writes=[bmus])
            if debug and j == JT - 1:
                K.op("sync", (lambda tw: lambda h: h.dma_start(
                    stdump[tw, :, :], stat[:, 0:6 * RSTR]))(tw),
                    reads=[b_stat[0], b_stat[1], b_stat[2], b_stat[3],
                           b_stat2[0], b_stat2[1]], dma=True)
            for o in range(HC):
                ys = (o * JT + j) % 2
                K.op("vector", (lambda o, j, ys, rs: lambda h: h.tensor_tensor(
                    ystage[:, ys * 512:ys * 512 + JW],
                    r1v[:, o * TW + j * JW:o * TW + (j + 1) * JW], rs,
                    op=AluOpType.mult))(o, j, ys, rs),
                    reads=[b_r1[o][j], brs], writes=[b_yst[ys]])
                K.op("vector", (lambda ys, mus: lambda h: h.tensor_tensor(
                    ystage[:, (2 + ys) * 512:(2 + ys) * 512 + JW],
                    ystage[:, ys * 512:ys * 512 + JW],
                    mus, op=AluOpType.subtract))(ys, mus),
                    reads=[bmus, b_yst[ys]], writes=[b_yst2[ys]])
                K.op("scalar", (lambda o, j, ys, t0: lambda h: h.activation(
                    aoslice(o, t0 + j * JW, JW), ystage[:, (2 + ys) * 512:(2 + ys) * 512 + JW],
                    AF.Copy))(o, j, ys, t0),
                    reads=[b_yst2[ys]], writes=aobufs(t0 + j * JW, JW, [o]))
                ev = K.op("sync", (lambda o, j, ys, t0: lambda h: h.dma_start(
                    yscr[o * 128:(o + 1) * 128, t0 + j * JW:t0 + (j + 1) * JW],
                    ystage[:, (2 + ys) * 512:(2 + ys) * 512 + JW]))(o, j, ys, t0),
                    reads=[b_yst2[ys]], dma=True)
                b_yscr[o][(t0 + j * JW) // (512 if T >= 512 else T)].w = ev

    K.barrier()
    for jt in range(T // QT):
        def rmm(h, jt=jt):
            for c in range(HC):
                mm = h.matmul(psA[0:8, jt * 512:jt * 512 + QT],
                              wr_sb[:, c * 8:(c + 1) * 8],
                              aoslice(c, jt * QT, QT),
                              start=(c == 0), stop=(c == HC - 1))
            return mm
        K.op("tensor", rmm, reads=[b_wr] + aobufs(jt * QT, QT), writes=[b_psA[jt % 4]])
        K.op("vector", (lambda jt: lambda h: h.tensor_copy(
            lgf[:, jt * QT:(jt + 1) * QT], psA[0:8, jt * 512:jt * 512 + QT]))(jt),
            reads=[b_psA[jt % 4]], writes=[b_lgf])
    for tb in range(NTB):
        K.op("tensor", (lambda tb: lambda h: h.transpose(
            psA[:, (tb % 4) * 8:(tb % 4) * 8 + 8],
            lgf[0:8, tb * 128:(tb + 1) * 128], ident8f))(tb),
            reads=[b_lgf, b_cstf], writes=[b_psA[0]])
        if tb % 4 == 3 or tb == NTB - 1:
            lo = (tb // 4) * 4
            cnt = tb - lo + 1
            K.op("vector", (lambda lo, cnt: lambda h: h.tensor_copy(
                gtok[:, lo * 8:(lo + cnt) * 8], psA[:, 0:cnt * 8]))(lo, cnt),
                reads=[b_psA[0]], writes=[b_gtok])
    seg = gtok[:, :].rearrange("p (s e) -> p s e", e=8)
    seg2 = gtk2[:, :].rearrange("p (s e) -> p s e", e=8)
    seg3 = gtk3[:, :].rearrange("p (s e) -> p s e", e=8)
    gb = lambda: gsc[:, :, None].broadcast_to((128, NTB, 8))
    gb2 = lambda: gsc2[:, :, None].broadcast_to((128, NTB, 8))
    K.op("vector", lambda h: h.tensor_reduce(gsc[:, :, None], seg, axis=mybir.AxisListType.X,
                                             op=AluOpType.max), reads=[b_gtok], writes=[b_gsc])
    K.op("vector", lambda h: h.tensor_tensor(seg2, seg, gb(), op=AluOpType.subtract),
         reads=[b_gtok, b_gsc], writes=[b_gtk2])
    K.op("scalar", lambda h: h.activation(gtok[:, :], gtk2[:, :], AF.Exp),
         reads=[b_gtk2], writes=[b_gtok])
    K.op("vector", lambda h: h.tensor_reduce(gsc[:, :, None], seg, axis=mybir.AxisListType.X,
                                             op=AluOpType.add), reads=[b_gtok], writes=[b_gsc])
    K.op("vector", lambda h: h.reciprocal(gsc2[:, :], gsc[:, :]), reads=[b_gsc], writes=[b_gsc2])
    K.op("vector", lambda h: h.tensor_tensor(seg2, seg, gb2(), op=AluOpType.mult),
         reads=[b_gsc2, b_gtok], writes=[b_gtk2])
    K.op("vector", lambda h: h.tensor_reduce(gsc[:, :, None], seg2, axis=mybir.AxisListType.X,
                                             op=AluOpType.max), reads=[b_gtk2], writes=[b_gsc])
    K.op("vector", lambda h: h.tensor_tensor(seg, seg2, gb(), op=AluOpType.is_ge),
         reads=[b_gtk2, b_gsc], writes=[b_gtok])
    K.op("vector", lambda h: h.scalar_tensor_tensor(gtk3[:, :], gtok[:, :], -2.0, gtk2[:, :],
                                                    op0=AluOpType.mult, op1=AluOpType.add),
         reads=[b_gtk2, b_gtok], writes=[b_gtk3])
    K.op("vector", lambda h: h.tensor_reduce(gsc[:, :, None], seg3, axis=mybir.AxisListType.X,
                                             op=AluOpType.max), reads=[b_gtk3], writes=[b_gsc])
    K.op("vector", lambda h: h.tensor_tensor(seg, seg2, gb(), op=AluOpType.is_ge),
         reads=[b_gtk2, b_gsc], writes=[b_gtok])
    K.op("vector", lambda h: h.tensor_tensor(seg3, seg, seg2, op=AluOpType.mult),
         reads=[b_gtk2, b_gtok], writes=[b_gtk3])
    for tb in range(NTB):
        K.op("tensor", (lambda tb: lambda h: h.transpose(
            psA[0:8, 512 + (tb % 4) * 128:512 + (tb % 4) * 128 + 128],
            gtk3[:, tb * 8:(tb + 1) * 8], identf))(tb),
            reads=[b_gtk3, b_cstf], writes=[b_psA[1]])
        K.op("vector", (lambda tb: lambda h: h.tensor_copy(
            gT_sb[:, tb * 128:(tb + 1) * 128],
            psA[0:8, 512 + (tb % 4) * 128:512 + (tb % 4) * 128 + 128]))(tb),
            reads=[b_psA[1]], writes=[b_gT])

    for tw in range(NTW):
        t0 = tw * TW
        for j in range(JT):
            jj = tw * JT + j
            K.op("tensor", (lambda t0, j: lambda h: h.matmul(
                psB[:, 1536:1536 + JW], sel_sb[:, :], gT_sb[:, t0 + j * JW:t0 + (j + 1) * JW],
                start=True, stop=True))(t0, j),
                reads=[b_sel, b_gT], writes=[b_psB[3]])
            def dmm(h, t0=t0, j=j):
                for c in range(HC):
                    mm = h.matmul(psB[:, 1024:1024 + JW], wd_sb[:, c * EI:(c + 1) * EI][:, 0:128],
                                  aoslice(c, t0 + j * JW, JW), start=(c == 0), stop=(c == HC - 1))
                return mm
            K.op("tensor", dmm, reads=[b_wd] + aobufs(t0 + j * JW, JW), writes=[b_psB[2]])
            sb_ = jj % 2
            K.op("scalar", (lambda sb_: lambda h: h.activation(
                agst[:, sb_ * 512:sb_ * 512 + JW], psB[:, 1024:1024 + JW], AF.Silu))(sb_),
                reads=[b_psB[2]], writes=[b_ag[sb_]])
            K.op("vector", (lambda sb_: lambda h: h.tensor_tensor(
                agst[:, 1024 + sb_ * 512:1024 + sb_ * 512 + JW],
                agst[:, sb_ * 512:sb_ * 512 + JW], psB[:, 1536:1536 + JW],
                op=AluOpType.mult))(sb_),
                reads=[b_ag[sb_], b_psB[3]], writes=[b_ag[2 + sb_]])
            for o in range(HC):
                oi = o * JT + j
                ps = oi % 2
                xs = (tw * HC * JT + o * JT + j) % 2
                K.op("sync", (lambda o, j, t0, xs: lambda h: h.dma_start(
                    x_res[:, (2 + xs) * 512:(2 + xs) * 512 + JW],
                    yscr[o * 128:(o + 1) * 128, t0 + j * JW:t0 + (j + 1) * JW]))(o, j, t0, xs),
                    reads=[b_yscr[o][(t0 + j * JW) // (512 if T >= 512 else T)]],
                    writes=[b_x[2 + xs]], dma=True)
                K.op("tensor", (lambda o, sb_, ps: lambda h: h.matmul(
                    psB[:, ps * 512:ps * 512 + JW], wu_sb[:, o * 128:(o + 1) * 128],
                    agst[:, 1024 + sb_ * 512:1024 + sb_ * 512 + JW],
                    start=True, stop=True))(o, sb_, ps),
                    reads=[b_wu, b_ag[2 + sb_]], writes=[b_psB[ps]])
                K.op("vector", (lambda o, j, ps, xs: lambda h: h.tensor_tensor(
                    r1v[:, o * TW + j * JW:o * TW + (j + 1) * JW],
                    psB[:, ps * 512:ps * 512 + JW], x_res[:, (2 + xs) * 512:(2 + xs) * 512 + JW],
                    op=AluOpType.add))(o, j, ps, xs),
                    reads=[b_psB[ps], b_x[2 + xs]], writes=[b_r1[o][j]])
                rs_ = oi % 2
                K.op("vector", (lambda o, j, rs_: lambda h: h.tensor_copy(
                    rbst[:, rs_ * 512:rs_ * 512 + JW],
                    r1v[:, o * TW + j * JW:o * TW + (j + 1) * JW]))(o, j, rs_),
                    reads=[b_r1[o][j]], writes=[b_rb[rs_]])
                K.op("vector", (lambda rs_: lambda h: h.tensor_tensor(
                    sqst[:, rs_ * 512:rs_ * 512 + JW], rbst[:, rs_ * 512:rs_ * 512 + JW],
                    rbst[:, rs_ * 512:rs_ * 512 + JW], op=AluOpType.mult))(rs_),
                    reads=[b_rb[rs_]], writes=[b_sq[rs_]])
                def stmm(h, o=o, j=j, rs_=rs_):
                    h.matmul(psA[:, j * 512:j * 512 + JW], ones_sb[:, 0:128],
                             rbst[:, rs_ * 512:rs_ * 512 + JW],
                             start=(o == 0), stop=(o == HC - 1))
                    return h.matmul(psA[:, 1024 + j * 512:1024 + j * 512 + JW],
                                    ones_sb[:, 0:128], sqst[:, rs_ * 512:rs_ * 512 + JW],
                                    start=(o == 0), stop=(o == HC - 1))
                K.op("tensor", stmm, reads=[b_rb[rs_], b_sq[rs_], b_ones],
                     writes=[b_psA[j], b_psA[2 + j]])
        for j in range(JT):
            mu = stat[:, 0 * RSTR + j * JW:0 * RSTR + j * JW + JW]
            rs = stat[:, 1 * RSTR + j * JW:1 * RSTR + j * JW + JW]
            mus = stat[:, 2 * RSTR + j * JW:2 * RSTR + j * JW + JW]
            scr2 = stat[:, 3 * RSTR + j * JW:3 * RSTR + j * JW + JW]
            bmu, brs, bmus, bscr = b_stat[0], b_stat[1], b_stat[2], b_stat[3]
            K.op("vector", (lambda mu, j: lambda h: h.tensor_scalar(
                mu, psA[:, j * 512:j * 512 + JW], IH, None, op0=AluOpType.mult))(mu, j),
                reads=[b_psA[j]], writes=[bmu])
            K.op("vector", (lambda scr2, j: lambda h: h.tensor_scalar(
                scr2, psA[:, 1024 + j * 512:1024 + j * 512 + JW], IH, cfg.EPS,
                op0=AluOpType.mult, op1=AluOpType.add))(scr2, j),
                reads=[b_psA[2 + j]], writes=[bscr])
            var2 = stat[:, 4 * RSTR + j * JW:4 * RSTR + j * JW + JW]
            rin2 = stat[:, 5 * RSTR + j * JW:5 * RSTR + j * JW + JW]
            bvar, brin = b_stat2[0], b_stat2[1]
            K.op("vector", (lambda rs, mu: lambda h: h.tensor_tensor(
                rs, mu, mu, op=AluOpType.mult))(rs, mu), reads=[bmu], writes=[brs])
            K.op("vector", (lambda var2, scr2, rs: lambda h: h.tensor_tensor(
                var2, scr2, rs, op=AluOpType.subtract))(var2, scr2, rs),
                reads=[brs, bscr], writes=[bvar])
            K.op("vector", (lambda rin2, var2: lambda h: h.reciprocal(rin2, var2))(rin2, var2),
                reads=[bvar], writes=[brin])
            K.op("scalar", (lambda rs, rin2: lambda h: h.activation(rs, rin2, AF.Sqrt))(rs, rin2),
                reads=[brin], writes=[brs])
            K.op("vector", (lambda mus, mu, rs: lambda h: h.tensor_tensor(
                mus, mu, rs, op=AluOpType.mult))(mus, mu, rs), reads=[bmu, brs], writes=[bmus])
            for o in range(HC):
                ys = (o * JT + j) % 2
                K.op("vector", (lambda o, j, ys, rs: lambda h: h.tensor_tensor(
                    ystage[:, ys * 512:ys * 512 + JW],
                    r1v[:, o * TW + j * JW:o * TW + (j + 1) * JW], rs,
                    op=AluOpType.mult))(o, j, ys, rs),
                    reads=[b_r1[o][j], brs], writes=[b_yst[ys]])
                K.op("vector", (lambda ys, mus: lambda h: h.tensor_tensor(
                    ystage[:, (2 + ys) * 512:(2 + ys) * 512 + JW],
                    ystage[:, ys * 512:ys * 512 + JW],
                    mus, op=AluOpType.subtract))(ys, mus),
                    reads=[bmus, b_yst[ys]], writes=[b_yst2[ys]])
                evo = K.op("sync", (lambda o, j, ys, t0: lambda h: h.dma_start(
                    outT[o * 128:(o + 1) * 128, t0 + j * JW:t0 + (j + 1) * JW],
                    ystage[:, (2 + ys) * 512:(2 + ys) * 512 + JW]))(o, j, ys, t0),
                    reads=[b_yst2[ys]], dma=True)
    K.op("sync", lambda h: None, waits=[evo])
    K.mark(evo)

    K.finalize(nc, block, sems)
    es.close()
    return nc


def prep_inputs(cfg, x_bh, w_qkv, b_qkv, w_out, b_out, router_w, down_w, up_w, own_first=True):
    H, NH, HC, S, VD, NVD, EI = cfg.H, cfg.NH, cfg.HC, cfg.S, cfg.VD, cfg.NVD, cfg.EI
    f32 = np.float32
    d = {}
    d["xT"] = np.ascontiguousarray(x_bh.T.astype(f32))
    Wq, Wk, Wv = w_qkv[0:H], w_qkv[H:2 * H], w_qkv[2 * H:3 * H]
    d["wq"] = np.ascontiguousarray(
        Wq.reshape(NH, 128, H).transpose(0, 2, 1))
    d["wk"] = np.ascontiguousarray(Wk.reshape(NH, 128, H).transpose(0, 2, 1))
    d["wv"] = np.ascontiguousarray(
        Wv.T.reshape(H, NVD, VD).transpose(1, 0, 2))
    d["wo"] = np.ascontiguousarray(w_out.reshape(HC, 128, H).transpose(0, 2, 1))
    d["bqk"] = np.ascontiguousarray(
        np.concatenate([b_qkv[0:H].reshape(NH, 128).T, b_qkv[H:2 * H].reshape(NH, 128).T], axis=1)).astype(f32)
    d["bo"] = np.ascontiguousarray(b_out.reshape(HC, 128).T).astype(f32)
    d["bv"] = b_qkv[2 * H:3 * H].reshape(1, H).astype(f32)
    d["wr"] = np.ascontiguousarray(
        router_w.T.reshape(HC, 128, 8).transpose(1, 0, 2).reshape(128, HC * 8)).astype(f32)
    Dcat = down_w.reshape(EI, H)
    d["wd"] = np.ascontiguousarray(
        Dcat.T.reshape(HC, 128, EI).transpose(1, 0, 2).reshape(128, HC * EI)).astype(f32)
    Ucat = up_w.transpose(1, 0, 2).reshape(H, EI)
    d["wu"] = np.ascontiguousarray(Ucat.T).astype(f32)
    cst = np.zeros((128, 256), dtype=BF)
    cst[:, 0:128] = np.eye(128, dtype=BF)
    sel = np.zeros((8, 128), dtype=BF)
    for e in range(8):
        sel[e, e * 16:(e + 1) * 16] = 1
    cst[0:8, 128:256] = sel
    d["cst"] = cst
    d["cstf"] = np.eye(128, dtype=f32)
    return d


from concourse.bass_utils import run_bass_kernel_spmd


def kernel(L0_attn_hidden_states, L0_attn_w_qkv, L0_attn_b_qkv, L0_attn_w_out,
           L0_attn_b_out, L0_ffn_router_w, L0_ffn_down_w, L0_ffn_up_w):
    x = np.asarray(L0_attn_hidden_states)
    w_qkv = np.asarray(L0_attn_w_qkv)
    b_qkv = np.asarray(L0_attn_b_qkv)
    w_out = np.asarray(L0_attn_w_out)
    b_out = np.asarray(L0_attn_b_out)
    router_w = np.asarray(L0_ffn_router_w)
    down_w = np.asarray(L0_ffn_down_w)
    up_w = np.asarray(L0_ffn_up_w)

    B, Sfull, H = x.shape
    cfg = Cfg(NH=16, S=4096, T=2048, QT=512, n_cores=8)
    nc = build(cfg)

    wdict = prep_inputs(cfg, np.zeros((cfg.T, H), np.float32), w_qkv, b_qkv,
                        w_out, b_out, router_w, down_w, up_w)
    del wdict["xT"]

    in_maps = []
    for c in range(8):
        b, hf = c // 2, c % 2
        m = dict(wdict)
        m["xT"] = np.ascontiguousarray(x[b, hf * cfg.T:(hf + 1) * cfg.T, :].T.astype(np.float32))
        in_maps.append(m)

    res = run_bass_kernel_spmd(nc, in_maps, core_ids=list(range(8)))

    out = np.empty((B, Sfull, H), dtype=np.float32)
    for c in range(8):
        b, hf = c // 2, c % 2
        out[b, hf * cfg.T:(hf + 1) * cfg.T, :] = res.results[c]["outT"].T
    return out


# version 14
# speedup vs baseline: 1.1747x; 1.0522x over previous
import math
import numpy as np
import ml_dtypes
from contextlib import ExitStack

import concourse.bass as bass
import concourse.mybir as mybir
from concourse.alu_op_type import AluOpType

F32 = mybir.dt.float32
BF16 = mybir.dt.bfloat16
AF = mybir.ActivationFunctionType
BF = ml_dtypes.bfloat16
ENGINES = ("tensor", "vector", "scalar", "gpsimd", "sync")
DMA_SEMS = {"vector": ["vd0"], "scalar": ["sd0", "sd1"],
            "gpsimd": ["gd0", "gd1", "gd2", "gd3"],
            "sync": ["yd0", "yd1", "yd2", "yd3"]}
ALL_SEMS = ["pe", "ve", "ac", "gp", "sy"] + [s for l in DMA_SEMS.values() for s in l]
ENG_SEM = {"tensor": "pe", "vector": "ve", "scalar": "ac", "gpsimd": "gp", "sync": "sy"}


class Ev:
    __slots__ = ("eng", "sem", "idx", "by", "used", "val")

    def __init__(self, eng, sem, idx, by):
        self.eng, self.sem, self.idx, self.by = eng, sem, idx, by
        self.used = False
        self.val = None


class Buf:
    __slots__ = ("w", "r", "name")

    def __init__(self, name=""):
        self.w = None
        self.r = []
        self.name = name


class Sched:
    def __init__(self):
        self.recs = {e: [] for e in ENGINES}
        self.dma_rot = {e: 0 for e in ENGINES}
        self.last_dma = {}
        self.tail = {e: None for e in ENGINES}
        self.pending = {e: [] for e in ENGINES}

    def barrier(self):
        evs = [e for e in self.tail.values() if e is not None]
        evs += [e for e in self.last_dma.values() if e is not None]
        for eng in ENGINES:
            self.pending[eng] = list(evs)

    def op(self, eng, fn, reads=(), writes=(), waits=(), dma=False, inc_by=None):
        if dma:
            sl = DMA_SEMS[eng]
            sem = sl[self.dma_rot[eng] % len(sl)]
            self.dma_rot[eng] += 1
        else:
            sem = ENG_SEM[eng]
        by = inc_by if inc_by is not None else (16 if dma else 1)
        ev = Ev(eng, sem, len(self.recs[eng]), by)
        wl = []
        if dma:
            ev.used = True
            prev = self.last_dma.get(sem)
            if prev is not None:
                wl.append(prev)
            self.last_dma[sem] = ev
        for b in reads:
            if b.w is not None:
                wl.append(b.w)
        for b in writes:
            if b.w is not None:
                wl.append(b.w)
            wl.extend(b.r)
        for e in waits:
            wl.append(e)
        seen, wl2 = set(), []
        for e in wl:
            if e is None or id(e) in seen:
                continue
            seen.add(id(e))
            wl2.append(e)
        if self.pending[eng]:
            for e in self.pending[eng]:
                if id(e) not in seen and e is not ev:
                    seen.add(id(e))
                    wl2.append(e)
            self.pending[eng] = []
        self.recs[eng].append((fn, wl2, ev))
        self.tail[eng] = ev
        for b in reads:
            b.r.append(ev)
        for b in writes:
            b.w = ev
            b.r = []
        return ev

    def mark(self, *evs):
        for e in evs:
            if e is not None:
                e.used = True

    def finalize(self, nc, block, sems):
        for eng in ENGINES:
            cnt = {}
            for fn, wl, ev in self.recs[eng]:
                for e in wl:
                    e.used = True
        for eng in ENGINES:
            cnt = {}
            for fn, wl, ev in self.recs[eng]:
                if ev.used:
                    cnt[ev.sem] = cnt.get(ev.sem, 0) + ev.by
                    ev.val = cnt[ev.sem]
        def make(eng):
            def emit(h):
                hw = {}
                for fn, wl, ev in self.recs[eng]:
                    for e in wl:
                        if e.val is None:
                            continue
                        key = e.sem
                        if hw.get(key, 0) >= e.val:
                            continue
                        h.wait_ge(sems[e.sem], e.val)
                        hw[key] = e.val
                    inst = fn(h)
                    if ev.used and ev.val is not None:
                        assert inst is not None, f"{eng} op needs inc but returned None"
                        inst.then_inc(sems[ev.sem], ev.by)
            return emit
        for eng in ENGINES:
            if not self.recs[eng]:
                continue
            getattr(block, eng)(make(eng))


class Cfg:
    def __init__(self, NH=16, S=4096, T=2048, QT=512, n_cores=8):
        self.NH = NH
        self.HC = NH
        self.H = NH * 128
        self.S = S
        self.T = T
        self.QT = QT
        self.KC = S // 128
        self.NQT = T // QT
        self.E, self.I = 8, 16
        self.EI = 128
        self.SCALE = 1.0 / math.sqrt(128.0)
        self.VD = min(512, self.H)
        self.NVD = self.H // self.VD
        self.TB = S // 128
        self.TBo = T // 128
        self.n_cores = n_cores
        self.EPS = 1e-5
        self.TW = min(2 * QT, T)
        self.NTW = T // self.TW
        self.JW = min(512, self.TW)
        self.JT = self.TW // self.JW


def build(cfg: Cfg, debug=False):
    nc = bass.Bass("TRN2", target_bir_lowering=False)
    K = Sched()
    H, S, T, QT, NH, HC, KC = cfg.H, cfg.S, cfg.T, cfg.QT, cfg.NH, cfg.HC, cfg.KC
    NQT, EI, VD, NVD, TB = cfg.NQT, cfg.EI, cfg.VD, cfg.NVD, cfg.TB
    TW, NTW, JW, JT = cfg.TW, cfg.NTW, cfg.JW, cfg.JT
    SC = cfg.SCALE
    IH = 1.0 / H

    xT = nc.dram_tensor("xT", [H, T], F32, kind="ExternalInput")
    wq_h = nc.dram_tensor("wq", [NH, HC * 128, 128], F32, kind="ExternalInput")
    wk_h = nc.dram_tensor("wk", [NH, HC * 128, 128], F32, kind="ExternalInput")
    wv_h = nc.dram_tensor("wv", [NVD, HC * 128, VD], F32, kind="ExternalInput")
    wo_h = nc.dram_tensor("wo", [HC, HC * 128, 128], F32, kind="ExternalInput")
    bqk_h = nc.dram_tensor("bqk", [128, 2 * NH], F32, kind="ExternalInput")
    bo_h = nc.dram_tensor("bo", [128, HC], F32, kind="ExternalInput")
    bv_h = nc.dram_tensor("bv", [1, H], F32, kind="ExternalInput")
    wr_h = nc.dram_tensor("wr", [128, HC * 8], F32, kind="ExternalInput")
    wd_h = nc.dram_tensor("wd", [128, HC * EI], F32, kind="ExternalInput")
    wu_h = nc.dram_tensor("wu", [EI, H], F32, kind="ExternalInput")
    cst_h = nc.dram_tensor("cst", [128, 256], BF16, kind="ExternalInput")
    cstf_h = nc.dram_tensor("cstf", [128, 128], F32, kind="ExternalInput")
    outT = nc.dram_tensor("outT", [H, T], F32, kind="ExternalOutput")

    _scr_kind = "ExternalOutput" if debug else "Internal"
    TBo = cfg.TBo
    qscr = nc.dram_tensor("qscr", [NH, 128, T], BF16, kind=_scr_kind)
    kag_i = nc.dram_tensor("kag_i", [NH, 128, T], BF16)
    kag_o = nc.dram_tensor("kag_o", [2, NH, 128, T], BF16, kind=_scr_kind)
    vag_i = nc.dram_tensor("vag_i", [TBo, NH, 128, 128], BF16)
    vag_o = nc.dram_tensor("vag_o", [2, TBo, NH, 128, 128], BF16, kind=_scr_kind)
    yscr = nc.dram_tensor("yscr", [H, T], F32, kind=_scr_kind)
    aodump = nc.dram_tensor("aodump", [128, HC * T], BF16, kind="ExternalOutput") if debug else None
    r1dump = nc.dram_tensor("r1dump", [NTW, 128, HC * TW], F32, kind="ExternalOutput") if debug else None
    accdump = nc.dram_tensor("accdump", [NH, T // 128, 128, 8], F32, kind="ExternalOutput") if debug else None
    stdump = nc.dram_tensor("stdump", [NTW, 128, 6 * JT * JW], F32, kind="ExternalOutput") if debug else None

    es = ExitStack()
    sbt = lambda name, shape, dt: es.enter_context(nc.sbuf_tensor(name, shape, dt))

    cst = sbt("cst_sb", [128, 256], BF16)
    cstf = sbt("cstf_sb", [128, 128], F32)
    sel_sb = sbt("sel_sb", [8, 128], BF16)
    ones_sb = sbt("ones_sb", [128, 128], BF16)
    bqk_sb = sbt("bqk_sb", [128, 2 * NH], F32)
    bo_sb = sbt("bo_sb", [128, HC], F32)
    bv_sb = sbt("bv_sb", [1, H], BF16)
    wr_sb = sbt("wr_sb", [128, HC * 8], BF16)
    wd_sb = sbt("wd_sb", [128, HC * EI], BF16)
    wu_sb = sbt("wu_sb", [128, H], BF16)

    NTB = T // 128
    RSTR = JT * JW
    def _al(x):
        return (x + 63) // 64 * 64
    A_XT = 0
    A_wv = _al(A_XT + HC * T)
    A_wqk = _al(A_wv + HC * VD)
    A_kst = _al(A_wqk + 2 * HC * 128)
    A_vst = _al(A_kst + 2 * 512)
    p1_end = _al(A_vst + 2 * 512)
    GQ = min(4, T // 128)
    QW = GQ * 128
    NGQ = T // QW
    SH = S // 2
    A_q2 = 0
    A_k2 = _al(A_q2 + 2 * T)
    A_v2 = _al(A_k2 + 2 * S)
    A_PT = _al(A_v2 + 2 * S)
    A_P2s = _al(A_PT + KC * QW)
    A_P2q = _al(A_P2s + 2 * S)
    A_junk = _al(A_P2q + 2 * S)
    NQS_ = T // 128
    A_accQ = _al(A_junk + 2048)
    A_mnegs = _al(A_accQ + 2 * 4 * NH * NQS_)
    A_acc = _al(A_mnegs + 2 * NH * NQS_)
    A_lcol = _al(A_acc + 64)
    A_lirow = _al(A_lcol + 16)
    A_lrep = _al(A_lirow + 1536)
    p2_end = _al(A_lrep + 1024)
    A_r1 = 0
    A_xres = _al(A_r1 + 2 * HC * TW)
    A_yst = _al(A_xres + 2 * 4 * 512)
    A_rb = _al(A_yst + 2 * 4 * 512)
    A_sq = _al(A_rb + 2 * 512)
    A_stat = _al(A_sq + 2 * 512)
    A_wo = _al(A_stat + 2 * 6 * RSTR)
    wo_end = _al(A_wo + 2 * HC * 128)
    A_lgf = A_wo
    A_gT = _al(A_lgf + 2 * T)
    A_gtok = _al(A_gT + T)
    A_gtk2 = _al(A_gtok + 2 * NTB * 8)
    A_gsc = _al(A_gtk2 + 2 * NTB * 8)
    A_gsc2 = _al(A_gsc + 2 * NTB)
    A_gtk3 = _al(A_gsc2 + 2 * NTB)
    A_ag = _al(A_gtk3 + 2 * NTB * 8)
    p34_end = max(_al(A_ag + 4 * 512), wo_end)
    AO_cols = HC * T
    AO_off = max(p2_end, p34_end)
    AR_COLS = _al(max(AO_off + AO_cols, p1_end) + 64)
    arena = sbt("arena", [128, AR_COLS], BF16)
    assert AR_COLS * 2 <= 190 * 1024, f"arena {AR_COLS * 2} B/partition too big"

    XT = arena[:, A_XT:A_XT + HC * T]
    wv_sb = arena[:, A_wv:A_wv + HC * VD]
    wqk0 = A_wqk
    kst = arena[:, A_kst:A_kst + 2 * 512]
    vst = arena[:, A_vst:A_vst + 2 * 512]
    AO = arena[:, AO_off:AO_off + AO_cols]
    q2o, k2o, v2o = A_q2, A_k2, A_v2
    PTsl = arena[:, A_PT:A_PT + KC * QW]
    P2s = arena[:, A_P2s:A_P2s + 2 * S]
    P2q = arena[:, A_P2q:A_P2q + 2 * S]
    junk = arena[:, A_junk:A_junk + 2048]
    accQ = arena[:, A_accQ:A_accQ + 2 * 4 * NH * NQS_].bitcast(F32)
    accQI = arena[:, A_accQ:A_accQ + 2 * 4 * NH * NQS_].bitcast(mybir.dt.int32)
    mnegs = arena[:, A_mnegs:A_mnegs + 2 * NH * NQS_].bitcast(F32)
    accs = arena[:, A_acc:A_acc + 64].bitcast(F32)
    accsI = arena[:, A_acc:A_acc + 64].bitcast(mybir.dt.int32)
    lcol = arena[:, A_lcol:A_lcol + 16].bitcast(F32)
    lirow = arena[0:1, A_lirow:A_lirow + 1024].bitcast(F32)
    lirow_bf = arena[0:1, A_lirow + 1024:A_lirow + 1536]
    lrep_sb = arena[:, A_lrep:A_lrep + 1024].bitcast(F32)
    r1v = arena[:, A_r1:A_r1 + 2 * HC * TW].bitcast(F32)
    x_res = arena[:, A_xres:A_xres + 2 * 4 * 512].bitcast(F32)
    ystage = arena[:, A_yst:A_yst + 2 * 4 * 512].bitcast(F32)
    rbst = arena[:, A_rb:A_rb + 2 * 512]
    sqst = arena[:, A_sq:A_sq + 2 * 512]
    stat = arena[:, A_stat:A_stat + 2 * 6 * RSTR].bitcast(F32)
    wo0 = A_wo
    lgf = arena[0:8, A_lgf:A_lgf + 2 * T].bitcast(F32)
    gT_sb = arena[0:8, A_gT:A_gT + T]
    gtok = arena[:, A_gtok:A_gtok + 2 * NTB * 8].bitcast(F32)
    gtk2 = arena[:, A_gtk2:A_gtk2 + 2 * NTB * 8].bitcast(F32)
    gsc = arena[:, A_gsc:A_gsc + 2 * NTB].bitcast(F32)
    gsc2 = arena[:, A_gsc2:A_gsc2 + 2 * NTB].bitcast(F32)
    gtk3 = arena[:, A_gtk3:A_gtk3 + 2 * NTB * 8].bitcast(F32)
    agst = arena[:, A_ag:A_ag + 4 * 512]

    psA = es.enter_context(nc.psum_tensor("psA", [128, 2048], F32))
    psB = es.enter_context(nc.psum_tensor("psB", [128, 2048], F32))

    sems = {}
    for s in ALL_SEMS:
        sems[s] = es.enter_context(nc.semaphore("sem_" + s))
    block = es.enter_context(nc.Block())

    b_cst, b_cstf, b_sel, b_ones = Buf("cst"), Buf("cstf"), Buf("sel"), Buf("ones")
    b_bqk, b_bo, b_bv = Buf("bqk"), Buf("bo"), Buf("bv")
    b_wr, b_wd, b_wu = Buf("wr"), Buf("wd"), Buf("wu")
    b_XT = Buf("XT")
    b_wv = Buf("wv")
    b_wqk = [Buf("wqk%d" % i) for i in range(2)]
    b_kst = [Buf("kst%d" % i) for i in range(2)]
    b_vst = [Buf("vst%d" % i) for i in range(2)]
    b_psA = [Buf("psA%d" % i) for i in range(4)]
    b_psB = [Buf("psB%d" % i) for i in range(4)]
    e_kscr = [[] for _ in range(NH)]
    e_qscr = [[] for _ in range(NH)]
    e_vscr = [[] for _ in range(NH)]
    b_k2 = [Buf("k2_%d" % i) for i in range(2)]
    b_q2 = [Buf("q2_%d" % i) for i in range(2)]
    b_v2 = [Buf("v2_%d" % i) for i in range(2)]
    b_P2s = [[Buf("P2s%d_%d" % (i, hh)) for hh in range(2)] for i in range(2)]
    b_P2q = [[Buf("P2q%d_%d" % (i, hh)) for hh in range(2)] for i in range(2)]
    b_accQ = Buf("accQ")
    b_mnegs = Buf("mnegs")
    b_PT = Buf("PT")
    b_junk = Buf("junk")
    b_acc = [Buf("acc%d" % i) for i in range(2)]
    b_lcol = Buf("lcol")
    b_lirow = Buf("lirow")
    b_lrep = Buf("lrep")
    b_AO = [[Buf("AO_%d_%d" % (c, j)) for j in range(T // 512 if T >= 512 else 1)] for c in range(HC)]
    b_wo = [Buf("wo%d" % i) for i in range(2)]
    b_x = [Buf("x%d" % i) for i in range(4)]
    b_r1 = [[Buf("r1_%d_%d" % (o, j)) for j in range(JT)] for o in range(HC)]
    b_rb = [Buf("rb%d" % i) for i in range(2)]
    b_sq = [Buf("sq%d" % i) for i in range(2)]
    b_stat = [Buf("stat%d" % j) for j in range(4)]
    b_yst = [Buf("yst%d" % i) for i in range(2)]
    b_yst2 = [Buf("yst2_%d" % i) for i in range(2)]
    b_stat2 = [Buf("stat2_%d" % i) for i in range(2)]
    b_yscr = [[Buf("yscr_%d_%d" % (o, j)) for j in range(T // 512 if T >= 512 else 1)] for o in range(HC)]
    b_lgf, b_gtok, b_gtk2, b_gsc, b_gT = Buf("lgf"), Buf("gtok"), Buf("gtk2"), Buf("gsc"), Buf("gT")
    b_gsc2, b_gtk3 = Buf("gsc2"), Buf("gtk3")
    b_ag = [Buf("ag%d" % i) for i in range(4)]

    AOW = 512 if T >= 512 else T

    def aoslice(c, t0, w):
        return AO[:, c * T + t0:c * T + t0 + w]

    def aobufs(t0, w, cs=None):
        out = []
        for c in (range(HC) if cs is None else cs):
            for j in range(t0 // AOW, (t0 + w + AOW - 1) // AOW):
                out.append(b_AO[c][j])
        return out

    g = "gpsimd"
    K.op(g, lambda h: h.dma_start(cst[:, :], cst_h[:, :]), writes=[b_cst], dma=True)
    K.op(g, lambda h: h.dma_start(cstf[:, :], cstf_h[:, :]), writes=[b_cstf], dma=True)
    K.op(g, lambda h: h.dma_start(sel_sb[:, :], cst_h[0:8, 128:256]), writes=[b_sel], dma=True)
    K.op(g, lambda h: h.dma_start(bqk_sb[:, :], bqk_h[:, :]), writes=[b_bqk], dma=True)
    K.op(g, lambda h: h.dma_start(bo_sb[:, :], bo_h[:, :]), writes=[b_bo], dma=True)
    K.op(g, lambda h: h.dma_start(bv_sb[:, :], bv_h[:, :]), writes=[b_bv], dma=True)
    K.op(g, lambda h: h.dma_start(wr_sb[:, :], wr_h[:, :]), writes=[b_wr], dma=True)
    K.op(g, lambda h: h.dma_start(wd_sb[:, :], wd_h[:, :]), writes=[b_wd], dma=True)
    K.op(g, lambda h: h.dma_start(wu_sb[:, :], wu_h[:, :]), writes=[b_wu], dma=True)
    for c in range(HC):
        K.op(g, (lambda c: lambda h: h.dma_start(
            XT[:, c * T:(c + 1) * T], xT[c * 128:(c + 1) * 128, :]))(c),
            writes=[b_XT], dma=True)
    K.op("vector", lambda h: h.memset(ones_sb[:, :], 1.0), writes=[b_ones])

    ident = cst[:, 0:128]
    ident8f = cstf[0:8, 0:8]
    identf = cstf[:, :]

    for dt in range(NVD):
        K.op(g, (lambda dt: lambda h: h.dma_start(
            wv_sb[:, :].rearrange("p (c d) -> p c d", d=VD),
            wv_h[dt, :, :].rearrange("(c p) d -> p c d", p=128)))(dt),
            writes=[b_wv], dma=True)
        for tb in range(TBo):
            slot = (dt * TBo + tb) % 4
            def mmg(h, dt=dt, tb=tb, slot=slot):
                for c in range(HC):
                    h.matmul(psA[:, slot * 512:slot * 512 + VD],
                             XT[:, c * T + tb * 128:c * T + (tb + 1) * 128],
                             wv_sb[:, c * VD:(c + 1) * VD],
                             start=(c == 0), stop=False)
                return h.matmul(psA[:, slot * 512:slot * 512 + VD],
                                ones_sb[0:1, 0:128], bv_sb[:, dt * VD:(dt + 1) * VD],
                                start=False, stop=True)
            K.op("tensor", mmg, reads=[b_XT, b_wv, b_ones, b_bv], writes=[b_psA[slot]])
            vs = (dt * TBo + tb) % 2
            K.op("vector", (lambda slot, vs: lambda h: h.tensor_copy(
                vst[:, vs * 512:vs * 512 + VD], psA[:, slot * 512:slot * 512 + VD]))(slot, vs),
                reads=[b_psA[slot]], writes=[b_vst[vs]])
            nh_t = VD // 128
            def spill(h, dt=dt, tb=tb, vs=vs, nh_t=nh_t):
                return h.dma_start(vag_i[tb, dt * nh_t:(dt + 1) * nh_t, :, :].transpose([1, 0, 2]),
                                   vst[:, vs * 512:vs * 512 + VD].rearrange("p (j d) -> p j d", d=128))
            ev = K.op("sync", spill, reads=[b_vst[vs]], dma=True)
            e_vscr[0].append(ev)

    def qk_proj(which):
        span = T
        tile_w = min(512, span)
        ntile = span // tile_w
        wsrc = wk_h if which == "k" else wq_h
        scr = kag_i if which == "k" else qscr
        escr = e_kscr if which == "k" else e_qscr
        bias0 = NH if which == "k" else 0
        for n in range(NH):
            slot = n % 2
            K.op(g, (lambda n, slot, wsrc=wsrc: lambda h: h.dma_start(
                arena[:, wqk0 + slot * HC * 128: wqk0 + (slot + 1) * HC * 128]
                .rearrange("p (c o) -> p c o", o=128),
                wsrc[n, :, :].rearrange("(c p) o -> p c o", p=128)))(n, slot),
                writes=[b_wqk[slot]], dma=True)
            for j in range(ntile):
                idx = n * ntile + j
                ps = idx % 4
                def mmg(h, n=n, j=j, slot=slot, ps=ps, tile_w=tile_w):
                    for c in range(HC):
                        mm = h.matmul(psB[:, ps * 512:ps * 512 + tile_w],
                                      arena[:, wqk0 + slot * HC * 128 + c * 128:
                                            wqk0 + slot * HC * 128 + (c + 1) * 128],
                                      XT[:, c * T + j * tile_w:c * T + (j + 1) * tile_w],
                                      start=(c == 0), stop=(c == HC - 1))
                    return mm
                K.op("tensor", mmg, reads=[b_XT, b_wqk[slot]], writes=[b_psB[ps]])
                ks = idx % 2
                K.op("scalar", (lambda ps, ks, n, tile_w: lambda h: h.activation(
                    kst[:, ks * 512:ks * 512 + tile_w], psB[:, ps * 512:ps * 512 + tile_w],
                    AF.Identity, bias=bqk_sb[:, bias0 + n:bias0 + n + 1], scale=1.0))(ps, ks, n, tile_w),
                    reads=[b_psB[ps], b_bqk], writes=[b_kst[ks]])
                ev = K.op("sync", (lambda n, j, ks, tile_w, scr=scr: lambda h: h.dma_start(
                    scr[n, :, j * tile_w:(j + 1) * tile_w],
                    kst[:, ks * 512:ks * 512 + tile_w]))(n, j, ks, tile_w),
                    reads=[b_kst[ks]], dma=True)
                escr[n].append(ev)
    qk_proj("k")
    qk_proj("q")

    pair_groups = [[i, i + 1] for i in range(0, cfg.n_cores, 2)]
    e_kag = K.op(g, (lambda: lambda h: h.collective_compute(
        "AllGather", AluOpType.bypass, replica_groups=pair_groups,
        ins=[kag_i.ap().opt()], outs=[kag_o.ap().opt()]))(),
        waits=[e for n in range(NH) for e in e_kscr[n]])
    e_vag = K.op(g, (lambda: lambda h: h.collective_compute(
        "AllGather", AluOpType.bypass, replica_groups=pair_groups,
        ins=[vag_i.ap().opt()], outs=[vag_o.ap().opt()]))(),
        waits=e_vscr[0])

    K.barrier()
    nSA = (SH + 511) // 512
    NBA = max(1, KC // 4)
    psA_banks = [b_psA[i] for i in range(min(4, nSA))]
    psB_banks = [b_psB[i] for i in range(min(4, nSA))]
    NQS = T // 128
    import functools

    nstr = max(1, KC // 4)
    NV = NH * NQS
    SSW = (nstr + 2) * 128
    for n in range(NH):
        sl = n % 2
        K.op("sync", (lambda n, sl: lambda h: h.dma_start(
            arena[:, q2o + sl * T:q2o + (sl + 1) * T], qscr[n, :, :]))(n, sl),
            waits=e_qscr[n], writes=[b_q2[sl]], dma=True)
        for half in range(2):
            K.op("sync", (lambda n, sl, half: lambda h: h.dma_start(
                arena[:, k2o + sl * S + half * T:k2o + sl * S + (half + 1) * T],
                kag_o[half, n, :, :]))(n, sl, half),
                waits=[e_kag], writes=[b_k2[sl]], dma=True)
        for qi in range(NQS):
            pp = qi % 2
            psS = psA if pp == 0 else psB
            bps = b_psA if pp == 0 else b_psB
            hn = max(1, nstr // 2)
            def smp(h, n=n, qi=qi, sl=sl, psS=psS, hn=hn):
                k2v = arena[:, k2o + sl * S:k2o + (sl + 1) * S]
                qv = arena[:, q2o + sl * T + qi * 128:q2o + sl * T + (qi + 1) * 128]
                kstr = k2v.rearrange("p (a b) -> p a b", b=512)[:, :, 0:128]
                h.matmul(psS[:, 0:hn * 128], qv, kstr[:, 0:hn, :], start=True, stop=True)
                if nstr > hn:
                    h.matmul(psS[:, 512:512 + (nstr - hn) * 128], qv, kstr[:, hn:nstr, :],
                             start=True, stop=True)
                return h.matmul(psS[:, 1024:1024 + 256],
                                qv, k2v.rearrange("p (h b) -> p h b", h=2)
                                [:, :, qi * 128:(qi + 1) * 128],
                                start=True, stop=True)
            K.op("tensor", smp, reads=[b_k2[sl], b_q2[sl]], writes=[bps[0], bps[1], bps[2]])
            c0 = n * NQS + qi
            if nstr == 8:
                K.op("scalar", (lambda c0, psS: lambda h: h.activation(
                    junk[:, 0:1280], psS[:, 0:1280], AF.Exp, scale=SC / 4.0,
                    accum_out=accQ[:, c0:c0 + 1]))(c0, psS),
                    reads=[bps[0], bps[1], bps[2]], writes=[b_junk, b_accQ])
            else:
                K.op("scalar", (lambda c0, psS, hn: lambda h: h.activation(
                    junk[:, 0:hn * 128], psS[:, 0:hn * 128], AF.Exp, scale=SC / 4.0,
                    accum_out=accQ[:, NV + c0:NV + c0 + 1]))(c0, psS, hn),
                    reads=[bps[0], bps[1], bps[2]], writes=[b_junk, b_accQ])
                K.op("scalar", (lambda c0, psS: lambda h: h.activation(
                    junk[:, 0:256], psS[:, 1024:1280], AF.Exp, scale=SC / 4.0,
                    accum_out=accQ[:, 2 * NV + c0:2 * NV + c0 + 1]))(c0, psS),
                    reads=[bps[0], bps[1], bps[2]], writes=[b_junk, b_accQ])
                K.op("vector", (lambda c0: lambda h: h.tensor_tensor(
                    accQ[:, c0:c0 + 1], accQ[:, NV + c0:NV + c0 + 1],
                    accQ[:, 2 * NV + c0:2 * NV + c0 + 1], op=AluOpType.add))(c0),
                    reads=[b_accQ], writes=[b_accQ])
        K.op("vector", (lambda n: lambda h: h.tensor_scalar(
            accQI[:, NH * NQS + n * NQS:NH * NQS + (n + 1) * NQS],
            accQI[:, n * NQS:(n + 1) * NQS], 23, None,
            op0=AluOpType.arith_shift_right))(n),
            reads=[b_accQ], writes=[b_accQ])
        K.op("vector", (lambda n: lambda h: h.tensor_copy(
            accQ[:, 2 * NH * NQS + n * NQS:2 * NH * NQS + (n + 1) * NQS],
            accQI[:, NH * NQS + n * NQS:NH * NQS + (n + 1) * NQS]))(n),
            reads=[b_accQ], writes=[b_accQ])
        K.op("vector", (lambda n: lambda h: h.tensor_scalar(
            mnegs[:, n * NQS:(n + 1) * NQS],
            accQ[:, 2 * NH * NQS + n * NQS:2 * NH * NQS + (n + 1) * NQS],
            -127.0, -2.0 * 0.6931471805599453,
            op0=AluOpType.add, op1=AluOpType.mult))(n),
            reads=[b_accQ], writes=[b_mnegs])

    pend = []

    def flush_pending():
        for f in pend:
            f()
        pend.clear()

    for n in range(NH):
        sl = n % 2
        K.op("sync", (lambda n, sl: lambda h: h.dma_start(
            arena[:, q2o + sl * T:q2o + (sl + 1) * T], qscr[n, :, :]))(n, sl),
            waits=e_qscr[n], writes=[b_q2[sl]], dma=True)
        for half in range(2):
            K.op("sync", (lambda n, sl, half: lambda h: h.dma_start(
                arena[:, k2o + sl * S + half * T:k2o + sl * S + (half + 1) * T],
                kag_o[half, n, :, :]))(n, sl, half),
                waits=[e_kag], writes=[b_k2[sl]], dma=True)
            K.op("sync", (lambda n, sl, half: lambda h: h.dma_start(
                arena[:, v2o + sl * S + half * T:v2o + sl * S + (half + 1) * T]
                .rearrange("p (tb d) -> p tb d", d=128),
                vag_o[half, :, n, :, :].transpose([1, 0, 2])))(n, sl, half),
                waits=[e_vag], writes=[b_v2[sl]], dma=True)
        for qg in range(NGQ):
            for qs in range(GQ):
                qi = qg * GQ + qs
                p = qi % 2
                def smm(h, n=n, qi=qi, sl=sl, half=0):
                    for m in range(nSA):
                        w = min(512, SH - m * 512)
                        ps = psA if half == 0 else psB
                        mm = h.matmul(ps[:, m * 512:m * 512 + w],
                                      arena[:, q2o + sl * T + qi * 128:q2o + sl * T + (qi + 1) * 128],
                                      arena[:, k2o + sl * S + half * SH + m * 512:
                                            k2o + sl * S + half * SH + m * 512 + w],
                                      start=True, stop=True)
                    return mm
                K.op("tensor", functools.partial(smm, half=0),
                     reads=[b_k2[sl], b_q2[sl]], writes=psA_banks)
                K.op("tensor", functools.partial(smm, half=1),
                     reads=[b_k2[sl], b_q2[sl]], writes=psB_banks)
                K.op("scalar", (lambda n, qi, p: lambda h: h.activation(
                    P2s[:, p * S:p * S + SH], psA[:, 0:SH], AF.Exp, scale=SC / 2.0,
                    bias=mnegs[:, n * NQS + qi:n * NQS + qi + 1]))(n, qi, p),
                    reads=psA_banks + [b_mnegs], writes=[b_P2s[p][0]])
                K.op("vector", (lambda p: lambda h: h.tensor_tensor(
                    P2q[:, p * S:p * S + SH], P2s[:, p * S:p * S + SH],
                    P2s[:, p * S:p * S + SH], op=AluOpType.mult))(p),
                    reads=[b_P2s[p][0]], writes=[b_P2q[p][0]])
                K.op("scalar", (lambda n, qi, p: lambda h: h.activation(
                    P2s[:, p * S + SH:p * S + 2 * SH], psB[:, 0:SH], AF.Exp, scale=SC / 2.0,
                    bias=mnegs[:, n * NQS + qi:n * NQS + qi + 1]))(n, qi, p),
                    reads=psB_banks + [b_mnegs], writes=[b_P2s[p][1]])
                K.op("vector", (lambda p: lambda h: h.tensor_tensor(
                    P2q[:, p * S + SH:p * S + 2 * SH], P2s[:, p * S + SH:p * S + 2 * SH],
                    P2s[:, p * S + SH:p * S + 2 * SH], op=AluOpType.mult))(p),
                    reads=[b_P2s[p][1]], writes=[b_P2q[p][1]])
                flush_pending()
                def def_T(p=p, qs=qs):
                    for bb in range(NBA):
                        base = (bb % 2) * 512
                        def trb(h, p=p, bb=bb, base=base):
                            for k_ in range(4):
                                j = bb * 4 + k_
                                if j >= KC:
                                    break
                                tr = h.transpose(
                                    psA[:, base + k_ * 64:base + k_ * 64 + 64].bitcast(BF16),
                                    P2q[:, p * S + j * 128:p * S + (j + 1) * 128],
                                    ident)
                            return tr
                        rb = [b_cst]
                        if bb * 4 * 128 < SH:
                            rb.append(b_P2q[p][0])
                        if min((bb * 4 + 4), KC) * 128 > SH:
                            rb.append(b_P2q[p][1])
                        K.op("tensor", trb, reads=rb, writes=[b_psA[bb % 2]])
                        nj = min(4, KC - bb * 4)
                        def evb(h, bb=bb, base=base, qs=qs, nj=nj):
                            return h.tensor_copy(
                                PTsl[:, :].rearrange("p (j q) -> p j q", q=QW)
                                [:, bb * 4:bb * 4 + nj, qs * 128:(qs + 1) * 128],
                                psA[:, base:base + nj * 64].bitcast(BF16)
                                .rearrange("p (j q) -> p j q", q=128))
                        K.op("vector", evb, reads=[b_psA[bb % 2]], writes=[b_PT])
                pend.append(def_T)
                if qs == GQ - 1:
                    def def_G(n=n, sl=sl, qg=qg):
                        def pv(h, sl=sl):
                            for kt in range(KC):
                                mm = h.matmul(psA[:, 1024:1024 + QW],
                                              arena[:, v2o + sl * S + kt * 128:v2o + sl * S + (kt + 1) * 128],
                                              PTsl[:, kt * QW:(kt + 1) * QW],
                                              start=(kt == 0), stop=(kt == KC - 1))
                            return mm
                        K.op("tensor", pv, reads=[b_v2[sl], b_PT], writes=[b_psA[2]])
                        def lmm(h):
                            for kt in range(KC):
                                mm = h.matmul(psA[:, 1536:1536 + QW], ones_sb[:, 0:128],
                                              PTsl[:, kt * QW:(kt + 1) * QW],
                                              start=(kt == 0), stop=(kt == KC - 1))
                            return mm
                        K.op("tensor", lmm, reads=[b_ones, b_PT], writes=[b_psA[3]])
                        K.op("vector", lambda h: h.reciprocal(lrep_sb[:, 0:QW], psA[:, 1536:1536 + QW]),
                             reads=[b_psA[3]], writes=[b_lrep])
                        K.op("vector", (lambda n, qg: lambda h: h.tensor_tensor(
                            aoslice(n, qg * QW, QW), psA[:, 1024:1024 + QW], lrep_sb[:, 0:QW],
                            op=AluOpType.mult))(n, qg),
                            reads=[b_psA[2], b_lrep], writes=aobufs(qg * QW, QW, [n]))
                    pend.append(def_G)
    flush_pending()

    K.barrier()
    for tw in range(NTW):
        t0 = tw * TW
        for o in range(HC):
            i2 = tw * HC + o
            ws = i2 % 2
            K.op(g, (lambda o, ws: lambda h: h.dma_start(
                arena[:, wo0 + ws * HC * 128:wo0 + (ws + 1) * HC * 128]
                .rearrange("p (c q) -> p c q", q=128),
                wo_h[o, :, :].rearrange("(c p) q -> p c q", p=128)))(o, ws),
                writes=[b_wo[ws]], dma=True)
            for j in range(JT):
                gi = i2 * JT + j
                ps = gi % 2
                xs = gi % 4
                K.op("sync", (lambda o, j, t0, xs: lambda h: h.dma_start(
                    x_res[:, xs * 512:xs * 512 + JW],
                    xT[o * 128:(o + 1) * 128, t0 + j * JW:t0 + (j + 1) * JW]))(o, j, t0, xs),
                    writes=[b_x[xs]], dma=True)
                def pmm(h, o=o, j=j, ws=ws, ps=ps, t0=t0):
                    for c in range(HC):
                        mm = h.matmul(psA[:, ps * 512:ps * 512 + JW],
                                      arena[:, wo0 + ws * HC * 128 + c * 128:
                                            wo0 + ws * HC * 128 + (c + 1) * 128],
                                      aoslice(c, t0 + j * JW, JW),
                                      start=(c == 0), stop=(c == HC - 1))
                    return mm
                K.op("tensor", pmm, reads=[b_wo[ws]] + aobufs(t0 + j * JW, JW),
                     writes=[b_psA[ps]])
                K.op("vector", (lambda o, j, ps, xs: lambda h: h.scalar_tensor_tensor(
                    r1v[:, o * TW + j * JW:o * TW + (j + 1) * JW],
                    psA[:, ps * 512:ps * 512 + JW], bo_sb[:, o:o + 1],
                    x_res[:, xs * 512:xs * 512 + JW],
                    op0=AluOpType.add, op1=AluOpType.add))(o, j, ps, xs),
                    reads=[b_psA[ps], b_bo, b_x[xs]], writes=[b_r1[o][j]])
                rs_ = gi % 2
                K.op("vector", (lambda o, j, rs_: lambda h: h.tensor_copy(
                    rbst[:, rs_ * 512:rs_ * 512 + JW],
                    r1v[:, o * TW + j * JW:o * TW + (j + 1) * JW]))(o, j, rs_),
                    reads=[b_r1[o][j]], writes=[b_rb[rs_]])
                K.op("vector", (lambda rs_: lambda h: h.tensor_tensor(
                    sqst[:, rs_ * 512:rs_ * 512 + JW], rbst[:, rs_ * 512:rs_ * 512 + JW],
                    rbst[:, rs_ * 512:rs_ * 512 + JW], op=AluOpType.mult))(rs_),
                    reads=[b_rb[rs_]], writes=[b_sq[rs_]])
                def stmm(h, o=o, j=j, rs_=rs_):
                    h.matmul(psB[:, j * 512:j * 512 + JW], ones_sb[:, 0:128],
                             rbst[:, rs_ * 512:rs_ * 512 + JW],
                             start=(o == 0), stop=(o == HC - 1))
                    return h.matmul(psB[:, 1024 + j * 512:1024 + j * 512 + JW],
                                    ones_sb[:, 0:128], sqst[:, rs_ * 512:rs_ * 512 + JW],
                                    start=(o == 0), stop=(o == HC - 1))
                K.op("tensor", stmm, reads=[b_rb[rs_], b_sq[rs_], b_ones],
                     writes=[b_psB[j], b_psB[2 + j]])
        if debug:
            K.op("sync", (lambda tw: lambda h: h.dma_start(
                r1dump[tw, :, :], r1v[:, 0:HC * TW]))(tw),
                reads=[b for row in b_r1 for b in row], dma=True)
        for j in range(JT):
            mu = stat[:, 0 * RSTR + j * JW:0 * RSTR + j * JW + JW]
            rs = stat[:, 1 * RSTR + j * JW:1 * RSTR + j * JW + JW]
            mus = stat[:, 2 * RSTR + j * JW:2 * RSTR + j * JW + JW]
            scr2 = stat[:, 3 * RSTR + j * JW:3 * RSTR + j * JW + JW]
            bmu, brs, bmus, bscr = b_stat[0], b_stat[1], b_stat[2], b_stat[3]
            K.op("vector", (lambda mu, j: lambda h: h.tensor_scalar(
                mu, psB[:, j * 512:j * 512 + JW], IH, None, op0=AluOpType.mult))(mu, j),
                reads=[b_psB[j]], writes=[bmu])
            K.op("vector", (lambda scr2, j: lambda h: h.tensor_scalar(
                scr2, psB[:, 1024 + j * 512:1024 + j * 512 + JW], IH, cfg.EPS,
                op0=AluOpType.mult, op1=AluOpType.add))(scr2, j),
                reads=[b_psB[2 + j]], writes=[bscr])
            var2 = stat[:, 4 * RSTR + j * JW:4 * RSTR + j * JW + JW]
            rin2 = stat[:, 5 * RSTR + j * JW:5 * RSTR + j * JW + JW]
            bvar, brin = b_stat2[0], b_stat2[1]
            K.op("vector", (lambda rs, mu: lambda h: h.tensor_tensor(
                rs, mu, mu, op=AluOpType.mult))(rs, mu), reads=[bmu], writes=[brs])
            K.op("vector", (lambda var2, scr2, rs: lambda h: h.tensor_tensor(
                var2, scr2, rs, op=AluOpType.subtract))(var2, scr2, rs),
                reads=[brs, bscr], writes=[bvar])
            K.op("vector", (lambda rin2, var2: lambda h: h.reciprocal(rin2, var2))(rin2, var2),
                reads=[bvar], writes=[brin])
            K.op("scalar", (lambda rs, rin2: lambda h: h.activation(rs, rin2, AF.Sqrt))(rs, rin2),
                reads=[brin], writes=[brs])
            K.op("vector", (lambda mus, mu, rs: lambda h: h.tensor_tensor(
                mus, mu, rs, op=AluOpType.mult))(mus, mu, rs), reads=[bmu, brs], writes=[bmus])
            if debug and j == JT - 1:
                K.op("sync", (lambda tw: lambda h: h.dma_start(
                    stdump[tw, :, :], stat[:, 0:6 * RSTR]))(tw),
                    reads=[b_stat[0], b_stat[1], b_stat[2], b_stat[3],
                           b_stat2[0], b_stat2[1]], dma=True)
            for o in range(HC):
                ys = (o * JT + j) % 2
                K.op("vector", (lambda o, j, ys, rs: lambda h: h.tensor_tensor(
                    ystage[:, ys * 512:ys * 512 + JW],
                    r1v[:, o * TW + j * JW:o * TW + (j + 1) * JW], rs,
                    op=AluOpType.mult))(o, j, ys, rs),
                    reads=[b_r1[o][j], brs], writes=[b_yst[ys]])
                K.op("vector", (lambda ys, mus: lambda h: h.tensor_tensor(
                    ystage[:, (2 + ys) * 512:(2 + ys) * 512 + JW],
                    ystage[:, ys * 512:ys * 512 + JW],
                    mus, op=AluOpType.subtract))(ys, mus),
                    reads=[bmus, b_yst[ys]], writes=[b_yst2[ys]])
                K.op("scalar", (lambda o, j, ys, t0: lambda h: h.activation(
                    aoslice(o, t0 + j * JW, JW), ystage[:, (2 + ys) * 512:(2 + ys) * 512 + JW],
                    AF.Copy))(o, j, ys, t0),
                    reads=[b_yst2[ys]], writes=aobufs(t0 + j * JW, JW, [o]))
                ev = K.op("sync", (lambda o, j, ys, t0: lambda h: h.dma_start(
                    yscr[o * 128:(o + 1) * 128, t0 + j * JW:t0 + (j + 1) * JW],
                    ystage[:, (2 + ys) * 512:(2 + ys) * 512 + JW]))(o, j, ys, t0),
                    reads=[b_yst2[ys]], dma=True)
                b_yscr[o][(t0 + j * JW) // (512 if T >= 512 else T)].w = ev

    K.barrier()
    for jt in range(T // QT):
        def rmm(h, jt=jt):
            for c in range(HC):
                mm = h.matmul(psA[0:8, jt * 512:jt * 512 + QT],
                              wr_sb[:, c * 8:(c + 1) * 8],
                              aoslice(c, jt * QT, QT),
                              start=(c == 0), stop=(c == HC - 1))
            return mm
        K.op("tensor", rmm, reads=[b_wr] + aobufs(jt * QT, QT), writes=[b_psA[jt % 4]])
        K.op("vector", (lambda jt: lambda h: h.tensor_copy(
            lgf[:, jt * QT:(jt + 1) * QT], psA[0:8, jt * 512:jt * 512 + QT]))(jt),
            reads=[b_psA[jt % 4]], writes=[b_lgf])
    for tb in range(NTB):
        K.op("tensor", (lambda tb: lambda h: h.transpose(
            psA[:, (tb % 4) * 8:(tb % 4) * 8 + 8],
            lgf[0:8, tb * 128:(tb + 1) * 128], ident8f))(tb),
            reads=[b_lgf, b_cstf], writes=[b_psA[0]])
        if tb % 4 == 3 or tb == NTB - 1:
            lo = (tb // 4) * 4
            cnt = tb - lo + 1
            K.op("vector", (lambda lo, cnt: lambda h: h.tensor_copy(
                gtok[:, lo * 8:(lo + cnt) * 8], psA[:, 0:cnt * 8]))(lo, cnt),
                reads=[b_psA[0]], writes=[b_gtok])
    seg = gtok[:, :].rearrange("p (s e) -> p s e", e=8)
    seg2 = gtk2[:, :].rearrange("p (s e) -> p s e", e=8)
    seg3 = gtk3[:, :].rearrange("p (s e) -> p s e", e=8)
    gb = lambda: gsc[:, :, None].broadcast_to((128, NTB, 8))
    gb2 = lambda: gsc2[:, :, None].broadcast_to((128, NTB, 8))
    K.op("vector", lambda h: h.tensor_reduce(gsc[:, :, None], seg, axis=mybir.AxisListType.X,
                                             op=AluOpType.max), reads=[b_gtok], writes=[b_gsc])
    K.op("vector", lambda h: h.tensor_tensor(seg2, seg, gb(), op=AluOpType.subtract),
         reads=[b_gtok, b_gsc], writes=[b_gtk2])
    K.op("scalar", lambda h: h.activation(gtok[:, :], gtk2[:, :], AF.Exp),
         reads=[b_gtk2], writes=[b_gtok])
    K.op("vector", lambda h: h.tensor_reduce(gsc[:, :, None], seg, axis=mybir.AxisListType.X,
                                             op=AluOpType.add), reads=[b_gtok], writes=[b_gsc])
    K.op("vector", lambda h: h.reciprocal(gsc2[:, :], gsc[:, :]), reads=[b_gsc], writes=[b_gsc2])
    K.op("vector", lambda h: h.tensor_tensor(seg2, seg, gb2(), op=AluOpType.mult),
         reads=[b_gsc2, b_gtok], writes=[b_gtk2])
    K.op("vector", lambda h: h.tensor_reduce(gsc[:, :, None], seg2, axis=mybir.AxisListType.X,
                                             op=AluOpType.max), reads=[b_gtk2], writes=[b_gsc])
    K.op("vector", lambda h: h.tensor_tensor(seg, seg2, gb(), op=AluOpType.is_ge),
         reads=[b_gtk2, b_gsc], writes=[b_gtok])
    K.op("vector", lambda h: h.scalar_tensor_tensor(gtk3[:, :], gtok[:, :], -2.0, gtk2[:, :],
                                                    op0=AluOpType.mult, op1=AluOpType.add),
         reads=[b_gtk2, b_gtok], writes=[b_gtk3])
    K.op("vector", lambda h: h.tensor_reduce(gsc[:, :, None], seg3, axis=mybir.AxisListType.X,
                                             op=AluOpType.max), reads=[b_gtk3], writes=[b_gsc])
    K.op("vector", lambda h: h.tensor_tensor(seg, seg2, gb(), op=AluOpType.is_ge),
         reads=[b_gtk2, b_gsc], writes=[b_gtok])
    K.op("vector", lambda h: h.tensor_tensor(seg3, seg, seg2, op=AluOpType.mult),
         reads=[b_gtk2, b_gtok], writes=[b_gtk3])
    for tb in range(NTB):
        K.op("tensor", (lambda tb: lambda h: h.transpose(
            psA[0:8, 512 + (tb % 4) * 128:512 + (tb % 4) * 128 + 128],
            gtk3[:, tb * 8:(tb + 1) * 8], identf))(tb),
            reads=[b_gtk3, b_cstf], writes=[b_psA[1]])
        K.op("vector", (lambda tb: lambda h: h.tensor_copy(
            gT_sb[:, tb * 128:(tb + 1) * 128],
            psA[0:8, 512 + (tb % 4) * 128:512 + (tb % 4) * 128 + 128]))(tb),
            reads=[b_psA[1]], writes=[b_gT])

    for tw in range(NTW):
        t0 = tw * TW
        for j in range(JT):
            jj = tw * JT + j
            K.op("tensor", (lambda t0, j: lambda h: h.matmul(
                psB[:, 1536:1536 + JW], sel_sb[:, :], gT_sb[:, t0 + j * JW:t0 + (j + 1) * JW],
                start=True, stop=True))(t0, j),
                reads=[b_sel, b_gT], writes=[b_psB[3]])
            def dmm(h, t0=t0, j=j):
                for c in range(HC):
                    mm = h.matmul(psB[:, 1024:1024 + JW], wd_sb[:, c * EI:(c + 1) * EI][:, 0:128],
                                  aoslice(c, t0 + j * JW, JW), start=(c == 0), stop=(c == HC - 1))
                return mm
            K.op("tensor", dmm, reads=[b_wd] + aobufs(t0 + j * JW, JW), writes=[b_psB[2]])
            sb_ = jj % 2
            K.op("scalar", (lambda sb_: lambda h: h.activation(
                agst[:, sb_ * 512:sb_ * 512 + JW], psB[:, 1024:1024 + JW], AF.Silu))(sb_),
                reads=[b_psB[2]], writes=[b_ag[sb_]])
            K.op("vector", (lambda sb_: lambda h: h.tensor_tensor(
                agst[:, 1024 + sb_ * 512:1024 + sb_ * 512 + JW],
                agst[:, sb_ * 512:sb_ * 512 + JW], psB[:, 1536:1536 + JW],
                op=AluOpType.mult))(sb_),
                reads=[b_ag[sb_], b_psB[3]], writes=[b_ag[2 + sb_]])
            for o in range(HC):
                oi = o * JT + j
                ps = oi % 2
                xs = (tw * HC * JT + o * JT + j) % 2
                K.op("sync", (lambda o, j, t0, xs: lambda h: h.dma_start(
                    x_res[:, (2 + xs) * 512:(2 + xs) * 512 + JW],
                    yscr[o * 128:(o + 1) * 128, t0 + j * JW:t0 + (j + 1) * JW]))(o, j, t0, xs),
                    reads=[b_yscr[o][(t0 + j * JW) // (512 if T >= 512 else T)]],
                    writes=[b_x[2 + xs]], dma=True)
                K.op("tensor", (lambda o, sb_, ps: lambda h: h.matmul(
                    psB[:, ps * 512:ps * 512 + JW], wu_sb[:, o * 128:(o + 1) * 128],
                    agst[:, 1024 + sb_ * 512:1024 + sb_ * 512 + JW],
                    start=True, stop=True))(o, sb_, ps),
                    reads=[b_wu, b_ag[2 + sb_]], writes=[b_psB[ps]])
                K.op("vector", (lambda o, j, ps, xs: lambda h: h.tensor_tensor(
                    r1v[:, o * TW + j * JW:o * TW + (j + 1) * JW],
                    psB[:, ps * 512:ps * 512 + JW], x_res[:, (2 + xs) * 512:(2 + xs) * 512 + JW],
                    op=AluOpType.add))(o, j, ps, xs),
                    reads=[b_psB[ps], b_x[2 + xs]], writes=[b_r1[o][j]])
                rs_ = oi % 2
                K.op("vector", (lambda o, j, rs_: lambda h: h.tensor_copy(
                    rbst[:, rs_ * 512:rs_ * 512 + JW],
                    r1v[:, o * TW + j * JW:o * TW + (j + 1) * JW]))(o, j, rs_),
                    reads=[b_r1[o][j]], writes=[b_rb[rs_]])
                K.op("vector", (lambda rs_: lambda h: h.tensor_tensor(
                    sqst[:, rs_ * 512:rs_ * 512 + JW], rbst[:, rs_ * 512:rs_ * 512 + JW],
                    rbst[:, rs_ * 512:rs_ * 512 + JW], op=AluOpType.mult))(rs_),
                    reads=[b_rb[rs_]], writes=[b_sq[rs_]])
                def stmm(h, o=o, j=j, rs_=rs_):
                    h.matmul(psA[:, j * 512:j * 512 + JW], ones_sb[:, 0:128],
                             rbst[:, rs_ * 512:rs_ * 512 + JW],
                             start=(o == 0), stop=(o == HC - 1))
                    return h.matmul(psA[:, 1024 + j * 512:1024 + j * 512 + JW],
                                    ones_sb[:, 0:128], sqst[:, rs_ * 512:rs_ * 512 + JW],
                                    start=(o == 0), stop=(o == HC - 1))
                K.op("tensor", stmm, reads=[b_rb[rs_], b_sq[rs_], b_ones],
                     writes=[b_psA[j], b_psA[2 + j]])
        for j in range(JT):
            mu = stat[:, 0 * RSTR + j * JW:0 * RSTR + j * JW + JW]
            rs = stat[:, 1 * RSTR + j * JW:1 * RSTR + j * JW + JW]
            mus = stat[:, 2 * RSTR + j * JW:2 * RSTR + j * JW + JW]
            scr2 = stat[:, 3 * RSTR + j * JW:3 * RSTR + j * JW + JW]
            bmu, brs, bmus, bscr = b_stat[0], b_stat[1], b_stat[2], b_stat[3]
            K.op("vector", (lambda mu, j: lambda h: h.tensor_scalar(
                mu, psA[:, j * 512:j * 512 + JW], IH, None, op0=AluOpType.mult))(mu, j),
                reads=[b_psA[j]], writes=[bmu])
            K.op("vector", (lambda scr2, j: lambda h: h.tensor_scalar(
                scr2, psA[:, 1024 + j * 512:1024 + j * 512 + JW], IH, cfg.EPS,
                op0=AluOpType.mult, op1=AluOpType.add))(scr2, j),
                reads=[b_psA[2 + j]], writes=[bscr])
            var2 = stat[:, 4 * RSTR + j * JW:4 * RSTR + j * JW + JW]
            rin2 = stat[:, 5 * RSTR + j * JW:5 * RSTR + j * JW + JW]
            bvar, brin = b_stat2[0], b_stat2[1]
            K.op("vector", (lambda rs, mu: lambda h: h.tensor_tensor(
                rs, mu, mu, op=AluOpType.mult))(rs, mu), reads=[bmu], writes=[brs])
            K.op("vector", (lambda var2, scr2, rs: lambda h: h.tensor_tensor(
                var2, scr2, rs, op=AluOpType.subtract))(var2, scr2, rs),
                reads=[brs, bscr], writes=[bvar])
            K.op("vector", (lambda rin2, var2: lambda h: h.reciprocal(rin2, var2))(rin2, var2),
                reads=[bvar], writes=[brin])
            K.op("scalar", (lambda rs, rin2: lambda h: h.activation(rs, rin2, AF.Sqrt))(rs, rin2),
                reads=[brin], writes=[brs])
            K.op("vector", (lambda mus, mu, rs: lambda h: h.tensor_tensor(
                mus, mu, rs, op=AluOpType.mult))(mus, mu, rs), reads=[bmu, brs], writes=[bmus])
            for o in range(HC):
                ys = (o * JT + j) % 2
                K.op("vector", (lambda o, j, ys, rs: lambda h: h.tensor_tensor(
                    ystage[:, ys * 512:ys * 512 + JW],
                    r1v[:, o * TW + j * JW:o * TW + (j + 1) * JW], rs,
                    op=AluOpType.mult))(o, j, ys, rs),
                    reads=[b_r1[o][j], brs], writes=[b_yst[ys]])
                K.op("vector", (lambda ys, mus: lambda h: h.tensor_tensor(
                    ystage[:, (2 + ys) * 512:(2 + ys) * 512 + JW],
                    ystage[:, ys * 512:ys * 512 + JW],
                    mus, op=AluOpType.subtract))(ys, mus),
                    reads=[bmus, b_yst[ys]], writes=[b_yst2[ys]])
                evo = K.op("sync", (lambda o, j, ys, t0: lambda h: h.dma_start(
                    outT[o * 128:(o + 1) * 128, t0 + j * JW:t0 + (j + 1) * JW],
                    ystage[:, (2 + ys) * 512:(2 + ys) * 512 + JW]))(o, j, ys, t0),
                    reads=[b_yst2[ys]], dma=True)
    K.op("sync", lambda h: None, waits=[evo])
    K.mark(evo)

    K.finalize(nc, block, sems)
    es.close()
    return nc


def prep_inputs(cfg, x_bh, w_qkv, b_qkv, w_out, b_out, router_w, down_w, up_w, own_first=True):
    H, NH, HC, S, VD, NVD, EI = cfg.H, cfg.NH, cfg.HC, cfg.S, cfg.VD, cfg.NVD, cfg.EI
    f32 = np.float32
    d = {}
    d["xT"] = np.ascontiguousarray(x_bh.T.astype(f32))
    Wq, Wk, Wv = w_qkv[0:H], w_qkv[H:2 * H], w_qkv[2 * H:3 * H]
    d["wq"] = np.ascontiguousarray(
        Wq.reshape(NH, 128, H).transpose(0, 2, 1))
    d["wk"] = np.ascontiguousarray(Wk.reshape(NH, 128, H).transpose(0, 2, 1))
    d["wv"] = np.ascontiguousarray(
        Wv.T.reshape(H, NVD, VD).transpose(1, 0, 2))
    d["wo"] = np.ascontiguousarray(w_out.reshape(HC, 128, H).transpose(0, 2, 1))
    d["bqk"] = np.ascontiguousarray(
        np.concatenate([b_qkv[0:H].reshape(NH, 128).T, b_qkv[H:2 * H].reshape(NH, 128).T], axis=1)).astype(f32)
    d["bo"] = np.ascontiguousarray(b_out.reshape(HC, 128).T).astype(f32)
    d["bv"] = b_qkv[2 * H:3 * H].reshape(1, H).astype(f32)
    d["wr"] = np.ascontiguousarray(
        router_w.T.reshape(HC, 128, 8).transpose(1, 0, 2).reshape(128, HC * 8)).astype(f32)
    Dcat = down_w.reshape(EI, H)
    d["wd"] = np.ascontiguousarray(
        Dcat.T.reshape(HC, 128, EI).transpose(1, 0, 2).reshape(128, HC * EI)).astype(f32)
    Ucat = up_w.transpose(1, 0, 2).reshape(H, EI)
    d["wu"] = np.ascontiguousarray(Ucat.T).astype(f32)
    cst = np.zeros((128, 256), dtype=BF)
    cst[:, 0:128] = np.eye(128, dtype=BF)
    sel = np.zeros((8, 128), dtype=BF)
    for e in range(8):
        sel[e, e * 16:(e + 1) * 16] = 1
    cst[0:8, 128:256] = sel
    d["cst"] = cst
    d["cstf"] = np.eye(128, dtype=f32)
    return d


from concourse.bass_utils import run_bass_kernel_spmd


def kernel(L0_attn_hidden_states, L0_attn_w_qkv, L0_attn_b_qkv, L0_attn_w_out,
           L0_attn_b_out, L0_ffn_router_w, L0_ffn_down_w, L0_ffn_up_w):
    x = np.asarray(L0_attn_hidden_states)
    w_qkv = np.asarray(L0_attn_w_qkv)
    b_qkv = np.asarray(L0_attn_b_qkv)
    w_out = np.asarray(L0_attn_w_out)
    b_out = np.asarray(L0_attn_b_out)
    router_w = np.asarray(L0_ffn_router_w)
    down_w = np.asarray(L0_ffn_down_w)
    up_w = np.asarray(L0_ffn_up_w)

    B, Sfull, H = x.shape
    cfg = Cfg(NH=16, S=4096, T=2048, QT=512, n_cores=8)
    nc = build(cfg)

    wdict = prep_inputs(cfg, np.zeros((cfg.T, H), np.float32), w_qkv, b_qkv,
                        w_out, b_out, router_w, down_w, up_w)
    del wdict["xT"]

    in_maps = []
    for c in range(8):
        b, hf = c // 2, c % 2
        m = dict(wdict)
        m["xT"] = np.ascontiguousarray(x[b, hf * cfg.T:(hf + 1) * cfg.T, :].T.astype(np.float32))
        in_maps.append(m)

    res = run_bass_kernel_spmd(nc, in_maps, core_ids=list(range(8)))

    out = np.empty((B, Sfull, H), dtype=np.float32)
    for c in range(8):
        b, hf = c // 2, c % 2
        out[b, hf * cfg.T:(hf + 1) * cfg.T, :] = res.results[c]["outT"].T
    return out


# version 16
# speedup vs baseline: 1.3406x; 1.1413x over previous
import math
import numpy as np
import ml_dtypes
from contextlib import ExitStack

import concourse.bass as bass
import concourse.mybir as mybir
from concourse.alu_op_type import AluOpType

F32 = mybir.dt.float32
BF16 = mybir.dt.bfloat16
AF = mybir.ActivationFunctionType
BF = ml_dtypes.bfloat16
ENGINES = ("tensor", "vector", "scalar", "gpsimd", "sync")
DMA_SEMS = {"vector": ["vd0"], "scalar": ["sd0", "sd1"],
            "gpsimd": ["gd0", "gd1", "gd2", "gd3"],
            "sync": ["yd0", "yd1", "yd2", "yd3"]}
ALL_SEMS = ["pe", "ve", "ac", "gp", "sy", "xd"] + [s for l in DMA_SEMS.values() for s in l]
ENG_SEM = {"tensor": "pe", "vector": "ve", "scalar": "ac", "gpsimd": "gp", "sync": "sy"}


class Ev:
    __slots__ = ("eng", "sem", "idx", "by", "used", "val")

    def __init__(self, eng, sem, idx, by):
        self.eng, self.sem, self.idx, self.by = eng, sem, idx, by
        self.used = False
        self.val = None


class Buf:
    __slots__ = ("w", "r", "name")

    def __init__(self, name=""):
        self.w = None
        self.r = []
        self.name = name


class Sched:
    def __init__(self):
        self.recs = {e: [] for e in ENGINES}
        self.dma_rot = {e: 0 for e in ENGINES}
        self.last_dma = {}
        self.tail = {e: None for e in ENGINES}
        self.pending = {e: [] for e in ENGINES}

    def barrier(self):
        evs = [e for e in self.tail.values() if e is not None]
        evs += [e for e in self.last_dma.values() if e is not None]
        for eng in ENGINES:
            self.pending[eng] = list(evs)

    def op(self, eng, fn, reads=(), writes=(), waits=(), dma=False, inc_by=None, dma_sem=None):
        if dma:
            if dma_sem is not None:
                sem = dma_sem
            else:
                sl = DMA_SEMS[eng]
                sem = sl[self.dma_rot[eng] % len(sl)]
                self.dma_rot[eng] += 1
        else:
            sem = ENG_SEM[eng]
        by = inc_by if inc_by is not None else (16 if dma else 1)
        ev = Ev(eng, sem, len(self.recs[eng]), by)
        wl = []
        if dma:
            ev.used = True
            prev = self.last_dma.get(sem)
            if prev is not None:
                wl.append(prev)
            self.last_dma[sem] = ev
        for b in reads:
            if b.w is not None:
                wl.append(b.w)
        for b in writes:
            if b.w is not None:
                wl.append(b.w)
            wl.extend(b.r)
        for e in waits:
            wl.append(e)
        seen, wl2 = set(), []
        for e in wl:
            if e is None or id(e) in seen:
                continue
            seen.add(id(e))
            wl2.append(e)
        if self.pending[eng]:
            for e in self.pending[eng]:
                if id(e) not in seen and e is not ev:
                    seen.add(id(e))
                    wl2.append(e)
            self.pending[eng] = []
        self.recs[eng].append((fn, wl2, ev))
        self.tail[eng] = ev
        for b in reads:
            b.r.append(ev)
        for b in writes:
            b.w = ev
            b.r = []
        return ev

    def mark(self, *evs):
        for e in evs:
            if e is not None:
                e.used = True

    def finalize(self, nc, block, sems):
        for eng in ENGINES:
            cnt = {}
            for fn, wl, ev in self.recs[eng]:
                for e in wl:
                    e.used = True
        for eng in ENGINES:
            cnt = {}
            for fn, wl, ev in self.recs[eng]:
                if ev.used:
                    cnt[ev.sem] = cnt.get(ev.sem, 0) + ev.by
                    ev.val = cnt[ev.sem]
        def make(eng):
            def emit(h):
                hw = {}
                for fn, wl, ev in self.recs[eng]:
                    for e in wl:
                        if e.val is None:
                            continue
                        key = e.sem
                        if hw.get(key, 0) >= e.val:
                            continue
                        h.wait_ge(sems[e.sem], e.val)
                        hw[key] = e.val
                    inst = fn(h)
                    if ev.used and ev.val is not None:
                        assert inst is not None, f"{eng} op needs inc but returned None"
                        inst.then_inc(sems[ev.sem], ev.by)
            return emit
        for eng in ENGINES:
            if not self.recs[eng]:
                continue
            getattr(block, eng)(make(eng))


class Cfg:
    def __init__(self, NH=16, S=4096, T=2048, QT=512, n_cores=8):
        self.NH = NH
        self.HC = NH
        self.H = NH * 128
        self.S = S
        self.T = T
        self.QT = QT
        self.KC = S // 128
        self.NQT = T // QT
        self.E, self.I = 8, 16
        self.EI = 128
        self.SCALE = 1.0 / math.sqrt(128.0)
        self.VD = min(512, self.H)
        self.NVD = self.H // self.VD
        self.TB = S // 128
        self.TBo = T // 128
        self.n_cores = n_cores
        self.EPS = 1e-5
        self.TW = min(2 * QT, T)
        self.NTW = T // self.TW
        self.JW = min(512, self.TW)
        self.JT = self.TW // self.JW


def build(cfg: Cfg, debug=False):
    nc = bass.Bass("TRN2", target_bir_lowering=False)
    K = Sched()
    H, S, T, QT, NH, HC, KC = cfg.H, cfg.S, cfg.T, cfg.QT, cfg.NH, cfg.HC, cfg.KC
    NQT, EI, VD, NVD, TB = cfg.NQT, cfg.EI, cfg.VD, cfg.NVD, cfg.TB
    TW, NTW, JW, JT = cfg.TW, cfg.NTW, cfg.JW, cfg.JT
    SC = cfg.SCALE
    IH = 1.0 / H

    xT = nc.dram_tensor("xT", [H, T], F32, kind="ExternalInput")
    wq_h = nc.dram_tensor("wq", [NH, HC * 128, 128], F32, kind="ExternalInput")
    wk_h = nc.dram_tensor("wk", [NH, HC * 128, 128], F32, kind="ExternalInput")
    wv_h = nc.dram_tensor("wv", [NVD, HC * 128, VD], F32, kind="ExternalInput")
    wo_h = nc.dram_tensor("wo", [HC, HC * 128, 128], F32, kind="ExternalInput")
    bqk_h = nc.dram_tensor("bqk", [128, 2 * NH], F32, kind="ExternalInput")
    bo_h = nc.dram_tensor("bo", [128, HC], F32, kind="ExternalInput")
    bv_h = nc.dram_tensor("bv", [1, H], F32, kind="ExternalInput")
    wr_h = nc.dram_tensor("wr", [128, HC * 8], F32, kind="ExternalInput")
    wd_h = nc.dram_tensor("wd", [128, HC * EI], F32, kind="ExternalInput")
    wu_h = nc.dram_tensor("wu", [EI, H], F32, kind="ExternalInput")
    cst_h = nc.dram_tensor("cst", [128, 256], BF16, kind="ExternalInput")
    cstf_h = nc.dram_tensor("cstf", [128, 128], F32, kind="ExternalInput")
    outT = nc.dram_tensor("outT", [H, T], F32, kind="ExternalOutput")

    _scr_kind = "ExternalOutput" if debug else "Internal"
    TBo = cfg.TBo
    qscr = nc.dram_tensor("qscr", [NH, 128, T], BF16, kind=_scr_kind)
    kag_i = nc.dram_tensor("kag_i", [NH, 128, T], BF16)
    kag_o = nc.dram_tensor("kag_o", [2, NH, 128, T], BF16, kind=_scr_kind)
    vag_i = nc.dram_tensor("vag_i", [TBo, NH, 128, 128], BF16)
    vag_o = nc.dram_tensor("vag_o", [2, TBo, NH, 128, 128], BF16, kind=_scr_kind)
    yscr = nc.dram_tensor("yscr", [H, T], F32, kind=_scr_kind)
    aodump = nc.dram_tensor("aodump", [128, HC * T], BF16, kind="ExternalOutput") if debug else None
    r1dump = nc.dram_tensor("r1dump", [NTW, 128, HC * TW], F32, kind="ExternalOutput") if debug else None
    accdump = nc.dram_tensor("accdump", [NH, T // 128, 128, 8], F32, kind="ExternalOutput") if debug else None
    stdump = nc.dram_tensor("stdump", [NTW, 128, 6 * JT * JW], F32, kind="ExternalOutput") if debug else None

    es = ExitStack()
    sbt = lambda name, shape, dt: es.enter_context(nc.sbuf_tensor(name, shape, dt))

    cst = sbt("cst_sb", [128, 256], BF16)
    cstf = sbt("cstf_sb", [128, 128], F32)
    sel_sb = sbt("sel_sb", [8, 128], BF16)
    ones_sb = sbt("ones_sb", [128, 128], BF16)
    onesf_sb = sbt("onesf_sb", [1, 128], F32)
    bqk_sb = sbt("bqk_sb", [128, 2 * NH], F32)
    bo_sb = sbt("bo_sb", [128, HC], F32)
    bv_sb = sbt("bv_sb", [1, H], BF16)
    wr_sb = sbt("wr_sb", [128, HC * 8], BF16)
    wd_sb = sbt("wd_sb", [128, HC * EI], BF16)
    wu_sb = sbt("wu_sb", [128, H], BF16)

    NTB = T // 128
    RSTR = JT * JW
    def _al(x):
        return (x + 63) // 64 * 64
    A_XT = 0
    A_wv = _al(A_XT + HC * T)
    A_wqk = _al(A_wv + HC * VD)
    A_kst = _al(A_wqk + 2 * HC * 128)
    A_vst = _al(A_kst + 2 * 512)
    p1_end = _al(A_vst + 2 * 512)
    GQ = min(4, T // 128)
    QW = GQ * 128
    NGQ = T // QW
    SH = S // 2
    A_q2 = 0
    A_k2 = _al(A_q2 + 2 * T)
    A_v2 = _al(A_k2 + 2 * S)
    A_PT = _al(A_v2 + 2 * S)
    A_P2s = _al(A_PT + KC * QW)
    A_P2q = _al(A_P2s + 2 * S)
    A_junk = _al(A_P2q + 2 * S)
    NQS_ = T // 128
    A_accQ = _al(A_junk + 2048)
    A_mnegs = _al(A_accQ + 2 * 4 * NH * NQS_)
    A_acc = _al(A_mnegs + 2 * NH * NQS_)
    A_lcol = _al(A_acc + 64)
    A_lirow = _al(A_lcol + 64)
    A_lrep = _al(A_lirow + 1536)
    p2_end = _al(A_lrep + 1024)
    A_r1 = 0
    A_xres = _al(A_r1 + 2 * HC * TW)
    A_yst = _al(A_xres + 2 * 4 * 512)
    A_rb = _al(A_yst + 2 * 4 * 512)
    A_sq = _al(A_rb + 2 * 512)
    A_stat = _al(A_sq + 2 * 512)
    A_wo = _al(A_stat + 2 * 6 * RSTR)
    wo_end = _al(A_wo + 2 * HC * 128)
    A_lgf = A_wo
    A_gT = _al(A_lgf + 2 * T)
    A_gtok = _al(A_gT + T)
    A_gtk2 = _al(A_gtok + 2 * NTB * 8)
    A_gsc = _al(A_gtk2 + 2 * NTB * 8)
    A_gsc2 = _al(A_gsc + 2 * NTB)
    A_gtk3 = _al(A_gsc2 + 2 * NTB)
    A_ag = _al(A_gtk3 + 2 * NTB * 8)
    p34_end = max(_al(A_ag + 4 * 512), wo_end)
    AO_cols = HC * T
    AO_off = max(p2_end, p34_end)
    AR_COLS = _al(max(AO_off + AO_cols, p1_end) + 64)
    arena = sbt("arena", [128, AR_COLS], BF16)
    assert AR_COLS * 2 <= 190 * 1024, f"arena {AR_COLS * 2} B/partition too big"

    XT = arena[:, A_XT:A_XT + HC * T]
    wv_sb = arena[:, A_wv:A_wv + HC * VD]
    wqk0 = A_wqk
    kst = arena[:, A_kst:A_kst + 2 * 512]
    vst = arena[:, A_vst:A_vst + 2 * 512]
    AO = arena[:, AO_off:AO_off + AO_cols]
    q2o, k2o, v2o = A_q2, A_k2, A_v2
    PTsl = arena[:, A_PT:A_PT + KC * QW]
    P2s = arena[:, A_P2s:A_P2s + 2 * S]
    P2q = arena[:, A_P2q:A_P2q + 2 * S]
    junk = arena[:, A_junk:A_junk + 2048]
    accQ = arena[:, A_accQ:A_accQ + 2 * 4 * NH * NQS_].bitcast(F32)
    accQI = arena[:, A_accQ:A_accQ + 2 * 4 * NH * NQS_].bitcast(mybir.dt.int32)
    mnegs = arena[:, A_mnegs:A_mnegs + 2 * NH * NQS_].bitcast(F32)
    accs = arena[:, A_acc:A_acc + 64].bitcast(F32)
    accsI = arena[:, A_acc:A_acc + 64].bitcast(mybir.dt.int32)
    lbuf = arena[:, A_lcol:A_lcol + 64].bitcast(F32)
    lirow = arena[0:1, A_lirow:A_lirow + 1024].bitcast(F32)
    lirow_bf = arena[0:1, A_lirow + 1024:A_lirow + 1536]
    lrep_sb = arena[:, A_lrep:A_lrep + 1024].bitcast(F32)
    r1v = arena[:, A_r1:A_r1 + 2 * HC * TW].bitcast(F32)
    x_res = arena[:, A_xres:A_xres + 2 * 4 * 512].bitcast(F32)
    ystage = arena[:, A_yst:A_yst + 2 * 4 * 512].bitcast(F32)
    rbst = arena[:, A_rb:A_rb + 2 * 512]
    sqst = arena[:, A_sq:A_sq + 2 * 512]
    stat = arena[:, A_stat:A_stat + 2 * 6 * RSTR].bitcast(F32)
    wo0 = A_wo
    lgf = arena[0:8, A_lgf:A_lgf + 2 * T].bitcast(F32)
    gT_sb = arena[0:8, A_gT:A_gT + T]
    gtok = arena[:, A_gtok:A_gtok + 2 * NTB * 8].bitcast(F32)
    gtk2 = arena[:, A_gtk2:A_gtk2 + 2 * NTB * 8].bitcast(F32)
    gsc = arena[:, A_gsc:A_gsc + 2 * NTB].bitcast(F32)
    gsc2 = arena[:, A_gsc2:A_gsc2 + 2 * NTB].bitcast(F32)
    gtk3 = arena[:, A_gtk3:A_gtk3 + 2 * NTB * 8].bitcast(F32)
    agst = arena[:, A_ag:A_ag + 4 * 512]

    psA = es.enter_context(nc.psum_tensor("psA", [128, 2048], F32))
    psB = es.enter_context(nc.psum_tensor("psB", [128, 2048], F32))

    sems = {}
    for s in ALL_SEMS:
        sems[s] = es.enter_context(nc.semaphore("sem_" + s))
    block = es.enter_context(nc.Block())

    b_cst, b_cstf, b_sel, b_ones = Buf("cst"), Buf("cstf"), Buf("sel"), Buf("ones")
    b_onesf = Buf("onesf")
    b_bqk, b_bo, b_bv = Buf("bqk"), Buf("bo"), Buf("bv")
    b_wr, b_wd, b_wu = Buf("wr"), Buf("wd"), Buf("wu")
    b_XT = Buf("XT")
    b_wv = Buf("wv")
    b_wqk = [Buf("wqk%d" % i) for i in range(2)]
    b_kst = [Buf("kst%d" % i) for i in range(2)]
    b_vst = [Buf("vst%d" % i) for i in range(2)]
    b_psA = [Buf("psA%d" % i) for i in range(4)]
    b_psB = [Buf("psB%d" % i) for i in range(4)]
    e_kscr = [[] for _ in range(NH)]
    e_qscr = [[] for _ in range(NH)]
    e_vscr = [[] for _ in range(NH)]
    b_k2 = [Buf("k2_%d" % i) for i in range(2)]
    b_q2 = [Buf("q2_%d" % i) for i in range(2)]
    b_v2 = [Buf("v2_%d" % i) for i in range(2)]
    b_P2s = [[Buf("P2s%d_%d" % (i, hh)) for hh in range(2)] for i in range(2)]
    b_P2q = [[Buf("P2q%d_%d" % (i, hh)) for hh in range(2)] for i in range(2)]
    b_accQ = Buf("accQ")
    b_mnegs = Buf("mnegs")
    b_PT = Buf("PT")
    b_junk = Buf("junk")
    b_acc = [Buf("acc%d" % i) for i in range(2)]
    b_lcol = Buf("lcol")
    b_lirow = Buf("lirow")
    b_lrep = Buf("lrep")
    b_AO = [[Buf("AO_%d_%d" % (c, j)) for j in range(T // 512 if T >= 512 else 1)] for c in range(HC)]
    b_wo = [Buf("wo%d" % i) for i in range(2)]
    b_x = [Buf("x%d" % i) for i in range(4)]
    b_r1 = [[Buf("r1_%d_%d" % (o, j)) for j in range(JT)] for o in range(HC)]
    b_rb = [Buf("rb%d" % i) for i in range(2)]
    b_sq = [Buf("sq%d" % i) for i in range(2)]
    b_stat = [Buf("stat%d" % j) for j in range(4)]
    b_yst = [Buf("yst%d" % i) for i in range(2)]
    b_yst2 = [Buf("yst2_%d" % i) for i in range(2)]
    b_stat2 = [Buf("stat2_%d" % i) for i in range(2)]
    b_yscr = [[Buf("yscr_%d_%d" % (o, j)) for j in range(T // 512 if T >= 512 else 1)] for o in range(HC)]
    b_lgf, b_gtok, b_gtk2, b_gsc, b_gT = Buf("lgf"), Buf("gtok"), Buf("gtk2"), Buf("gsc"), Buf("gT")
    b_gsc2, b_gtk3 = Buf("gsc2"), Buf("gtk3")
    b_ag = [Buf("ag%d" % i) for i in range(4)]

    AOW = 512 if T >= 512 else T

    def aoslice(c, t0, w):
        return AO[:, c * T + t0:c * T + t0 + w]

    def aobufs(t0, w, cs=None):
        out = []
        for c in (range(HC) if cs is None else cs):
            for j in range(t0 // AOW, (t0 + w + AOW - 1) // AOW):
                out.append(b_AO[c][j])
        return out

    g = "gpsimd"
    K.op(g, lambda h: h.dma_start(cst[:, :], cst_h[:, :]), writes=[b_cst], dma=True)
    K.op(g, lambda h: h.dma_start(cstf[:, :], cstf_h[:, :]), writes=[b_cstf], dma=True)
    K.op(g, lambda h: h.dma_start(sel_sb[:, :], cst_h[0:8, 128:256]), writes=[b_sel], dma=True)
    K.op(g, lambda h: h.dma_start(bqk_sb[:, :], bqk_h[:, :]), writes=[b_bqk], dma=True)
    K.op(g, lambda h: h.dma_start(bo_sb[:, :], bo_h[:, :]), writes=[b_bo], dma=True)
    K.op(g, lambda h: h.dma_start(bv_sb[:, :], bv_h[:, :]), writes=[b_bv], dma=True)
    K.op(g, lambda h: h.dma_start(wr_sb[:, :], wr_h[:, :]), writes=[b_wr], dma=True)
    K.op(g, lambda h: h.dma_start(wd_sb[:, :], wd_h[:, :]), writes=[b_wd], dma=True)
    K.op(g, lambda h: h.dma_start(wu_sb[:, :], wu_h[:, :]), writes=[b_wu], dma=True)
    for c in range(HC):
        K.op(g, (lambda c: lambda h: h.dma_start(
            XT[:, c * T:(c + 1) * T], xT[c * 128:(c + 1) * 128, :]))(c),
            writes=[b_XT], dma=True)
    K.op("vector", lambda h: h.memset(ones_sb[:, :], 1.0), writes=[b_ones])
    K.op("vector", lambda h: h.memset(onesf_sb[:, :], 1.0), writes=[b_onesf])

    ident = cst[:, 0:128]
    ident8f = cstf[0:8, 0:8]
    identf = cstf[:, :]

    for dt in range(NVD):
        K.op(g, (lambda dt: lambda h: h.dma_start(
            wv_sb[:, :].rearrange("p (c d) -> p c d", d=VD),
            wv_h[dt, :, :].rearrange("(c p) d -> p c d", p=128)))(dt),
            writes=[b_wv], dma=True)
        for tb in range(TBo):
            slot = (dt * TBo + tb) % 4
            def mmg(h, dt=dt, tb=tb, slot=slot):
                for c in range(HC):
                    h.matmul(psA[:, slot * 512:slot * 512 + VD],
                             XT[:, c * T + tb * 128:c * T + (tb + 1) * 128],
                             wv_sb[:, c * VD:(c + 1) * VD],
                             start=(c == 0), stop=False)
                return h.matmul(psA[:, slot * 512:slot * 512 + VD],
                                ones_sb[0:1, 0:128], bv_sb[:, dt * VD:(dt + 1) * VD],
                                start=False, stop=True)
            K.op("tensor", mmg, reads=[b_XT, b_wv, b_ones, b_bv], writes=[b_psA[slot]])
            vs = (dt * TBo + tb) % 2
            K.op("vector", (lambda slot, vs: lambda h: h.tensor_copy(
                vst[:, vs * 512:vs * 512 + VD], psA[:, slot * 512:slot * 512 + VD]))(slot, vs),
                reads=[b_psA[slot]], writes=[b_vst[vs]])
            nh_t = VD // 128
            def spill(h, dt=dt, tb=tb, vs=vs, nh_t=nh_t):
                return h.dma_start(vag_i[tb, dt * nh_t:(dt + 1) * nh_t, :, :].transpose([1, 0, 2]),
                                   vst[:, vs * 512:vs * 512 + VD].rearrange("p (j d) -> p j d", d=128))
            ev = K.op("sync", spill, reads=[b_vst[vs]], dma=True)
            e_vscr[0].append(ev)

    def qk_proj(which):
        span = T
        tile_w = min(512, span)
        ntile = span // tile_w
        wsrc = wk_h if which == "k" else wq_h
        scr = kag_i if which == "k" else qscr
        escr = e_kscr if which == "k" else e_qscr
        bias0 = NH if which == "k" else 0
        for n in range(NH):
            slot = n % 2
            K.op(g, (lambda n, slot, wsrc=wsrc: lambda h: h.dma_start(
                arena[:, wqk0 + slot * HC * 128: wqk0 + (slot + 1) * HC * 128]
                .rearrange("p (c o) -> p c o", o=128),
                wsrc[n, :, :].rearrange("(c p) o -> p c o", p=128)))(n, slot),
                writes=[b_wqk[slot]], dma=True)
            for j in range(ntile):
                idx = n * ntile + j
                ps = idx % 4
                def mmg(h, n=n, j=j, slot=slot, ps=ps, tile_w=tile_w):
                    for c in range(HC):
                        mm = h.matmul(psB[:, ps * 512:ps * 512 + tile_w],
                                      arena[:, wqk0 + slot * HC * 128 + c * 128:
                                            wqk0 + slot * HC * 128 + (c + 1) * 128],
                                      XT[:, c * T + j * tile_w:c * T + (j + 1) * tile_w],
                                      start=(c == 0), stop=(c == HC - 1))
                    return mm
                K.op("tensor", mmg, reads=[b_XT, b_wqk[slot]], writes=[b_psB[ps]])
                ks = idx % 2
                K.op("scalar", (lambda ps, ks, n, tile_w: lambda h: h.activation(
                    kst[:, ks * 512:ks * 512 + tile_w], psB[:, ps * 512:ps * 512 + tile_w],
                    AF.Identity, bias=bqk_sb[:, bias0 + n:bias0 + n + 1], scale=1.0))(ps, ks, n, tile_w),
                    reads=[b_psB[ps], b_bqk], writes=[b_kst[ks]])
                ev = K.op("sync", (lambda n, j, ks, tile_w, scr=scr: lambda h: h.dma_start(
                    scr[n, :, j * tile_w:(j + 1) * tile_w],
                    kst[:, ks * 512:ks * 512 + tile_w]))(n, j, ks, tile_w),
                    reads=[b_kst[ks]], dma=True)
                escr[n].append(ev)
    qk_proj("k")
    qk_proj("q")

    pair_groups = [[i, i + 1] for i in range(0, cfg.n_cores, 2)]
    e_kag = K.op(g, (lambda: lambda h: h.collective_compute(
        "AllGather", AluOpType.bypass, replica_groups=pair_groups,
        ins=[kag_i.ap().opt()], outs=[kag_o.ap().opt()]))(),
        waits=[e for n in range(NH) for e in e_kscr[n]])
    e_vag = K.op(g, (lambda: lambda h: h.collective_compute(
        "AllGather", AluOpType.bypass, replica_groups=pair_groups,
        ins=[vag_i.ap().opt()], outs=[vag_o.ap().opt()]))(),
        waits=e_vscr[0])

    K.barrier()
    nSA = (SH + 511) // 512
    NBA = max(1, KC // 4)
    psA_banks = [b_psA[i] for i in range(min(4, nSA))]
    psB_banks = [b_psB[i] for i in range(min(4, nSA))]
    NQS = T // 128
    import functools

    nstr = max(1, KC // 4)
    NV = NH * NQS
    SSW = (nstr + 2) * 128
    for n in range(NH):
        sl = n % 2
        K.op("sync", (lambda n, sl: lambda h: h.dma_start(
            arena[:, q2o + sl * T:q2o + (sl + 1) * T], qscr[n, :, :]))(n, sl),
            waits=e_qscr[n], writes=[b_q2[sl]], dma=True)
        for half in range(2):
            K.op("sync", (lambda n, sl, half: lambda h: h.dma_start(
                arena[:, k2o + sl * S + half * T:k2o + sl * S + (half + 1) * T],
                kag_o[half, n, :, :]))(n, sl, half),
                waits=[e_kag], writes=[b_k2[sl]], dma=True)
        for qi in range(NQS):
            pp = qi % 2
            psS = psA if pp == 0 else psB
            bps = b_psA if pp == 0 else b_psB
            hn = max(1, nstr // 2)
            def smp(h, n=n, qi=qi, sl=sl, psS=psS, hn=hn):
                k2v = arena[:, k2o + sl * S:k2o + (sl + 1) * S]
                qv = arena[:, q2o + sl * T + qi * 128:q2o + sl * T + (qi + 1) * 128]
                kstr = k2v.rearrange("p (a b) -> p a b", b=512)[:, :, 0:128]
                h.matmul(psS[:, 0:hn * 128], qv, kstr[:, 0:hn, :], start=True, stop=True)
                if nstr > hn:
                    h.matmul(psS[:, 512:512 + (nstr - hn) * 128], qv, kstr[:, hn:nstr, :],
                             start=True, stop=True)
                return h.matmul(psS[:, 1024:1024 + 256],
                                qv, k2v.rearrange("p (h b) -> p h b", h=2)
                                [:, :, qi * 128:(qi + 1) * 128],
                                start=True, stop=True)
            K.op("tensor", smp, reads=[b_k2[sl], b_q2[sl]], writes=[bps[0], bps[1], bps[2]])
            c0 = n * NQS + qi
            if nstr == 8:
                K.op("scalar", (lambda c0, psS: lambda h: h.activation(
                    junk[:, 0:1280], psS[:, 0:1280], AF.Exp, scale=SC / 4.0,
                    accum_out=accQ[:, c0:c0 + 1]))(c0, psS),
                    reads=[bps[0], bps[1], bps[2]], writes=[b_junk, b_accQ])
            else:
                K.op("scalar", (lambda c0, psS, hn: lambda h: h.activation(
                    junk[:, 0:hn * 128], psS[:, 0:hn * 128], AF.Exp, scale=SC / 4.0,
                    accum_out=accQ[:, NV + c0:NV + c0 + 1]))(c0, psS, hn),
                    reads=[bps[0], bps[1], bps[2]], writes=[b_junk, b_accQ])
                K.op("scalar", (lambda c0, psS: lambda h: h.activation(
                    junk[:, 0:256], psS[:, 1024:1280], AF.Exp, scale=SC / 4.0,
                    accum_out=accQ[:, 2 * NV + c0:2 * NV + c0 + 1]))(c0, psS),
                    reads=[bps[0], bps[1], bps[2]], writes=[b_junk, b_accQ])
                K.op("vector", (lambda c0: lambda h: h.tensor_tensor(
                    accQ[:, c0:c0 + 1], accQ[:, NV + c0:NV + c0 + 1],
                    accQ[:, 2 * NV + c0:2 * NV + c0 + 1], op=AluOpType.add))(c0),
                    reads=[b_accQ], writes=[b_accQ])
        K.op("vector", (lambda n: lambda h: h.tensor_scalar(
            accQI[:, NH * NQS + n * NQS:NH * NQS + (n + 1) * NQS],
            accQI[:, n * NQS:(n + 1) * NQS], 23, None,
            op0=AluOpType.arith_shift_right))(n),
            reads=[b_accQ], writes=[b_accQ])
        K.op("vector", (lambda n: lambda h: h.tensor_copy(
            accQ[:, 2 * NH * NQS + n * NQS:2 * NH * NQS + (n + 1) * NQS],
            accQI[:, NH * NQS + n * NQS:NH * NQS + (n + 1) * NQS]))(n),
            reads=[b_accQ], writes=[b_accQ])
        K.op("vector", (lambda n: lambda h: h.tensor_scalar(
            mnegs[:, n * NQS:(n + 1) * NQS],
            accQ[:, 2 * NH * NQS + n * NQS:2 * NH * NQS + (n + 1) * NQS],
            -127.0, -2.0 * 0.6931471805599453,
            op0=AluOpType.add, op1=AluOpType.mult))(n),
            reads=[b_accQ], writes=[b_mnegs])

    pend = []

    def flush_pending():
        for f in pend:
            f()
        pend.clear()

    for n in range(NH):
        sl = n % 2
        K.op("sync", (lambda n, sl: lambda h: h.dma_start(
            arena[:, q2o + sl * T:q2o + (sl + 1) * T], qscr[n, :, :]))(n, sl),
            waits=e_qscr[n], writes=[b_q2[sl]], dma=True)
        for half in range(2):
            K.op("sync", (lambda n, sl, half: lambda h: h.dma_start(
                arena[:, k2o + sl * S + half * T:k2o + sl * S + (half + 1) * T],
                kag_o[half, n, :, :]))(n, sl, half),
                waits=[e_kag], writes=[b_k2[sl]], dma=True)
            K.op("sync", (lambda n, sl, half: lambda h: h.dma_start(
                arena[:, v2o + sl * S + half * T:v2o + sl * S + (half + 1) * T]
                .rearrange("p (tb d) -> p tb d", d=128),
                vag_o[half, :, n, :, :].transpose([1, 0, 2])))(n, sl, half),
                waits=[e_vag], writes=[b_v2[sl]], dma=True)
        for qg in range(NGQ):
            for qs in range(GQ):
                qi = qg * GQ + qs
                p = qi % 2
                def smm(h, n=n, qi=qi, sl=sl, half=0):
                    for m in range(nSA):
                        w = min(512, SH - m * 512)
                        ps = psA if half == 0 else psB
                        mm = h.matmul(ps[:, m * 512:m * 512 + w],
                                      arena[:, q2o + sl * T + qi * 128:q2o + sl * T + (qi + 1) * 128],
                                      arena[:, k2o + sl * S + half * SH + m * 512:
                                            k2o + sl * S + half * SH + m * 512 + w],
                                      start=True, stop=True)
                    return mm
                K.op("tensor", functools.partial(smm, half=0),
                     reads=[b_k2[sl], b_q2[sl]], writes=psA_banks)
                K.op("tensor", functools.partial(smm, half=1),
                     reads=[b_k2[sl], b_q2[sl]], writes=psB_banks)
                K.op("scalar", (lambda n, qi, p: lambda h: h.activation(
                    P2s[:, p * S:p * S + SH], psA[:, 0:SH], AF.Exp, scale=SC / 2.0,
                    bias=mnegs[:, n * NQS + qi:n * NQS + qi + 1]))(n, qi, p),
                    reads=psA_banks + [b_mnegs], writes=[b_P2s[p][0]])
                K.op("vector", (lambda p: lambda h: h.scalar_tensor_tensor(
                    P2q[:, p * S:p * S + SH], P2s[:, p * S:p * S + SH], 1.0,
                    P2s[:, p * S:p * S + SH], op0=AluOpType.mult, op1=AluOpType.mult,
                    accum_out=lbuf[:, p * 2:p * 2 + 1]))(p),
                    reads=[b_P2s[p][0]], writes=[b_P2q[p][0], b_lcol])
                K.op("scalar", (lambda n, qi, p: lambda h: h.activation(
                    P2s[:, p * S + SH:p * S + 2 * SH], psB[:, 0:SH], AF.Exp, scale=SC / 2.0,
                    bias=mnegs[:, n * NQS + qi:n * NQS + qi + 1]))(n, qi, p),
                    reads=psB_banks + [b_mnegs], writes=[b_P2s[p][1]])
                K.op("vector", (lambda p: lambda h: h.scalar_tensor_tensor(
                    P2q[:, p * S + SH:p * S + 2 * SH], P2s[:, p * S + SH:p * S + 2 * SH], 1.0,
                    P2s[:, p * S + SH:p * S + 2 * SH], op0=AluOpType.mult, op1=AluOpType.mult,
                    accum_out=lbuf[:, p * 2 + 1:p * 2 + 2]))(p),
                    reads=[b_P2s[p][1]], writes=[b_P2q[p][1], b_lcol])
                K.op("vector", (lambda p, qg, qs: lambda h: h.tensor_tensor(
                    lbuf[:, 4 + (qg % 2) * 4 + qs:4 + (qg % 2) * 4 + qs + 1],
                    lbuf[:, p * 2:p * 2 + 1], lbuf[:, p * 2 + 1:p * 2 + 2],
                    op=AluOpType.add))(p, qg, qs),
                    reads=[b_lcol], writes=[b_lcol])
                flush_pending()
                K.op("sync", (lambda p, qs: lambda h: h.dma_start_transpose(
                    PTsl[:, :].rearrange("p (j q) -> p j q", q=QW)
                    [:, :, qs * 128:(qs + 1) * 128],
                    P2q[:, p * S:(p + 1) * S]))(p, qs),
                    reads=[b_P2q[p][0], b_P2q[p][1]], writes=[b_PT],
                    dma=True, dma_sem="xd")
                if qs == GQ - 1:
                    def def_G(n=n, sl=sl, qg=qg):
                        def pv(h, sl=sl):
                            for kt in range(KC):
                                mm = h.matmul(psA[:, 1024:1024 + QW],
                                              arena[:, v2o + sl * S + kt * 128:v2o + sl * S + (kt + 1) * 128],
                                              PTsl[:, kt * QW:(kt + 1) * QW],
                                              start=(kt == 0), stop=(kt == KC - 1))
                            return mm
                        K.op("tensor", pv, reads=[b_v2[sl], b_PT], writes=[b_psA[2]])
                        def lmm(h):
                            for kt in range(KC):
                                mm = h.matmul(psA[:, 1536:1536 + QW], ones_sb[:, 0:128],
                                              PTsl[:, kt * QW:(kt + 1) * QW],
                                              start=(kt == 0), stop=(kt == KC - 1))
                            return mm
                        K.op("tensor", lmm, reads=[b_ones, b_PT], writes=[b_psA[3]])
                        K.op("vector", lambda h: h.reciprocal(lrep_sb[:, 0:QW], psA[:, 1536:1536 + QW]),
                             reads=[b_psA[3]], writes=[b_lrep])
                        K.op("vector", (lambda n, qg: lambda h: h.tensor_tensor(
                            aoslice(n, qg * QW, QW), psA[:, 1024:1024 + QW], lrep_sb[:, 0:QW],
                            op=AluOpType.mult))(n, qg),
                            reads=[b_psA[2], b_lrep], writes=aobufs(qg * QW, QW, [n]))
                    pend.append(def_G)
    flush_pending()

    K.barrier()
    for tw in range(NTW):
        t0 = tw * TW
        for o in range(HC):
            i2 = tw * HC + o
            ws = i2 % 2
            K.op(g, (lambda o, ws: lambda h: h.dma_start(
                arena[:, wo0 + ws * HC * 128:wo0 + (ws + 1) * HC * 128]
                .rearrange("p (c q) -> p c q", q=128),
                wo_h[o, :, :].rearrange("(c p) q -> p c q", p=128)))(o, ws),
                writes=[b_wo[ws]], dma=True)
            for j in range(JT):
                gi = i2 * JT + j
                ps = gi % 2
                xs = gi % 4
                K.op("sync", (lambda o, j, t0, xs: lambda h: h.dma_start(
                    x_res[:, xs * 512:xs * 512 + JW],
                    xT[o * 128:(o + 1) * 128, t0 + j * JW:t0 + (j + 1) * JW]))(o, j, t0, xs),
                    writes=[b_x[xs]], dma=True)
                def pmm(h, o=o, j=j, ws=ws, ps=ps, t0=t0):
                    for c in range(HC):
                        mm = h.matmul(psA[:, ps * 512:ps * 512 + JW],
                                      arena[:, wo0 + ws * HC * 128 + c * 128:
                                            wo0 + ws * HC * 128 + (c + 1) * 128],
                                      aoslice(c, t0 + j * JW, JW),
                                      start=(c == 0), stop=(c == HC - 1))
                    return mm
                K.op("tensor", pmm, reads=[b_wo[ws]] + aobufs(t0 + j * JW, JW),
                     writes=[b_psA[ps]])
                K.op("vector", (lambda o, j, ps, xs: lambda h: h.scalar_tensor_tensor(
                    r1v[:, o * TW + j * JW:o * TW + (j + 1) * JW],
                    psA[:, ps * 512:ps * 512 + JW], bo_sb[:, o:o + 1],
                    x_res[:, xs * 512:xs * 512 + JW],
                    op0=AluOpType.add, op1=AluOpType.add))(o, j, ps, xs),
                    reads=[b_psA[ps], b_bo, b_x[xs]], writes=[b_r1[o][j]])
                rs_ = gi % 2
                K.op("vector", (lambda o, j, rs_: lambda h: h.tensor_copy(
                    rbst[:, rs_ * 512:rs_ * 512 + JW],
                    r1v[:, o * TW + j * JW:o * TW + (j + 1) * JW]))(o, j, rs_),
                    reads=[b_r1[o][j]], writes=[b_rb[rs_]])
                K.op("vector", (lambda rs_: lambda h: h.tensor_tensor(
                    sqst[:, rs_ * 512:rs_ * 512 + JW], rbst[:, rs_ * 512:rs_ * 512 + JW],
                    rbst[:, rs_ * 512:rs_ * 512 + JW], op=AluOpType.mult))(rs_),
                    reads=[b_rb[rs_]], writes=[b_sq[rs_]])
                def stmm(h, o=o, j=j, rs_=rs_):
                    h.matmul(psB[:, j * 512:j * 512 + JW], ones_sb[:, 0:128],
                             rbst[:, rs_ * 512:rs_ * 512 + JW],
                             start=(o == 0), stop=(o == HC - 1))
                    return h.matmul(psB[:, 1024 + j * 512:1024 + j * 512 + JW],
                                    ones_sb[:, 0:128], sqst[:, rs_ * 512:rs_ * 512 + JW],
                                    start=(o == 0), stop=(o == HC - 1))
                K.op("tensor", stmm, reads=[b_rb[rs_], b_sq[rs_], b_ones],
                     writes=[b_psB[j], b_psB[2 + j]])
        if debug:
            K.op("sync", (lambda tw: lambda h: h.dma_start(
                r1dump[tw, :, :], r1v[:, 0:HC * TW]))(tw),
                reads=[b for row in b_r1 for b in row], dma=True)
        for j in range(JT):
            mu = stat[:, 0 * RSTR + j * JW:0 * RSTR + j * JW + JW]
            rs = stat[:, 1 * RSTR + j * JW:1 * RSTR + j * JW + JW]
            mus = stat[:, 2 * RSTR + j * JW:2 * RSTR + j * JW + JW]
            scr2 = stat[:, 3 * RSTR + j * JW:3 * RSTR + j * JW + JW]
            bmu, brs, bmus, bscr = b_stat[0], b_stat[1], b_stat[2], b_stat[3]
            K.op("vector", (lambda mu, j: lambda h: h.tensor_scalar(
                mu, psB[:, j * 512:j * 512 + JW], IH, None, op0=AluOpType.mult))(mu, j),
                reads=[b_psB[j]], writes=[bmu])
            K.op("vector", (lambda scr2, j: lambda h: h.tensor_scalar(
                scr2, psB[:, 1024 + j * 512:1024 + j * 512 + JW], IH, cfg.EPS,
                op0=AluOpType.mult, op1=AluOpType.add))(scr2, j),
                reads=[b_psB[2 + j]], writes=[bscr])
            var2 = stat[:, 4 * RSTR + j * JW:4 * RSTR + j * JW + JW]
            rin2 = stat[:, 5 * RSTR + j * JW:5 * RSTR + j * JW + JW]
            bvar, brin = b_stat2[0], b_stat2[1]
            K.op("vector", (lambda rs, mu: lambda h: h.tensor_tensor(
                rs, mu, mu, op=AluOpType.mult))(rs, mu), reads=[bmu], writes=[brs])
            K.op("vector", (lambda var2, scr2, rs: lambda h: h.tensor_tensor(
                var2, scr2, rs, op=AluOpType.subtract))(var2, scr2, rs),
                reads=[brs, bscr], writes=[bvar])
            K.op("vector", (lambda rin2, var2: lambda h: h.reciprocal(rin2, var2))(rin2, var2),
                reads=[bvar], writes=[brin])
            K.op("scalar", (lambda rs, rin2: lambda h: h.activation(rs, rin2, AF.Sqrt))(rs, rin2),
                reads=[brin], writes=[brs])
            K.op("vector", (lambda mus, mu, rs: lambda h: h.tensor_tensor(
                mus, mu, rs, op=AluOpType.mult))(mus, mu, rs), reads=[bmu, brs], writes=[bmus])
            if debug and j == JT - 1:
                K.op("sync", (lambda tw: lambda h: h.dma_start(
                    stdump[tw, :, :], stat[:, 0:6 * RSTR]))(tw),
                    reads=[b_stat[0], b_stat[1], b_stat[2], b_stat[3],
                           b_stat2[0], b_stat2[1]], dma=True)
            for o in range(HC):
                ys = (o * JT + j) % 2
                K.op("vector", (lambda o, j, ys, rs: lambda h: h.tensor_tensor(
                    ystage[:, ys * 512:ys * 512 + JW],
                    r1v[:, o * TW + j * JW:o * TW + (j + 1) * JW], rs,
                    op=AluOpType.mult))(o, j, ys, rs),
                    reads=[b_r1[o][j], brs], writes=[b_yst[ys]])
                K.op("vector", (lambda ys, mus: lambda h: h.tensor_tensor(
                    ystage[:, (2 + ys) * 512:(2 + ys) * 512 + JW],
                    ystage[:, ys * 512:ys * 512 + JW],
                    mus, op=AluOpType.subtract))(ys, mus),
                    reads=[bmus, b_yst[ys]], writes=[b_yst2[ys]])
                K.op("scalar", (lambda o, j, ys, t0: lambda h: h.activation(
                    aoslice(o, t0 + j * JW, JW), ystage[:, (2 + ys) * 512:(2 + ys) * 512 + JW],
                    AF.Copy))(o, j, ys, t0),
                    reads=[b_yst2[ys]], writes=aobufs(t0 + j * JW, JW, [o]))
                ev = K.op("sync", (lambda o, j, ys, t0: lambda h: h.dma_start(
                    yscr[o * 128:(o + 1) * 128, t0 + j * JW:t0 + (j + 1) * JW],
                    ystage[:, (2 + ys) * 512:(2 + ys) * 512 + JW]))(o, j, ys, t0),
                    reads=[b_yst2[ys]], dma=True)
                b_yscr[o][(t0 + j * JW) // (512 if T >= 512 else T)].w = ev

    K.barrier()
    for jt in range(T // QT):
        def rmm(h, jt=jt):
            for c in range(HC):
                mm = h.matmul(psA[0:8, jt * 512:jt * 512 + QT],
                              wr_sb[:, c * 8:(c + 1) * 8],
                              aoslice(c, jt * QT, QT),
                              start=(c == 0), stop=(c == HC - 1))
            return mm
        K.op("tensor", rmm, reads=[b_wr] + aobufs(jt * QT, QT), writes=[b_psA[jt % 4]])
        K.op("vector", (lambda jt: lambda h: h.tensor_copy(
            lgf[:, jt * QT:(jt + 1) * QT], psA[0:8, jt * 512:jt * 512 + QT]))(jt),
            reads=[b_psA[jt % 4]], writes=[b_lgf])
    for tb in range(NTB):
        K.op("tensor", (lambda tb: lambda h: h.transpose(
            psA[:, (tb % 4) * 8:(tb % 4) * 8 + 8],
            lgf[0:8, tb * 128:(tb + 1) * 128], ident8f))(tb),
            reads=[b_lgf, b_cstf], writes=[b_psA[0]])
        if tb % 4 == 3 or tb == NTB - 1:
            lo = (tb // 4) * 4
            cnt = tb - lo + 1
            K.op("vector", (lambda lo, cnt: lambda h: h.tensor_copy(
                gtok[:, lo * 8:(lo + cnt) * 8], psA[:, 0:cnt * 8]))(lo, cnt),
                reads=[b_psA[0]], writes=[b_gtok])
    seg = gtok[:, :].rearrange("p (s e) -> p s e", e=8)
    seg2 = gtk2[:, :].rearrange("p (s e) -> p s e", e=8)
    seg3 = gtk3[:, :].rearrange("p (s e) -> p s e", e=8)
    gb = lambda: gsc[:, :, None].broadcast_to((128, NTB, 8))
    gb2 = lambda: gsc2[:, :, None].broadcast_to((128, NTB, 8))
    K.op("vector", lambda h: h.tensor_reduce(gsc[:, :, None], seg, axis=mybir.AxisListType.X,
                                             op=AluOpType.max), reads=[b_gtok], writes=[b_gsc])
    K.op("vector", lambda h: h.tensor_tensor(seg2, seg, gb(), op=AluOpType.subtract),
         reads=[b_gtok, b_gsc], writes=[b_gtk2])
    K.op("scalar", lambda h: h.activation(gtok[:, :], gtk2[:, :], AF.Exp),
         reads=[b_gtk2], writes=[b_gtok])
    K.op("vector", lambda h: h.tensor_reduce(gsc[:, :, None], seg, axis=mybir.AxisListType.X,
                                             op=AluOpType.add), reads=[b_gtok], writes=[b_gsc])
    K.op("vector", lambda h: h.reciprocal(gsc2[:, :], gsc[:, :]), reads=[b_gsc], writes=[b_gsc2])
    K.op("vector", lambda h: h.tensor_tensor(seg2, seg, gb2(), op=AluOpType.mult),
         reads=[b_gsc2, b_gtok], writes=[b_gtk2])
    K.op("vector", lambda h: h.tensor_reduce(gsc[:, :, None], seg2, axis=mybir.AxisListType.X,
                                             op=AluOpType.max), reads=[b_gtk2], writes=[b_gsc])
    K.op("vector", lambda h: h.tensor_tensor(seg, seg2, gb(), op=AluOpType.is_ge),
         reads=[b_gtk2, b_gsc], writes=[b_gtok])
    K.op("vector", lambda h: h.scalar_tensor_tensor(gtk3[:, :], gtok[:, :], -2.0, gtk2[:, :],
                                                    op0=AluOpType.mult, op1=AluOpType.add),
         reads=[b_gtk2, b_gtok], writes=[b_gtk3])
    K.op("vector", lambda h: h.tensor_reduce(gsc[:, :, None], seg3, axis=mybir.AxisListType.X,
                                             op=AluOpType.max), reads=[b_gtk3], writes=[b_gsc])
    K.op("vector", lambda h: h.tensor_tensor(seg, seg2, gb(), op=AluOpType.is_ge),
         reads=[b_gtk2, b_gsc], writes=[b_gtok])
    K.op("vector", lambda h: h.tensor_tensor(seg3, seg, seg2, op=AluOpType.mult),
         reads=[b_gtk2, b_gtok], writes=[b_gtk3])
    for tb in range(NTB):
        K.op("tensor", (lambda tb: lambda h: h.transpose(
            psA[0:8, 512 + (tb % 4) * 128:512 + (tb % 4) * 128 + 128],
            gtk3[:, tb * 8:(tb + 1) * 8], identf))(tb),
            reads=[b_gtk3, b_cstf], writes=[b_psA[1]])
        K.op("vector", (lambda tb: lambda h: h.tensor_copy(
            gT_sb[:, tb * 128:(tb + 1) * 128],
            psA[0:8, 512 + (tb % 4) * 128:512 + (tb % 4) * 128 + 128]))(tb),
            reads=[b_psA[1]], writes=[b_gT])

    for tw in range(NTW):
        t0 = tw * TW
        for j in range(JT):
            jj = tw * JT + j
            K.op("tensor", (lambda t0, j: lambda h: h.matmul(
                psB[:, 1536:1536 + JW], sel_sb[:, :], gT_sb[:, t0 + j * JW:t0 + (j + 1) * JW],
                start=True, stop=True))(t0, j),
                reads=[b_sel, b_gT], writes=[b_psB[3]])
            def dmm(h, t0=t0, j=j):
                for c in range(HC):
                    mm = h.matmul(psB[:, 1024:1024 + JW], wd_sb[:, c * EI:(c + 1) * EI][:, 0:128],
                                  aoslice(c, t0 + j * JW, JW), start=(c == 0), stop=(c == HC - 1))
                return mm
            K.op("tensor", dmm, reads=[b_wd] + aobufs(t0 + j * JW, JW), writes=[b_psB[2]])
            sb_ = jj % 2
            K.op("scalar", (lambda sb_: lambda h: h.activation(
                agst[:, sb_ * 512:sb_ * 512 + JW], psB[:, 1024:1024 + JW], AF.Silu))(sb_),
                reads=[b_psB[2]], writes=[b_ag[sb_]])
            K.op("vector", (lambda sb_: lambda h: h.tensor_tensor(
                agst[:, 1024 + sb_ * 512:1024 + sb_ * 512 + JW],
                agst[:, sb_ * 512:sb_ * 512 + JW], psB[:, 1536:1536 + JW],
                op=AluOpType.mult))(sb_),
                reads=[b_ag[sb_], b_psB[3]], writes=[b_ag[2 + sb_]])
            for o in range(HC):
                oi = o * JT + j
                ps = oi % 2
                xs = (tw * HC * JT + o * JT + j) % 2
                K.op("sync", (lambda o, j, t0, xs: lambda h: h.dma_start(
                    x_res[:, (2 + xs) * 512:(2 + xs) * 512 + JW],
                    yscr[o * 128:(o + 1) * 128, t0 + j * JW:t0 + (j + 1) * JW]))(o, j, t0, xs),
                    reads=[b_yscr[o][(t0 + j * JW) // (512 if T >= 512 else T)]],
                    writes=[b_x[2 + xs]], dma=True)
                K.op("tensor", (lambda o, sb_, ps: lambda h: h.matmul(
                    psB[:, ps * 512:ps * 512 + JW], wu_sb[:, o * 128:(o + 1) * 128],
                    agst[:, 1024 + sb_ * 512:1024 + sb_ * 512 + JW],
                    start=True, stop=True))(o, sb_, ps),
                    reads=[b_wu, b_ag[2 + sb_]], writes=[b_psB[ps]])
                K.op("vector", (lambda o, j, ps, xs: lambda h: h.tensor_tensor(
                    r1v[:, o * TW + j * JW:o * TW + (j + 1) * JW],
                    psB[:, ps * 512:ps * 512 + JW], x_res[:, (2 + xs) * 512:(2 + xs) * 512 + JW],
                    op=AluOpType.add))(o, j, ps, xs),
                    reads=[b_psB[ps], b_x[2 + xs]], writes=[b_r1[o][j]])
                rs_ = oi % 2
                K.op("vector", (lambda o, j, rs_: lambda h: h.tensor_copy(
                    rbst[:, rs_ * 512:rs_ * 512 + JW],
                    r1v[:, o * TW + j * JW:o * TW + (j + 1) * JW]))(o, j, rs_),
                    reads=[b_r1[o][j]], writes=[b_rb[rs_]])
                K.op("vector", (lambda rs_: lambda h: h.tensor_tensor(
                    sqst[:, rs_ * 512:rs_ * 512 + JW], rbst[:, rs_ * 512:rs_ * 512 + JW],
                    rbst[:, rs_ * 512:rs_ * 512 + JW], op=AluOpType.mult))(rs_),
                    reads=[b_rb[rs_]], writes=[b_sq[rs_]])
                def stmm(h, o=o, j=j, rs_=rs_):
                    h.matmul(psA[:, j * 512:j * 512 + JW], ones_sb[:, 0:128],
                             rbst[:, rs_ * 512:rs_ * 512 + JW],
                             start=(o == 0), stop=(o == HC - 1))
                    return h.matmul(psA[:, 1024 + j * 512:1024 + j * 512 + JW],
                                    ones_sb[:, 0:128], sqst[:, rs_ * 512:rs_ * 512 + JW],
                                    start=(o == 0), stop=(o == HC - 1))
                K.op("tensor", stmm, reads=[b_rb[rs_], b_sq[rs_], b_ones],
                     writes=[b_psA[j], b_psA[2 + j]])
        for j in range(JT):
            mu = stat[:, 0 * RSTR + j * JW:0 * RSTR + j * JW + JW]
            rs = stat[:, 1 * RSTR + j * JW:1 * RSTR + j * JW + JW]
            mus = stat[:, 2 * RSTR + j * JW:2 * RSTR + j * JW + JW]
            scr2 = stat[:, 3 * RSTR + j * JW:3 * RSTR + j * JW + JW]
            bmu, brs, bmus, bscr = b_stat[0], b_stat[1], b_stat[2], b_stat[3]
            K.op("vector", (lambda mu, j: lambda h: h.tensor_scalar(
                mu, psA[:, j * 512:j * 512 + JW], IH, None, op0=AluOpType.mult))(mu, j),
                reads=[b_psA[j]], writes=[bmu])
            K.op("vector", (lambda scr2, j: lambda h: h.tensor_scalar(
                scr2, psA[:, 1024 + j * 512:1024 + j * 512 + JW], IH, cfg.EPS,
                op0=AluOpType.mult, op1=AluOpType.add))(scr2, j),
                reads=[b_psA[2 + j]], writes=[bscr])
            var2 = stat[:, 4 * RSTR + j * JW:4 * RSTR + j * JW + JW]
            rin2 = stat[:, 5 * RSTR + j * JW:5 * RSTR + j * JW + JW]
            bvar, brin = b_stat2[0], b_stat2[1]
            K.op("vector", (lambda rs, mu: lambda h: h.tensor_tensor(
                rs, mu, mu, op=AluOpType.mult))(rs, mu), reads=[bmu], writes=[brs])
            K.op("vector", (lambda var2, scr2, rs: lambda h: h.tensor_tensor(
                var2, scr2, rs, op=AluOpType.subtract))(var2, scr2, rs),
                reads=[brs, bscr], writes=[bvar])
            K.op("vector", (lambda rin2, var2: lambda h: h.reciprocal(rin2, var2))(rin2, var2),
                reads=[bvar], writes=[brin])
            K.op("scalar", (lambda rs, rin2: lambda h: h.activation(rs, rin2, AF.Sqrt))(rs, rin2),
                reads=[brin], writes=[brs])
            K.op("vector", (lambda mus, mu, rs: lambda h: h.tensor_tensor(
                mus, mu, rs, op=AluOpType.mult))(mus, mu, rs), reads=[bmu, brs], writes=[bmus])
            for o in range(HC):
                ys = (o * JT + j) % 2
                K.op("vector", (lambda o, j, ys, rs: lambda h: h.tensor_tensor(
                    ystage[:, ys * 512:ys * 512 + JW],
                    r1v[:, o * TW + j * JW:o * TW + (j + 1) * JW], rs,
                    op=AluOpType.mult))(o, j, ys, rs),
                    reads=[b_r1[o][j], brs], writes=[b_yst[ys]])
                K.op("vector", (lambda ys, mus: lambda h: h.tensor_tensor(
                    ystage[:, (2 + ys) * 512:(2 + ys) * 512 + JW],
                    ystage[:, ys * 512:ys * 512 + JW],
                    mus, op=AluOpType.subtract))(ys, mus),
                    reads=[bmus, b_yst[ys]], writes=[b_yst2[ys]])
                evo = K.op("sync", (lambda o, j, ys, t0: lambda h: h.dma_start(
                    outT[o * 128:(o + 1) * 128, t0 + j * JW:t0 + (j + 1) * JW],
                    ystage[:, (2 + ys) * 512:(2 + ys) * 512 + JW]))(o, j, ys, t0),
                    reads=[b_yst2[ys]], dma=True)
    K.op("sync", lambda h: None, waits=[evo])
    K.mark(evo)

    K.finalize(nc, block, sems)
    es.close()
    return nc


def prep_inputs(cfg, x_bh, w_qkv, b_qkv, w_out, b_out, router_w, down_w, up_w, own_first=True):
    H, NH, HC, S, VD, NVD, EI = cfg.H, cfg.NH, cfg.HC, cfg.S, cfg.VD, cfg.NVD, cfg.EI
    f32 = np.float32
    d = {}
    d["xT"] = np.ascontiguousarray(x_bh.T.astype(f32))
    Wq, Wk, Wv = w_qkv[0:H], w_qkv[H:2 * H], w_qkv[2 * H:3 * H]
    d["wq"] = np.ascontiguousarray(
        Wq.reshape(NH, 128, H).transpose(0, 2, 1))
    d["wk"] = np.ascontiguousarray(Wk.reshape(NH, 128, H).transpose(0, 2, 1))
    d["wv"] = np.ascontiguousarray(
        Wv.T.reshape(H, NVD, VD).transpose(1, 0, 2))
    d["wo"] = np.ascontiguousarray(w_out.reshape(HC, 128, H).transpose(0, 2, 1))
    d["bqk"] = np.ascontiguousarray(
        np.concatenate([b_qkv[0:H].reshape(NH, 128).T, b_qkv[H:2 * H].reshape(NH, 128).T], axis=1)).astype(f32)
    d["bo"] = np.ascontiguousarray(b_out.reshape(HC, 128).T).astype(f32)
    d["bv"] = b_qkv[2 * H:3 * H].reshape(1, H).astype(f32)
    d["wr"] = np.ascontiguousarray(
        router_w.T.reshape(HC, 128, 8).transpose(1, 0, 2).reshape(128, HC * 8)).astype(f32)
    Dcat = down_w.reshape(EI, H)
    d["wd"] = np.ascontiguousarray(
        Dcat.T.reshape(HC, 128, EI).transpose(1, 0, 2).reshape(128, HC * EI)).astype(f32)
    Ucat = up_w.transpose(1, 0, 2).reshape(H, EI)
    d["wu"] = np.ascontiguousarray(Ucat.T).astype(f32)
    cst = np.zeros((128, 256), dtype=BF)
    cst[:, 0:128] = np.eye(128, dtype=BF)
    sel = np.zeros((8, 128), dtype=BF)
    for e in range(8):
        sel[e, e * 16:(e + 1) * 16] = 1
    cst[0:8, 128:256] = sel
    d["cst"] = cst
    d["cstf"] = np.eye(128, dtype=f32)
    return d


from concourse.bass_utils import run_bass_kernel_spmd


def kernel(L0_attn_hidden_states, L0_attn_w_qkv, L0_attn_b_qkv, L0_attn_w_out,
           L0_attn_b_out, L0_ffn_router_w, L0_ffn_down_w, L0_ffn_up_w):
    x = np.asarray(L0_attn_hidden_states)
    w_qkv = np.asarray(L0_attn_w_qkv)
    b_qkv = np.asarray(L0_attn_b_qkv)
    w_out = np.asarray(L0_attn_w_out)
    b_out = np.asarray(L0_attn_b_out)
    router_w = np.asarray(L0_ffn_router_w)
    down_w = np.asarray(L0_ffn_down_w)
    up_w = np.asarray(L0_ffn_up_w)

    B, Sfull, H = x.shape
    cfg = Cfg(NH=16, S=4096, T=2048, QT=512, n_cores=8)
    nc = build(cfg)

    wdict = prep_inputs(cfg, np.zeros((cfg.T, H), np.float32), w_qkv, b_qkv,
                        w_out, b_out, router_w, down_w, up_w)
    del wdict["xT"]

    in_maps = []
    for c in range(8):
        b, hf = c // 2, c % 2
        m = dict(wdict)
        m["xT"] = np.ascontiguousarray(x[b, hf * cfg.T:(hf + 1) * cfg.T, :].T.astype(np.float32))
        in_maps.append(m)

    res = run_bass_kernel_spmd(nc, in_maps, core_ids=list(range(8)))

    out = np.empty((B, Sfull, H), dtype=np.float32)
    for c in range(8):
        b, hf = c // 2, c % 2
        out[b, hf * cfg.T:(hf + 1) * cfg.T, :] = res.results[c]["outT"].T
    return out


# version 18
# speedup vs baseline: 1.3961x; 1.0414x over previous
import math
import numpy as np
import ml_dtypes
from contextlib import ExitStack

import concourse.bass as bass
import concourse.mybir as mybir
from concourse.alu_op_type import AluOpType

F32 = mybir.dt.float32
BF16 = mybir.dt.bfloat16
AF = mybir.ActivationFunctionType
BF = ml_dtypes.bfloat16
ENGINES = ("tensor", "vector", "scalar", "gpsimd", "sync")
DMA_SEMS = {"vector": ["vd0"], "scalar": ["sd0", "sd1"],
            "gpsimd": ["gd0", "gd1", "gd2", "gd3"],
            "sync": ["yd0", "yd1", "yd2", "yd3"]}
ALL_SEMS = ["pe", "ve", "ac", "gp", "sy", "xd"] + [s for l in DMA_SEMS.values() for s in l]
ENG_SEM = {"tensor": "pe", "vector": "ve", "scalar": "ac", "gpsimd": "gp", "sync": "sy"}


class Ev:
    __slots__ = ("eng", "sem", "idx", "by", "used", "val")

    def __init__(self, eng, sem, idx, by):
        self.eng, self.sem, self.idx, self.by = eng, sem, idx, by
        self.used = False
        self.val = None


class Buf:
    __slots__ = ("w", "r", "name")

    def __init__(self, name=""):
        self.w = None
        self.r = []
        self.name = name


class Sched:
    def __init__(self):
        self.recs = {e: [] for e in ENGINES}
        self.dma_rot = {e: 0 for e in ENGINES}
        self.last_dma = {}
        self.tail = {e: None for e in ENGINES}
        self.pending = {e: [] for e in ENGINES}

    def barrier(self):
        evs = [e for e in self.tail.values() if e is not None]
        evs += [e for e in self.last_dma.values() if e is not None]
        for eng in ENGINES:
            self.pending[eng] = list(evs)

    def op(self, eng, fn, reads=(), writes=(), waits=(), dma=False, inc_by=None, dma_sem=None):
        if dma:
            if dma_sem is not None:
                sem = dma_sem
            else:
                sl = DMA_SEMS[eng]
                sem = sl[self.dma_rot[eng] % len(sl)]
                self.dma_rot[eng] += 1
        else:
            sem = ENG_SEM[eng]
        by = inc_by if inc_by is not None else (16 if dma else 1)
        ev = Ev(eng, sem, len(self.recs[eng]), by)
        wl = []
        if dma:
            ev.used = True
            prev = self.last_dma.get(sem)
            if prev is not None:
                wl.append(prev)
            self.last_dma[sem] = ev
        for b in reads:
            if b.w is not None:
                wl.append(b.w)
        for b in writes:
            if b.w is not None:
                wl.append(b.w)
            wl.extend(b.r)
        for e in waits:
            wl.append(e)
        seen, wl2 = set(), []
        for e in wl:
            if e is None or id(e) in seen:
                continue
            seen.add(id(e))
            wl2.append(e)
        if self.pending[eng]:
            for e in self.pending[eng]:
                if id(e) not in seen and e is not ev:
                    seen.add(id(e))
                    wl2.append(e)
            self.pending[eng] = []
        self.recs[eng].append((fn, wl2, ev))
        self.tail[eng] = ev
        for b in reads:
            b.r.append(ev)
        for b in writes:
            b.w = ev
            b.r = []
        return ev

    def mark(self, *evs):
        for e in evs:
            if e is not None:
                e.used = True

    def finalize(self, nc, block, sems):
        for eng in ENGINES:
            cnt = {}
            for fn, wl, ev in self.recs[eng]:
                for e in wl:
                    e.used = True
        for eng in ENGINES:
            cnt = {}
            for fn, wl, ev in self.recs[eng]:
                if ev.used:
                    cnt[ev.sem] = cnt.get(ev.sem, 0) + ev.by
                    ev.val = cnt[ev.sem]
        def make(eng):
            def emit(h):
                hw = {}
                for fn, wl, ev in self.recs[eng]:
                    for e in wl:
                        if e.val is None:
                            continue
                        key = e.sem
                        if hw.get(key, 0) >= e.val:
                            continue
                        h.wait_ge(sems[e.sem], e.val)
                        hw[key] = e.val
                    inst = fn(h)
                    if ev.used and ev.val is not None:
                        assert inst is not None, f"{eng} op needs inc but returned None"
                        inst.then_inc(sems[ev.sem], ev.by)
            return emit
        for eng in ENGINES:
            if not self.recs[eng]:
                continue
            getattr(block, eng)(make(eng))


class Cfg:
    def __init__(self, NH=16, S=4096, T=2048, QT=512, n_cores=8):
        self.NH = NH
        self.HC = NH
        self.H = NH * 128
        self.S = S
        self.T = T
        self.QT = QT
        self.KC = S // 128
        self.NQT = T // QT
        self.E, self.I = 8, 16
        self.EI = 128
        self.SCALE = 1.0 / math.sqrt(128.0)
        self.VD = min(512, self.H)
        self.NVD = self.H // self.VD
        self.TB = S // 128
        self.TBo = T // 128
        self.n_cores = n_cores
        self.EPS = 1e-5
        self.TW = min(2 * QT, T)
        self.NTW = T // self.TW
        self.JW = min(512, self.TW)
        self.JT = self.TW // self.JW


def build(cfg: Cfg, debug=False):
    nc = bass.Bass("TRN2", target_bir_lowering=False)
    K = Sched()
    H, S, T, QT, NH, HC, KC = cfg.H, cfg.S, cfg.T, cfg.QT, cfg.NH, cfg.HC, cfg.KC
    NQT, EI, VD, NVD, TB = cfg.NQT, cfg.EI, cfg.VD, cfg.NVD, cfg.TB
    TW, NTW, JW, JT = cfg.TW, cfg.NTW, cfg.JW, cfg.JT
    SC = cfg.SCALE
    IH = 1.0 / H

    xT = nc.dram_tensor("xT", [H, T], F32, kind="ExternalInput")
    wq_h = nc.dram_tensor("wq", [NH, HC * 128, 128], F32, kind="ExternalInput")
    wk_h = nc.dram_tensor("wk", [NH, HC * 128, 128], F32, kind="ExternalInput")
    wv_h = nc.dram_tensor("wv", [NVD, HC * 128, VD], F32, kind="ExternalInput")
    wo_h = nc.dram_tensor("wo", [HC, HC * 128, 128], F32, kind="ExternalInput")
    bqk_h = nc.dram_tensor("bqk", [128, 2 * NH], F32, kind="ExternalInput")
    bo_h = nc.dram_tensor("bo", [128, HC], F32, kind="ExternalInput")
    bv_h = nc.dram_tensor("bv", [1, H], F32, kind="ExternalInput")
    wr_h = nc.dram_tensor("wr", [128, HC * 8], F32, kind="ExternalInput")
    wd_h = nc.dram_tensor("wd", [128, HC * EI], F32, kind="ExternalInput")
    wu_h = nc.dram_tensor("wu", [EI, H], F32, kind="ExternalInput")
    cst_h = nc.dram_tensor("cst", [128, 256], BF16, kind="ExternalInput")
    cstf_h = nc.dram_tensor("cstf", [128, 128], F32, kind="ExternalInput")
    outT = nc.dram_tensor("outT", [H, T], F32, kind="ExternalOutput")

    _scr_kind = "ExternalOutput" if debug else "Internal"
    TBo = cfg.TBo
    qscr = nc.dram_tensor("qscr", [NH, 128, T], BF16, kind=_scr_kind)
    kag_i = nc.dram_tensor("kag_i", [NH, 128, T], BF16)
    kag_o = nc.dram_tensor("kag_o", [2, NH, 128, T], BF16, kind=_scr_kind)
    vag_i = nc.dram_tensor("vag_i", [TBo, NH, 128, 128], BF16)
    vag_o = nc.dram_tensor("vag_o", [2, TBo, NH, 128, 128], BF16, kind=_scr_kind)
    yscr = nc.dram_tensor("yscr", [H, T], F32, kind=_scr_kind)
    aodump = nc.dram_tensor("aodump", [128, HC * T], BF16, kind="ExternalOutput") if debug else None
    r1dump = nc.dram_tensor("r1dump", [NTW, 128, HC * TW], F32, kind="ExternalOutput") if debug else None
    accdump = nc.dram_tensor("accdump", [NH, T // 128, 128, 8], F32, kind="ExternalOutput") if debug else None
    stdump = nc.dram_tensor("stdump", [NTW, 128, 6 * JT * JW], F32, kind="ExternalOutput") if debug else None

    es = ExitStack()
    sbt = lambda name, shape, dt: es.enter_context(nc.sbuf_tensor(name, shape, dt))

    cst = sbt("cst_sb", [128, 256], BF16)
    cstf = sbt("cstf_sb", [128, 128], F32)
    sel_sb = sbt("sel_sb", [8, 128], BF16)
    ones_sb = sbt("ones_sb", [128, 128], BF16)
    onesf_sb = sbt("onesf_sb", [1, 128], F32)
    bqk_sb = sbt("bqk_sb", [128, 2 * NH], F32)
    bo_sb = sbt("bo_sb", [128, HC], F32)
    bv_sb = sbt("bv_sb", [1, H], BF16)
    wr_sb = sbt("wr_sb", [128, HC * 8], BF16)
    wd_sb = sbt("wd_sb", [128, HC * EI], BF16)
    wu_sb = sbt("wu_sb", [128, H], BF16)

    NTB = T // 128
    RSTR = JT * JW
    def _al(x):
        return (x + 63) // 64 * 64
    A_XT = 0
    A_wv = _al(A_XT + HC * T)
    A_wqk = _al(A_wv + HC * VD)
    A_kst = _al(A_wqk + 2 * HC * 128)
    A_vst = _al(A_kst + 2 * 512)
    p1_end = _al(A_vst + 2 * 512)
    GQ = min(4, T // 128)
    QW = GQ * 128
    NGQ = T // QW
    SH = S // 2
    A_q2 = 0
    A_k2 = _al(A_q2 + 2 * T)
    A_v2 = _al(A_k2 + 2 * S)
    A_PT = _al(A_v2 + 2 * S)
    A_P2s = _al(A_PT + KC * QW)
    A_P2q = _al(A_P2s + 2 * S)
    A_junk = _al(A_P2q + 2 * S)
    NQS_ = T // 128
    A_accQ = _al(A_junk + 2048)
    A_mnegs = _al(A_accQ + 2 * 4 * NH * NQS_)
    A_acc = _al(A_mnegs + 2 * NH * NQS_)
    A_lcol = _al(A_acc + 64)
    A_lirow = _al(A_lcol + 64)
    A_lrep = _al(A_lirow + 1536)
    p2_end = _al(A_lrep + 1024)
    A_r1 = 0
    A_xres = _al(A_r1 + 2 * HC * TW)
    A_yst = _al(A_xres + 2 * 4 * 512)
    A_rb = _al(A_yst + 2 * 4 * 512)
    A_sq = _al(A_rb + 2 * 512)
    A_stat = _al(A_sq + 2 * 512)
    A_wo = _al(A_stat + 2 * 6 * RSTR)
    wo_end = _al(A_wo + 2 * HC * 128)
    A_lgf = A_wo
    A_gT = _al(A_lgf + 2 * T)
    A_gtok = _al(A_gT + T)
    A_gtk2 = _al(A_gtok + 2 * NTB * 8)
    A_gsc = _al(A_gtk2 + 2 * NTB * 8)
    A_gsc2 = _al(A_gsc + 2 * NTB)
    A_gtk3 = _al(A_gsc2 + 2 * NTB)
    A_ag = _al(A_gtk3 + 2 * NTB * 8)
    p34_end = max(_al(A_ag + 4 * 512), wo_end)
    AO_cols = HC * T
    AO_off = max(p2_end, p34_end)
    AR_COLS = _al(max(AO_off + AO_cols, p1_end) + 64)
    arena = sbt("arena", [128, AR_COLS], BF16)
    assert AR_COLS * 2 <= 190 * 1024, f"arena {AR_COLS * 2} B/partition too big"

    XT = arena[:, A_XT:A_XT + HC * T]
    wv_sb = arena[:, A_wv:A_wv + HC * VD]
    wqk0 = A_wqk
    kst = arena[:, A_kst:A_kst + 2 * 512]
    vst = arena[:, A_vst:A_vst + 2 * 512]
    AO = arena[:, AO_off:AO_off + AO_cols]
    q2o, k2o, v2o = A_q2, A_k2, A_v2
    PTsl = arena[:, A_PT:A_PT + KC * QW]
    P2s = arena[:, A_P2s:A_P2s + 2 * S]
    P2q = arena[:, A_P2q:A_P2q + 2 * S]
    junk = arena[:, A_junk:A_junk + 2048]
    accQ = arena[:, A_accQ:A_accQ + 2 * 4 * NH * NQS_].bitcast(F32)
    accQI = arena[:, A_accQ:A_accQ + 2 * 4 * NH * NQS_].bitcast(mybir.dt.int32)
    mnegs = arena[:, A_mnegs:A_mnegs + 2 * NH * NQS_].bitcast(F32)
    accs = arena[:, A_acc:A_acc + 64].bitcast(F32)
    accsI = arena[:, A_acc:A_acc + 64].bitcast(mybir.dt.int32)
    lbuf = arena[:, A_lcol:A_lcol + 64].bitcast(F32)
    lirow = arena[0:1, A_lirow:A_lirow + 1024].bitcast(F32)
    lirow_bf = arena[0:1, A_lirow + 1024:A_lirow + 1536]
    lrep_sb = arena[:, A_lrep:A_lrep + 1024].bitcast(F32)
    r1v = arena[:, A_r1:A_r1 + 2 * HC * TW].bitcast(F32)
    x_res = arena[:, A_xres:A_xres + 2 * 4 * 512].bitcast(F32)
    ystage = arena[:, A_yst:A_yst + 2 * 4 * 512].bitcast(F32)
    rbst = arena[:, A_rb:A_rb + 2 * 512]
    sqst = arena[:, A_sq:A_sq + 2 * 512]
    stat = arena[:, A_stat:A_stat + 2 * 6 * RSTR].bitcast(F32)
    wo0 = A_wo
    lgf = arena[0:8, A_lgf:A_lgf + 2 * T].bitcast(F32)
    gT_sb = arena[0:8, A_gT:A_gT + T]
    gtok = arena[:, A_gtok:A_gtok + 2 * NTB * 8].bitcast(F32)
    gtk2 = arena[:, A_gtk2:A_gtk2 + 2 * NTB * 8].bitcast(F32)
    gsc = arena[:, A_gsc:A_gsc + 2 * NTB].bitcast(F32)
    gsc2 = arena[:, A_gsc2:A_gsc2 + 2 * NTB].bitcast(F32)
    gtk3 = arena[:, A_gtk3:A_gtk3 + 2 * NTB * 8].bitcast(F32)
    agst = arena[:, A_ag:A_ag + 4 * 512]

    psA = es.enter_context(nc.psum_tensor("psA", [128, 2048], F32))
    psB = es.enter_context(nc.psum_tensor("psB", [128, 2048], F32))

    sems = {}
    for s in ALL_SEMS:
        sems[s] = es.enter_context(nc.semaphore("sem_" + s))
    block = es.enter_context(nc.Block())

    b_cst, b_cstf, b_sel, b_ones = Buf("cst"), Buf("cstf"), Buf("sel"), Buf("ones")
    b_onesf = Buf("onesf")
    b_bqk, b_bo, b_bv = Buf("bqk"), Buf("bo"), Buf("bv")
    b_wr, b_wd, b_wu = Buf("wr"), Buf("wd"), Buf("wu")
    b_XT = Buf("XT")
    b_wv = Buf("wv")
    b_wqk = [Buf("wqk%d" % i) for i in range(2)]
    b_kst = [Buf("kst%d" % i) for i in range(2)]
    b_vst = [Buf("vst%d" % i) for i in range(2)]
    b_psA = [Buf("psA%d" % i) for i in range(4)]
    b_psB = [Buf("psB%d" % i) for i in range(4)]
    e_kscr = [[] for _ in range(NH)]
    e_qscr = [[] for _ in range(NH)]
    e_vscr = [[] for _ in range(NH)]
    b_k2 = [Buf("k2_%d" % i) for i in range(2)]
    b_q2 = [Buf("q2_%d" % i) for i in range(2)]
    b_v2 = [Buf("v2_%d" % i) for i in range(2)]
    b_P2s = [[Buf("P2s%d_%d" % (i, hh)) for hh in range(2)] for i in range(2)]
    b_P2q = [[Buf("P2q%d_%d" % (i, hh)) for hh in range(2)] for i in range(2)]
    b_accQ = Buf("accQ")
    b_mnegs = Buf("mnegs")
    b_PT = Buf("PT")
    b_junk = Buf("junk")
    b_acc = [Buf("acc%d" % i) for i in range(2)]
    b_lcol = Buf("lcol")
    b_lirow = Buf("lirow")
    b_lrep = Buf("lrep")
    b_AO = [[Buf("AO_%d_%d" % (c, j)) for j in range(T // 512 if T >= 512 else 1)] for c in range(HC)]
    b_wo = [Buf("wo%d" % i) for i in range(2)]
    b_x = [Buf("x%d" % i) for i in range(4)]
    b_r1 = [[Buf("r1_%d_%d" % (o, j)) for j in range(JT)] for o in range(HC)]
    b_rb = [Buf("rb%d" % i) for i in range(2)]
    b_sq = [Buf("sq%d" % i) for i in range(2)]
    b_stat = [Buf("stat%d" % j) for j in range(4)]
    b_yst = [Buf("yst%d" % i) for i in range(2)]
    b_yst2 = [Buf("yst2_%d" % i) for i in range(2)]
    b_stat2 = [Buf("stat2_%d" % i) for i in range(2)]
    b_yscr = [[Buf("yscr_%d_%d" % (o, j)) for j in range(T // 512 if T >= 512 else 1)] for o in range(HC)]
    b_lgf, b_gtok, b_gtk2, b_gsc, b_gT = Buf("lgf"), Buf("gtok"), Buf("gtk2"), Buf("gsc"), Buf("gT")
    b_gsc2, b_gtk3 = Buf("gsc2"), Buf("gtk3")
    b_ag = [Buf("ag%d" % i) for i in range(4)]

    AOW = 512 if T >= 512 else T

    def aoslice(c, t0, w):
        return AO[:, c * T + t0:c * T + t0 + w]

    def aobufs(t0, w, cs=None):
        out = []
        for c in (range(HC) if cs is None else cs):
            for j in range(t0 // AOW, (t0 + w + AOW - 1) // AOW):
                out.append(b_AO[c][j])
        return out

    g = "gpsimd"
    K.op(g, lambda h: h.dma_start(cst[:, :], cst_h[:, :]), writes=[b_cst], dma=True)
    K.op(g, lambda h: h.dma_start(cstf[:, :], cstf_h[:, :]), writes=[b_cstf], dma=True)
    K.op(g, lambda h: h.dma_start(sel_sb[:, :], cst_h[0:8, 128:256]), writes=[b_sel], dma=True)
    K.op(g, lambda h: h.dma_start(bqk_sb[:, :], bqk_h[:, :]), writes=[b_bqk], dma=True)
    K.op(g, lambda h: h.dma_start(bo_sb[:, :], bo_h[:, :]), writes=[b_bo], dma=True)
    K.op(g, lambda h: h.dma_start(bv_sb[:, :], bv_h[:, :]), writes=[b_bv], dma=True)
    K.op(g, lambda h: h.dma_start(wr_sb[:, :], wr_h[:, :]), writes=[b_wr], dma=True)
    K.op(g, lambda h: h.dma_start(wd_sb[:, :], wd_h[:, :]), writes=[b_wd], dma=True)
    K.op(g, lambda h: h.dma_start(wu_sb[:, :], wu_h[:, :]), writes=[b_wu], dma=True)
    for c in range(HC):
        K.op(g, (lambda c: lambda h: h.dma_start(
            XT[:, c * T:(c + 1) * T], xT[c * 128:(c + 1) * 128, :]))(c),
            writes=[b_XT], dma=True)
    K.op("vector", lambda h: h.memset(ones_sb[:, :], 1.0), writes=[b_ones])
    K.op("vector", lambda h: h.memset(onesf_sb[:, :], 1.0), writes=[b_onesf])

    ident = cst[:, 0:128]
    ident8f = cstf[0:8, 0:8]
    identf = cstf[:, :]

    for dt in range(NVD):
        K.op(g, (lambda dt: lambda h: h.dma_start(
            wv_sb[:, :].rearrange("p (c d) -> p c d", d=VD),
            wv_h[dt, :, :].rearrange("(c p) d -> p c d", p=128)))(dt),
            writes=[b_wv], dma=True)
        for tb in range(TBo):
            slot = (dt * TBo + tb) % 4
            def mmg(h, dt=dt, tb=tb, slot=slot):
                for c in range(HC):
                    h.matmul(psA[:, slot * 512:slot * 512 + VD],
                             XT[:, c * T + tb * 128:c * T + (tb + 1) * 128],
                             wv_sb[:, c * VD:(c + 1) * VD],
                             start=(c == 0), stop=False)
                return h.matmul(psA[:, slot * 512:slot * 512 + VD],
                                ones_sb[0:1, 0:128], bv_sb[:, dt * VD:(dt + 1) * VD],
                                start=False, stop=True)
            K.op("tensor", mmg, reads=[b_XT, b_wv, b_ones, b_bv], writes=[b_psA[slot]])
            vs = (dt * TBo + tb) % 2
            K.op("vector", (lambda slot, vs: lambda h: h.tensor_copy(
                vst[:, vs * 512:vs * 512 + VD], psA[:, slot * 512:slot * 512 + VD]))(slot, vs),
                reads=[b_psA[slot]], writes=[b_vst[vs]])
            nh_t = VD // 128
            def spill(h, dt=dt, tb=tb, vs=vs, nh_t=nh_t):
                return h.dma_start(vag_i[tb, dt * nh_t:(dt + 1) * nh_t, :, :].transpose([1, 0, 2]),
                                   vst[:, vs * 512:vs * 512 + VD].rearrange("p (j d) -> p j d", d=128))
            ev = K.op("sync", spill, reads=[b_vst[vs]], dma=True)
            e_vscr[0].append(ev)

    def qk_proj(which):
        span = T
        tile_w = min(512, span)
        ntile = span // tile_w
        wsrc = wk_h if which == "k" else wq_h
        scr = kag_i if which == "k" else qscr
        escr = e_kscr if which == "k" else e_qscr
        bias0 = NH if which == "k" else 0
        for n in range(NH):
            slot = n % 2
            K.op(g, (lambda n, slot, wsrc=wsrc: lambda h: h.dma_start(
                arena[:, wqk0 + slot * HC * 128: wqk0 + (slot + 1) * HC * 128]
                .rearrange("p (c o) -> p c o", o=128),
                wsrc[n, :, :].rearrange("(c p) o -> p c o", p=128)))(n, slot),
                writes=[b_wqk[slot]], dma=True)
            for j in range(ntile):
                idx = n * ntile + j
                ps = idx % 4
                def mmg(h, n=n, j=j, slot=slot, ps=ps, tile_w=tile_w):
                    for c in range(HC):
                        mm = h.matmul(psB[:, ps * 512:ps * 512 + tile_w],
                                      arena[:, wqk0 + slot * HC * 128 + c * 128:
                                            wqk0 + slot * HC * 128 + (c + 1) * 128],
                                      XT[:, c * T + j * tile_w:c * T + (j + 1) * tile_w],
                                      start=(c == 0), stop=(c == HC - 1))
                    return mm
                K.op("tensor", mmg, reads=[b_XT, b_wqk[slot]], writes=[b_psB[ps]])
                ks = idx % 2
                K.op("scalar", (lambda ps, ks, n, tile_w: lambda h: h.activation(
                    kst[:, ks * 512:ks * 512 + tile_w], psB[:, ps * 512:ps * 512 + tile_w],
                    AF.Identity, bias=bqk_sb[:, bias0 + n:bias0 + n + 1], scale=1.0))(ps, ks, n, tile_w),
                    reads=[b_psB[ps], b_bqk], writes=[b_kst[ks]])
                ev = K.op("sync", (lambda n, j, ks, tile_w, scr=scr: lambda h: h.dma_start(
                    scr[n, :, j * tile_w:(j + 1) * tile_w],
                    kst[:, ks * 512:ks * 512 + tile_w]))(n, j, ks, tile_w),
                    reads=[b_kst[ks]], dma=True)
                escr[n].append(ev)
    qk_proj("k")

    pair_groups = [[i, i + 1] for i in range(0, cfg.n_cores, 2)]
    e_kag = K.op(g, (lambda: lambda h: h.collective_compute(
        "AllGather", AluOpType.bypass, replica_groups=pair_groups,
        ins=[kag_i.ap().opt()], outs=[kag_o.ap().opt()]))(),
        waits=[e for n in range(NH) for e in e_kscr[n]])
    e_vag = K.op(g, (lambda: lambda h: h.collective_compute(
        "AllGather", AluOpType.bypass, replica_groups=pair_groups,
        ins=[vag_i.ap().opt()], outs=[vag_o.ap().opt()]))(),
        waits=e_vscr[0])
    qk_proj("q")

    K.barrier()
    nSA = (SH + 511) // 512
    NBA = max(1, KC // 4)
    psA_banks = [b_psA[i] for i in range(min(4, nSA))]
    psB_banks = [b_psB[i] for i in range(min(4, nSA))]
    NQS = T // 128
    import functools

    nstr = max(1, KC // 4)
    NV = NH * NQS
    SSW = (nstr + 2) * 128
    for n in range(NH):
        sl = n % 2
        K.op("sync", (lambda n, sl: lambda h: h.dma_start(
            arena[:, q2o + sl * T:q2o + (sl + 1) * T], qscr[n, :, :]))(n, sl),
            waits=e_qscr[n], writes=[b_q2[sl]], dma=True)
        for half in range(2):
            K.op("sync", (lambda n, sl, half: lambda h: h.dma_start(
                arena[:, k2o + sl * S + half * T:k2o + sl * S + (half + 1) * T],
                kag_o[half, n, :, :]))(n, sl, half),
                waits=[e_kag], writes=[b_k2[sl]], dma=True)
        for qi in range(NQS):
            pp = qi % 2
            psS = psA if pp == 0 else psB
            bps = b_psA if pp == 0 else b_psB
            hn = max(1, nstr // 2)
            def smp(h, n=n, qi=qi, sl=sl, psS=psS, hn=hn):
                k2v = arena[:, k2o + sl * S:k2o + (sl + 1) * S]
                qv = arena[:, q2o + sl * T + qi * 128:q2o + sl * T + (qi + 1) * 128]
                kstr = k2v.rearrange("p (a b) -> p a b", b=512)[:, :, 0:128]
                h.matmul(psS[:, 0:hn * 128], qv, kstr[:, 0:hn, :], start=True, stop=True)
                if nstr > hn:
                    h.matmul(psS[:, 512:512 + (nstr - hn) * 128], qv, kstr[:, hn:nstr, :],
                             start=True, stop=True)
                return h.matmul(psS[:, 1024:1024 + 256],
                                qv, k2v.rearrange("p (h b) -> p h b", h=2)
                                [:, :, qi * 128:(qi + 1) * 128],
                                start=True, stop=True)
            K.op("tensor", smp, reads=[b_k2[sl], b_q2[sl]], writes=[bps[0], bps[1], bps[2]])
            c0 = n * NQS + qi
            if nstr == 8:
                K.op("scalar", (lambda c0, psS: lambda h: h.activation(
                    junk[:, 0:1280], psS[:, 0:1280], AF.Exp, scale=SC / 4.0,
                    accum_out=accQ[:, c0:c0 + 1]))(c0, psS),
                    reads=[bps[0], bps[1], bps[2]], writes=[b_junk, b_accQ])
            else:
                K.op("scalar", (lambda c0, psS, hn: lambda h: h.activation(
                    junk[:, 0:hn * 128], psS[:, 0:hn * 128], AF.Exp, scale=SC / 4.0,
                    accum_out=accQ[:, NV + c0:NV + c0 + 1]))(c0, psS, hn),
                    reads=[bps[0], bps[1], bps[2]], writes=[b_junk, b_accQ])
                K.op("scalar", (lambda c0, psS: lambda h: h.activation(
                    junk[:, 0:256], psS[:, 1024:1280], AF.Exp, scale=SC / 4.0,
                    accum_out=accQ[:, 2 * NV + c0:2 * NV + c0 + 1]))(c0, psS),
                    reads=[bps[0], bps[1], bps[2]], writes=[b_junk, b_accQ])
                K.op("vector", (lambda c0: lambda h: h.tensor_tensor(
                    accQ[:, c0:c0 + 1], accQ[:, NV + c0:NV + c0 + 1],
                    accQ[:, 2 * NV + c0:2 * NV + c0 + 1], op=AluOpType.add))(c0),
                    reads=[b_accQ], writes=[b_accQ])
        K.op("vector", (lambda n: lambda h: h.tensor_scalar(
            accQI[:, NH * NQS + n * NQS:NH * NQS + (n + 1) * NQS],
            accQI[:, n * NQS:(n + 1) * NQS], 23, None,
            op0=AluOpType.arith_shift_right))(n),
            reads=[b_accQ], writes=[b_accQ])
        K.op("vector", (lambda n: lambda h: h.tensor_copy(
            accQ[:, 2 * NH * NQS + n * NQS:2 * NH * NQS + (n + 1) * NQS],
            accQI[:, NH * NQS + n * NQS:NH * NQS + (n + 1) * NQS]))(n),
            reads=[b_accQ], writes=[b_accQ])
        K.op("vector", (lambda n: lambda h: h.tensor_scalar(
            mnegs[:, n * NQS:(n + 1) * NQS],
            accQ[:, 2 * NH * NQS + n * NQS:2 * NH * NQS + (n + 1) * NQS],
            -127.0, -2.0 * 0.6931471805599453,
            op0=AluOpType.add, op1=AluOpType.mult))(n),
            reads=[b_accQ], writes=[b_mnegs])

    pend = []

    def flush_pending():
        for f in pend:
            f()
        pend.clear()

    for n in range(NH):
        sl = n % 2
        K.op("gpsimd", (lambda n, sl: lambda h: h.dma_start(
            arena[:, q2o + sl * T:q2o + (sl + 1) * T], qscr[n, :, :]))(n, sl),
            waits=e_qscr[n], writes=[b_q2[sl]], dma=True)
        for half in range(2):
            K.op("gpsimd", (lambda n, sl, half: lambda h: h.dma_start(
                arena[:, k2o + sl * S + half * T:k2o + sl * S + (half + 1) * T],
                kag_o[half, n, :, :]))(n, sl, half),
                waits=[e_kag], writes=[b_k2[sl]], dma=True)
            K.op("gpsimd", (lambda n, sl, half: lambda h: h.dma_start(
                arena[:, v2o + sl * S + half * T:v2o + sl * S + (half + 1) * T]
                .rearrange("p (tb d) -> p tb d", d=128),
                vag_o[half, :, n, :, :].transpose([1, 0, 2])))(n, sl, half),
                waits=[e_vag], writes=[b_v2[sl]], dma=True)
        for qg in range(NGQ):
            for qs in range(GQ):
                qi = qg * GQ + qs
                p = qi % 2
                def smm(h, n=n, qi=qi, sl=sl, half=0):
                    for m in range(nSA):
                        w = min(512, SH - m * 512)
                        ps = psA if half == 0 else psB
                        mm = h.matmul(ps[:, m * 512:m * 512 + w],
                                      arena[:, q2o + sl * T + qi * 128:q2o + sl * T + (qi + 1) * 128],
                                      arena[:, k2o + sl * S + half * SH + m * 512:
                                            k2o + sl * S + half * SH + m * 512 + w],
                                      start=True, stop=True)
                    return mm
                K.op("tensor", functools.partial(smm, half=0),
                     reads=[b_k2[sl], b_q2[sl]], writes=psA_banks)
                K.op("tensor", functools.partial(smm, half=1),
                     reads=[b_k2[sl], b_q2[sl]], writes=psB_banks)
                K.op("scalar", (lambda n, qi, p: lambda h: h.activation(
                    P2s[:, p * S:p * S + SH], psA[:, 0:SH], AF.Exp, scale=SC / 2.0,
                    bias=mnegs[:, n * NQS + qi:n * NQS + qi + 1]))(n, qi, p),
                    reads=psA_banks + [b_mnegs], writes=[b_P2s[p][0]])
                K.op("vector", (lambda p: lambda h: h.tensor_tensor(
                    P2q[:, p * S:p * S + SH], P2s[:, p * S:p * S + SH],
                    P2s[:, p * S:p * S + SH], op=AluOpType.mult))(p),
                    reads=[b_P2s[p][0]], writes=[b_P2q[p][0]])
                K.op("scalar", (lambda n, qi, p: lambda h: h.activation(
                    P2s[:, p * S + SH:p * S + 2 * SH], psB[:, 0:SH], AF.Exp, scale=SC / 2.0,
                    bias=mnegs[:, n * NQS + qi:n * NQS + qi + 1]))(n, qi, p),
                    reads=psB_banks + [b_mnegs], writes=[b_P2s[p][1]])
                K.op("vector", (lambda p: lambda h: h.tensor_tensor(
                    P2q[:, p * S + SH:p * S + 2 * SH], P2s[:, p * S + SH:p * S + 2 * SH],
                    P2s[:, p * S + SH:p * S + 2 * SH], op=AluOpType.mult))(p),
                    reads=[b_P2s[p][1]], writes=[b_P2q[p][1]])
                flush_pending()
                KH = SH // 128
                K.op("sync", (lambda p, qs, KH: lambda h: h.dma_start_transpose(
                    PTsl[:, :].rearrange("p (j q) -> p j q", q=QW)
                    [:, 0:KH, qs * 128:(qs + 1) * 128],
                    P2q[:, p * S:p * S + SH]))(p, qs, KH),
                    reads=[b_P2q[p][0]], writes=[b_PT], dma=True, dma_sem="xd")
                K.op("sync", (lambda p, qs, KH: lambda h: h.dma_start_transpose(
                    PTsl[:, :].rearrange("p (j q) -> p j q", q=QW)
                    [:, KH:2 * KH, qs * 128:(qs + 1) * 128],
                    P2q[:, p * S + SH:p * S + 2 * SH]))(p, qs, KH),
                    reads=[b_P2q[p][1]], writes=[b_PT], dma=True, dma_sem="xd")
                if qs == GQ - 1:
                    def def_G(n=n, sl=sl, qg=qg):
                        def pv(h, sl=sl):
                            for kt in range(KC):
                                mm = h.matmul(psA[:, 1024:1024 + QW],
                                              arena[:, v2o + sl * S + kt * 128:v2o + sl * S + (kt + 1) * 128],
                                              PTsl[:, kt * QW:(kt + 1) * QW],
                                              start=(kt == 0), stop=(kt == KC - 1))
                            return mm
                        K.op("tensor", pv, reads=[b_v2[sl], b_PT], writes=[b_psA[2]])
                        def lmm(h):
                            for kt in range(KC):
                                mm = h.matmul(psA[:, 1536:1536 + QW], ones_sb[:, 0:128],
                                              PTsl[:, kt * QW:(kt + 1) * QW],
                                              start=(kt == 0), stop=(kt == KC - 1))
                            return mm
                        K.op("tensor", lmm, reads=[b_ones, b_PT], writes=[b_psA[3]])
                        K.op("vector", lambda h: h.reciprocal(lrep_sb[:, 0:QW], psA[:, 1536:1536 + QW]),
                             reads=[b_psA[3]], writes=[b_lrep])
                        K.op("vector", (lambda n, qg: lambda h: h.tensor_tensor(
                            aoslice(n, qg * QW, QW), psA[:, 1024:1024 + QW], lrep_sb[:, 0:QW],
                            op=AluOpType.mult))(n, qg),
                            reads=[b_psA[2], b_lrep], writes=aobufs(qg * QW, QW, [n]))
                    pend.append(def_G)
    flush_pending()

    K.barrier()
    for tw in range(NTW):
        t0 = tw * TW
        for o in range(HC):
            i2 = tw * HC + o
            ws = i2 % 2
            K.op(g, (lambda o, ws: lambda h: h.dma_start(
                arena[:, wo0 + ws * HC * 128:wo0 + (ws + 1) * HC * 128]
                .rearrange("p (c q) -> p c q", q=128),
                wo_h[o, :, :].rearrange("(c p) q -> p c q", p=128)))(o, ws),
                writes=[b_wo[ws]], dma=True)
            for j in range(JT):
                gi = i2 * JT + j
                ps = gi % 2
                xs = gi % 4
                K.op("sync", (lambda o, j, t0, xs: lambda h: h.dma_start(
                    x_res[:, xs * 512:xs * 512 + JW],
                    xT[o * 128:(o + 1) * 128, t0 + j * JW:t0 + (j + 1) * JW]))(o, j, t0, xs),
                    writes=[b_x[xs]], dma=True)
                def pmm(h, o=o, j=j, ws=ws, ps=ps, t0=t0):
                    for c in range(HC):
                        mm = h.matmul(psA[:, ps * 512:ps * 512 + JW],
                                      arena[:, wo0 + ws * HC * 128 + c * 128:
                                            wo0 + ws * HC * 128 + (c + 1) * 128],
                                      aoslice(c, t0 + j * JW, JW),
                                      start=(c == 0), stop=(c == HC - 1))
                    return mm
                K.op("tensor", pmm, reads=[b_wo[ws]] + aobufs(t0 + j * JW, JW),
                     writes=[b_psA[ps]])
                K.op("vector", (lambda o, j, ps, xs: lambda h: h.scalar_tensor_tensor(
                    r1v[:, o * TW + j * JW:o * TW + (j + 1) * JW],
                    psA[:, ps * 512:ps * 512 + JW], bo_sb[:, o:o + 1],
                    x_res[:, xs * 512:xs * 512 + JW],
                    op0=AluOpType.add, op1=AluOpType.add))(o, j, ps, xs),
                    reads=[b_psA[ps], b_bo, b_x[xs]], writes=[b_r1[o][j]])
                rs_ = gi % 2
                K.op("vector", (lambda o, j, rs_: lambda h: h.tensor_copy(
                    rbst[:, rs_ * 512:rs_ * 512 + JW],
                    r1v[:, o * TW + j * JW:o * TW + (j + 1) * JW]))(o, j, rs_),
                    reads=[b_r1[o][j]], writes=[b_rb[rs_]])
                K.op("vector", (lambda rs_: lambda h: h.tensor_tensor(
                    sqst[:, rs_ * 512:rs_ * 512 + JW], rbst[:, rs_ * 512:rs_ * 512 + JW],
                    rbst[:, rs_ * 512:rs_ * 512 + JW], op=AluOpType.mult))(rs_),
                    reads=[b_rb[rs_]], writes=[b_sq[rs_]])
                def stmm(h, o=o, j=j, rs_=rs_):
                    h.matmul(psB[:, j * 512:j * 512 + JW], ones_sb[:, 0:128],
                             rbst[:, rs_ * 512:rs_ * 512 + JW],
                             start=(o == 0), stop=(o == HC - 1))
                    return h.matmul(psB[:, 1024 + j * 512:1024 + j * 512 + JW],
                                    ones_sb[:, 0:128], sqst[:, rs_ * 512:rs_ * 512 + JW],
                                    start=(o == 0), stop=(o == HC - 1))
                K.op("tensor", stmm, reads=[b_rb[rs_], b_sq[rs_], b_ones],
                     writes=[b_psB[j], b_psB[2 + j]])
        if debug:
            K.op("sync", (lambda tw: lambda h: h.dma_start(
                r1dump[tw, :, :], r1v[:, 0:HC * TW]))(tw),
                reads=[b for row in b_r1 for b in row], dma=True)
        for j in range(JT):
            mu = stat[:, 0 * RSTR + j * JW:0 * RSTR + j * JW + JW]
            rs = stat[:, 1 * RSTR + j * JW:1 * RSTR + j * JW + JW]
            mus = stat[:, 2 * RSTR + j * JW:2 * RSTR + j * JW + JW]
            scr2 = stat[:, 3 * RSTR + j * JW:3 * RSTR + j * JW + JW]
            bmu, brs, bmus, bscr = b_stat[0], b_stat[1], b_stat[2], b_stat[3]
            K.op("vector", (lambda mu, j: lambda h: h.tensor_scalar(
                mu, psB[:, j * 512:j * 512 + JW], IH, None, op0=AluOpType.mult))(mu, j),
                reads=[b_psB[j]], writes=[bmu])
            K.op("vector", (lambda scr2, j: lambda h: h.tensor_scalar(
                scr2, psB[:, 1024 + j * 512:1024 + j * 512 + JW], IH, cfg.EPS,
                op0=AluOpType.mult, op1=AluOpType.add))(scr2, j),
                reads=[b_psB[2 + j]], writes=[bscr])
            var2 = stat[:, 4 * RSTR + j * JW:4 * RSTR + j * JW + JW]
            rin2 = stat[:, 5 * RSTR + j * JW:5 * RSTR + j * JW + JW]
            bvar, brin = b_stat2[0], b_stat2[1]
            K.op("vector", (lambda rs, mu: lambda h: h.tensor_tensor(
                rs, mu, mu, op=AluOpType.mult))(rs, mu), reads=[bmu], writes=[brs])
            K.op("vector", (lambda var2, scr2, rs: lambda h: h.tensor_tensor(
                var2, scr2, rs, op=AluOpType.subtract))(var2, scr2, rs),
                reads=[brs, bscr], writes=[bvar])
            K.op("vector", (lambda rin2, var2: lambda h: h.reciprocal(rin2, var2))(rin2, var2),
                reads=[bvar], writes=[brin])
            K.op("scalar", (lambda rs, rin2: lambda h: h.activation(rs, rin2, AF.Sqrt))(rs, rin2),
                reads=[brin], writes=[brs])
            K.op("vector", (lambda mus, mu, rs: lambda h: h.tensor_tensor(
                mus, mu, rs, op=AluOpType.mult))(mus, mu, rs), reads=[bmu, brs], writes=[bmus])
            if debug and j == JT - 1:
                K.op("sync", (lambda tw: lambda h: h.dma_start(
                    stdump[tw, :, :], stat[:, 0:6 * RSTR]))(tw),
                    reads=[b_stat[0], b_stat[1], b_stat[2], b_stat[3],
                           b_stat2[0], b_stat2[1]], dma=True)
            for o in range(HC):
                ys = (o * JT + j) % 2
                K.op("vector", (lambda o, j, ys, rs: lambda h: h.tensor_tensor(
                    ystage[:, ys * 512:ys * 512 + JW],
                    r1v[:, o * TW + j * JW:o * TW + (j + 1) * JW], rs,
                    op=AluOpType.mult))(o, j, ys, rs),
                    reads=[b_r1[o][j], brs], writes=[b_yst[ys]])
                K.op("vector", (lambda ys, mus: lambda h: h.tensor_tensor(
                    ystage[:, (2 + ys) * 512:(2 + ys) * 512 + JW],
                    ystage[:, ys * 512:ys * 512 + JW],
                    mus, op=AluOpType.subtract))(ys, mus),
                    reads=[bmus, b_yst[ys]], writes=[b_yst2[ys]])
                K.op("scalar", (lambda o, j, ys, t0: lambda h: h.activation(
                    aoslice(o, t0 + j * JW, JW), ystage[:, (2 + ys) * 512:(2 + ys) * 512 + JW],
                    AF.Copy))(o, j, ys, t0),
                    reads=[b_yst2[ys]], writes=aobufs(t0 + j * JW, JW, [o]))
                ev = K.op("sync", (lambda o, j, ys, t0: lambda h: h.dma_start(
                    yscr[o * 128:(o + 1) * 128, t0 + j * JW:t0 + (j + 1) * JW],
                    ystage[:, (2 + ys) * 512:(2 + ys) * 512 + JW]))(o, j, ys, t0),
                    reads=[b_yst2[ys]], dma=True)
                b_yscr[o][(t0 + j * JW) // (512 if T >= 512 else T)].w = ev

    K.barrier()
    for jt in range(T // QT):
        def rmm(h, jt=jt):
            for c in range(HC):
                mm = h.matmul(psA[0:8, jt * 512:jt * 512 + QT],
                              wr_sb[:, c * 8:(c + 1) * 8],
                              aoslice(c, jt * QT, QT),
                              start=(c == 0), stop=(c == HC - 1))
            return mm
        K.op("tensor", rmm, reads=[b_wr] + aobufs(jt * QT, QT), writes=[b_psA[jt % 4]])
        K.op("vector", (lambda jt: lambda h: h.tensor_copy(
            lgf[:, jt * QT:(jt + 1) * QT], psA[0:8, jt * 512:jt * 512 + QT]))(jt),
            reads=[b_psA[jt % 4]], writes=[b_lgf])
    for tb in range(NTB):
        K.op("tensor", (lambda tb: lambda h: h.transpose(
            psA[:, (tb % 4) * 8:(tb % 4) * 8 + 8],
            lgf[0:8, tb * 128:(tb + 1) * 128], ident8f))(tb),
            reads=[b_lgf, b_cstf], writes=[b_psA[0]])
        if tb % 4 == 3 or tb == NTB - 1:
            lo = (tb // 4) * 4
            cnt = tb - lo + 1
            K.op("vector", (lambda lo, cnt: lambda h: h.tensor_copy(
                gtok[:, lo * 8:(lo + cnt) * 8], psA[:, 0:cnt * 8]))(lo, cnt),
                reads=[b_psA[0]], writes=[b_gtok])
    seg = gtok[:, :].rearrange("p (s e) -> p s e", e=8)
    seg2 = gtk2[:, :].rearrange("p (s e) -> p s e", e=8)
    seg3 = gtk3[:, :].rearrange("p (s e) -> p s e", e=8)
    gb = lambda: gsc[:, :, None].broadcast_to((128, NTB, 8))
    gb2 = lambda: gsc2[:, :, None].broadcast_to((128, NTB, 8))
    K.op("vector", lambda h: h.tensor_reduce(gsc[:, :, None], seg, axis=mybir.AxisListType.X,
                                             op=AluOpType.max), reads=[b_gtok], writes=[b_gsc])
    K.op("vector", lambda h: h.tensor_tensor(seg2, seg, gb(), op=AluOpType.subtract),
         reads=[b_gtok, b_gsc], writes=[b_gtk2])
    K.op("scalar", lambda h: h.activation(gtok[:, :], gtk2[:, :], AF.Exp),
         reads=[b_gtk2], writes=[b_gtok])
    K.op("vector", lambda h: h.tensor_reduce(gsc[:, :, None], seg, axis=mybir.AxisListType.X,
                                             op=AluOpType.add), reads=[b_gtok], writes=[b_gsc])
    K.op("vector", lambda h: h.reciprocal(gsc2[:, :], gsc[:, :]), reads=[b_gsc], writes=[b_gsc2])
    K.op("vector", lambda h: h.tensor_tensor(seg2, seg, gb2(), op=AluOpType.mult),
         reads=[b_gsc2, b_gtok], writes=[b_gtk2])
    K.op("vector", lambda h: h.tensor_reduce(gsc[:, :, None], seg2, axis=mybir.AxisListType.X,
                                             op=AluOpType.max), reads=[b_gtk2], writes=[b_gsc])
    K.op("vector", lambda h: h.tensor_tensor(seg, seg2, gb(), op=AluOpType.is_ge),
         reads=[b_gtk2, b_gsc], writes=[b_gtok])
    K.op("vector", lambda h: h.scalar_tensor_tensor(gtk3[:, :], gtok[:, :], -2.0, gtk2[:, :],
                                                    op0=AluOpType.mult, op1=AluOpType.add),
         reads=[b_gtk2, b_gtok], writes=[b_gtk3])
    K.op("vector", lambda h: h.tensor_reduce(gsc[:, :, None], seg3, axis=mybir.AxisListType.X,
                                             op=AluOpType.max), reads=[b_gtk3], writes=[b_gsc])
    K.op("vector", lambda h: h.tensor_tensor(seg, seg2, gb(), op=AluOpType.is_ge),
         reads=[b_gtk2, b_gsc], writes=[b_gtok])
    K.op("vector", lambda h: h.tensor_tensor(seg3, seg, seg2, op=AluOpType.mult),
         reads=[b_gtk2, b_gtok], writes=[b_gtk3])
    for tb in range(NTB):
        K.op("tensor", (lambda tb: lambda h: h.transpose(
            psA[0:8, 512 + (tb % 4) * 128:512 + (tb % 4) * 128 + 128],
            gtk3[:, tb * 8:(tb + 1) * 8], identf))(tb),
            reads=[b_gtk3, b_cstf], writes=[b_psA[1]])
        K.op("vector", (lambda tb: lambda h: h.tensor_copy(
            gT_sb[:, tb * 128:(tb + 1) * 128],
            psA[0:8, 512 + (tb % 4) * 128:512 + (tb % 4) * 128 + 128]))(tb),
            reads=[b_psA[1]], writes=[b_gT])

    for tw in range(NTW):
        t0 = tw * TW
        for j in range(JT):
            jj = tw * JT + j
            K.op("tensor", (lambda t0, j: lambda h: h.matmul(
                psB[:, 1536:1536 + JW], sel_sb[:, :], gT_sb[:, t0 + j * JW:t0 + (j + 1) * JW],
                start=True, stop=True))(t0, j),
                reads=[b_sel, b_gT], writes=[b_psB[3]])
            def dmm(h, t0=t0, j=j):
                for c in range(HC):
                    mm = h.matmul(psB[:, 1024:1024 + JW], wd_sb[:, c * EI:(c + 1) * EI][:, 0:128],
                                  aoslice(c, t0 + j * JW, JW), start=(c == 0), stop=(c == HC - 1))
                return mm
            K.op("tensor", dmm, reads=[b_wd] + aobufs(t0 + j * JW, JW), writes=[b_psB[2]])
            sb_ = jj % 2
            K.op("scalar", (lambda sb_: lambda h: h.activation(
                agst[:, sb_ * 512:sb_ * 512 + JW], psB[:, 1024:1024 + JW], AF.Silu))(sb_),
                reads=[b_psB[2]], writes=[b_ag[sb_]])
            K.op("vector", (lambda sb_: lambda h: h.tensor_tensor(
                agst[:, 1024 + sb_ * 512:1024 + sb_ * 512 + JW],
                agst[:, sb_ * 512:sb_ * 512 + JW], psB[:, 1536:1536 + JW],
                op=AluOpType.mult))(sb_),
                reads=[b_ag[sb_], b_psB[3]], writes=[b_ag[2 + sb_]])
            for o in range(HC):
                oi = o * JT + j
                ps = oi % 2
                xs = (tw * HC * JT + o * JT + j) % 2
                K.op("sync", (lambda o, j, t0, xs: lambda h: h.dma_start(
                    x_res[:, (2 + xs) * 512:(2 + xs) * 512 + JW],
                    yscr[o * 128:(o + 1) * 128, t0 + j * JW:t0 + (j + 1) * JW]))(o, j, t0, xs),
                    reads=[b_yscr[o][(t0 + j * JW) // (512 if T >= 512 else T)]],
                    writes=[b_x[2 + xs]], dma=True)
                K.op("tensor", (lambda o, sb_, ps: lambda h: h.matmul(
                    psB[:, ps * 512:ps * 512 + JW], wu_sb[:, o * 128:(o + 1) * 128],
                    agst[:, 1024 + sb_ * 512:1024 + sb_ * 512 + JW],
                    start=True, stop=True))(o, sb_, ps),
                    reads=[b_wu, b_ag[2 + sb_]], writes=[b_psB[ps]])
                K.op("vector", (lambda o, j, ps, xs: lambda h: h.tensor_tensor(
                    r1v[:, o * TW + j * JW:o * TW + (j + 1) * JW],
                    psB[:, ps * 512:ps * 512 + JW], x_res[:, (2 + xs) * 512:(2 + xs) * 512 + JW],
                    op=AluOpType.add))(o, j, ps, xs),
                    reads=[b_psB[ps], b_x[2 + xs]], writes=[b_r1[o][j]])
                rs_ = oi % 2
                K.op("vector", (lambda o, j, rs_: lambda h: h.tensor_copy(
                    rbst[:, rs_ * 512:rs_ * 512 + JW],
                    r1v[:, o * TW + j * JW:o * TW + (j + 1) * JW]))(o, j, rs_),
                    reads=[b_r1[o][j]], writes=[b_rb[rs_]])
                K.op("vector", (lambda rs_: lambda h: h.tensor_tensor(
                    sqst[:, rs_ * 512:rs_ * 512 + JW], rbst[:, rs_ * 512:rs_ * 512 + JW],
                    rbst[:, rs_ * 512:rs_ * 512 + JW], op=AluOpType.mult))(rs_),
                    reads=[b_rb[rs_]], writes=[b_sq[rs_]])
                def stmm(h, o=o, j=j, rs_=rs_):
                    h.matmul(psA[:, j * 512:j * 512 + JW], ones_sb[:, 0:128],
                             rbst[:, rs_ * 512:rs_ * 512 + JW],
                             start=(o == 0), stop=(o == HC - 1))
                    return h.matmul(psA[:, 1024 + j * 512:1024 + j * 512 + JW],
                                    ones_sb[:, 0:128], sqst[:, rs_ * 512:rs_ * 512 + JW],
                                    start=(o == 0), stop=(o == HC - 1))
                K.op("tensor", stmm, reads=[b_rb[rs_], b_sq[rs_], b_ones],
                     writes=[b_psA[j], b_psA[2 + j]])
        for j in range(JT):
            mu = stat[:, 0 * RSTR + j * JW:0 * RSTR + j * JW + JW]
            rs = stat[:, 1 * RSTR + j * JW:1 * RSTR + j * JW + JW]
            mus = stat[:, 2 * RSTR + j * JW:2 * RSTR + j * JW + JW]
            scr2 = stat[:, 3 * RSTR + j * JW:3 * RSTR + j * JW + JW]
            bmu, brs, bmus, bscr = b_stat[0], b_stat[1], b_stat[2], b_stat[3]
            K.op("vector", (lambda mu, j: lambda h: h.tensor_scalar(
                mu, psA[:, j * 512:j * 512 + JW], IH, None, op0=AluOpType.mult))(mu, j),
                reads=[b_psA[j]], writes=[bmu])
            K.op("vector", (lambda scr2, j: lambda h: h.tensor_scalar(
                scr2, psA[:, 1024 + j * 512:1024 + j * 512 + JW], IH, cfg.EPS,
                op0=AluOpType.mult, op1=AluOpType.add))(scr2, j),
                reads=[b_psA[2 + j]], writes=[bscr])
            var2 = stat[:, 4 * RSTR + j * JW:4 * RSTR + j * JW + JW]
            rin2 = stat[:, 5 * RSTR + j * JW:5 * RSTR + j * JW + JW]
            bvar, brin = b_stat2[0], b_stat2[1]
            K.op("vector", (lambda rs, mu: lambda h: h.tensor_tensor(
                rs, mu, mu, op=AluOpType.mult))(rs, mu), reads=[bmu], writes=[brs])
            K.op("vector", (lambda var2, scr2, rs: lambda h: h.tensor_tensor(
                var2, scr2, rs, op=AluOpType.subtract))(var2, scr2, rs),
                reads=[brs, bscr], writes=[bvar])
            K.op("vector", (lambda rin2, var2: lambda h: h.reciprocal(rin2, var2))(rin2, var2),
                reads=[bvar], writes=[brin])
            K.op("scalar", (lambda rs, rin2: lambda h: h.activation(rs, rin2, AF.Sqrt))(rs, rin2),
                reads=[brin], writes=[brs])
            K.op("vector", (lambda mus, mu, rs: lambda h: h.tensor_tensor(
                mus, mu, rs, op=AluOpType.mult))(mus, mu, rs), reads=[bmu, brs], writes=[bmus])
            for o in range(HC):
                ys = (o * JT + j) % 2
                K.op("vector", (lambda o, j, ys, rs: lambda h: h.tensor_tensor(
                    ystage[:, ys * 512:ys * 512 + JW],
                    r1v[:, o * TW + j * JW:o * TW + (j + 1) * JW], rs,
                    op=AluOpType.mult))(o, j, ys, rs),
                    reads=[b_r1[o][j], brs], writes=[b_yst[ys]])
                K.op("vector", (lambda ys, mus: lambda h: h.tensor_tensor(
                    ystage[:, (2 + ys) * 512:(2 + ys) * 512 + JW],
                    ystage[:, ys * 512:ys * 512 + JW],
                    mus, op=AluOpType.subtract))(ys, mus),
                    reads=[bmus, b_yst[ys]], writes=[b_yst2[ys]])
                evo = K.op("sync", (lambda o, j, ys, t0: lambda h: h.dma_start(
                    outT[o * 128:(o + 1) * 128, t0 + j * JW:t0 + (j + 1) * JW],
                    ystage[:, (2 + ys) * 512:(2 + ys) * 512 + JW]))(o, j, ys, t0),
                    reads=[b_yst2[ys]], dma=True)
    K.op("sync", lambda h: None, waits=[evo])
    K.mark(evo)

    K.finalize(nc, block, sems)
    es.close()
    return nc


def prep_inputs(cfg, x_bh, w_qkv, b_qkv, w_out, b_out, router_w, down_w, up_w, own_first=True):
    H, NH, HC, S, VD, NVD, EI = cfg.H, cfg.NH, cfg.HC, cfg.S, cfg.VD, cfg.NVD, cfg.EI
    f32 = np.float32
    d = {}
    d["xT"] = np.ascontiguousarray(x_bh.T.astype(f32))
    Wq, Wk, Wv = w_qkv[0:H], w_qkv[H:2 * H], w_qkv[2 * H:3 * H]
    d["wq"] = np.ascontiguousarray(
        Wq.reshape(NH, 128, H).transpose(0, 2, 1))
    d["wk"] = np.ascontiguousarray(Wk.reshape(NH, 128, H).transpose(0, 2, 1))
    d["wv"] = np.ascontiguousarray(
        Wv.T.reshape(H, NVD, VD).transpose(1, 0, 2))
    d["wo"] = np.ascontiguousarray(w_out.reshape(HC, 128, H).transpose(0, 2, 1))
    d["bqk"] = np.ascontiguousarray(
        np.concatenate([b_qkv[0:H].reshape(NH, 128).T, b_qkv[H:2 * H].reshape(NH, 128).T], axis=1)).astype(f32)
    d["bo"] = np.ascontiguousarray(b_out.reshape(HC, 128).T).astype(f32)
    d["bv"] = b_qkv[2 * H:3 * H].reshape(1, H).astype(f32)
    d["wr"] = np.ascontiguousarray(
        router_w.T.reshape(HC, 128, 8).transpose(1, 0, 2).reshape(128, HC * 8)).astype(f32)
    Dcat = down_w.reshape(EI, H)
    d["wd"] = np.ascontiguousarray(
        Dcat.T.reshape(HC, 128, EI).transpose(1, 0, 2).reshape(128, HC * EI)).astype(f32)
    Ucat = up_w.transpose(1, 0, 2).reshape(H, EI)
    d["wu"] = np.ascontiguousarray(Ucat.T).astype(f32)
    cst = np.zeros((128, 256), dtype=BF)
    cst[:, 0:128] = np.eye(128, dtype=BF)
    sel = np.zeros((8, 128), dtype=BF)
    for e in range(8):
        sel[e, e * 16:(e + 1) * 16] = 1
    cst[0:8, 128:256] = sel
    d["cst"] = cst
    d["cstf"] = np.eye(128, dtype=f32)
    return d


from concourse.bass_utils import run_bass_kernel_spmd


def kernel(L0_attn_hidden_states, L0_attn_w_qkv, L0_attn_b_qkv, L0_attn_w_out,
           L0_attn_b_out, L0_ffn_router_w, L0_ffn_down_w, L0_ffn_up_w):
    x = np.asarray(L0_attn_hidden_states)
    w_qkv = np.asarray(L0_attn_w_qkv)
    b_qkv = np.asarray(L0_attn_b_qkv)
    w_out = np.asarray(L0_attn_w_out)
    b_out = np.asarray(L0_attn_b_out)
    router_w = np.asarray(L0_ffn_router_w)
    down_w = np.asarray(L0_ffn_down_w)
    up_w = np.asarray(L0_ffn_up_w)

    B, Sfull, H = x.shape
    cfg = Cfg(NH=16, S=4096, T=2048, QT=512, n_cores=8)
    nc = build(cfg)

    wdict = prep_inputs(cfg, np.zeros((cfg.T, H), np.float32), w_qkv, b_qkv,
                        w_out, b_out, router_w, down_w, up_w)
    del wdict["xT"]

    in_maps = []
    for c in range(8):
        b, hf = c // 2, c % 2
        m = dict(wdict)
        m["xT"] = np.ascontiguousarray(x[b, hf * cfg.T:(hf + 1) * cfg.T, :].T.astype(np.float32))
        in_maps.append(m)

    res = run_bass_kernel_spmd(nc, in_maps, core_ids=list(range(8)))

    out = np.empty((B, Sfull, H), dtype=np.float32)
    for c in range(8):
        b, hf = c // 2, c % 2
        out[b, hf * cfg.T:(hf + 1) * cfg.T, :] = res.results[c]["outT"].T
    return out


# version 19
# speedup vs baseline: 1.4289x; 1.0235x over previous
import math
import numpy as np
import ml_dtypes
from contextlib import ExitStack

import concourse.bass as bass
import concourse.mybir as mybir
from concourse.alu_op_type import AluOpType

F32 = mybir.dt.float32
BF16 = mybir.dt.bfloat16
AF = mybir.ActivationFunctionType
BF = ml_dtypes.bfloat16
ENGINES = ("tensor", "vector", "scalar", "gpsimd", "sync")
DMA_SEMS = {"vector": ["vd0"], "scalar": ["sd0", "sd1"],
            "gpsimd": ["gd0", "gd1", "gd2", "gd3"],
            "sync": ["yd0", "yd1", "yd2", "yd3"]}
ALL_SEMS = ["pe", "ve", "ac", "gp", "sy", "xd"] + [s for l in DMA_SEMS.values() for s in l]
ENG_SEM = {"tensor": "pe", "vector": "ve", "scalar": "ac", "gpsimd": "gp", "sync": "sy"}


class Ev:
    __slots__ = ("eng", "sem", "idx", "by", "used", "val")

    def __init__(self, eng, sem, idx, by):
        self.eng, self.sem, self.idx, self.by = eng, sem, idx, by
        self.used = False
        self.val = None


class Buf:
    __slots__ = ("w", "r", "name")

    def __init__(self, name=""):
        self.w = None
        self.r = []
        self.name = name


class Sched:
    def __init__(self):
        self.recs = {e: [] for e in ENGINES}
        self.dma_rot = {e: 0 for e in ENGINES}
        self.last_dma = {}
        self.tail = {e: None for e in ENGINES}
        self.pending = {e: [] for e in ENGINES}

    def barrier(self):
        evs = [e for e in self.tail.values() if e is not None]
        evs += [e for e in self.last_dma.values() if e is not None]
        for eng in ENGINES:
            self.pending[eng] = list(evs)

    def op(self, eng, fn, reads=(), writes=(), waits=(), dma=False, inc_by=None, dma_sem=None):
        if dma:
            if dma_sem is not None:
                sem = dma_sem
            else:
                sl = DMA_SEMS[eng]
                sem = sl[self.dma_rot[eng] % len(sl)]
                self.dma_rot[eng] += 1
        else:
            sem = ENG_SEM[eng]
        by = inc_by if inc_by is not None else (16 if dma else 1)
        ev = Ev(eng, sem, len(self.recs[eng]), by)
        wl = []
        if dma:
            ev.used = True
            prev = self.last_dma.get(sem)
            if prev is not None:
                wl.append(prev)
            self.last_dma[sem] = ev
        for b in reads:
            if b.w is not None:
                wl.append(b.w)
        for b in writes:
            if b.w is not None:
                wl.append(b.w)
            wl.extend(b.r)
        for e in waits:
            wl.append(e)
        seen, wl2 = set(), []
        for e in wl:
            if e is None or id(e) in seen:
                continue
            seen.add(id(e))
            wl2.append(e)
        if self.pending[eng]:
            for e in self.pending[eng]:
                if id(e) not in seen and e is not ev:
                    seen.add(id(e))
                    wl2.append(e)
            self.pending[eng] = []
        self.recs[eng].append((fn, wl2, ev))
        self.tail[eng] = ev
        for b in reads:
            b.r.append(ev)
        for b in writes:
            b.w = ev
            b.r = []
        return ev

    def mark(self, *evs):
        for e in evs:
            if e is not None:
                e.used = True

    def finalize(self, nc, block, sems):
        for eng in ENGINES:
            cnt = {}
            for fn, wl, ev in self.recs[eng]:
                for e in wl:
                    e.used = True
        for eng in ENGINES:
            cnt = {}
            for fn, wl, ev in self.recs[eng]:
                if ev.used:
                    cnt[ev.sem] = cnt.get(ev.sem, 0) + ev.by
                    ev.val = cnt[ev.sem]
        def make(eng):
            def emit(h):
                hw = {}
                for fn, wl, ev in self.recs[eng]:
                    for e in wl:
                        if e.val is None:
                            continue
                        key = e.sem
                        if hw.get(key, 0) >= e.val:
                            continue
                        h.wait_ge(sems[e.sem], e.val)
                        hw[key] = e.val
                    inst = fn(h)
                    if ev.used and ev.val is not None:
                        assert inst is not None, f"{eng} op needs inc but returned None"
                        inst.then_inc(sems[ev.sem], ev.by)
            return emit
        for eng in ENGINES:
            if not self.recs[eng]:
                continue
            getattr(block, eng)(make(eng))


class Cfg:
    def __init__(self, NH=16, S=4096, T=2048, QT=512, n_cores=8):
        self.NH = NH
        self.HC = NH
        self.H = NH * 128
        self.S = S
        self.T = T
        self.QT = QT
        self.KC = S // 128
        self.NQT = T // QT
        self.E, self.I = 8, 16
        self.EI = 128
        self.SCALE = 1.0 / math.sqrt(128.0)
        self.VD = min(512, self.H)
        self.NVD = self.H // self.VD
        self.TB = S // 128
        self.TBo = T // 128
        self.n_cores = n_cores
        self.EPS = 1e-5
        self.TW = min(2 * QT, T)
        self.NTW = T // self.TW
        self.JW = min(512, self.TW)
        self.JT = self.TW // self.JW


def build(cfg: Cfg, debug=False):
    nc = bass.Bass("TRN2", target_bir_lowering=False)
    K = Sched()
    H, S, T, QT, NH, HC, KC = cfg.H, cfg.S, cfg.T, cfg.QT, cfg.NH, cfg.HC, cfg.KC
    NQT, EI, VD, NVD, TB = cfg.NQT, cfg.EI, cfg.VD, cfg.NVD, cfg.TB
    TW, NTW, JW, JT = cfg.TW, cfg.NTW, cfg.JW, cfg.JT
    SC = cfg.SCALE
    IH = 1.0 / H

    xT = nc.dram_tensor("xT", [H, T], F32, kind="ExternalInput")
    wq_h = nc.dram_tensor("wq", [NH, HC * 128, 128], F32, kind="ExternalInput")
    wk_h = nc.dram_tensor("wk", [NH, HC * 128, 128], F32, kind="ExternalInput")
    wv_h = nc.dram_tensor("wv", [NVD, HC * 128, VD], F32, kind="ExternalInput")
    wo_h = nc.dram_tensor("wo", [HC, HC * 128, 128], F32, kind="ExternalInput")
    bqk_h = nc.dram_tensor("bqk", [128, 2 * NH], F32, kind="ExternalInput")
    bo_h = nc.dram_tensor("bo", [128, HC], F32, kind="ExternalInput")
    bv_h = nc.dram_tensor("bv", [1, H], F32, kind="ExternalInput")
    wr_h = nc.dram_tensor("wr", [128, HC * 8], F32, kind="ExternalInput")
    wd_h = nc.dram_tensor("wd", [128, HC * EI], F32, kind="ExternalInput")
    wu_h = nc.dram_tensor("wu", [EI, H], F32, kind="ExternalInput")
    cst_h = nc.dram_tensor("cst", [128, 256], BF16, kind="ExternalInput")
    cstf_h = nc.dram_tensor("cstf", [128, 128], F32, kind="ExternalInput")
    outT = nc.dram_tensor("outT", [H, T], F32, kind="ExternalOutput")

    _scr_kind = "ExternalOutput" if debug else "Internal"
    TBo = cfg.TBo
    qscr = nc.dram_tensor("qscr", [NH, 128, T], BF16, kind=_scr_kind)
    kag_i = nc.dram_tensor("kag_i", [NH, 128, T], BF16)
    kag_o = nc.dram_tensor("kag_o", [2, NH, 128, T], BF16, kind=_scr_kind)
    vag_i = nc.dram_tensor("vag_i", [TBo, NH, 128, 128], BF16)
    vag_o = nc.dram_tensor("vag_o", [2, TBo, NH, 128, 128], BF16, kind=_scr_kind)
    yscr = nc.dram_tensor("yscr", [H, T], F32, kind=_scr_kind)
    aodump = nc.dram_tensor("aodump", [128, HC * T], BF16, kind="ExternalOutput") if debug else None
    r1dump = nc.dram_tensor("r1dump", [NTW, 128, HC * TW], F32, kind="ExternalOutput") if debug else None
    accdump = nc.dram_tensor("accdump", [NH, T // 128, 128, 8], F32, kind="ExternalOutput") if debug else None
    stdump = nc.dram_tensor("stdump", [NTW, 128, 6 * JT * JW], F32, kind="ExternalOutput") if debug else None

    es = ExitStack()
    sbt = lambda name, shape, dt: es.enter_context(nc.sbuf_tensor(name, shape, dt))

    cst = sbt("cst_sb", [128, 256], BF16)
    cstf = sbt("cstf_sb", [128, 128], F32)
    sel_sb = sbt("sel_sb", [8, 128], BF16)
    ones_sb = sbt("ones_sb", [128, 128], BF16)
    onesf_sb = sbt("onesf_sb", [1, 128], F32)
    bqk_sb = sbt("bqk_sb", [128, 2 * NH], F32)
    bo_sb = sbt("bo_sb", [128, HC], F32)
    bv_sb = sbt("bv_sb", [1, H], BF16)
    wr_sb = sbt("wr_sb", [128, HC * 8], BF16)
    wd_sb = sbt("wd_sb", [128, HC * EI], BF16)
    wu_sb = sbt("wu_sb", [128, H], BF16)

    NTB = T // 128
    RSTR = JT * JW
    def _al(x):
        return (x + 63) // 64 * 64
    A_XT = 0
    A_wv = _al(A_XT + HC * T)
    A_wqk = _al(A_wv + HC * VD)
    A_kst = _al(A_wqk + 2 * HC * 128)
    A_vst = _al(A_kst + 2 * 512)
    p1_end = _al(A_vst + 2 * 512)
    GQ = min(4, T // 128)
    QW = GQ * 128
    NGQ = T // QW
    SH = S // 2
    A_q2 = 0
    A_k2 = _al(A_q2 + 2 * T)
    A_v2 = _al(A_k2 + 2 * S)
    A_PT = _al(A_v2 + 2 * S)
    A_P2s = _al(A_PT + KC * QW)
    A_P2q = _al(A_P2s + 2 * S)
    A_junk = _al(A_P2q + 2 * S)
    NQS_ = T // 128
    A_accQ = _al(A_junk + 2048)
    A_mnegs = _al(A_accQ + 2 * 4 * NH * NQS_)
    A_acc = _al(A_mnegs + 2 * NH * NQS_)
    A_lcol = _al(A_acc + 64)
    A_lirow = _al(A_lcol + 64)
    A_lrep = _al(A_lirow + 1536)
    p2_end = _al(A_lrep + 1024)
    A_r1 = 0
    A_xres = _al(A_r1 + 2 * HC * TW)
    A_yst = _al(A_xres + 2 * 4 * 512)
    A_rb = _al(A_yst + 2 * 4 * 512)
    A_sq = _al(A_rb + 2 * 512)
    A_stat = _al(A_sq + 2 * 512)
    A_wo = _al(A_stat + 2 * 6 * RSTR)
    wo_end = _al(A_wo + 2 * HC * 128)
    A_lgf = A_wo
    A_gT = _al(A_lgf + 2 * T)
    A_gtok = _al(A_gT + T)
    A_gtk2 = _al(A_gtok + 2 * NTB * 8)
    A_gsc = _al(A_gtk2 + 2 * NTB * 8)
    A_gsc2 = _al(A_gsc + 2 * NTB)
    A_gtk3 = _al(A_gsc2 + 2 * NTB)
    A_ag = _al(A_gtk3 + 2 * NTB * 8)
    p34_end = max(_al(A_ag + 4 * 512), wo_end)
    AO_cols = HC * T
    AO_off = max(p2_end, p34_end)
    AR_COLS = _al(max(AO_off + AO_cols, p1_end) + 64)
    arena = sbt("arena", [128, AR_COLS], BF16)
    assert AR_COLS * 2 <= 190 * 1024, f"arena {AR_COLS * 2} B/partition too big"

    XT = arena[:, A_XT:A_XT + HC * T]
    wv_sb = arena[:, A_wv:A_wv + HC * VD]
    wqk0 = A_wqk
    kst = arena[:, A_kst:A_kst + 2 * 512]
    vst = arena[:, A_vst:A_vst + 2 * 512]
    AO = arena[:, AO_off:AO_off + AO_cols]
    q2o, k2o, v2o = A_q2, A_k2, A_v2
    PTsl = arena[:, A_PT:A_PT + KC * QW]
    P2s = arena[:, A_P2s:A_P2s + 2 * S]
    P2q = arena[:, A_P2q:A_P2q + 2 * S]
    junk = arena[:, A_junk:A_junk + 2048]
    accQ = arena[:, A_accQ:A_accQ + 2 * 4 * NH * NQS_].bitcast(F32)
    accQI = arena[:, A_accQ:A_accQ + 2 * 4 * NH * NQS_].bitcast(mybir.dt.int32)
    mnegs = arena[:, A_mnegs:A_mnegs + 2 * NH * NQS_].bitcast(F32)
    accs = arena[:, A_acc:A_acc + 64].bitcast(F32)
    accsI = arena[:, A_acc:A_acc + 64].bitcast(mybir.dt.int32)
    lbuf = arena[:, A_lcol:A_lcol + 64].bitcast(F32)
    lirow = arena[0:1, A_lirow:A_lirow + 1024].bitcast(F32)
    lirow_bf = arena[0:1, A_lirow + 1024:A_lirow + 1536]
    lrep_sb = arena[:, A_lrep:A_lrep + 1024].bitcast(F32)
    r1v = arena[:, A_r1:A_r1 + 2 * HC * TW].bitcast(F32)
    x_res = arena[:, A_xres:A_xres + 2 * 4 * 512].bitcast(F32)
    ystage = arena[:, A_yst:A_yst + 2 * 4 * 512].bitcast(F32)
    rbst = arena[:, A_rb:A_rb + 2 * 512]
    sqst = arena[:, A_sq:A_sq + 2 * 512]
    stat = arena[:, A_stat:A_stat + 2 * 6 * RSTR].bitcast(F32)
    wo0 = A_wo
    lgf = arena[0:8, A_lgf:A_lgf + 2 * T].bitcast(F32)
    gT_sb = arena[0:8, A_gT:A_gT + T]
    gtok = arena[:, A_gtok:A_gtok + 2 * NTB * 8].bitcast(F32)
    gtk2 = arena[:, A_gtk2:A_gtk2 + 2 * NTB * 8].bitcast(F32)
    gsc = arena[:, A_gsc:A_gsc + 2 * NTB].bitcast(F32)
    gsc2 = arena[:, A_gsc2:A_gsc2 + 2 * NTB].bitcast(F32)
    gtk3 = arena[:, A_gtk3:A_gtk3 + 2 * NTB * 8].bitcast(F32)
    agst = arena[:, A_ag:A_ag + 4 * 512]

    psA = es.enter_context(nc.psum_tensor("psA", [128, 2048], F32))
    psB = es.enter_context(nc.psum_tensor("psB", [128, 2048], F32))

    sems = {}
    for s in ALL_SEMS:
        sems[s] = es.enter_context(nc.semaphore("sem_" + s))
    block = es.enter_context(nc.Block())

    b_cst, b_cstf, b_sel, b_ones = Buf("cst"), Buf("cstf"), Buf("sel"), Buf("ones")
    b_onesf = Buf("onesf")
    b_bqk, b_bo, b_bv = Buf("bqk"), Buf("bo"), Buf("bv")
    b_wr, b_wd, b_wu = Buf("wr"), Buf("wd"), Buf("wu")
    b_XT = Buf("XT")
    b_wv = Buf("wv")
    b_wqk = [Buf("wqk%d" % i) for i in range(2)]
    b_kst = [Buf("kst%d" % i) for i in range(2)]
    b_vst = [Buf("vst%d" % i) for i in range(2)]
    b_psA = [Buf("psA%d" % i) for i in range(4)]
    b_psB = [Buf("psB%d" % i) for i in range(4)]
    e_kscr = [[] for _ in range(NH)]
    e_qscr = [[] for _ in range(NH)]
    e_vscr = [[] for _ in range(NH)]
    b_k2 = [Buf("k2_%d" % i) for i in range(2)]
    b_q2 = [Buf("q2_%d" % i) for i in range(2)]
    b_v2 = [Buf("v2_%d" % i) for i in range(2)]
    b_P2s = [[Buf("P2s%d_%d" % (i, hh)) for hh in range(2)] for i in range(2)]
    b_P2q = [[Buf("P2q%d_%d" % (i, hh)) for hh in range(2)] for i in range(2)]
    b_accQ = Buf("accQ")
    b_mnegs = Buf("mnegs")
    b_PT = Buf("PT")
    b_junk = Buf("junk")
    b_acc = [Buf("acc%d" % i) for i in range(2)]
    b_lcol = Buf("lcol")
    b_lirow = Buf("lirow")
    b_lrep = Buf("lrep")
    b_AO = [[Buf("AO_%d_%d" % (c, j)) for j in range(T // 512 if T >= 512 else 1)] for c in range(HC)]
    b_wo = [Buf("wo%d" % i) for i in range(2)]
    b_x = [Buf("x%d" % i) for i in range(4)]
    b_r1 = [[Buf("r1_%d_%d" % (o, j)) for j in range(JT)] for o in range(HC)]
    b_rb = [Buf("rb%d" % i) for i in range(2)]
    b_sq = [Buf("sq%d" % i) for i in range(2)]
    b_stat = [Buf("stat%d" % j) for j in range(4)]
    b_yst = [Buf("yst%d" % i) for i in range(2)]
    b_yst2 = [Buf("yst2_%d" % i) for i in range(2)]
    b_stat2 = [Buf("stat2_%d" % i) for i in range(2)]
    b_yscr = [[Buf("yscr_%d_%d" % (o, j)) for j in range(T // 512 if T >= 512 else 1)] for o in range(HC)]
    b_lgf, b_gtok, b_gtk2, b_gsc, b_gT = Buf("lgf"), Buf("gtok"), Buf("gtk2"), Buf("gsc"), Buf("gT")
    b_gsc2, b_gtk3 = Buf("gsc2"), Buf("gtk3")
    b_ag = [Buf("ag%d" % i) for i in range(4)]

    AOW = 512 if T >= 512 else T

    def aoslice(c, t0, w):
        return AO[:, c * T + t0:c * T + t0 + w]

    def aobufs(t0, w, cs=None):
        out = []
        for c in (range(HC) if cs is None else cs):
            for j in range(t0 // AOW, (t0 + w + AOW - 1) // AOW):
                out.append(b_AO[c][j])
        return out

    g = "gpsimd"
    K.op(g, lambda h: h.dma_start(cst[:, :], cst_h[:, :]), writes=[b_cst], dma=True)
    K.op(g, lambda h: h.dma_start(cstf[:, :], cstf_h[:, :]), writes=[b_cstf], dma=True)
    K.op(g, lambda h: h.dma_start(sel_sb[:, :], cst_h[0:8, 128:256]), writes=[b_sel], dma=True)
    K.op(g, lambda h: h.dma_start(bqk_sb[:, :], bqk_h[:, :]), writes=[b_bqk], dma=True)
    K.op(g, lambda h: h.dma_start(bo_sb[:, :], bo_h[:, :]), writes=[b_bo], dma=True)
    K.op(g, lambda h: h.dma_start(bv_sb[:, :], bv_h[:, :]), writes=[b_bv], dma=True)
    K.op(g, lambda h: h.dma_start(wr_sb[:, :], wr_h[:, :]), writes=[b_wr], dma=True)
    K.op(g, lambda h: h.dma_start(wd_sb[:, :], wd_h[:, :]), writes=[b_wd], dma=True)
    K.op(g, lambda h: h.dma_start(wu_sb[:, :], wu_h[:, :]), writes=[b_wu], dma=True)
    for c in range(HC):
        K.op(g, (lambda c: lambda h: h.dma_start(
            XT[:, c * T:(c + 1) * T], xT[c * 128:(c + 1) * 128, :]))(c),
            writes=[b_XT], dma=True)
    K.op("vector", lambda h: h.memset(ones_sb[:, :], 1.0), writes=[b_ones])
    K.op("vector", lambda h: h.memset(onesf_sb[:, :], 1.0), writes=[b_onesf])

    ident = cst[:, 0:128]
    ident8f = cstf[0:8, 0:8]
    identf = cstf[:, :]

    for dt in range(NVD):
        K.op(g, (lambda dt: lambda h: h.dma_start(
            wv_sb[:, :].rearrange("p (c d) -> p c d", d=VD),
            wv_h[dt, :, :].rearrange("(c p) d -> p c d", p=128)))(dt),
            writes=[b_wv], dma=True)
        for tb in range(TBo):
            slot = (dt * TBo + tb) % 4
            def mmg(h, dt=dt, tb=tb, slot=slot):
                for c in range(HC):
                    h.matmul(psA[:, slot * 512:slot * 512 + VD],
                             XT[:, c * T + tb * 128:c * T + (tb + 1) * 128],
                             wv_sb[:, c * VD:(c + 1) * VD],
                             start=(c == 0), stop=False)
                return h.matmul(psA[:, slot * 512:slot * 512 + VD],
                                ones_sb[0:1, 0:128], bv_sb[:, dt * VD:(dt + 1) * VD],
                                start=False, stop=True)
            K.op("tensor", mmg, reads=[b_XT, b_wv, b_ones, b_bv], writes=[b_psA[slot]])
            vs = (dt * TBo + tb) % 2
            K.op("vector", (lambda slot, vs: lambda h: h.tensor_copy(
                vst[:, vs * 512:vs * 512 + VD], psA[:, slot * 512:slot * 512 + VD]))(slot, vs),
                reads=[b_psA[slot]], writes=[b_vst[vs]])
            nh_t = VD // 128
            def spill(h, dt=dt, tb=tb, vs=vs, nh_t=nh_t):
                return h.dma_start(vag_i[tb, dt * nh_t:(dt + 1) * nh_t, :, :].transpose([1, 0, 2]),
                                   vst[:, vs * 512:vs * 512 + VD].rearrange("p (j d) -> p j d", d=128))
            ev = K.op("sync", spill, reads=[b_vst[vs]], dma=True)
            e_vscr[0].append(ev)

    def qk_proj(which):
        span = T
        tile_w = min(512, span)
        ntile = span // tile_w
        wsrc = wk_h if which == "k" else wq_h
        scr = kag_i if which == "k" else qscr
        escr = e_kscr if which == "k" else e_qscr
        bias0 = NH if which == "k" else 0
        for n in range(NH):
            slot = n % 2
            K.op(g, (lambda n, slot, wsrc=wsrc: lambda h: h.dma_start(
                arena[:, wqk0 + slot * HC * 128: wqk0 + (slot + 1) * HC * 128]
                .rearrange("p (c o) -> p c o", o=128),
                wsrc[n, :, :].rearrange("(c p) o -> p c o", p=128)))(n, slot),
                writes=[b_wqk[slot]], dma=True)
            for j in range(ntile):
                idx = n * ntile + j
                ps = idx % 4
                def mmg(h, n=n, j=j, slot=slot, ps=ps, tile_w=tile_w):
                    for c in range(HC):
                        mm = h.matmul(psB[:, ps * 512:ps * 512 + tile_w],
                                      arena[:, wqk0 + slot * HC * 128 + c * 128:
                                            wqk0 + slot * HC * 128 + (c + 1) * 128],
                                      XT[:, c * T + j * tile_w:c * T + (j + 1) * tile_w],
                                      start=(c == 0), stop=(c == HC - 1))
                    return mm
                K.op("tensor", mmg, reads=[b_XT, b_wqk[slot]], writes=[b_psB[ps]])
                ks = idx % 2
                K.op("scalar", (lambda ps, ks, n, tile_w: lambda h: h.activation(
                    kst[:, ks * 512:ks * 512 + tile_w], psB[:, ps * 512:ps * 512 + tile_w],
                    AF.Identity, bias=bqk_sb[:, bias0 + n:bias0 + n + 1], scale=1.0))(ps, ks, n, tile_w),
                    reads=[b_psB[ps], b_bqk], writes=[b_kst[ks]])
                ev = K.op("sync", (lambda n, j, ks, tile_w, scr=scr: lambda h: h.dma_start(
                    scr[n, :, j * tile_w:(j + 1) * tile_w],
                    kst[:, ks * 512:ks * 512 + tile_w]))(n, j, ks, tile_w),
                    reads=[b_kst[ks]], dma=True)
                escr[n].append(ev)
    qk_proj("k")

    pair_groups = [[i, i + 1] for i in range(0, cfg.n_cores, 2)]
    e_kag = K.op(g, (lambda: lambda h: h.collective_compute(
        "AllGather", AluOpType.bypass, replica_groups=pair_groups,
        ins=[kag_i.ap().opt()], outs=[kag_o.ap().opt()]))(),
        waits=[e for n in range(NH) for e in e_kscr[n]])
    e_vag = K.op(g, (lambda: lambda h: h.collective_compute(
        "AllGather", AluOpType.bypass, replica_groups=pair_groups,
        ins=[vag_i.ap().opt()], outs=[vag_o.ap().opt()]))(),
        waits=e_vscr[0])
    qk_proj("q")

    K.barrier()
    nSA = (SH + 511) // 512
    NBA = max(1, KC // 4)
    psA_banks = [b_psA[i] for i in range(min(4, nSA))]
    psB_banks = [b_psB[i] for i in range(min(4, nSA))]
    NQS = T // 128
    import functools

    nstr = max(1, KC // 4)
    NV = NH * NQS
    SSW = (nstr + 2) * 128
    for n in range(NH):
        sl = n % 2
        K.op("sync", (lambda n, sl: lambda h: h.dma_start(
            arena[:, q2o + sl * T:q2o + (sl + 1) * T], qscr[n, :, :]))(n, sl),
            waits=e_qscr[n], writes=[b_q2[sl]], dma=True)
        for half in range(2):
            K.op("sync", (lambda n, sl, half: lambda h: h.dma_start(
                arena[:, k2o + sl * S + half * T:k2o + sl * S + (half + 1) * T],
                kag_o[half, n, :, :]))(n, sl, half),
                waits=[e_kag], writes=[b_k2[sl]], dma=True)
        for qi in range(NQS):
            pp = qi % 2
            psS = psA if pp == 0 else psB
            bps = b_psA if pp == 0 else b_psB
            hn = max(1, nstr // 2)
            def smp(h, n=n, qi=qi, sl=sl, psS=psS, hn=hn):
                k2v = arena[:, k2o + sl * S:k2o + (sl + 1) * S]
                qv = arena[:, q2o + sl * T + qi * 128:q2o + sl * T + (qi + 1) * 128]
                kstr = k2v.rearrange("p (a b) -> p a b", b=512)[:, :, 0:128]
                h.matmul(psS[:, 0:hn * 128], qv, kstr[:, 0:hn, :], start=True, stop=True)
                if nstr > hn:
                    h.matmul(psS[:, 512:512 + (nstr - hn) * 128], qv, kstr[:, hn:nstr, :],
                             start=True, stop=True)
                return h.matmul(psS[:, 1024:1024 + 256],
                                qv, k2v.rearrange("p (h b) -> p h b", h=2)
                                [:, :, qi * 128:(qi + 1) * 128],
                                start=True, stop=True)
            K.op("tensor", smp, reads=[b_k2[sl], b_q2[sl]], writes=[bps[0], bps[1], bps[2]])
            c0 = n * NQS + qi
            if nstr == 8:
                K.op("scalar", (lambda c0, psS: lambda h: h.activation(
                    junk[:, 0:1280], psS[:, 0:1280], AF.Exp, scale=SC / 4.0,
                    accum_out=accQ[:, c0:c0 + 1]))(c0, psS),
                    reads=[bps[0], bps[1], bps[2]], writes=[b_junk, b_accQ])
            else:
                K.op("scalar", (lambda c0, psS, hn: lambda h: h.activation(
                    junk[:, 0:hn * 128], psS[:, 0:hn * 128], AF.Exp, scale=SC / 4.0,
                    accum_out=accQ[:, NV + c0:NV + c0 + 1]))(c0, psS, hn),
                    reads=[bps[0], bps[1], bps[2]], writes=[b_junk, b_accQ])
                K.op("scalar", (lambda c0, psS: lambda h: h.activation(
                    junk[:, 0:256], psS[:, 1024:1280], AF.Exp, scale=SC / 4.0,
                    accum_out=accQ[:, 2 * NV + c0:2 * NV + c0 + 1]))(c0, psS),
                    reads=[bps[0], bps[1], bps[2]], writes=[b_junk, b_accQ])
                K.op("vector", (lambda c0: lambda h: h.tensor_tensor(
                    accQ[:, c0:c0 + 1], accQ[:, NV + c0:NV + c0 + 1],
                    accQ[:, 2 * NV + c0:2 * NV + c0 + 1], op=AluOpType.add))(c0),
                    reads=[b_accQ], writes=[b_accQ])
        K.op("vector", (lambda n: lambda h: h.tensor_scalar(
            accQI[:, NH * NQS + n * NQS:NH * NQS + (n + 1) * NQS],
            accQI[:, n * NQS:(n + 1) * NQS], 23, None,
            op0=AluOpType.arith_shift_right))(n),
            reads=[b_accQ], writes=[b_accQ])
        K.op("vector", (lambda n: lambda h: h.tensor_copy(
            accQ[:, 2 * NH * NQS + n * NQS:2 * NH * NQS + (n + 1) * NQS],
            accQI[:, NH * NQS + n * NQS:NH * NQS + (n + 1) * NQS]))(n),
            reads=[b_accQ], writes=[b_accQ])
        K.op("vector", (lambda n: lambda h: h.tensor_scalar(
            mnegs[:, n * NQS:(n + 1) * NQS],
            accQ[:, 2 * NH * NQS + n * NQS:2 * NH * NQS + (n + 1) * NQS],
            -127.0, -2.0 * 0.6931471805599453,
            op0=AluOpType.add, op1=AluOpType.mult))(n),
            reads=[b_accQ], writes=[b_mnegs])

    pend = []

    def flush_pending():
        for f in pend:
            f()
        pend.clear()

    for n in range(NH):
        sl = n % 2
        K.op("gpsimd", (lambda n, sl: lambda h: h.dma_start(
            arena[:, q2o + sl * T:q2o + (sl + 1) * T], qscr[n, :, :]))(n, sl),
            waits=e_qscr[n], writes=[b_q2[sl]], dma=True)
        for half in range(2):
            K.op("gpsimd", (lambda n, sl, half: lambda h: h.dma_start(
                arena[:, k2o + sl * S + half * T:k2o + sl * S + (half + 1) * T],
                kag_o[half, n, :, :]))(n, sl, half),
                waits=[e_kag], writes=[b_k2[sl]], dma=True)
            K.op("gpsimd", (lambda n, sl, half: lambda h: h.dma_start(
                arena[:, v2o + sl * S + half * T:v2o + sl * S + (half + 1) * T]
                .rearrange("p (tb d) -> p tb d", d=128),
                vag_o[half, :, n, :, :].transpose([1, 0, 2])))(n, sl, half),
                waits=[e_vag], writes=[b_v2[sl]], dma=True)
        for qg in range(NGQ):
            for qs in range(GQ):
                qi = qg * GQ + qs
                p = qi % 2
                def smm(h, n=n, qi=qi, sl=sl, half=0):
                    for m in range(nSA):
                        w = min(512, SH - m * 512)
                        ps = psA if half == 0 else psB
                        mm = h.matmul(ps[:, m * 512:m * 512 + w],
                                      arena[:, q2o + sl * T + qi * 128:q2o + sl * T + (qi + 1) * 128],
                                      arena[:, k2o + sl * S + half * SH + m * 512:
                                            k2o + sl * S + half * SH + m * 512 + w],
                                      start=True, stop=True)
                    return mm
                K.op("tensor", functools.partial(smm, half=0),
                     reads=[b_k2[sl], b_q2[sl]], writes=psA_banks)
                K.op("tensor", functools.partial(smm, half=1),
                     reads=[b_k2[sl], b_q2[sl]], writes=psB_banks)
                K.op("scalar", (lambda n, qi, p: lambda h: h.activation(
                    P2s[:, p * S:p * S + SH], psA[:, 0:SH], AF.Exp, scale=SC / 2.0,
                    bias=mnegs[:, n * NQS + qi:n * NQS + qi + 1]))(n, qi, p),
                    reads=psA_banks + [b_mnegs], writes=[b_P2s[p][0]])
                K.op("vector", (lambda p: lambda h: h.tensor_tensor(
                    P2q[:, p * S:p * S + SH], P2s[:, p * S:p * S + SH],
                    P2s[:, p * S:p * S + SH], op=AluOpType.mult))(p),
                    reads=[b_P2s[p][0]], writes=[b_P2q[p][0]])
                K.op("scalar", (lambda n, qi, p: lambda h: h.activation(
                    P2s[:, p * S + SH:p * S + 2 * SH], psB[:, 0:SH], AF.Exp, scale=SC / 2.0,
                    bias=mnegs[:, n * NQS + qi:n * NQS + qi + 1]))(n, qi, p),
                    reads=psB_banks + [b_mnegs], writes=[b_P2s[p][1]])
                K.op("vector", (lambda p: lambda h: h.tensor_tensor(
                    P2q[:, p * S + SH:p * S + 2 * SH], P2s[:, p * S + SH:p * S + 2 * SH],
                    P2s[:, p * S + SH:p * S + 2 * SH], op=AluOpType.mult))(p),
                    reads=[b_P2s[p][1]], writes=[b_P2q[p][1]])
                flush_pending()
                K.op("sync", (lambda p, qs: lambda h: h.dma_start_transpose(
                    PTsl[:, :].rearrange("p (j q) -> p j q", q=QW)
                    [:, :, qs * 128:(qs + 1) * 128],
                    P2q[:, p * S:(p + 1) * S]))(p, qs),
                    reads=[b_P2q[p][0], b_P2q[p][1]], writes=[b_PT],
                    dma=True, dma_sem="xd")
                if qs == GQ - 1:
                    def def_G(n=n, sl=sl, qg=qg):
                        def pv(h, sl=sl):
                            for kt in range(KC):
                                mm = h.matmul(psA[:, 1024:1024 + QW],
                                              arena[:, v2o + sl * S + kt * 128:v2o + sl * S + (kt + 1) * 128],
                                              PTsl[:, kt * QW:(kt + 1) * QW],
                                              start=(kt == 0), stop=(kt == KC - 1))
                            return mm
                        K.op("tensor", pv, reads=[b_v2[sl], b_PT], writes=[b_psA[2]])
                        def lmm(h):
                            for kt in range(KC):
                                mm = h.matmul(psA[:, 1536:1536 + QW], ones_sb[:, 0:128],
                                              PTsl[:, kt * QW:(kt + 1) * QW],
                                              start=(kt == 0), stop=(kt == KC - 1))
                            return mm
                        K.op("tensor", lmm, reads=[b_ones, b_PT], writes=[b_psA[3]])
                        K.op("vector", lambda h: h.reciprocal(lrep_sb[:, 0:QW], psA[:, 1536:1536 + QW]),
                             reads=[b_psA[3]], writes=[b_lrep])
                        K.op("vector", (lambda n, qg: lambda h: h.tensor_tensor(
                            aoslice(n, qg * QW, QW), psA[:, 1024:1024 + QW], lrep_sb[:, 0:QW],
                            op=AluOpType.mult))(n, qg),
                            reads=[b_psA[2], b_lrep], writes=aobufs(qg * QW, QW, [n]))
                    pend.append(def_G)
    flush_pending()

    K.barrier()
    for tw in range(NTW):
        t0 = tw * TW
        for o in range(HC):
            i2 = tw * HC + o
            ws = i2 % 2
            K.op(g, (lambda o, ws: lambda h: h.dma_start(
                arena[:, wo0 + ws * HC * 128:wo0 + (ws + 1) * HC * 128]
                .rearrange("p (c q) -> p c q", q=128),
                wo_h[o, :, :].rearrange("(c p) q -> p c q", p=128)))(o, ws),
                writes=[b_wo[ws]], dma=True)
            for j in range(JT):
                gi = i2 * JT + j
                ps = gi % 2
                xs = gi % 4
                K.op("sync", (lambda o, j, t0, xs: lambda h: h.dma_start(
                    x_res[:, xs * 512:xs * 512 + JW],
                    xT[o * 128:(o + 1) * 128, t0 + j * JW:t0 + (j + 1) * JW]))(o, j, t0, xs),
                    writes=[b_x[xs]], dma=True)
                def pmm(h, o=o, j=j, ws=ws, ps=ps, t0=t0):
                    for c in range(HC):
                        mm = h.matmul(psA[:, ps * 512:ps * 512 + JW],
                                      arena[:, wo0 + ws * HC * 128 + c * 128:
                                            wo0 + ws * HC * 128 + (c + 1) * 128],
                                      aoslice(c, t0 + j * JW, JW),
                                      start=(c == 0), stop=(c == HC - 1))
                    return mm
                K.op("tensor", pmm, reads=[b_wo[ws]] + aobufs(t0 + j * JW, JW),
                     writes=[b_psA[ps]])
                K.op("vector", (lambda o, j, ps, xs: lambda h: h.scalar_tensor_tensor(
                    r1v[:, o * TW + j * JW:o * TW + (j + 1) * JW],
                    psA[:, ps * 512:ps * 512 + JW], bo_sb[:, o:o + 1],
                    x_res[:, xs * 512:xs * 512 + JW],
                    op0=AluOpType.add, op1=AluOpType.add))(o, j, ps, xs),
                    reads=[b_psA[ps], b_bo, b_x[xs]], writes=[b_r1[o][j]])
                rs_ = gi % 2
                K.op("vector", (lambda o, j, rs_: lambda h: h.tensor_copy(
                    rbst[:, rs_ * 512:rs_ * 512 + JW],
                    r1v[:, o * TW + j * JW:o * TW + (j + 1) * JW]))(o, j, rs_),
                    reads=[b_r1[o][j]], writes=[b_rb[rs_]])
                K.op("vector", (lambda rs_: lambda h: h.tensor_tensor(
                    sqst[:, rs_ * 512:rs_ * 512 + JW], rbst[:, rs_ * 512:rs_ * 512 + JW],
                    rbst[:, rs_ * 512:rs_ * 512 + JW], op=AluOpType.mult))(rs_),
                    reads=[b_rb[rs_]], writes=[b_sq[rs_]])
                def stmm(h, o=o, j=j, rs_=rs_):
                    h.matmul(psB[:, j * 512:j * 512 + JW], ones_sb[:, 0:128],
                             rbst[:, rs_ * 512:rs_ * 512 + JW],
                             start=(o == 0), stop=(o == HC - 1))
                    return h.matmul(psB[:, 1024 + j * 512:1024 + j * 512 + JW],
                                    ones_sb[:, 0:128], sqst[:, rs_ * 512:rs_ * 512 + JW],
                                    start=(o == 0), stop=(o == HC - 1))
                K.op("tensor", stmm, reads=[b_rb[rs_], b_sq[rs_], b_ones],
                     writes=[b_psB[j], b_psB[2 + j]])
        if debug:
            K.op("sync", (lambda tw: lambda h: h.dma_start(
                r1dump[tw, :, :], r1v[:, 0:HC * TW]))(tw),
                reads=[b for row in b_r1 for b in row], dma=True)
        for j in range(JT):
            mu = stat[:, 0 * RSTR + j * JW:0 * RSTR + j * JW + JW]
            rs = stat[:, 1 * RSTR + j * JW:1 * RSTR + j * JW + JW]
            mus = stat[:, 2 * RSTR + j * JW:2 * RSTR + j * JW + JW]
            scr2 = stat[:, 3 * RSTR + j * JW:3 * RSTR + j * JW + JW]
            bmu, brs, bmus, bscr = b_stat[0], b_stat[1], b_stat[2], b_stat[3]
            K.op("vector", (lambda mu, j: lambda h: h.tensor_scalar(
                mu, psB[:, j * 512:j * 512 + JW], IH, None, op0=AluOpType.mult))(mu, j),
                reads=[b_psB[j]], writes=[bmu])
            K.op("vector", (lambda scr2, j: lambda h: h.tensor_scalar(
                scr2, psB[:, 1024 + j * 512:1024 + j * 512 + JW], IH, cfg.EPS,
                op0=AluOpType.mult, op1=AluOpType.add))(scr2, j),
                reads=[b_psB[2 + j]], writes=[bscr])
            var2 = stat[:, 4 * RSTR + j * JW:4 * RSTR + j * JW + JW]
            rin2 = stat[:, 5 * RSTR + j * JW:5 * RSTR + j * JW + JW]
            bvar, brin = b_stat2[0], b_stat2[1]
            K.op("vector", (lambda rs, mu: lambda h: h.tensor_tensor(
                rs, mu, mu, op=AluOpType.mult))(rs, mu), reads=[bmu], writes=[brs])
            K.op("vector", (lambda var2, scr2, rs: lambda h: h.tensor_tensor(
                var2, scr2, rs, op=AluOpType.subtract))(var2, scr2, rs),
                reads=[brs, bscr], writes=[bvar])
            K.op("vector", (lambda rin2, var2: lambda h: h.reciprocal(rin2, var2))(rin2, var2),
                reads=[bvar], writes=[brin])
            K.op("scalar", (lambda rs, rin2: lambda h: h.activation(rs, rin2, AF.Sqrt))(rs, rin2),
                reads=[brin], writes=[brs])
            K.op("vector", (lambda mus, mu, rs: lambda h: h.tensor_tensor(
                mus, mu, rs, op=AluOpType.mult))(mus, mu, rs), reads=[bmu, brs], writes=[bmus])
            if debug and j == JT - 1:
                K.op("sync", (lambda tw: lambda h: h.dma_start(
                    stdump[tw, :, :], stat[:, 0:6 * RSTR]))(tw),
                    reads=[b_stat[0], b_stat[1], b_stat[2], b_stat[3],
                           b_stat2[0], b_stat2[1]], dma=True)
            for o in range(HC):
                ys = (o * JT + j) % 2
                K.op("vector", (lambda o, j, ys, rs: lambda h: h.tensor_tensor(
                    ystage[:, ys * 512:ys * 512 + JW],
                    r1v[:, o * TW + j * JW:o * TW + (j + 1) * JW], rs,
                    op=AluOpType.mult))(o, j, ys, rs),
                    reads=[b_r1[o][j], brs], writes=[b_yst[ys]])
                K.op("vector", (lambda ys, mus: lambda h: h.tensor_tensor(
                    ystage[:, (2 + ys) * 512:(2 + ys) * 512 + JW],
                    ystage[:, ys * 512:ys * 512 + JW],
                    mus, op=AluOpType.subtract))(ys, mus),
                    reads=[bmus, b_yst[ys]], writes=[b_yst2[ys]])
                K.op("scalar", (lambda o, j, ys, t0: lambda h: h.activation(
                    aoslice(o, t0 + j * JW, JW), ystage[:, (2 + ys) * 512:(2 + ys) * 512 + JW],
                    AF.Copy))(o, j, ys, t0),
                    reads=[b_yst2[ys]], writes=aobufs(t0 + j * JW, JW, [o]))
                ev = K.op("sync", (lambda o, j, ys, t0: lambda h: h.dma_start(
                    yscr[o * 128:(o + 1) * 128, t0 + j * JW:t0 + (j + 1) * JW],
                    ystage[:, (2 + ys) * 512:(2 + ys) * 512 + JW]))(o, j, ys, t0),
                    reads=[b_yst2[ys]], dma=True)
                b_yscr[o][(t0 + j * JW) // (512 if T >= 512 else T)].w = ev

    K.barrier()
    for jt in range(T // QT):
        def rmm(h, jt=jt):
            for c in range(HC):
                mm = h.matmul(psA[0:8, jt * 512:jt * 512 + QT],
                              wr_sb[:, c * 8:(c + 1) * 8],
                              aoslice(c, jt * QT, QT),
                              start=(c == 0), stop=(c == HC - 1))
            return mm
        K.op("tensor", rmm, reads=[b_wr] + aobufs(jt * QT, QT), writes=[b_psA[jt % 4]])
        K.op("vector", (lambda jt: lambda h: h.tensor_copy(
            lgf[:, jt * QT:(jt + 1) * QT], psA[0:8, jt * 512:jt * 512 + QT]))(jt),
            reads=[b_psA[jt % 4]], writes=[b_lgf])
    for tb in range(NTB):
        K.op("tensor", (lambda tb: lambda h: h.transpose(
            psA[:, (tb % 4) * 8:(tb % 4) * 8 + 8],
            lgf[0:8, tb * 128:(tb + 1) * 128], ident8f))(tb),
            reads=[b_lgf, b_cstf], writes=[b_psA[0]])
        if tb % 4 == 3 or tb == NTB - 1:
            lo = (tb // 4) * 4
            cnt = tb - lo + 1
            K.op("vector", (lambda lo, cnt: lambda h: h.tensor_copy(
                gtok[:, lo * 8:(lo + cnt) * 8], psA[:, 0:cnt * 8]))(lo, cnt),
                reads=[b_psA[0]], writes=[b_gtok])
    seg = gtok[:, :].rearrange("p (s e) -> p s e", e=8)
    seg2 = gtk2[:, :].rearrange("p (s e) -> p s e", e=8)
    seg3 = gtk3[:, :].rearrange("p (s e) -> p s e", e=8)
    gb = lambda: gsc[:, :, None].broadcast_to((128, NTB, 8))
    gb2 = lambda: gsc2[:, :, None].broadcast_to((128, NTB, 8))
    K.op("vector", lambda h: h.tensor_reduce(gsc[:, :, None], seg, axis=mybir.AxisListType.X,
                                             op=AluOpType.max), reads=[b_gtok], writes=[b_gsc])
    K.op("vector", lambda h: h.tensor_tensor(seg2, seg, gb(), op=AluOpType.subtract),
         reads=[b_gtok, b_gsc], writes=[b_gtk2])
    K.op("scalar", lambda h: h.activation(gtok[:, :], gtk2[:, :], AF.Exp),
         reads=[b_gtk2], writes=[b_gtok])
    K.op("vector", lambda h: h.tensor_reduce(gsc[:, :, None], seg, axis=mybir.AxisListType.X,
                                             op=AluOpType.add), reads=[b_gtok], writes=[b_gsc])
    K.op("vector", lambda h: h.reciprocal(gsc2[:, :], gsc[:, :]), reads=[b_gsc], writes=[b_gsc2])
    K.op("vector", lambda h: h.tensor_tensor(seg2, seg, gb2(), op=AluOpType.mult),
         reads=[b_gsc2, b_gtok], writes=[b_gtk2])
    K.op("vector", lambda h: h.tensor_reduce(gsc[:, :, None], seg2, axis=mybir.AxisListType.X,
                                             op=AluOpType.max), reads=[b_gtk2], writes=[b_gsc])
    K.op("vector", lambda h: h.tensor_tensor(seg, seg2, gb(), op=AluOpType.is_ge),
         reads=[b_gtk2, b_gsc], writes=[b_gtok])
    K.op("vector", lambda h: h.scalar_tensor_tensor(gtk3[:, :], gtok[:, :], -2.0, gtk2[:, :],
                                                    op0=AluOpType.mult, op1=AluOpType.add),
         reads=[b_gtk2, b_gtok], writes=[b_gtk3])
    K.op("vector", lambda h: h.tensor_reduce(gsc[:, :, None], seg3, axis=mybir.AxisListType.X,
                                             op=AluOpType.max), reads=[b_gtk3], writes=[b_gsc])
    K.op("vector", lambda h: h.tensor_tensor(seg, seg2, gb(), op=AluOpType.is_ge),
         reads=[b_gtk2, b_gsc], writes=[b_gtok])
    K.op("vector", lambda h: h.tensor_tensor(seg3, seg, seg2, op=AluOpType.mult),
         reads=[b_gtk2, b_gtok], writes=[b_gtk3])
    for tb in range(NTB):
        K.op("tensor", (lambda tb: lambda h: h.transpose(
            psA[0:8, 512 + (tb % 4) * 128:512 + (tb % 4) * 128 + 128],
            gtk3[:, tb * 8:(tb + 1) * 8], identf))(tb),
            reads=[b_gtk3, b_cstf], writes=[b_psA[1]])
        K.op("vector", (lambda tb: lambda h: h.tensor_copy(
            gT_sb[:, tb * 128:(tb + 1) * 128],
            psA[0:8, 512 + (tb % 4) * 128:512 + (tb % 4) * 128 + 128]))(tb),
            reads=[b_psA[1]], writes=[b_gT])

    for tw in range(NTW):
        t0 = tw * TW
        for j in range(JT):
            jj = tw * JT + j
            K.op("tensor", (lambda t0, j: lambda h: h.matmul(
                psB[:, 1536:1536 + JW], sel_sb[:, :], gT_sb[:, t0 + j * JW:t0 + (j + 1) * JW],
                start=True, stop=True))(t0, j),
                reads=[b_sel, b_gT], writes=[b_psB[3]])
            def dmm(h, t0=t0, j=j):
                for c in range(HC):
                    mm = h.matmul(psB[:, 1024:1024 + JW], wd_sb[:, c * EI:(c + 1) * EI][:, 0:128],
                                  aoslice(c, t0 + j * JW, JW), start=(c == 0), stop=(c == HC - 1))
                return mm
            K.op("tensor", dmm, reads=[b_wd] + aobufs(t0 + j * JW, JW), writes=[b_psB[2]])
            sb_ = jj % 2
            K.op("scalar", (lambda sb_: lambda h: h.activation(
                agst[:, sb_ * 512:sb_ * 512 + JW], psB[:, 1024:1024 + JW], AF.Silu))(sb_),
                reads=[b_psB[2]], writes=[b_ag[sb_]])
            K.op("vector", (lambda sb_: lambda h: h.tensor_tensor(
                agst[:, 1024 + sb_ * 512:1024 + sb_ * 512 + JW],
                agst[:, sb_ * 512:sb_ * 512 + JW], psB[:, 1536:1536 + JW],
                op=AluOpType.mult))(sb_),
                reads=[b_ag[sb_], b_psB[3]], writes=[b_ag[2 + sb_]])
            for o in range(HC):
                oi = o * JT + j
                ps = oi % 2
                xs = (tw * HC * JT + o * JT + j) % 2
                K.op("sync", (lambda o, j, t0, xs: lambda h: h.dma_start(
                    x_res[:, (2 + xs) * 512:(2 + xs) * 512 + JW],
                    yscr[o * 128:(o + 1) * 128, t0 + j * JW:t0 + (j + 1) * JW]))(o, j, t0, xs),
                    reads=[b_yscr[o][(t0 + j * JW) // (512 if T >= 512 else T)]],
                    writes=[b_x[2 + xs]], dma=True)
                K.op("tensor", (lambda o, sb_, ps: lambda h: h.matmul(
                    psB[:, ps * 512:ps * 512 + JW], wu_sb[:, o * 128:(o + 1) * 128],
                    agst[:, 1024 + sb_ * 512:1024 + sb_ * 512 + JW],
                    start=True, stop=True))(o, sb_, ps),
                    reads=[b_wu, b_ag[2 + sb_]], writes=[b_psB[ps]])
                K.op("vector", (lambda o, j, ps, xs: lambda h: h.tensor_tensor(
                    r1v[:, o * TW + j * JW:o * TW + (j + 1) * JW],
                    psB[:, ps * 512:ps * 512 + JW], x_res[:, (2 + xs) * 512:(2 + xs) * 512 + JW],
                    op=AluOpType.add))(o, j, ps, xs),
                    reads=[b_psB[ps], b_x[2 + xs]], writes=[b_r1[o][j]])
                rs_ = oi % 2
                K.op("vector", (lambda o, j, rs_: lambda h: h.tensor_copy(
                    rbst[:, rs_ * 512:rs_ * 512 + JW],
                    r1v[:, o * TW + j * JW:o * TW + (j + 1) * JW]))(o, j, rs_),
                    reads=[b_r1[o][j]], writes=[b_rb[rs_]])
                K.op("vector", (lambda rs_: lambda h: h.tensor_tensor(
                    sqst[:, rs_ * 512:rs_ * 512 + JW], rbst[:, rs_ * 512:rs_ * 512 + JW],
                    rbst[:, rs_ * 512:rs_ * 512 + JW], op=AluOpType.mult))(rs_),
                    reads=[b_rb[rs_]], writes=[b_sq[rs_]])
                def stmm(h, o=o, j=j, rs_=rs_):
                    h.matmul(psA[:, j * 512:j * 512 + JW], ones_sb[:, 0:128],
                             rbst[:, rs_ * 512:rs_ * 512 + JW],
                             start=(o == 0), stop=(o == HC - 1))
                    return h.matmul(psA[:, 1024 + j * 512:1024 + j * 512 + JW],
                                    ones_sb[:, 0:128], sqst[:, rs_ * 512:rs_ * 512 + JW],
                                    start=(o == 0), stop=(o == HC - 1))
                K.op("tensor", stmm, reads=[b_rb[rs_], b_sq[rs_], b_ones],
                     writes=[b_psA[j], b_psA[2 + j]])
        for j in range(JT):
            mu = stat[:, 0 * RSTR + j * JW:0 * RSTR + j * JW + JW]
            rs = stat[:, 1 * RSTR + j * JW:1 * RSTR + j * JW + JW]
            mus = stat[:, 2 * RSTR + j * JW:2 * RSTR + j * JW + JW]
            scr2 = stat[:, 3 * RSTR + j * JW:3 * RSTR + j * JW + JW]
            bmu, brs, bmus, bscr = b_stat[0], b_stat[1], b_stat[2], b_stat[3]
            K.op("vector", (lambda mu, j: lambda h: h.tensor_scalar(
                mu, psA[:, j * 512:j * 512 + JW], IH, None, op0=AluOpType.mult))(mu, j),
                reads=[b_psA[j]], writes=[bmu])
            K.op("vector", (lambda scr2, j: lambda h: h.tensor_scalar(
                scr2, psA[:, 1024 + j * 512:1024 + j * 512 + JW], IH, cfg.EPS,
                op0=AluOpType.mult, op1=AluOpType.add))(scr2, j),
                reads=[b_psA[2 + j]], writes=[bscr])
            var2 = stat[:, 4 * RSTR + j * JW:4 * RSTR + j * JW + JW]
            rin2 = stat[:, 5 * RSTR + j * JW:5 * RSTR + j * JW + JW]
            bvar, brin = b_stat2[0], b_stat2[1]
            K.op("vector", (lambda rs, mu: lambda h: h.tensor_tensor(
                rs, mu, mu, op=AluOpType.mult))(rs, mu), reads=[bmu], writes=[brs])
            K.op("vector", (lambda var2, scr2, rs: lambda h: h.tensor_tensor(
                var2, scr2, rs, op=AluOpType.subtract))(var2, scr2, rs),
                reads=[brs, bscr], writes=[bvar])
            K.op("vector", (lambda rin2, var2: lambda h: h.reciprocal(rin2, var2))(rin2, var2),
                reads=[bvar], writes=[brin])
            K.op("scalar", (lambda rs, rin2: lambda h: h.activation(rs, rin2, AF.Sqrt))(rs, rin2),
                reads=[brin], writes=[brs])
            K.op("vector", (lambda mus, mu, rs: lambda h: h.tensor_tensor(
                mus, mu, rs, op=AluOpType.mult))(mus, mu, rs), reads=[bmu, brs], writes=[bmus])
            for o in range(HC):
                ys = (o * JT + j) % 2
                K.op("vector", (lambda o, j, ys, rs: lambda h: h.tensor_tensor(
                    ystage[:, ys * 512:ys * 512 + JW],
                    r1v[:, o * TW + j * JW:o * TW + (j + 1) * JW], rs,
                    op=AluOpType.mult))(o, j, ys, rs),
                    reads=[b_r1[o][j], brs], writes=[b_yst[ys]])
                K.op("vector", (lambda ys, mus: lambda h: h.tensor_tensor(
                    ystage[:, (2 + ys) * 512:(2 + ys) * 512 + JW],
                    ystage[:, ys * 512:ys * 512 + JW],
                    mus, op=AluOpType.subtract))(ys, mus),
                    reads=[bmus, b_yst[ys]], writes=[b_yst2[ys]])
                evo = K.op("sync", (lambda o, j, ys, t0: lambda h: h.dma_start(
                    outT[o * 128:(o + 1) * 128, t0 + j * JW:t0 + (j + 1) * JW],
                    ystage[:, (2 + ys) * 512:(2 + ys) * 512 + JW]))(o, j, ys, t0),
                    reads=[b_yst2[ys]], dma=True)
    K.op("sync", lambda h: None, waits=[evo])
    K.mark(evo)

    K.finalize(nc, block, sems)
    es.close()
    return nc


def prep_inputs(cfg, x_bh, w_qkv, b_qkv, w_out, b_out, router_w, down_w, up_w, own_first=True):
    H, NH, HC, S, VD, NVD, EI = cfg.H, cfg.NH, cfg.HC, cfg.S, cfg.VD, cfg.NVD, cfg.EI
    f32 = np.float32
    d = {}
    d["xT"] = np.ascontiguousarray(x_bh.T.astype(f32))
    Wq, Wk, Wv = w_qkv[0:H], w_qkv[H:2 * H], w_qkv[2 * H:3 * H]
    d["wq"] = np.ascontiguousarray(
        Wq.reshape(NH, 128, H).transpose(0, 2, 1))
    d["wk"] = np.ascontiguousarray(Wk.reshape(NH, 128, H).transpose(0, 2, 1))
    d["wv"] = np.ascontiguousarray(
        Wv.T.reshape(H, NVD, VD).transpose(1, 0, 2))
    d["wo"] = np.ascontiguousarray(w_out.reshape(HC, 128, H).transpose(0, 2, 1))
    d["bqk"] = np.ascontiguousarray(
        np.concatenate([b_qkv[0:H].reshape(NH, 128).T, b_qkv[H:2 * H].reshape(NH, 128).T], axis=1)).astype(f32)
    d["bo"] = np.ascontiguousarray(b_out.reshape(HC, 128).T).astype(f32)
    d["bv"] = b_qkv[2 * H:3 * H].reshape(1, H).astype(f32)
    d["wr"] = np.ascontiguousarray(
        router_w.T.reshape(HC, 128, 8).transpose(1, 0, 2).reshape(128, HC * 8)).astype(f32)
    Dcat = down_w.reshape(EI, H)
    d["wd"] = np.ascontiguousarray(
        Dcat.T.reshape(HC, 128, EI).transpose(1, 0, 2).reshape(128, HC * EI)).astype(f32)
    Ucat = up_w.transpose(1, 0, 2).reshape(H, EI)
    d["wu"] = np.ascontiguousarray(Ucat.T).astype(f32)
    cst = np.zeros((128, 256), dtype=BF)
    cst[:, 0:128] = np.eye(128, dtype=BF)
    sel = np.zeros((8, 128), dtype=BF)
    for e in range(8):
        sel[e, e * 16:(e + 1) * 16] = 1
    cst[0:8, 128:256] = sel
    d["cst"] = cst
    d["cstf"] = np.eye(128, dtype=f32)
    return d


from concourse.bass_utils import run_bass_kernel_spmd


def kernel(L0_attn_hidden_states, L0_attn_w_qkv, L0_attn_b_qkv, L0_attn_w_out,
           L0_attn_b_out, L0_ffn_router_w, L0_ffn_down_w, L0_ffn_up_w):
    x = np.asarray(L0_attn_hidden_states)
    w_qkv = np.asarray(L0_attn_w_qkv)
    b_qkv = np.asarray(L0_attn_b_qkv)
    w_out = np.asarray(L0_attn_w_out)
    b_out = np.asarray(L0_attn_b_out)
    router_w = np.asarray(L0_ffn_router_w)
    down_w = np.asarray(L0_ffn_down_w)
    up_w = np.asarray(L0_ffn_up_w)

    B, Sfull, H = x.shape
    cfg = Cfg(NH=16, S=4096, T=2048, QT=512, n_cores=8)
    nc = build(cfg)

    wdict = prep_inputs(cfg, np.zeros((cfg.T, H), np.float32), w_qkv, b_qkv,
                        w_out, b_out, router_w, down_w, up_w)
    del wdict["xT"]

    in_maps = []
    for c in range(8):
        b, hf = c // 2, c % 2
        m = dict(wdict)
        m["xT"] = np.ascontiguousarray(x[b, hf * cfg.T:(hf + 1) * cfg.T, :].T.astype(np.float32))
        in_maps.append(m)

    res = run_bass_kernel_spmd(nc, in_maps, core_ids=list(range(8)))

    out = np.empty((B, Sfull, H), dtype=np.float32)
    for c in range(8):
        b, hf = c // 2, c % 2
        out[b, hf * cfg.T:(hf + 1) * cfg.T, :] = res.results[c]["outT"].T
    return out
